# version 08
# speedup vs baseline: 1.1745x; 1.0869x over previous
from contextlib import ExitStack

import numpy as np
import concourse.bass as bass
import concourse.mybir as mybir
from concourse.bass_utils import run_bass_kernel_spmd

F32 = mybir.dt.float32
BF16 = mybir.dt.bfloat16
I32 = mybir.dt.int32
AF = mybir.ActivationFunctionType
ALU = mybir.AluOpType
AX = mybir.AxisListType.X
AXC = mybir.AxisListType.C

T = 2048
D = 1024
DI = 2048
NS = 16
NCC = DI // 128
NDC = D // 128
TQ = 512
NQ = T // TQ
FI = 2816
NIC = FI // 128
NT = T // 128
L4 = 4 * (TQ + 1)
RB = 16384.0


class B:

    def __init__(self, nc):
        self.nc = nc
        self.ops = {"sync": [], "gpsimd": [], "vector": [], "scalar": [], "tensor": []}
        self.es = ExitStack()
        self.sems = {}
        self.maxw = {}

    def dma_fence(self, eng, sname):
        v = self.maxw.get(sname, 0)
        if v > 0:
            h = self.sems[sname][0]
            self.ops[eng].append(lambda e, h=h, v=v: e.wait_ge(h, v))

    def sem(self, name, step=1):
        h = self.es.enter_context(self.nc.semaphore(name))
        self.sems[name] = [h, step, 0]
        return name

    def wait(self, eng, sname, val):
        h = self.sems[sname][0]
        if val > 0:
            self.maxw[sname] = max(self.maxw.get(sname, 0), val)
            self.ops[eng].append(lambda e, h=h, v=val: e.wait_ge(h, v))

    OWN = {"vector": "DV", "scalar": "AC", "gpsimd": "GC"}

    def op(self, eng, fn, inc=None, nosync=False):
        own = self.OWN.get(eng)
        if own is not None and own in self.sems and not nosync:
            if eng != "scalar":
                self.wait(eng, own, self.sems[own][2])
            if inc is None:
                inc = own
        if inc is None:
            self.ops[eng].append(fn)
            return None
        h, step, _ = self.sems[inc]
        def g(e, fn=fn, h=h, step=step):
            fn(e).then_inc(h, step)
        self.ops[eng].append(g)
        self.sems[inc][2] += step
        return self.sems[inc][2]

    def v(self, sname):
        return self.sems[sname][2]

    def run(self, block):
        for eng in ("sync", "gpsimd", "vector", "scalar", "tensor"):
            ops = self.ops[eng]
            def body(e, ops=ops):
                for f in ops:
                    f(e)
            getattr(block, eng)(body)


def rkpm(ap, p=128):
    return ap.rearrange("(k p) m -> p k m", p=p)


def colv(ap):
    return ap.rearrange("(c p) o -> p (c o)", p=128)


def build_launch1(dbg=False, dbgq=0):
    nc = bass.Bass("TRN2", target_bir_lowering=False)

    def din(name, shape):
        return nc.dram_tensor(name, shape, F32, kind="ExternalInput").ap()

    x0_d = din("x0", [T, D])
    Win_d = din("W_in", [D, 2 * DI])
    convw_d = din("conv_w", [DI, 4])
    convb_d = din("conv_b", [128, NCC])
    Wxp_d = din("W_xproj", [DI, 32])
    Wdt_d = din("W_dt", [DI, DI])
    bdt_d = din("b_dt", [128, NCC])
    Alog_d = din("A_log", [DI, NS])
    Dp_d = din("D_param", [128, NCC])
    Wout_d = din("W_out", [DI, D])
    fWin_d = din("fWin", [D, 64])
    fbin_d = din("fbin", [64, 1])
    fW1_d = din("fW1", [64, 16])
    fb1_d = din("fb1", [64, 16])
    fW2_d = din("fW2", [64, 16])
    fb2_d = din("fb2", [64, 1])
    fWout_d = din("fWout", [64, D])
    fbout_d = din("fbout", [1, D])

    x1_d = nc.dram_tensor("x1", [T, D], F32, kind="ExternalOutput").ap()
    if dbg:
        dbg_d = {nm: nc.dram_tensor("dbg_" + nm, sh, F32, kind="ExternalOutput").ap()
                 for nm, sh in [("xn0", [128, NDC, T]), ("xc", [128, NCC, TQ + 3]),
                                ("xs", [128, NCC, TQ]), ("dlt", [128, NCC, TQ]),
                                ("slz", [128, NCC, TQ]), ("ub", [128, NCC, TQ]),
                                ("y", [128, NCC, TQ]), ("bc", [32, TQ]),
                                ("gacc", [65, T]), ("ofm", [128, NDC, TQ]),
                                ("carry", [128, NS * NCC]), ("stat", [128, 8, 128]), ("stat2", [128, 8, 128])]}
    mrow_d = nc.dram_tensor("mrow", [128, 2 * NT], F32, kind="ExternalOutput").ap()
    m1_d = nc.dram_tensor("m1", [1, 1], F32, kind="ExternalOutput").ap()

    b = B(nc)
    es = b.es
    sb = lambda n, s, d: es.enter_context(nc.sbuf_tensor(n, s, d))
    psal = lambda n, s: es.enter_context(nc.psum_tensor(n, s, F32))

    arA = sb("arA", [128, 16384], F32)
    x0t = arA[:].rearrange("p (a d) -> p a d", d=D)
    arAb = nc.alloc_sbuf_tensor_at("arAb", [128, 32768], BF16,
                                   offset=nc.lookup_mloc(arA).addr)
    dlt = arAb[:, 0 * 8192:1 * 8192].rearrange("p (c t) -> p c t", t=TQ)
    ub = arAb[:, 1 * 8192:2 * 8192].rearrange("p (c t) -> p c t", t=TQ)
    slz = arAb[:, 2 * 8192:3 * 8192].rearrange("p (c t) -> p c t", t=TQ)
    xsb = arAb[:, 3 * 8192:4 * 8192].rearrange("p (c t) -> p c t", t=TQ)
    x1sA = arA[:].rearrange("p (a d) -> p a d", d=1024)
    x1st = [x1sA[:, 0, :], x1sA[:, 1, :]]
    x0st = [x1sA[:, 2, :], x1sA[:, 3, :]]
    sq4 = x1sA[:, 4, :]
    xn1s = x1sA[:, 5, :]
    arB = sb("arB", [128, 16 * (TQ + 3)], BF16)
    xcb = arB[:].rearrange("p (c t) -> p c t", t=TQ + 3)
    ftmp = arB[0:64, 0:T]
    ftm2 = arB[0:64, T:2 * T]
    ofm = arB[:, 0:NDC * TQ].rearrange("p (a t) -> p a t", t=TQ)

    xn0 = sb("xn0", [128, NDC, T], BF16)
    carry = sb("carry", [128, NS * NCC], F32)
    bcrows = sb("bcrows", [32, TQ], BF16)
    bbt = sb("bbt", [128, TQ], BF16)
    cbt = sb("cbt", [128, TQ], BF16)
    dA2 = [sb(f"dA{i}", [128, L4], F32) for i in range(2)]
    dBp = sb("dBp", [128, L4], BF16)
    hhp = sb("hhp", [128, L4], BF16)
    tpp = sb("tpp", [128, L4], BF16)
    dA0b = nc.alloc_sbuf_tensor_at("dA0b", [128, 4104], BF16,
                                   offset=nc.lookup_mloc(dA2[0]).addr)
    xtm2 = [dA0b[:, 0:D], dA0b[:, D:2 * D]]
    sq1 = dA2[1][:, 0:D]
    wsl = [sb(f"wsl{i}", [128, 8, 512], BF16) for i in range(2)]
    wdt = [sb(f"wdt{i}", [128, NCC, 256], BF16) for i in range(2)]
    wxp = sb("wxp", [128, NCC, 32], BF16)
    wou = [sb(f"wou{i}", [128, NCC, 128], BF16) for i in range(2)]
    convw = sb("convw", [128, NCC, 4], F32)
    diag4 = [sb(f"diag4{i}", [128, 4, 128], BF16) for i in range(2)]
    ident = sb("ident", [128, 128], BF16)
    identf = sb("identf", [128, 32], F32)
    esel = [sb(f"esel{i}", [32, 128], BF16) for i in range(4)]
    ones32 = sb("ones32", [32, 128], BF16)
    ones1 = sb("ones1", [1, 128], BF16)
    Asb = sb("Asb", [128, NCC, NS], F32)
    Dsb = sb("Dsb", [128, NCC], F32)
    convb = sb("convb", [128, NCC], F32)
    bdt = sb("bdt", [128, NCC], F32)
    fwin = sb("fwin", [128, NDC, 64], BF16)
    sfw1 = sb("sfw1", [64, 16], F32)
    sfb1 = sb("sfb1", [64, 16], F32)
    sfw2 = sb("sfw2", [64, 16], F32)
    sfb2 = sb("sfb2", [64, 1], F32)
    sfbin = sb("sfbin", [64, 1], F32)
    fwoc = sb("fwoc", [65, D], BF16)
    gacc = sb("gacc", [65, T], BF16)
    statc = sb("statc", [128, 8, 128], F32)
    stat2 = statc
    m1c = sb("m1c", [1, 1], F32)
    marg = sb("marg", [128, NCC, 3], BF16)
    epsb = sb("epsb", [128, 1], F32)

    pb = [psal(f"pb{i}", [128, 512]) for i in range(5)]
    px = psal("px", [128, 512])
    tpx = es.enter_context(nc.psum_tensor("tpx", [128, 1024], BF16))
    tpy = es.enter_context(nc.psum_tensor("tpy", [128, 1024], BF16))

    b.sem("HW", 16)
    b.sem("GP", 16)
    b.sem("PE", 1)
    b.sem("AC", 1)
    b.sem("DV", 1)
    b.sem("GC", 1)
    b.sem("ST", 16)

    def gdma(out, in_):
        b.dma_fence("gpsimd", "GP")
        return b.op("gpsimd", lambda g, o=out, i=in_: g.dma_start(out=o, in_=i),
                    inc="GP")

    def hdma(out, in_, sem="HW"):
        b.dma_fence("sync", sem)
        return b.op("sync", lambda s, o=out, i=in_: s.dma_start(out=o, in_=i),
                    inc=sem)

    def gfn(f, inc=None):
        return b.op("gpsimd", f, inc=inc)

    def qbar():
        tgt = []
        m = b.op("vector", lambda e: e.engine_nop(), inc="DV")
        tgt.append(("DV", m))
        m = b.op("scalar", lambda e: e.activation(out=epsb[:], in_=epsb[:],
                                                  func=AF.Identity, bias=0.0,
                                                  scale=1.0), inc="AC")
        tgt.append(("AC", m))
        m = b.op("gpsimd", lambda e: e.engine_nop(), inc="GC")
        tgt.append(("GC", m))
        tgt.append(("PE", b.v("PE")))
        for eng in ("vector", "scalar", "gpsimd", "tensor", "sync"):
            for sem, m in tgt:
                b.wait(eng, sem, m)

    def dbg_dump(name, src):
        if not dbg:
            return
        b.wait("gpsimd", "AC", b.v("AC"))
        b.wait("gpsimd", "DV", b.v("DV"))
        mst = b.op("gpsimd", lambda g, o=dbg_d[name], i=src: g.dma_start(out=o, in_=i),
                   inc="GP")
        for eng in ("vector", "scalar", "tensor", "sync"):
            b.wait(eng, "GP", mst)

    gdma(Asb[:], Alog_d.rearrange("(c p) s -> p c s", p=128))
    gdma(Dsb[:], Dp_d[:, :])
    gdma(convb[:], convb_d[:, :])
    gdma(bdt[:], bdt_d[:, :])
    gdma(convw[:], convw_d.rearrange("(c p) k -> p c k", p=128))
    gdma(wxp[:], Wxp_d.rearrange("(c p) s -> p c s", p=128))
    gdma(fwin[:], fWin_d.rearrange("(c p) m -> p c m", p=128))
    gdma(sfw1[:], fW1_d[:, :])
    gdma(sfb1[:], fb1_d[:, :])
    gdma(sfw2[:], fW2_d[:, :])
    gdma(sfb2[:], fb2_d[:, :])
    gdma(sfbin[:], fbin_d[:, :])
    gdma(fwoc[0:64, :], fWout_d[:, :])
    params_ld = gdma(fwoc[64:65, :], fbout_d[:, :])
    x0_ld = hdma(x0t, x0_d.rearrange("(a p) d -> p a d", p=128))

    b.wait("gpsimd", "GP", params_ld)
    gfn(lambda g: g.memset(ident[:], 0.0))
    gfn(lambda g: g.affine_select(out=ident[:], in_=ident[:],
                                  compare_op=ALU.not_equal, fill=1.0, base=0,
                                  pattern=[[-1, 128]], channel_multiplier=1))
    gfn(lambda g: g.memset(ones1[:], 1.0))
    gfn(lambda g: g.memset(ones32[:], 1.0))
    gfn(lambda g: g.memset(identf[:], 0.0))
    gfn(lambda g: g.affine_select(out=identf[:], in_=identf[:],
                                  compare_op=ALU.not_equal, fill=1.0, base=0,
                                  pattern=[[-1, 32]], channel_multiplier=1))

    gfn(lambda g: g.memset(carry[:], 0.0))
    gfn(lambda g: g.memset(gacc[64:65, :], 1.0))
    gfn(lambda g: g.memset(statc[:], 0.0))
    gfn(lambda g: g.memset(marg[:], 0.0))
    gfn(lambda g: g.memset(epsb[:], 1e-5))
    gfn(lambda g: g.memset(dA2[0][:], 0.0))
    const_done = gfn(lambda g: g.engine_nop(), inc="GC")

    b.wait("scalar", "GP", params_ld)
    aexp = b.op("scalar", lambda sc: sc.activation(
        out=Asb[:], in_=Asb[:], func=AF.Exp, bias=0.0, scale=1.0), inc="AC")
    b.wait("vector", "AC", aexp)
    b.wait("vector", "GC", const_done)
    b.op("vector", lambda v: v.tensor_scalar(
        out=Asb[:], in0=Asb[:], scalar1=-1.0, scalar2=None, op0=ALU.mult),
        inc="DV")

    b.wait("scalar", "HW", x0_ld)
    b.wait("scalar", "GC", const_done)
    b.wait("vector", "HW", x0_ld)
    b.wait("vector", "GC", const_done)
    sq_red = {}
    for tt in range(NT):
        if tt > 0:
            b.wait("scalar", "DV", sq_red[tt - 1])
        m_sq = b.op("scalar", lambda sc, tt=tt: sc.activation(
            out=sq1, in_=x0t[:, tt, :], func=AF.Square, bias=0.0, scale=1.0),
            inc="AC")
        b.wait("vector", "AC", m_sq)
        sq_red[tt] = b.op("vector", lambda v, tt=tt: v.tensor_reduce(
            out=statc[:, 1, tt:tt + 1], in_=sq1, axis=AX, op=ALU.add), inc="DV")
        b.op("vector", lambda v, tt=tt: v.tensor_reduce(
            out=statc[:, 2, tt:tt + 1], in_=x0t[:, tt, :], axis=AX, op=ALU.add))
    m_syn = b.op("vector", lambda v: v.engine_nop(), inc="DV")
    b.wait("vector", "DV", m_syn)
    b.op("vector", lambda v: v.tensor_scalar(
        out=statc[:, 3, :], in0=statc[:, 2, :], scalar1=1.0 / D, scalar2=None,
        op0=ALU.mult))
    b.op("vector", lambda v: v.tensor_scalar(
        out=statc[:, 4, :], in0=statc[:, 1, :], scalar1=1.0 / D, scalar2=None,
        op0=ALU.mult))
    b.op("vector", lambda v: v.tensor_tensor(
        out=statc[:, 5, :], in0=statc[:, 3, :], in1=statc[:, 3, :], op=ALU.mult))
    m_var = b.op("vector", lambda v: v.tensor_tensor(
        out=statc[:, 5, :], in0=statc[:, 4, :], in1=statc[:, 5, :],
        op=ALU.subtract), inc="DV")
    b.wait("scalar", "DV", m_var)
    m_sd = b.op("scalar", lambda sc: sc.activation(
        out=statc[:, 6, :], in_=statc[:, 5, :], func=AF.Sqrt,
        bias=epsb[:, 0:1], scale=1.0), inc="AC")
    b.wait("vector", "AC", m_sd)
    b.op("vector", lambda v: v.reciprocal(out=statc[:, 7, :], in_=statc[:, 6, :]))
    b.op("vector", lambda v: v.tensor_tensor(
        out=statc[:, 0, :], in0=statc[:, 3, :], in1=statc[:, 7, :], op=ALU.mult))
    m_nb = b.op("vector", lambda v: v.tensor_scalar(
        out=statc[:, 0, :], in0=statc[:, 0, :], scalar1=-1.0, scalar2=None,
        op0=ALU.mult), inc="DV")
    b.wait("scalar", "DV", m_nb)
    ev1 = []
    for tt in range(NT):
        par = tt % 2
        m_xn = b.op("scalar", lambda sc, tt=tt, par=par: sc.activation(
            out=xtm2[par], in_=x0t[:, tt, :], func=AF.Identity,
            bias=statc[:, 0, tt:tt + 1], scale=statc[:, 7, tt:tt + 1]), inc="AC")
        pst = tpx if par == 0 else tpy
        b.wait("tensor", "AC", m_xn)
        if tt >= 2:
            b.wait("tensor", "AC", ev1[tt - 2])
        for dc in range(NDC):
            m_tr = b.op("tensor", lambda t, dc=dc, par=par, pst=pst: t.transpose(
                pst[:, dc * 128:(dc + 1) * 128],
                xtm2[par][:, dc * 128:(dc + 1) * 128], ident[:]),
                inc="PE" if dc == NDC - 1 else None)
        b.wait("scalar", "PE", m_tr)
        m_ev = b.op("scalar", lambda sc, tt=tt, pst=pst: sc.activation(
            out=xn0[:, :, tt * 128:(tt + 1) * 128],
            in_=pst[:, 0:1024], func=AF.Identity, bias=0.0,
            scale=1.0), inc="AC")
        ev1.append(m_ev)
    xn0_done = b.v("AC")
    dbg_dump("stat", statc[:])

    b.wait("gpsimd", "AC", xn0_done)
    gfn(lambda g: g.memset(dA2[0][:], 0.0))
    gfn(lambda g: g.memset(dBp[:], 0.0))
    gfn(lambda g: g.memset(hhp[:], 0.0))
    gfn(lambda g: g.memset(tpp[:], 0.0))
    bndz = gfn(lambda g: g.memset(dA2[1][:], 0.0), inc="GC")
    b.wait("scalar", "GC", bndz)
    b.wait("vector", "GC", bndz)

    b.wait("tensor", "AC", xn0_done)
    for t4 in range(4):
        for k in range(NDC):
            m_mm = b.op("tensor", lambda t, t4=t4, k=k: t.matmul(
                pb[t4][0:64, :], fwin[:, k, 0:64],
                xn0[:, k, t4 * 512:(t4 + 1) * 512],
                start=(k == 0), stop=(k == NDC - 1)),
                inc="PE" if k == NDC - 1 else None)
        b.wait("scalar", "PE", m_mm)
        b.op("scalar", lambda sc, t4=t4: sc.activation(
            out=gacc[0:64, t4 * 512:(t4 + 1) * 512], in_=pb[t4][0:64, :],
            func=AF.Identity, bias=sfbin[:, 0:1], scale=1.0),
            inc="AC" if t4 == 3 else None)
    spre_done = b.v("AC")

    b.wait("scalar", "AC", spre_done)
    for j in range(16):
        m_rl = b.op("scalar", lambda sc, j=j: sc.activation(
            out=ftmp, in_=gacc[0:64, :], func=AF.Relu,
            bias=sfb1[:, j:j + 1], scale=sfw1[:, j:j + 1]), inc="AC")
        b.wait("vector", "AC", m_rl)
        if j == 0:
            m_g = b.op("vector", lambda v: v.tensor_scalar(
                out=ftm2, in0=ftmp, scalar1=sfw2[:, 0:1], scalar2=None,
                op0=ALU.mult), inc="DV")
        else:
            b.op("vector", lambda v, j=j: v.tensor_scalar(
                out=ftmp, in0=ftmp, scalar1=sfw2[:, j:j + 1], scalar2=None,
                op0=ALU.mult))
            m_g = b.op("vector", lambda v: v.tensor_tensor(
                out=ftm2, in0=ftm2, in1=ftmp, op=ALU.add), inc="DV")
        if j < 15:
            b.wait("scalar", "DV", m_g)
    b.wait("scalar", "DV", b.v("DV"))
    gacc_done = b.op("scalar", lambda sc: sc.activation(
        out=gacc[0:64, :], in_=ftm2, func=AF.Identity, bias=sfb2[:, 0:1],
        scale=1.0), inc="AC")

    b.wait("gpsimd", "DV", b.v("DV"))
    b.wait("gpsimd", "AC", b.v("AC"))
    z2 = gfn(lambda g: g.memset(statc[:], 0.0), inc="GC")
    b.wait("vector", "GC", z2)

    x1_asm = {}
    sq4_red = {}
    x1_st_mark = {}
    x1_sq_mark = {}
    seg = [j * (TQ + 1) for j in range(4)]
    bnd = [j * (TQ + 1) + TQ for j in range(3)]

    def v3(ap):
        return ap.rearrange("p t -> p 1 t")

    for q in range(NQ):
        t0 = q * TQ
        for g8 in range(8):
            buf = g8 % 2
            b.wait("gpsimd", "PE", b.v("PE"))
            mark_w = gdma(wsl[buf][:], rkpm(Win_d)[:, :, g8 * 512:(g8 + 1) * 512])
            b.wait("tensor", "GP", mark_w)
            if q == 0 and g8 == 0:
                b.wait("tensor", "AC", gacc_done)
            b.wait("tensor", "AC", b.v("AC"))
            for k in range(NDC):
                for m in range(4):
                    m_mm = b.op("tensor", lambda t, buf=buf, k=k, m=m, t0=t0: t.matmul(
                        pb[m][:], wsl[buf][:, k, m * 128:(m + 1) * 128],
                        xn0[:, k, t0:t0 + TQ],
                        start=(k == 0), stop=(k == NDC - 1)),
                        inc="PE" if (k == NDC - 1 and m == 3) else None)
            b.wait("scalar", "PE", m_mm)
            for m in range(4):
                if g8 < 4:
                    cc = g8 * 4 + m
                    b.op("scalar", lambda sc, cc=cc, m=m: sc.activation(
                        out=xcb[:, cc, 3:3 + TQ], in_=pb[m][:], func=AF.Identity,
                        bias=0.0, scale=1.0), inc="AC" if m == 3 else None)
                else:
                    cc = (g8 - 4) * 4 + m
                    b.op("scalar", lambda sc, cc=cc, m=m: sc.activation(
                        out=slz[:, cc, :], in_=pb[m][:], func=AF.Silu,
                        bias=0.0, scale=1.0), inc="AC" if m == 3 else None)
        xz_done = b.v("AC")
        if q == dbgq:
            dbg_dump("xc", xcb[:])
            dbg_dump("slz", slz[:])

        if q > 0:
            b.wait("gpsimd", "AC", xz_done)
            marg_restored = gfn(lambda g: g.tensor_copy(xcb[:, :, 0:3], marg[:]),
                                inc="GC")

        if q == 0:
            b.wait("gpsimd", "AC", gacc_done)
            marg_restored = gfn(lambda g: g.memset(xcb[:, :, 0:3], 0.0), inc="GC")
        b.wait("tensor", "AC", xz_done)
        b.wait("tensor", "GC", marg_restored)
        conv_ev = {}
        for cc in range(NCC):
            dpar = cc % 2
            bank = 3 + (cc % 2)
            if cc >= 2:
                b.wait("vector", "PE", conv_pe[cc - 2])
            b.wait("vector", "GC", const_done)
            for k in range(4):
                m_d4 = b.op("vector", lambda v, cc=cc, k=k, dpar=dpar: v.tensor_scalar(
                    out=diag4[dpar][:, k, :], in0=ident[:],
                    scalar1=convw[:, cc, k:k + 1], scalar2=None, op0=ALU.mult),
                    inc="DV" if k == 3 else None)
            b.wait("tensor", "DV", m_d4)
            if cc >= 2:
                b.wait("tensor", "AC", conv_ev[cc - 2])
            for k in range(4):
                m_mm = b.op("tensor", lambda t, cc=cc, k=k, dpar=dpar, bank=bank: t.matmul(
                    pb[bank][:], diag4[dpar][:, k, :],
                    xcb[:, cc, k:k + TQ], start=(k == 0), stop=(k == 3)),
                    inc="PE" if k == 3 else None)
            if cc == 0:
                conv_pe = {}
            conv_pe[cc] = m_mm
            b.wait("scalar", "PE", m_mm)
            m_e = b.op("scalar", lambda sc, cc=cc, bank=bank: sc.activation(
                out=xsb[:, cc, :], in_=pb[bank][:], func=AF.Silu,
                bias=convb[:, cc:cc + 1], scale=1.0), inc="AC")
            conv_ev[cc] = m_e
        conv_done = b.v("AC")
        if q == dbgq:
            dbg_dump("xs", xsb[:])
        if q < NQ - 1:
            b.wait("gpsimd", "AC", conv_done)
            gfn(lambda g: g.tensor_copy(marg[:], xcb[:, :, TQ:TQ + 3]), inc="GC")

        b.wait("tensor", "DV", b.v("DV"))
        b.wait("scalar", "ST", b.v("ST"))
        for g4 in range(8):
            pass
        for g2 in range(8):
            buf = g2 % 2
            b.wait("gpsimd", "PE", b.v("PE"))
            mark_w = gdma(wdt[buf][:], rkpm(Wdt_d)[:, :, g2 * 256:(g2 + 1) * 256])
            b.wait("tensor", "GP", mark_w)
            b.wait("tensor", "AC", b.v("AC"))
            for k in range(NCC):
                for m in range(2):
                    m_mm = b.op("tensor", lambda t, buf=buf, k=k, m=m: t.matmul(
                        pb[m][:], wdt[buf][:, k, m * 128:(m + 1) * 128],
                        xsb[:, k, :], start=(k == 0), stop=(k == NCC - 1)),
                        inc="PE" if (k == NCC - 1 and m == 1) else None)
            b.wait("scalar", "PE", m_mm)
            for m in range(2):
                cc = g2 * 2 + m
                b.op("scalar", lambda sc, cc=cc, m=m: sc.activation(
                    out=tpp[:, 0:TQ], in_=pb[m][:], func=AF.Exp,
                    bias=bdt[:, cc:cc + 1], scale=1.0))
                b.op("scalar", lambda sc, cc=cc: sc.activation(
                    out=dlt[:, cc, :], in_=tpp[:, 0:TQ], func=AF.Ln,
                    bias=1.0, scale=1.0), inc="AC" if m == 1 else None)
        dlt_done = b.v("AC")
        if q == dbgq:
            dbg_dump("dlt", dlt[:])

        b.wait("tensor", "AC", b.v("AC"))
        for k in range(NCC):
            m_mm = b.op("tensor", lambda t, k=k: t.matmul(
                pb[2][0:32, :], wxp[:, k, :], xsb[:, k, :],
                start=(k == 0), stop=(k == NCC - 1)),
                inc="PE" if k == NCC - 1 else None)
        xproj_pe = m_mm
        b.wait("scalar", "PE", xproj_pe)
        bcrows_done = b.op("scalar", lambda sc: sc.activation(
            out=bcrows[:, :], in_=pb[2][0:32, :], func=AF.Identity, bias=0.0,
            scale=1.0), inc="AC")

        b.wait("vector", "AC", dlt_done)
        for cc in range(NCC):
            b.op("vector", lambda v, cc=cc: v.tensor_tensor(
                out=ub[:, cc, :], in0=xsb[:, cc, :], in1=dlt[:, cc, :],
                op=ALU.mult))
        b.wait("vector", "PE", xproj_pe)
        for cc in range(NCC):
            m_yi = b.op("vector", lambda v, cc=cc: v.tensor_scalar(
                out=xsb[:, cc, :], in0=xsb[:, cc, :], scalar1=Dsb[:, cc:cc + 1],
                scalar2=None, op0=ALU.mult),
                inc="DV" if cc == NCC - 1 else None)
        yinit_done = m_yi
        if q == dbgq:
            dbg_dump("ub", ub[:])
            dbg_dump("bc", bcrows[:])

        b.wait("tensor", "AC", bcrows_done)
        if q > 0:
            b.wait("tensor", "DV", x1_asm[q * 4 - 1])
        for s in range(NS):
            pp = s % 2
            eB, eC = esel[pp * 2], esel[pp * 2 + 1]
            b.wait("vector", "PE", b.v("PE"))
            b.op("vector", lambda v, eB=eB, s=s: v.tensor_scalar(
                out=eB[:], in0=ones32[:], scalar1=identf[0:32, s:s + 1],
                scalar2=None, op0=ALU.mult))
            m_es = b.op("vector", lambda v, eC=eC, s=s: v.tensor_scalar(
                out=eC[:], in0=ones32[:], scalar1=identf[0:32, 16 + s:17 + s],
                scalar2=None, op0=ALU.mult), inc="DV")
            b.wait("tensor", "DV", m_es)
            if s > 0:
                b.wait("tensor", "AC", bc_ev)
                b.wait("tensor", "DV", last_y)
            m_pe1 = b.op("tensor", lambda t, eB=eB: t.matmul(
                pb[3][:], eB[:, :], bcrows[:, :], start=True, stop=True))
            m_pe2 = b.op("tensor", lambda t, eC=eC: t.matmul(
                px[:], eC[:, :], bcrows[:, :], start=True, stop=True), inc="PE")
            b.wait("scalar", "PE", m_pe2)
            if s > 0:
                b.wait("scalar", "DV", last_y)
            b.op("scalar", lambda sc: sc.activation(
                out=bbt[:, :], in_=pb[3][:], func=AF.Identity, bias=0.0, scale=1.0))
            bc_ev = b.op("scalar", lambda sc: sc.activation(
                out=cbt[:, :], in_=px[:], func=AF.Identity, bias=0.0, scale=1.0),
                inc="AC")
            for p4 in range(4):
                i = s * 4 + p4
                par = i % 2
                ccs = [p4 * 4 + j for j in range(4)]
                if i >= 2:
                    b.wait("scalar", "DV", scan_done[i - 2])
                for j, cc in enumerate(ccs):
                    m_dA = b.op("scalar", lambda sc, cc=cc, j=j, s=s, par=par: sc.activation(
                        out=dA2[par][:, seg[j]:seg[j] + TQ], in_=dlt[:, cc, :],
                        func=AF.Identity, bias=1.0, scale=Asb[:, cc, s:s + 1]),
                        inc="AC" if j == 3 else None)
                b.wait("vector", "AC", bc_ev)
                if i == 0:
                    b.wait("vector", "DV", yinit_done)
                b.op("vector", lambda v, p4=p4: v.tensor_tensor(
                    out=dBp[:].rearrange("p (a t) -> p a t", t=TQ + 1)[:, :, 0:TQ],
                    in0=ub[:, p4 * 4:p4 * 4 + 4, :],
                    in1=bbt[:].rearrange("p (o t) -> p o t", o=1).broadcast_to([128, 4, TQ]),
                    op=ALU.mult))
                ci = s * NCC + p4 * 4
                b.op("vector", lambda v, ci=ci: v.tensor_copy(
                    dBp[:].rearrange("p (a t) -> p a t", t=TQ + 1)[:, 0:3, TQ:TQ + 1],
                    carry[:, ci + 1:ci + 4].rearrange("p (c o) -> p c o", o=1)))
                b.wait("vector", "AC", m_dA)
                m_scan = b.op("vector", lambda v, par=par, ci=ci: v.tensor_tensor_scan(
                    out=hhp[:], data0=dA2[par][:], data1=dBp[:],
                    initial=carry[:, ci:ci + 1], op0=ALU.mult, op1=ALU.add))
                b.op("vector", lambda v: v.tensor_tensor(
                    out=tpp[:].rearrange("p (a t) -> p a t", t=TQ + 1)[:, :, 0:TQ],
                    in0=hhp[:].rearrange("p (a t) -> p a t", t=TQ + 1)[:, :, 0:TQ],
                    in1=cbt[:].rearrange("p (o t) -> p o t", o=1).broadcast_to([128, 4, TQ]),
                    op=ALU.mult))
                b.op("vector", lambda v, p4=p4: v.tensor_tensor(
                    out=xsb[:, p4 * 4:p4 * 4 + 4, :],
                    in0=xsb[:, p4 * 4:p4 * 4 + 4, :],
                    in1=tpp[:].rearrange("p (a t) -> p a t", t=TQ + 1)[:, :, 0:TQ],
                    op=ALU.add))
                m_y = b.op("vector", lambda v, ci=ci: v.tensor_copy(
                    carry[:, ci:ci + 4].rearrange("p (c o) -> p c o", o=1),
                    hhp[:].rearrange("p (a t) -> p a t", t=TQ + 1)[:, :, TQ - 1:TQ]),
                    inc="DV")
                if i == 0 and s == 0:
                    scan_done = {}
                scan_done[i] = m_y
            last_y = scan_done[s * 4 + 3]
        sloop_done = b.v("DV")

        for cc in range(NCC):
            m_ys = b.op("vector", lambda v, cc=cc: v.tensor_tensor(
                out=xsb[:, cc, :], in0=xsb[:, cc, :], in1=slz[:, cc, :],
                op=ALU.mult), inc="DV" if cc == NCC - 1 else None)
        y_done = m_ys

        qbar()
        if q == dbgq:
            dbg_dump("y", xsb[:])
            dbg_dump("carry", carry[:])
        b.wait("tensor", "DV", y_done)
        for m in range(NDC):
            buf = m % 2
            b.wait("gpsimd", "PE", b.v("PE"))
            mark_w = gdma(wou[buf][:], rkpm(Wout_d)[:, :, m * 128:(m + 1) * 128])
            b.wait("tensor", "GP", mark_w)
            bank = m % 4
            b.wait("tensor", "AC", b.v("AC"))
            for k in range(NCC):
                m_mm = b.op("tensor", lambda t, buf=buf, k=k, bank=bank: t.matmul(
                    pb[bank][:], wou[buf][:, k, :], xsb[:, k, :],
                    start=(k == 0), stop=(k == NCC - 1)),
                    inc="PE" if k == NCC - 1 else None)
            if m > 0:
                b.wait("tensor", "DV", ffn_add)
            m_mf = b.op("tensor", lambda t, m=m, t0=t0: t.matmul(
                px[:], fwoc[:, m * 128:(m + 1) * 128],
                gacc[:, t0:t0 + TQ], start=True, stop=True), inc="PE")
            b.wait("scalar", "PE", m_mm)
            m_ev = b.op("scalar", lambda sc, m=m, bank=bank: sc.activation(
                out=ofm[:, m, :], in_=pb[bank][:], func=AF.Identity, bias=0.0,
                scale=1.0), inc="AC")
            b.wait("vector", "PE", m_mf)
            b.wait("vector", "AC", m_ev)
            ffn_add = b.op("vector", lambda v, m=m: v.tensor_tensor(
                out=ofm[:, m, :], in0=ofm[:, m, :], in1=px[:], op=ALU.add),
                inc="DV")
        out_fm = b.v("AC")
        out_fm_dv = ffn_add
        if q == dbgq:
            dbg_dump("ofm", ofm[:])

        for sub in range(4):
            tt = q * 4 + sub
            par = sub % 2
            pst = tpx if par == 0 else tpy
            if tt >= 2:
                b.wait("sync", "DV", x1_asm[tt - 2])
            m_x0 = hdma(x0st[par], x0_d[tt * 128:(tt + 1) * 128, :])
            b.wait("tensor", "AC", out_fm)
            b.wait("tensor", "DV", out_fm_dv)
            if tt >= 2:
                b.wait("tensor", "DV", x1_asm[tt - 2])
            for dc in range(NDC):
                m_tr = b.op("tensor", lambda t, dc=dc, sub=sub, pst=pst: t.transpose(
                    pst[:, dc * 128:(dc + 1) * 128],
                    ofm[:, dc, sub * 128:(sub + 1) * 128], ident[:]),
                    inc="PE" if dc == NDC - 1 else None)
            b.wait("vector", "PE", m_tr)
            b.wait("vector", "HW", m_x0)
            if tt >= 2:
                b.wait("vector", "ST", b.v("ST"))
                b.wait("vector", "AC", x1_sq_mark[tt - 2])
            m_x1 = b.op("vector", lambda v, par=par, pst=pst: v.tensor_tensor(
                out=x1st[par], in0=pst[:, 0:1024],
                in1=x0st[par], op=ALU.add), inc="DV")
            x1_asm[tt] = m_x1
            b.wait("sync", "DV", m_x1)
            x1_st_mark[tt] = hdma(x1_d[tt * 128:(tt + 1) * 128, :], x1st[par],
                                  sem="ST")
            b.wait("scalar", "DV", m_x1)
            if tt > 0:
                b.wait("scalar", "DV", sq4_red[tt - 1])
            m_sq = b.op("scalar", lambda sc, par=par, tt=tt: sc.activation(
                out=sq4, in_=x1st[par], func=AF.Square, bias=0.0, scale=1.0),
                inc="AC")
            x1_sq_mark[tt] = m_sq
            b.wait("vector", "AC", m_sq)
            sq4_red[tt] = b.op("vector", lambda v, tt=tt: v.tensor_reduce(
                out=stat2[:, 1, tt:tt + 1], in_=sq4, axis=AX, op=ALU.add), inc="DV")
            b.op("vector", lambda v, par=par, tt=tt: v.tensor_reduce(
                out=stat2[:, 2, tt:tt + 1], in_=x1st[par], axis=AX, op=ALU.add))
            b.op("vector", lambda v, par=par, tt=tt: v.tensor_reduce(
                out=stat2[:, 3, tt:tt + 1], in_=x1st[par], axis=AX, op=ALU.max))
            b.op("vector", lambda v, par=par, tt=tt: v.tensor_reduce(
                out=stat2[:, 4, tt:tt + 1], in_=x1st[par], axis=AX, op=ALU.min))

    if dbg:
        dbg_dump("xn0", xn0[:])
        dbg_dump("gacc", gacc[:])
    m_syn2 = b.op("vector", lambda v: v.engine_nop(), inc="DV")
    b.wait("vector", "DV", m_syn2)
    b.op("vector", lambda v: v.tensor_scalar(
        out=stat2[:, 5, :], in0=stat2[:, 2, :], scalar1=1.0 / D, scalar2=None,
        op0=ALU.mult))
    b.op("vector", lambda v: v.tensor_scalar(
        out=stat2[:, 6, :], in0=stat2[:, 1, :], scalar1=1.0 / D, scalar2=None,
        op0=ALU.mult))
    b.op("vector", lambda v: v.tensor_tensor(
        out=stat2[:, 7, :], in0=stat2[:, 5, :], in1=stat2[:, 5, :], op=ALU.mult))
    m_var2 = b.op("vector", lambda v: v.tensor_tensor(
        out=stat2[:, 7, :], in0=stat2[:, 6, :], in1=stat2[:, 7, :],
        op=ALU.subtract), inc="DV")
    b.wait("scalar", "DV", m_var2)
    m_sd2 = b.op("scalar", lambda sc: sc.activation(
        out=stat2[:, 7, :], in_=stat2[:, 7, :], func=AF.Sqrt,
        bias=epsb[:, 0:1], scale=1.0), inc="AC")
    b.wait("vector", "AC", m_sd2)
    b.op("vector", lambda v: v.reciprocal(out=stat2[:, 6, :], in_=stat2[:, 7, :]))
    b.op("vector", lambda v: v.tensor_tensor(
        out=stat2[:, 3, :], in0=stat2[:, 3, :], in1=stat2[:, 5, :],
        op=ALU.subtract))
    b.op("vector", lambda v: v.tensor_tensor(
        out=stat2[:, 4, :], in0=stat2[:, 5, :], in1=stat2[:, 4, :],
        op=ALU.subtract))
    b.op("vector", lambda v: v.tensor_tensor(
        out=stat2[:, 3, :], in0=stat2[:, 3, :], in1=stat2[:, 4, :], op=ALU.max))
    m_z = b.op("vector", lambda v: v.tensor_tensor(
        out=stat2[:, 4, :], in0=stat2[:, 3, :], in1=stat2[:, 6, :],
        op=ALU.mult), inc="DV")
    b.wait("vector", "DV", m_z)
    m_m1c = b.op("vector", lambda v: v.tensor_reduce(
        out=stat2[:, 0, 0:1], in_=stat2[:, 4, 0:NT], axis=AX, op=ALU.max),
        inc="DV")
    b.wait("gpsimd", "DV", m_m1c)
    m_m1 = gfn(lambda g: g.tensor_reduce(out=m1c[:], in_=stat2[:, 0, 0:1],
                                         axis=AXC, op=ALU.max), inc="GC")
    if dbg:
        dbg_dump("stat2", stat2[:])
    b.wait("sync", "GC", m_m1)
    hdma(m1_d[:, :], m1c[:], sem="ST")
    b.wait("sync", "DV", b.v("DV"))
    hdma(mrow_d[:, 0:NT], stat2[:, 5, 0:NT], sem="ST")
    hdma(mrow_d[:, NT:2 * NT], stat2[:, 6, 0:NT], sem="ST")
    b.wait("sync", "ST", b.v("ST"))

    with b.es, nc.Block() as block:
        b.run(block)
    return nc




def build_launch2():
    nc = bass.Bass("TRN2", target_bir_lowering=False)

    def din(name, shape):
        return nc.dram_tensor(name, shape, F32, kind="ExternalInput").ap()

    x1_d = din("x1", [T, D])
    mrow_d = din("mrow", [128, 2 * NT])
    m1v_d = din("m1v", [1, 8])
    clipx_d = din("clip_x", [1, 1])
    Wz_d = [din("W_zf", [D, 512]), din("W_zs", [D, 512])]
    Wh_d = [din("W_hf", [D, 512]), din("W_hs", [D, 512])]
    bz_d = din("b_z", [128, 8])
    bh_d = din("b_h", [128, 8])
    Wg_d = din("W_g", [D, D])
    bg_d = din("b_g", [128, NDC])
    wg_d = din("w_gate", [FI, D])
    wu_d = din("w_up", [FI, D])

    x1p_d = nc.dram_tensor("x1p", [T, D], F32, kind="ExternalOutput").ap()
    h_d = nc.dram_tensor("hact", [128, NIC, T], BF16, kind="ExternalOutput").ap()
    m2_d = nc.dram_tensor("m2", [1, 1], F32, kind="ExternalOutput").ap()

    b = B(nc)
    es = b.es
    sb = lambda n, s, d: es.enter_context(nc.sbuf_tensor(n, s, d))
    psal = lambda n, s: es.enter_context(nc.psum_tensor(n, s, F32))

    qx = sb("qx", [128, NDC, T], BF16)
    xn1b = sb("xn1b", [128, NDC, T], BF16)
    hcm = sb("hcm", [128, 8, T], BF16)
    zab = sb("zab", [128, 3 * T], BF16)
    zbuf = zab[:, 0:T]
    abuf = zab[:, T:2 * T]
    bbuf = zab[:, 2 * T:3 * T]
    aofm = nc.alloc_sbuf_tensor_at("aofm", [128, NDC, TQ], BF16,
                                   offset=nc.lookup_mloc(zab).addr)
    hbuf = [sb(f"hbuf{i}", [128, T], BF16) for i in range(2)]
    wsl = [sb(f"wsl{i}", [128, 8, 512], BF16) for i in range(2)]
    wgsl = sb("wgsl", [128, 8, D], BF16)
    wqsl = [sb(f"wqsl{i}", [128, D], F32) for i in range(2)]
    qs1 = sb("qs1", [128, D], F32)
    qsi = sb("qsi", [128, D], I32)
    qmk = sb("qmk", [128, D], F32)
    sc1 = sb("sc1", [128, D], F32)
    qg = sb("qg", [128, D], BF16)
    qu = sb("qu", [128, D], BF16)
    wqgT = sb("wqgT", [128, NDC, 128], BF16)
    wquT = sb("wquT", [128, NDC, 128], BF16)
    x1t = [sb(f"x1t{i}", [128, D], F32) for i in range(2)]
    outb = [sb(f"outb{i}", [128, D], F32) for i in range(2)]
    scb = [sb(f"scb{i}", [128, D], BF16) for i in range(2)]
    th = [sb(f"th{i}", [128, 512], BF16) for i in range(2)]
    panel = sb("panel", [128, 8, 64], F32)
    rowp = sb("rowp", [1, 128], F32)
    rowq = sb("rowq", [1, 128], F32)
    recipx = sb("recipx", [128, 1], F32)
    sxc127 = sb("sxc127", [128, 1], F32)
    rmxg = sb("rmxg", [128, 1], F32)
    rmxu = sb("rmxu", [128, 1], F32)
    tqg = sb("tqg", [128, 1], F32)
    tqu = sb("tqu", [128, 1], F32)
    rcg = sb("rcg", [128, 1], F32)
    rcu = sb("rcu", [128, 1], F32)
    bz = sb("bz", [128, 8], F32)
    bh = sb("bh", [128, 8], F32)
    nbz = sb("nbz", [128, 8], F32)
    bg = sb("bg", [128, NDC], F32)
    ident = sb("ident", [128, 128], BF16)
    onesf = sb("onesf", [1, 128], F32)
    m2c = sb("m2c", [1, 1], F32)

    pb = [psal(f"pb{i}", [128, 512]) for i in range(4)]
    px = psal("px", [128, 512])
    tpx = es.enter_context(nc.psum_tensor("tpx", [128, 1024], BF16))
    tpy = es.enter_context(nc.psum_tensor("tpy", [128, 1024], BF16))

    b.sem("HW", 16)
    b.sem("GP", 16)
    b.sem("PE", 1)
    b.sem("AC", 1)
    b.sem("DV", 1)
    b.sem("GC", 1)
    b.sem("ST", 16)

    def gdma(out, in_):
        b.dma_fence("gpsimd", "GP")
        return b.op("gpsimd", lambda g, o=out, i=in_: g.dma_start(out=o, in_=i),
                    inc="GP")

    def hdma(out, in_, sem="HW"):
        b.dma_fence("sync", sem)
        return b.op("sync", lambda s, o=out, i=in_: s.dma_start(out=o, in_=i),
                    inc=sem)

    def gfn(f, inc=None):
        return b.op("gpsimd", f, inc=inc)

    gfn(lambda g: g.memset(ident[:], 0.0))
    gfn(lambda g: g.affine_select(out=ident[:], in_=ident[:],
                                  compare_op=ALU.not_equal, fill=1.0, base=0,
                                  pattern=[[-1, 128]], channel_multiplier=1))
    gfn(lambda g: g.memset(onesf[:], 1.0))
    gfn(lambda g: g.memset(panel[:], 0.0))
    gfn(lambda g: g.memset(rowp[:], 0.0))
    const_done = gfn(lambda g: g.memset(rowq[:], 0.0), inc="GC")
    b.wait("gpsimd", "GC", const_done)
    gdma(panel[:, 0, 0:NT], mrow_d[:, 0:NT])
    gdma(panel[:, 1, 0:NT], mrow_d[:, NT:2 * NT])
    gdma(rowp[0:1, 0:8], m1v_d[:, :])
    gdma(rowq[0:1, 0:1], clipx_d[:, :])
    gdma(bz[:], bz_d[:, :])
    gdma(bh[:], bh_d[:, :])
    gdma(bg[:], bg_d[:, :])
    m_wg = gdma(wgsl[:], rkpm(Wg_d))
    loads_done = b.v("GP")

    b.wait("vector", "GP", loads_done)
    b.wait("vector", "GC", const_done)
    b.wait("tensor", "GC", const_done)
    b.op("vector", lambda v: v.tensor_reduce(
        out=rowp[0:1, 8:9], in_=rowp[0:1, 0:8], axis=AX, op=ALU.max))
    b.op("vector", lambda v: v.tensor_scalar(
        out=rowp[0:1, 16:32], in0=onesf[0:1, 0:16], scalar1=rowp[0:1, 8:9],
        scalar2=None, op0=ALU.mult))
    m_rq = b.op("vector", lambda v: v.tensor_scalar(
        out=rowq[0:1, 16:32], in0=onesf[0:1, 0:16], scalar1=rowq[0:1, 0:1],
        scalar2=None, op0=ALU.mult))
    b.wait("scalar", "DV", m_rq)
    m_sg = b.op("scalar", lambda sc: sc.activation(
        out=rowq[0:1, 32:48], in_=rowq[0:1, 16:32], func=AF.Sigmoid, bias=0.0,
        scale=1.0))
    b.wait("vector", "AC", m_sg)
    b.op("vector", lambda v: v.tensor_tensor(
        out=rowp[0:1, 32:48], in0=rowp[0:1, 16:32], in1=rowq[0:1, 32:48],
        op=ALU.mult))
    b.op("vector", lambda v: v.tensor_scalar(
        out=rowp[0:1, 48:64], in0=rowp[0:1, 32:48], scalar1=1.0 / 127.0,
        scalar2=None, op0=ALU.mult))
    b.op("vector", lambda v: v.reciprocal(
        out=rowp[0:1, 64:80], in_=rowp[0:1, 48:64]))
    m_row = b.op("vector", lambda v: v.tensor_scalar(
        out=rowp[0:1, 80:96], in0=rowp[0:1, 48:64], scalar1=1.0 / 127.0,
        scalar2=None, op0=ALU.mult))
    b.wait("tensor", "DV", m_row)
    b.op("tensor", lambda t: t.matmul(
        px[:, 0:1], onesf[:, :], rowp[0:1, 64:65], start=True, stop=True))
    m_bc = b.op("tensor", lambda t: t.matmul(
        px[:, 1:2], onesf[:, :], rowp[0:1, 80:81], start=True, stop=True),
        inc="PE")
    b.wait("vector", "PE", m_bc)
    b.op("vector", lambda v: v.tensor_copy(recipx[:], px[:, 0:1]))
    m_bcv = b.op("vector", lambda v: v.tensor_copy(sxc127[:], px[:, 1:2]))
    b.op("vector", lambda v: v.tensor_tensor(
        out=panel[:, 2, :], in0=panel[:, 0, :], in1=panel[:, 1, :],
        op=ALU.mult))
    b.op("vector", lambda v: v.tensor_scalar(
        out=panel[:, 2, :], in0=panel[:, 2, :], scalar1=-1.0, scalar2=None,
        op0=ALU.mult))
    m_nbz = b.op("vector", lambda v: v.tensor_scalar(
        out=nbz[:], in0=bz[:], scalar1=-1.0, scalar2=None, op0=ALU.mult))

    b.wait("scalar", "DV", m_nbz)
    xn_mark = {}
    q_mark = {}
    tr_mark = {}
    ev_mark = {}
    for tt in range(NT):
        par = tt % 2
        if tt >= 2:
            b.wait("sync", "AC", xn_mark[tt - 2])
        m_x1 = hdma(x1t[par][:], x1_d[tt * 128:(tt + 1) * 128, :])
        b.wait("scalar", "HW", m_x1)
        if tt >= 1:
            b.wait("scalar", "DV", q_mark[tt - 1])
        m_xn = b.op("scalar", lambda sc, par=par, tt=tt: sc.activation(
            out=sc1[:], in_=x1t[par][:], func=AF.Identity,
            bias=panel[:, 2, tt:tt + 1], scale=panel[:, 1, tt:tt + 1]))
        xn_mark[tt] = m_xn
        if tt >= 1:
            b.wait("scalar", "PE", tr_mark[tt - 1])
        m_cb = b.op("scalar", lambda sc: sc.activation(
            out=scb[0][:], in_=sc1[:], func=AF.Identity, bias=0.0, scale=1.0))
        b.wait("vector", "AC", m_cb)
        if tt == 0:
            b.wait("vector", "DV", m_bcv)
        b.op("vector", lambda v: v.tensor_scalar(
            out=qs1[:], in0=sc1[:], scalar1=recipx[:, 0:1], scalar2=RB + 0.5,
            op0=ALU.mult, op1=ALU.add))
        b.op("vector", lambda v: v.tensor_copy(qsi[:], qs1[:]))
        b.op("vector", lambda v: v.tensor_tensor(
            out=qmk[:], in0=qsi[:], in1=qs1[:], op=ALU.is_gt))
        b.op("vector", lambda v: v.tensor_tensor(
            out=qs1[:], in0=qsi[:], in1=qmk[:], op=ALU.subtract))
        b.op("vector", lambda v: v.tensor_scalar(
            out=qs1[:], in0=qs1[:], scalar1=-RB, scalar2=127.0, op0=ALU.add,
            op1=ALU.min))
        m_q = b.op("vector", lambda v: v.tensor_scalar(
            out=scb[1][:], in0=qs1[:], scalar1=-128.0, scalar2=None,
            op0=ALU.max))
        q_mark[tt] = m_q
        b.wait("tensor", "AC", m_cb)
        b.wait("tensor", "DV", m_q)
        if tt >= 1:
            b.wait("tensor", "AC", ev_mark[tt - 1][0])
            b.wait("tensor", "DV", ev_mark[tt - 1][1])
        for dc in range(NDC):
            b.op("tensor", lambda t, dc=dc: t.transpose(
                tpx[:, dc * 128:(dc + 1) * 128],
                scb[0][:, dc * 128:(dc + 1) * 128], ident[:]))
            m_tr = b.op("tensor", lambda t, dc=dc: t.transpose(
                tpy[:, dc * 128:(dc + 1) * 128],
                scb[1][:, dc * 128:(dc + 1) * 128], ident[:]),
                inc="PE" if dc == NDC - 1 else None)
        tr_mark[tt] = m_tr
        b.wait("scalar", "PE", m_tr)
        b.wait("vector", "PE", m_tr)
        m_e1 = b.op("scalar", lambda sc, tt=tt: sc.activation(
            out=xn1b[:, :, tt * 128:(tt + 1) * 128], in_=tpx[:, 0:1024],
            func=AF.Identity, bias=0.0, scale=1.0))
        m_e2 = b.op("vector", lambda v, tt=tt: v.tensor_copy(
            qx[:, :, tt * 128:(tt + 1) * 128], tpy[:, 0:1024]))
        ev_mark[tt] = (m_e1, m_e2)
    p2_ac = b.v("AC")
    p2_dv = b.v("DV")

    scan_mark = {}
    for p in range(2):
        if p == 1:
            b.wait("gpsimd", "PE", b.v("PE"))
        mz = gdma(wsl[0][:], rkpm(Wz_d[p]))
        mh = gdma(wsl[1][:], rkpm(Wh_d[p]))
        b.wait("tensor", "GP", mh)
        if p == 0:
            b.wait("tensor", "AC", p2_ac)
            b.wait("tensor", "DV", p2_dv)
        for m in range(4):
            fc = p * 4 + m
            b.wait("tensor", "AC", b.v("AC"))
            b.wait("tensor", "DV", b.v("DV"))
            for k in range(NDC):
                for t4 in range(4):
                    m_mmz = b.op("tensor", lambda t, k=k, m=m, t4=t4: t.matmul(
                        pb[t4][:], wsl[0][:, k, m * 128:(m + 1) * 128],
                        xn1b[:, k, t4 * 512:(t4 + 1) * 512],
                        start=(k == 0), stop=(k == NDC - 1)),
                        inc="PE" if (k == NDC - 1 and t4 == 3) else None)
            b.wait("scalar", "PE", m_mmz)
            if fc >= 1:
                b.wait("scalar", "DV", scan_mark[fc - 1])
            for t4 in range(4):
                b.op("scalar", lambda sc, t4=t4, fc=fc: sc.activation(
                    out=zbuf[:, t4 * 512:(t4 + 1) * 512], in_=pb[t4][:],
                    func=AF.Sigmoid, bias=bz[:, fc:fc + 1], scale=1.0))
                m_za = b.op("scalar", lambda sc, t4=t4, fc=fc: sc.activation(
                    out=abuf[:, t4 * 512:(t4 + 1) * 512], in_=pb[t4][:],
                    func=AF.Sigmoid, bias=nbz[:, fc:fc + 1], scale=-1.0))
            b.wait("tensor", "AC", m_za)
            for k in range(NDC):
                for t4 in range(4):
                    m_mmh = b.op("tensor", lambda t, k=k, m=m, t4=t4: t.matmul(
                        pb[t4][:], wsl[1][:, k, m * 128:(m + 1) * 128],
                        xn1b[:, k, t4 * 512:(t4 + 1) * 512],
                        start=(k == 0), stop=(k == NDC - 1)),
                        inc="PE" if (k == NDC - 1 and t4 == 3) else None)
            b.wait("vector", "PE", m_mmh)
            b.wait("vector", "AC", m_za)
            for t4 in range(4):
                b.op("vector", lambda v, t4=t4, fc=fc: v.scalar_tensor_tensor(
                    out=bbuf[:, t4 * 512:(t4 + 1) * 512], in0=pb[t4][:],
                    scalar=bh[:, fc:fc + 1],
                    in1=zbuf[:, t4 * 512:(t4 + 1) * 512],
                    op0=ALU.add, op1=ALU.mult))
            m_scan = b.op("vector", lambda v, fc=fc: v.tensor_tensor_scan(
                out=hcm[:, fc, :], data0=abuf[:], data1=bbuf[:],
                initial=0.0, op0=ALU.mult, op1=ALU.add))
            scan_mark[fc] = m_scan
    attn_dv = b.v("DV")

    aofm_mark = {}
    th_mark = {}
    outb_mark = {}
    asm_tr = {}
    for t4 in range(4):
        b.wait("tensor", "GP", loads_done)
        if t4 == 0:
            b.wait("tensor", "DV", attn_dv)
        for m in range(8):
            bank = m % 4
            if m >= 4:
                b.wait("tensor", "DV", aofm_mark[(t4, m - 4)])
                b.wait("tensor", "AC", th_mark[(t4, m - 4)])
            for k in range(NDC):
                m_mm = b.op("tensor", lambda t, k=k, m=m, t4=t4, bank=bank:
                            t.matmul(
                    pb[bank][:], wgsl[:, k, m * 128:(m + 1) * 128],
                    hcm[:, k, t4 * 512:(t4 + 1) * 512],
                    start=(k == 0), stop=(k == NDC - 1)),
                    inc="PE" if k == NDC - 1 else None)
            b.wait("scalar", "PE", m_mm)
            if m >= 2:
                b.wait("scalar", "DV", aofm_mark[(t4, m - 2)])
            m_th = b.op("scalar", lambda sc, m=m, bank=bank: sc.activation(
                out=th[m % 2][:], in_=pb[bank][:], func=AF.Sigmoid,
                bias=bg[:, m:m + 1], scale=1.0))
            th_mark[(t4, m)] = m_th
            b.wait("vector", "AC", m_th)
            if t4 >= 1 and m == 0:
                b.wait("vector", "PE", asm_tr[t4 - 1])
            m_ao = b.op("vector", lambda v, m=m, t4=t4: v.tensor_tensor(
                out=aofm[:, m, :], in0=th[m % 2][:],
                in1=hcm[:, m, t4 * 512:(t4 + 1) * 512], op=ALU.mult))
            aofm_mark[(t4, m)] = m_ao
        ao_dv = b.v("DV")
        for sub in range(4):
            tt = t4 * 4 + sub
            par = sub % 2
            pst = tpx if par == 0 else tpy
            if tt >= 2:
                b.wait("sync", "DV", outb_mark[tt - 2])
            else:
                b.wait("sync", "AC", p2_ac)
                b.wait("sync", "DV", p2_dv)
            m_x1 = hdma(x1t[par][:], x1_d[tt * 128:(tt + 1) * 128, :])
            b.wait("tensor", "DV", ao_dv)
            if tt >= 2:
                b.wait("tensor", "DV", outb_mark[tt - 2])
            for dc in range(NDC):
                m_tr = b.op("tensor", lambda t, dc=dc, sub=sub, pst=pst:
                            t.transpose(
                    pst[:, dc * 128:(dc + 1) * 128],
                    aofm[:, dc, sub * 128:(sub + 1) * 128], ident[:]),
                    inc="PE" if dc == NDC - 1 else None)
            asm_tr[t4] = m_tr
            b.wait("vector", "PE", m_tr)
            b.wait("vector", "HW", m_x1)
            if tt >= 2:
                b.wait("vector", "ST", b.v("ST"))
            m_ob = b.op("vector", lambda v, par=par, pst=pst: v.tensor_tensor(
                out=outb[par][:], in0=pst[:, 0:1024], in1=x1t[par][:],
                op=ALU.add))
            outb_mark[tt] = m_ob
            b.wait("sync", "DV", m_ob)
            hdma(x1p_d[tt * 128:(tt + 1) * 128, :], outb[par][:], sem="ST")

    w_mark = {}
    qdone = {}
    hstore = {}
    wtr_ev = {}
    ffn_hist = []
    for ic in range(NIC):
        if ic >= 1:
            b.wait("sync", "DV", qdone[ic - 1])
        mw1 = hdma(wqsl[0][:], wg_d[ic * 128:(ic + 1) * 128, :])
        mw2 = hdma(wqsl[1][:], wu_d[ic * 128:(ic + 1) * 128, :])
        w_mark[ic] = mw2
        b.wait("scalar", "HW", mw2)
        if ic >= 1:
            b.wait("scalar", "DV", qdone[ic - 1])
        m_ab1 = b.op("scalar", lambda sc: sc.activation(
            out=qs1[:], in_=wqsl[0][:], func=AF.Abs, bias=0.0, scale=1.0))
        b.wait("vector", "AC", m_ab1)
        m_rx1 = b.op("vector", lambda v: v.tensor_reduce(
            out=rmxg[:], in_=qs1[:], axis=AX, op=ALU.max))
        b.wait("scalar", "DV", m_rx1)
        m_ab2 = b.op("scalar", lambda sc: sc.activation(
            out=qs1[:], in_=wqsl[1][:], func=AF.Abs, bias=0.0, scale=1.0))
        b.wait("vector", "AC", m_ab2)
        b.op("vector", lambda v: v.tensor_reduce(
            out=rmxu[:], in_=qs1[:], axis=AX, op=ALU.max))
        b.op("vector", lambda v, ic=ic: v.tensor_tensor(
            out=panel[:, 5, ic:ic + 1], in0=rmxg[:], in1=sxc127[:],
            op=ALU.mult))
        b.op("vector", lambda v, ic=ic: v.tensor_tensor(
            out=panel[:, 6, ic:ic + 1], in0=rmxu[:], in1=sxc127[:],
            op=ALU.mult))
        b.op("vector", lambda v: v.tensor_scalar(
            out=tqg[:], in0=rmxg[:], scalar1=1.0 / 127.0, scalar2=None,
            op0=ALU.mult))
        b.op("vector", lambda v: v.tensor_scalar(
            out=tqu[:], in0=rmxu[:], scalar1=1.0 / 127.0, scalar2=None,
            op0=ALU.mult))
        b.op("vector", lambda v: v.reciprocal(out=rcg[:], in_=tqg[:]))
        b.op("vector", lambda v: v.reciprocal(out=rcu[:], in_=tqu[:]))
        for j, (src, rc, dst) in enumerate(((wqsl[0], rcg, qg),
                                            (wqsl[1], rcu, qu))):
            if ic >= 1 and j == 0:
                b.wait("vector", "PE", wtr_ev[ic - 1])
            b.op("vector", lambda v, src=src, rc=rc: v.tensor_scalar(
                out=qs1[:], in0=src[:], scalar1=rc[:, 0:1], scalar2=RB + 0.5,
                op0=ALU.mult, op1=ALU.add))
            b.op("vector", lambda v: v.tensor_copy(qsi[:], qs1[:]))
            b.op("vector", lambda v: v.tensor_tensor(
                out=qmk[:], in0=qsi[:], in1=qs1[:], op=ALU.is_gt))
            b.op("vector", lambda v: v.tensor_tensor(
                out=qs1[:], in0=qsi[:], in1=qmk[:], op=ALU.subtract))
            b.op("vector", lambda v: v.tensor_scalar(
                out=qs1[:], in0=qs1[:], scalar1=-RB, scalar2=127.0,
                op0=ALU.add, op1=ALU.min))
            m_qw = b.op("vector", lambda v, dst=dst: v.tensor_scalar(
                out=dst[:], in0=qs1[:], scalar1=-128.0, scalar2=None,
                op0=ALU.max))
        qdone[ic] = m_qw
        b.wait("tensor", "DV", m_qw)
        if ic >= 1:
            b.wait("tensor", "AC", wev_ac)
            b.wait("tensor", "DV", wev_dv)
        for dc in range(NDC):
            b.op("tensor", lambda t, dc=dc: t.transpose(
                tpx[:, dc * 128:(dc + 1) * 128],
                qg[:, dc * 128:(dc + 1) * 128], ident[:]))
            m_wtr = b.op("tensor", lambda t, dc=dc: t.transpose(
                tpy[:, dc * 128:(dc + 1) * 128],
                qu[:, dc * 128:(dc + 1) * 128], ident[:]),
                inc="PE" if dc == NDC - 1 else None)
        wtr_ev[ic] = m_wtr
        b.wait("scalar", "PE", m_wtr)
        b.wait("vector", "PE", m_wtr)
        wev_ac = b.op("scalar", lambda sc: sc.activation(
            out=wqgT[:].rearrange("p a c -> p (a c)"), in_=tpx[:, 0:1024],
            func=AF.Identity, bias=0.0, scale=1.0))
        wev_dv = b.op("vector", lambda v: v.tensor_copy(
            wquT[:].rearrange("p a c -> p (a c)"), tpy[:, 0:1024]))
        b.wait("tensor", "AC", wev_ac)
        b.wait("tensor", "DV", wev_dv)
        for t4 in range(4):
            bg_bank = t4 % 2
            bu_bank = 2 + t4 % 2
            if len(ffn_hist) >= 2:
                b.wait("tensor", "AC", ffn_hist[-2][0])
                b.wait("tensor", "DV", ffn_hist[-2][1])
            for k in range(NDC):
                b.op("tensor", lambda t, k=k, t4=t4, bank=bg_bank: t.matmul(
                    pb[bank][:], wqgT[:, k, :],
                    qx[:, k, t4 * 512:(t4 + 1) * 512],
                    start=(k == 0), stop=(k == NDC - 1)))
            for k in range(NDC):
                m_mm = b.op("tensor", lambda t, k=k, t4=t4, bank=bu_bank:
                            t.matmul(
                    pb[bank][:], wquT[:, k, :],
                    qx[:, k, t4 * 512:(t4 + 1) * 512],
                    start=(k == 0), stop=(k == NDC - 1)),
                    inc="PE" if k == NDC - 1 else None)
            b.wait("scalar", "PE", m_mm)
            if ffn_hist:
                b.wait("scalar", "DV", ffn_hist[-1][1])
            b.op("scalar", lambda sc, ic=ic, bank=bg_bank: sc.activation(
                out=th[0][:], in_=pb[bank][:], func=AF.Silu, bias=0.0,
                scale=panel[:, 5, ic:ic + 1]))
            ev_up = b.op("scalar", lambda sc, ic=ic, bank=bu_bank:
                         sc.activation(
                out=th[1][:], in_=pb[bank][:], func=AF.Identity, bias=0.0,
                scale=panel[:, 6, ic:ic + 1]))
            b.wait("vector", "AC", ev_up)
            if ic >= 2 and t4 == 0:
                b.wait("vector", "GP", b.v("GP"))
            m_h = b.op("vector", lambda v, ic=ic, t4=t4: v.tensor_tensor(
                out=hbuf[ic % 2][:, t4 * 512:(t4 + 1) * 512], in0=th[0][:],
                in1=th[1][:], op=ALU.mult))
            ffn_hist.append((ev_up, m_h))
        b.wait("vector", "DV", m_h)
        b.op("vector", lambda v, ic=ic: v.tensor_reduce(
            out=panel[:, 3, ic:ic + 1], in_=hbuf[ic % 2][:], axis=AX,
            op=ALU.max))
        m_st = b.op("vector", lambda v, ic=ic: v.tensor_reduce(
            out=panel[:, 4, ic:ic + 1], in_=hbuf[ic % 2][:], axis=AX,
            op=ALU.min))
        b.wait("gpsimd", "DV", m_st)
        hstore[ic] = gdma(h_d[:, ic, :], hbuf[ic % 2][:])

    b.op("vector", lambda v: v.tensor_scalar(
        out=panel[:, 7, 0:NIC], in0=panel[:, 4, 0:NIC], scalar1=-1.0,
        scalar2=None, op0=ALU.mult))
    b.op("vector", lambda v: v.tensor_tensor(
        out=panel[:, 7, 0:NIC], in0=panel[:, 3, 0:NIC],
        in1=panel[:, 7, 0:NIC], op=ALU.max))
    m_m2r = b.op("vector", lambda v: v.tensor_reduce(
        out=panel[:, 7, 32:33], in_=panel[:, 7, 0:NIC], axis=AX, op=ALU.max))
    b.wait("gpsimd", "DV", m_m2r)
    m_m2 = gfn(lambda g: g.tensor_reduce(
        out=m2c[:], in_=panel[:, 7, 32:33], axis=AXC, op=ALU.max), inc="GC")
    b.wait("sync", "GC", m_m2)
    hdma(m2_d[:, :], m2c[:], sem="ST")
    b.wait("sync", "ST", b.v("ST"))
    b.wait("sync", "GP", b.v("GP"))
    b.wait("sync", "HW", b.v("HW"))

    with b.es, nc.Block() as block:
        b.run(block)
    return nc

def build_mega(dbg=False, dbgq=0):
    nc = bass.Bass("TRN2", target_bir_lowering=False)

    def din(name, shape):
        return nc.dram_tensor(name, shape, F32, kind="ExternalInput").ap()

    x0_d = din("x0", [T, D])
    Win_d = din("W_in", [D, 2 * DI])
    convw_d = din("conv_w", [DI, 4])
    convb_d = din("conv_b", [128, NCC])
    Wxp_d = din("W_xproj", [DI, 32])
    Wdt_d = din("W_dt", [DI, DI])
    bdt_d = din("b_dt", [128, NCC])
    Alog_d = din("A_log", [DI, NS])
    Dp_d = din("D_param", [128, NCC])
    Wout_d = din("W_out", [DI, D])
    fWin_d = din("fWin", [D, 64])
    fbin_d = din("fbin", [64, 1])
    fW1_d = din("fW1", [64, 16])
    fb1_d = din("fb1", [64, 16])
    fW2_d = din("fW2", [64, 16])
    fb2_d = din("fb2", [64, 1])
    fWout_d = din("fWout", [64, D])
    fbout_d = din("fbout", [1, D])
    clipx_d = din("clip_x", [1, 1])
    clipd_d = din("clip_down", [1, 1])
    Wz_d = [din("W_zf", [D, 512]), din("W_zs", [D, 512])]
    Wh_d = [din("W_hf", [D, 512]), din("W_hs", [D, 512])]
    bz_d = din("b_z", [128, 8])
    bh_d = din("b_h", [128, 8])
    Wg_d = din("W_g", [D, D])
    bg_d = din("b_g", [128, NDC])
    wg_d = din("w_gate", [FI, D])
    wu_d = din("w_up", [FI, D])
    wd_d = din("w_down", [D, FI])

    out_d = nc.dram_tensor("out", [T, D], F32, kind="ExternalOutput").ap()
    x1i_t = nc.dram_tensor("x1i", [T, D], F32)
    x1i_d = x1i_t.ap()
    x1pi_t = nc.dram_tensor("x1pi", [T, D], F32)
    x1pi_d = x1pi_t.ap()
    hi_t = nc.dram_tensor("hi", [128, NIC, T], BF16)
    hi_d = hi_t.ap()
    c1i_t = nc.dram_tensor("c1i", [1, 1], F32)
    c1o_t = nc.dram_tensor("c1o", [1, 1], F32)
    c2i_t = nc.dram_tensor("c2i", [1, 1], F32)
    c2o_t = nc.dram_tensor("c2o", [1, 1], F32)
    if dbg:
        dbg_d = {nm: nc.dram_tensor("dbg_" + nm, sh, F32, kind="ExternalOutput").ap()
                 for nm, sh in [("xn0", [128, NDC, T]), ("xc", [128, NCC, TQ + 3]),
                                ("xs", [128, NCC, TQ]), ("dlt", [128, NCC, TQ]),
                                ("slz", [128, NCC, TQ]), ("ub", [128, NCC, TQ]),
                                ("y", [128, NCC, TQ]), ("bc", [32, TQ]),
                                ("gacc", [65, T]), ("ofm", [128, NDC, TQ]),
                                ("carry", [128, NS * NCC]), ("stat", [128, 8, 128]),
                                ("stat2", [128, 8, 128])]}

    b = B(nc)
    es = b.es
    sb = lambda n, s, d: es.enter_context(nc.sbuf_tensor(n, s, d))
    psal = lambda n, s: es.enter_context(nc.psum_tensor(n, s, F32))

    carry = sb("carry", [128, NS * NCC], F32)
    bcrows = sb("bcrows", [32, TQ], BF16)
    bbt = sb("bbt", [128, TQ], BF16)
    cbt = sb("cbt", [128, TQ], BF16)
    convw = sb("convw", [128, NCC, 4], F32)
    diag4 = [sb(f"diag4{i}", [128, 4, 128], BF16) for i in range(2)]
    ident = sb("ident", [128, 128], BF16)
    identf = sb("identf", [128, 32], F32)
    esel = [sb(f"esel{i}", [32, 128], BF16) for i in range(4)]
    ones32 = sb("ones32", [32, 128], BF16)
    ones1 = sb("ones1", [1, 128], BF16)
    onesf = sb("onesf", [1, 128], F32)
    Asb = sb("Asb", [128, NCC, NS], F32)
    Dsb = sb("Dsb", [128, NCC], F32)
    convb = sb("convb", [128, NCC], F32)
    bdt = sb("bdt", [128, NCC], F32)
    sfw1 = sb("sfw1", [64, 16], F32)
    sfb1 = sb("sfb1", [64, 16], F32)
    sfw2 = sb("sfw2", [64, 16], F32)
    sfb2 = sb("sfb2", [64, 1], F32)
    sfbin = sb("sfbin", [64, 1], F32)
    fwoc = sb("fwoc", [65, D], BF16)
    statc = sb("statc", [128, 8, 128], F32)
    stat2 = statc
    panel = sb("panel", [128, 8, 64], F32)
    m1c = sb("m1c", [1, 1], F32)
    m2c = sb("m2c", [1, 1], F32)
    marg = sb("marg", [128, NCC, 3], BF16)
    epsb = sb("epsb", [128, 1], F32)
    rowp = sb("rowp", [1, 128], F32)
    rowq = sb("rowq", [1, 128], F32)
    recipx = sb("recipx", [128, 1], F32)
    sxc127 = sb("sxc127", [128, 1], F32)
    rmxg = sb("rmxg", [128, 1], F32)
    rmxu = sb("rmxu", [128, 1], F32)
    tqg = sb("tqg", [128, 1], F32)
    tqu = sb("tqu", [128, 1], F32)
    rcg = sb("rcg", [128, 1], F32)
    rcu = sb("rcu", [128, 1], F32)
    reciph = sb("reciph", [128, 1], F32)
    shc127 = sb("shc127", [128, 1], F32)
    bz = sb("bz", [128, 8], F32)
    bh = sb("bh", [128, 8], F32)
    nbz = sb("nbz", [128, 8], F32)
    bg = sb("bg", [128, NDC], F32)

    HEAPW = 48000
    heap = sb("heap", [128, HEAPW], F32)
    hbase = nc.lookup_mloc(heap).addr

    def mkph(tag):
        cur = [0]
        def alloc(name, shape, dt):
            elt = 2 if dt == BF16 else 4
            n = 1
            for s_ in shape[1:]:
                n *= s_
            off = (cur[0] + 31) & ~31
            cur[0] = off + n * elt
            assert cur[0] <= HEAPW * 4, f"heap overflow {tag}/{name}: {cur[0]}"
            return nc.alloc_sbuf_tensor_at(name, shape, dt, offset=hbase + off)
        return alloc

    ha1 = mkph("L1")
    arA = ha1("arA", [128, 16384], F32)
    x0t = arA[:].rearrange("p (a d) -> p a d", d=D)
    arAb = nc.alloc_sbuf_tensor_at("arAb", [128, 32768], BF16,
                                   offset=nc.lookup_mloc(arA).addr)
    dlt = arAb[:, 0 * 8192:1 * 8192].rearrange("p (c t) -> p c t", t=TQ)
    ub = arAb[:, 1 * 8192:2 * 8192].rearrange("p (c t) -> p c t", t=TQ)
    slz = arAb[:, 2 * 8192:3 * 8192].rearrange("p (c t) -> p c t", t=TQ)
    xsb = arAb[:, 3 * 8192:4 * 8192].rearrange("p (c t) -> p c t", t=TQ)
    x1sA = arA[:].rearrange("p (a d) -> p a d", d=1024)
    x1st = [x1sA[:, 0, :], x1sA[:, 1, :]]
    x0st = [x1sA[:, 2, :], x1sA[:, 3, :]]
    sq4 = x1sA[:, 4, :]
    xn1s = x1sA[:, 5, :]
    arB = ha1("arB", [128, 16 * (TQ + 3)], BF16)
    xcb = arB[:].rearrange("p (c t) -> p c t", t=TQ + 3)
    ftmp = arB[0:64, 0:T]
    ftm2 = arB[0:64, T:2 * T]
    ofm = arB[:, 0:NDC * TQ].rearrange("p (a t) -> p a t", t=TQ)
    xn0 = ha1("xn0", [128, NDC, T], BF16)
    dA2 = [ha1(f"dA{i}", [128, L4], F32) for i in range(2)]
    dBp = ha1("dBp", [128, L4], BF16)
    hhp = ha1("hhp", [128, L4], BF16)
    tpp = ha1("tpp", [128, L4], BF16)
    dA0b = nc.alloc_sbuf_tensor_at("dA0b", [128, 4104], BF16,
                                   offset=nc.lookup_mloc(dA2[0]).addr)
    xtm2 = [dA0b[:, 0:D], dA0b[:, D:2 * D]]
    sq1 = dA2[1][:, 0:D]
    wsl = [ha1(f"wsl{i}", [128, 8, 512], BF16) for i in range(2)]
    wdt = [ha1(f"wdt{i}", [128, NCC, 256], BF16) for i in range(2)]
    wxp = ha1("wxp", [128, NCC, 32], BF16)
    wou = [ha1(f"wou{i}", [128, NCC, 128], BF16) for i in range(2)]
    fwin = ha1("fwin", [128, NDC, 64], BF16)
    gacc = ha1("gacc", [65, T], BF16)

    ha2 = mkph("L2")
    qx = ha2("qx", [128, NDC, T], BF16)
    xn1b = ha2("xn1b", [128, NDC, T], BF16)
    hcm = ha2("hcm", [128, 8, T], BF16)
    zab = ha2("zab", [128, 3 * T], BF16)
    zbuf = zab[:, 0:T]
    abuf = zab[:, T:2 * T]
    bbuf = zab[:, 2 * T:3 * T]
    aofm = nc.alloc_sbuf_tensor_at("aofm", [128, NDC, TQ], BF16,
                                   offset=nc.lookup_mloc(zab).addr)
    wsl2 = [ha2(f"wsl2_{i}", [128, 8, 512], BF16) for i in range(2)]
    wgsl = nc.alloc_sbuf_tensor_at("wgsl", [128, 8, D], BF16,
                                   offset=nc.lookup_mloc(wsl2[0]).addr)
    wqsl = [ha2(f"wqsl{i}", [128, D], F32) for i in range(2)]
    qs1 = ha2("qs1", [128, D], F32)
    qsi = ha2("qsi", [128, D], I32)
    qmk = ha2("qmk", [128, D], F32)
    sc1 = ha2("sc1", [128, D], F32)
    qg = ha2("qg", [128, D], BF16)
    qu = ha2("qu", [128, D], BF16)
    wqgT = ha2("wqgT", [128, NDC, 128], BF16)
    wquT = ha2("wquT", [128, NDC, 128], BF16)
    x1t = [ha2(f"x1t{i}", [128, D], F32) for i in range(2)]
    outb = [ha2(f"outb{i}", [128, D], F32) for i in range(2)]
    hbuf = [nc.alloc_sbuf_tensor_at(f"hbuf{i}", [128, T], BF16,
                                    offset=nc.lookup_mloc(x1t[i]).addr)
            for i in range(2)]
    scb = [ha2(f"scb{i}", [128, D], BF16) for i in range(2)]
    th = [ha2(f"th{i}", [128, 512], BF16) for i in range(2)]

    ha3 = mkph("L3")
    wdT = ha3("wdT", [128, NDC, NIC, 128], BF16)
    w3sl = [ha3(f"w3sl{i}", [128, FI], F32) for i in range(2)]
    hsl = ha3("hsl", [128, NIC, TQ], BF16)
    hq3 = ha3("hq3", [128, NIC, TQ], BF16)
    q3s1 = ha3("q3s1", [128, FI], F32)
    q3si = ha3("q3si", [128, FI], I32)
    q3mk = ha3("q3mk", [128, FI], F32)
    qd3 = ha3("qd3", [128, FI], BF16)
    dfm = ha3("dfm", [128, NDC, TQ], BF16)
    x3t = [ha3(f"x3t{i}", [128, D], F32) for i in range(2)]
    ob3 = [ha3(f"ob3_{i}", [128, D], F32) for i in range(2)]

    pb = [psal(f"pb{i}", [128, 512]) for i in range(5)]
    px = psal("px", [128, 512])
    tpx = es.enter_context(nc.psum_tensor("tpx", [128, 1024], BF16))
    tpy = es.enter_context(nc.psum_tensor("tpy", [128, 1024], BF16))

    b.sem("HW", 16)
    b.sem("GP", 16)
    b.sem("PE", 1)
    b.sem("AC", 1)
    b.sem("DV", 1)
    b.sem("GC", 1)
    b.sem("ST", 16)

    def gdma(out, in_):
        b.dma_fence("gpsimd", "GP")
        return b.op("gpsimd", lambda g, o=out, i=in_: g.dma_start(out=o, in_=i),
                    inc="GP")

    def hdma(out, in_, sem="HW"):
        b.dma_fence("sync", sem)
        return b.op("sync", lambda s, o=out, i=in_: s.dma_start(out=o, in_=i),
                    inc=sem)

    def gfn(f, inc=None):
        return b.op("gpsimd", f, inc=inc)

    def qbar():
        tgt = []
        m = b.op("vector", lambda e: e.engine_nop(), inc="DV")
        tgt.append(("DV", m))
        m = b.op("scalar", lambda e: e.activation(out=epsb[:], in_=epsb[:],
                                                  func=AF.Identity, bias=0.0,
                                                  scale=1.0), inc="AC")
        tgt.append(("AC", m))
        m = b.op("gpsimd", lambda e: e.engine_nop(), inc="GC")
        tgt.append(("GC", m))
        tgt.append(("PE", b.v("PE")))
        for eng in ("vector", "scalar", "gpsimd", "tensor", "sync"):
            for sem, m in tgt:
                b.wait(eng, sem, m)

    def mega_bar():
        qbar()
        for eng in ("vector", "scalar", "gpsimd", "tensor", "sync"):
            for semn in ("HW", "ST", "GP"):
                b.wait(eng, semn, b.v(semn))

    def dbg_dump(name, src):
        if not dbg:
            return
        b.wait("gpsimd", "AC", b.v("AC"))
        b.wait("gpsimd", "DV", b.v("DV"))
        mst = b.op("gpsimd", lambda g, o=dbg_d[name], i=src: g.dma_start(out=o, in_=i),
                   inc="GP")
        for eng in ("vector", "scalar", "tensor", "sync"):
            b.wait(eng, "GP", mst)

    gdma(Asb[:], Alog_d.rearrange("(c p) s -> p c s", p=128))
    gdma(Dsb[:], Dp_d[:, :])
    gdma(convb[:], convb_d[:, :])
    gdma(bdt[:], bdt_d[:, :])
    gdma(convw[:], convw_d.rearrange("(c p) k -> p c k", p=128))
    gdma(wxp[:], Wxp_d.rearrange("(c p) s -> p c s", p=128))
    gdma(fwin[:], fWin_d.rearrange("(c p) m -> p c m", p=128))
    gdma(sfw1[:], fW1_d[:, :])
    gdma(sfb1[:], fb1_d[:, :])
    gdma(sfw2[:], fW2_d[:, :])
    gdma(sfb2[:], fb2_d[:, :])
    gdma(sfbin[:], fbin_d[:, :])
    gdma(fwoc[0:64, :], fWout_d[:, :])
    params_ld = gdma(fwoc[64:65, :], fbout_d[:, :])
    x0_ld = hdma(x0t, x0_d.rearrange("(a p) d -> p a d", p=128))

    b.wait("gpsimd", "GP", params_ld)
    gfn(lambda g: g.memset(ident[:], 0.0))
    gfn(lambda g: g.affine_select(out=ident[:], in_=ident[:],
                                  compare_op=ALU.not_equal, fill=1.0, base=0,
                                  pattern=[[-1, 128]], channel_multiplier=1))
    gfn(lambda g: g.memset(ones1[:], 1.0))
    gfn(lambda g: g.memset(ones32[:], 1.0))
    gfn(lambda g: g.memset(identf[:], 0.0))
    gfn(lambda g: g.affine_select(out=identf[:], in_=identf[:],
                                  compare_op=ALU.not_equal, fill=1.0, base=0,
                                  pattern=[[-1, 32]], channel_multiplier=1))

    gfn(lambda g: g.memset(carry[:], 0.0))
    gfn(lambda g: g.memset(gacc[64:65, :], 1.0))
    gfn(lambda g: g.memset(statc[:], 0.0))
    gfn(lambda g: g.memset(marg[:], 0.0))
    gfn(lambda g: g.memset(epsb[:], 1e-5))
    gfn(lambda g: g.memset(dA2[0][:], 0.0))
    const_done = gfn(lambda g: g.engine_nop(), inc="GC")

    b.wait("scalar", "GP", params_ld)
    aexp = b.op("scalar", lambda sc: sc.activation(
        out=Asb[:], in_=Asb[:], func=AF.Exp, bias=0.0, scale=1.0), inc="AC")
    b.wait("vector", "AC", aexp)
    b.wait("vector", "GC", const_done)
    b.op("vector", lambda v: v.tensor_scalar(
        out=Asb[:], in0=Asb[:], scalar1=-1.0, scalar2=None, op0=ALU.mult),
        inc="DV")

    b.wait("scalar", "HW", x0_ld)
    b.wait("scalar", "GC", const_done)
    b.wait("vector", "HW", x0_ld)
    b.wait("vector", "GC", const_done)
    sq_red = {}
    for tt in range(NT):
        if tt > 0:
            b.wait("scalar", "DV", sq_red[tt - 1])
        m_sq = b.op("scalar", lambda sc, tt=tt: sc.activation(
            out=sq1, in_=x0t[:, tt, :], func=AF.Square, bias=0.0, scale=1.0),
            inc="AC")
        b.wait("vector", "AC", m_sq)
        sq_red[tt] = b.op("vector", lambda v, tt=tt: v.tensor_reduce(
            out=statc[:, 1, tt:tt + 1], in_=sq1, axis=AX, op=ALU.add), inc="DV")
        b.op("vector", lambda v, tt=tt: v.tensor_reduce(
            out=statc[:, 2, tt:tt + 1], in_=x0t[:, tt, :], axis=AX, op=ALU.add))
    m_syn = b.op("vector", lambda v: v.engine_nop(), inc="DV")
    b.wait("vector", "DV", m_syn)
    b.op("vector", lambda v: v.tensor_scalar(
        out=statc[:, 3, :], in0=statc[:, 2, :], scalar1=1.0 / D, scalar2=None,
        op0=ALU.mult))
    b.op("vector", lambda v: v.tensor_scalar(
        out=statc[:, 4, :], in0=statc[:, 1, :], scalar1=1.0 / D, scalar2=None,
        op0=ALU.mult))
    b.op("vector", lambda v: v.tensor_tensor(
        out=statc[:, 5, :], in0=statc[:, 3, :], in1=statc[:, 3, :], op=ALU.mult))
    m_var = b.op("vector", lambda v: v.tensor_tensor(
        out=statc[:, 5, :], in0=statc[:, 4, :], in1=statc[:, 5, :],
        op=ALU.subtract), inc="DV")
    b.wait("scalar", "DV", m_var)
    m_sd = b.op("scalar", lambda sc: sc.activation(
        out=statc[:, 6, :], in_=statc[:, 5, :], func=AF.Sqrt,
        bias=epsb[:, 0:1], scale=1.0), inc="AC")
    b.wait("vector", "AC", m_sd)
    b.op("vector", lambda v: v.reciprocal(out=statc[:, 7, :], in_=statc[:, 6, :]))
    b.op("vector", lambda v: v.tensor_tensor(
        out=statc[:, 0, :], in0=statc[:, 3, :], in1=statc[:, 7, :], op=ALU.mult))
    m_nb = b.op("vector", lambda v: v.tensor_scalar(
        out=statc[:, 0, :], in0=statc[:, 0, :], scalar1=-1.0, scalar2=None,
        op0=ALU.mult), inc="DV")
    b.wait("scalar", "DV", m_nb)
    ev1 = []
    for tt in range(NT):
        par = tt % 2
        m_xn = b.op("scalar", lambda sc, tt=tt, par=par: sc.activation(
            out=xtm2[par], in_=x0t[:, tt, :], func=AF.Identity,
            bias=statc[:, 0, tt:tt + 1], scale=statc[:, 7, tt:tt + 1]), inc="AC")
        pst = tpx if par == 0 else tpy
        b.wait("tensor", "AC", m_xn)
        if tt >= 2:
            b.wait("tensor", "AC", ev1[tt - 2])
        for dc in range(NDC):
            m_tr = b.op("tensor", lambda t, dc=dc, par=par, pst=pst: t.transpose(
                pst[:, dc * 128:(dc + 1) * 128],
                xtm2[par][:, dc * 128:(dc + 1) * 128], ident[:]),
                inc="PE" if dc == NDC - 1 else None)
        b.wait("scalar", "PE", m_tr)
        m_ev = b.op("scalar", lambda sc, tt=tt, pst=pst: sc.activation(
            out=xn0[:, :, tt * 128:(tt + 1) * 128],
            in_=pst[:, 0:1024], func=AF.Identity, bias=0.0,
            scale=1.0), inc="AC")
        ev1.append(m_ev)
    xn0_done = b.v("AC")
    dbg_dump("stat", statc[:])

    b.wait("gpsimd", "AC", xn0_done)
    gfn(lambda g: g.memset(dA2[0][:], 0.0))
    gfn(lambda g: g.memset(dBp[:], 0.0))
    gfn(lambda g: g.memset(hhp[:], 0.0))
    gfn(lambda g: g.memset(tpp[:], 0.0))
    bndz = gfn(lambda g: g.memset(dA2[1][:], 0.0), inc="GC")
    b.wait("scalar", "GC", bndz)
    b.wait("vector", "GC", bndz)

    b.wait("tensor", "AC", xn0_done)
    for t4 in range(4):
        for k in range(NDC):
            m_mm = b.op("tensor", lambda t, t4=t4, k=k: t.matmul(
                pb[t4][0:64, :], fwin[:, k, 0:64],
                xn0[:, k, t4 * 512:(t4 + 1) * 512],
                start=(k == 0), stop=(k == NDC - 1)),
                inc="PE" if k == NDC - 1 else None)
        b.wait("scalar", "PE", m_mm)
        b.op("scalar", lambda sc, t4=t4: sc.activation(
            out=gacc[0:64, t4 * 512:(t4 + 1) * 512], in_=pb[t4][0:64, :],
            func=AF.Identity, bias=sfbin[:, 0:1], scale=1.0),
            inc="AC" if t4 == 3 else None)
    spre_done = b.v("AC")

    b.wait("scalar", "AC", spre_done)
    for j in range(16):
        m_rl = b.op("scalar", lambda sc, j=j: sc.activation(
            out=ftmp, in_=gacc[0:64, :], func=AF.Relu,
            bias=sfb1[:, j:j + 1], scale=sfw1[:, j:j + 1]), inc="AC")
        b.wait("vector", "AC", m_rl)
        if j == 0:
            m_g = b.op("vector", lambda v: v.tensor_scalar(
                out=ftm2, in0=ftmp, scalar1=sfw2[:, 0:1], scalar2=None,
                op0=ALU.mult), inc="DV")
        else:
            b.op("vector", lambda v, j=j: v.tensor_scalar(
                out=ftmp, in0=ftmp, scalar1=sfw2[:, j:j + 1], scalar2=None,
                op0=ALU.mult))
            m_g = b.op("vector", lambda v: v.tensor_tensor(
                out=ftm2, in0=ftm2, in1=ftmp, op=ALU.add), inc="DV")
        if j < 15:
            b.wait("scalar", "DV", m_g)
    b.wait("scalar", "DV", b.v("DV"))
    gacc_done = b.op("scalar", lambda sc: sc.activation(
        out=gacc[0:64, :], in_=ftm2, func=AF.Identity, bias=sfb2[:, 0:1],
        scale=1.0), inc="AC")

    b.wait("gpsimd", "DV", b.v("DV"))
    b.wait("gpsimd", "AC", b.v("AC"))
    z2 = gfn(lambda g: g.memset(statc[:], 0.0), inc="GC")
    b.wait("vector", "GC", z2)

    x1_asm = {}
    sq4_red = {}
    x1_st_mark = {}
    x1_sq_mark = {}
    xz_marks = []
    dt_marks = []
    ou_marks = []
    seg = [j * (TQ + 1) for j in range(4)]
    bnd = [j * (TQ + 1) + TQ for j in range(3)]

    def v3(ap):
        return ap.rearrange("p t -> p 1 t")

    for q in range(NQ):
        t0 = q * TQ
        for g8 in range(8):
            buf = g8 % 2
            if len(xz_marks) >= 2:
                b.wait("gpsimd", "PE", xz_marks[-2])
            mark_w = gdma(wsl[buf][:], rkpm(Win_d)[:, :, g8 * 512:(g8 + 1) * 512])
            b.wait("tensor", "GP", mark_w)
            if q == 0 and g8 == 0:
                b.wait("tensor", "AC", gacc_done)
            b.wait("tensor", "AC", b.v("AC"))
            for k in range(NDC):
                for m in range(4):
                    m_mm = b.op("tensor", lambda t, buf=buf, k=k, m=m, t0=t0: t.matmul(
                        pb[m][:], wsl[buf][:, k, m * 128:(m + 1) * 128],
                        xn0[:, k, t0:t0 + TQ],
                        start=(k == 0), stop=(k == NDC - 1)),
                        inc="PE" if (k == NDC - 1 and m == 3) else None)
            xz_marks.append(m_mm)
            b.wait("scalar", "PE", m_mm)
            for m in range(4):
                if g8 < 4:
                    cc = g8 * 4 + m
                    b.op("scalar", lambda sc, cc=cc, m=m: sc.activation(
                        out=xcb[:, cc, 3:3 + TQ], in_=pb[m][:], func=AF.Identity,
                        bias=0.0, scale=1.0), inc="AC" if m == 3 else None)
                else:
                    cc = (g8 - 4) * 4 + m
                    b.op("scalar", lambda sc, cc=cc, m=m: sc.activation(
                        out=slz[:, cc, :], in_=pb[m][:], func=AF.Silu,
                        bias=0.0, scale=1.0), inc="AC" if m == 3 else None)
        xz_done = b.v("AC")
        if q == dbgq:
            dbg_dump("xc", xcb[:])
            dbg_dump("slz", slz[:])

        if q > 0:
            b.wait("gpsimd", "AC", xz_done)
            marg_restored = gfn(lambda g: g.tensor_copy(xcb[:, :, 0:3], marg[:]),
                                inc="GC")

        if q == 0:
            b.wait("gpsimd", "AC", gacc_done)
            marg_restored = gfn(lambda g: g.memset(xcb[:, :, 0:3], 0.0), inc="GC")
        b.wait("tensor", "AC", xz_done)
        b.wait("tensor", "GC", marg_restored)
        conv_ev = {}
        for cc in range(NCC):
            dpar = cc % 2
            bank = 3 + (cc % 2)
            if cc >= 2:
                b.wait("vector", "PE", conv_pe[cc - 2])
            b.wait("vector", "GC", const_done)
            for k in range(4):
                m_d4 = b.op("vector", lambda v, cc=cc, k=k, dpar=dpar: v.tensor_scalar(
                    out=diag4[dpar][:, k, :], in0=ident[:],
                    scalar1=convw[:, cc, k:k + 1], scalar2=None, op0=ALU.mult),
                    inc="DV" if k == 3 else None, nosync=(k != 3))
            b.wait("tensor", "DV", m_d4)
            if cc >= 2:
                b.wait("tensor", "AC", conv_ev[cc - 2])
            for k in range(4):
                m_mm = b.op("tensor", lambda t, cc=cc, k=k, dpar=dpar, bank=bank: t.matmul(
                    pb[bank][:], diag4[dpar][:, k, :],
                    xcb[:, cc, k:k + TQ], start=(k == 0), stop=(k == 3)),
                    inc="PE" if k == 3 else None)
            if cc == 0:
                conv_pe = {}
            conv_pe[cc] = m_mm
            b.wait("scalar", "PE", m_mm)
            m_e = b.op("scalar", lambda sc, cc=cc, bank=bank: sc.activation(
                out=xsb[:, cc, :], in_=pb[bank][:], func=AF.Silu,
                bias=convb[:, cc:cc + 1], scale=1.0), inc="AC")
            conv_ev[cc] = m_e
        conv_done = b.v("AC")
        if q == dbgq:
            dbg_dump("xs", xsb[:])
        if q < NQ - 1:
            b.wait("gpsimd", "AC", conv_done)
            gfn(lambda g: g.tensor_copy(marg[:], xcb[:, :, TQ:TQ + 3]), inc="GC")

        b.wait("tensor", "DV", b.v("DV"))
        b.wait("scalar", "ST", b.v("ST"))
        for g4 in range(8):
            pass
        for g2 in range(8):
            buf = g2 % 2
            if len(dt_marks) >= 2:
                b.wait("gpsimd", "PE", dt_marks[-2])
            mark_w = gdma(wdt[buf][:], rkpm(Wdt_d)[:, :, g2 * 256:(g2 + 1) * 256])
            b.wait("tensor", "GP", mark_w)
            b.wait("tensor", "AC", b.v("AC"))
            for k in range(NCC):
                for m in range(2):
                    m_mm = b.op("tensor", lambda t, buf=buf, k=k, m=m: t.matmul(
                        pb[m][:], wdt[buf][:, k, m * 128:(m + 1) * 128],
                        xsb[:, k, :], start=(k == 0), stop=(k == NCC - 1)),
                        inc="PE" if (k == NCC - 1 and m == 1) else None)
            dt_marks.append(m_mm)
            b.wait("scalar", "PE", m_mm)
            for m in range(2):
                cc = g2 * 2 + m
                b.op("scalar", lambda sc, cc=cc, m=m: sc.activation(
                    out=tpp[:, 0:TQ], in_=pb[m][:], func=AF.Exp,
                    bias=bdt[:, cc:cc + 1], scale=1.0))
                b.op("scalar", lambda sc, cc=cc: sc.activation(
                    out=dlt[:, cc, :], in_=tpp[:, 0:TQ], func=AF.Ln,
                    bias=1.0, scale=1.0), inc="AC" if m == 1 else None)
        dlt_done = b.v("AC")
        if q == dbgq:
            dbg_dump("dlt", dlt[:])

        b.wait("tensor", "AC", b.v("AC"))
        for k in range(NCC):
            m_mm = b.op("tensor", lambda t, k=k: t.matmul(
                pb[2][0:32, :], wxp[:, k, :], xsb[:, k, :],
                start=(k == 0), stop=(k == NCC - 1)),
                inc="PE" if k == NCC - 1 else None)
        xproj_pe = m_mm
        b.wait("scalar", "PE", xproj_pe)
        bcrows_done = b.op("scalar", lambda sc: sc.activation(
            out=bcrows[:, :], in_=pb[2][0:32, :], func=AF.Identity, bias=0.0,
            scale=1.0), inc="AC")

        b.wait("vector", "AC", dlt_done)
        for cc in range(NCC):
            b.op("vector", lambda v, cc=cc: v.tensor_tensor(
                out=ub[:, cc, :], in0=xsb[:, cc, :], in1=dlt[:, cc, :],
                op=ALU.mult), nosync=(cc != 0))
        b.wait("vector", "PE", xproj_pe)
        for cc in range(NCC):
            m_yi = b.op("vector", lambda v, cc=cc: v.tensor_scalar(
                out=xsb[:, cc, :], in0=xsb[:, cc, :], scalar1=Dsb[:, cc:cc + 1],
                scalar2=None, op0=ALU.mult),
                inc="DV" if cc == NCC - 1 else None, nosync=(cc != NCC - 1))
        yinit_done = m_yi
        if q == dbgq:
            dbg_dump("ub", ub[:])
            dbg_dump("bc", bcrows[:])

        b.wait("tensor", "AC", bcrows_done)
        if q > 0:
            b.wait("tensor", "DV", x1_asm[q * 4 - 1])
        for s in range(NS):
            pp = s % 2
            eB, eC = esel[pp * 2], esel[pp * 2 + 1]
            b.wait("vector", "PE", b.v("PE"))
            b.op("vector", lambda v, eB=eB, s=s: v.tensor_scalar(
                out=eB[:], in0=ones32[:], scalar1=identf[0:32, s:s + 1],
                scalar2=None, op0=ALU.mult))
            m_es = b.op("vector", lambda v, eC=eC, s=s: v.tensor_scalar(
                out=eC[:], in0=ones32[:], scalar1=identf[0:32, 16 + s:17 + s],
                scalar2=None, op0=ALU.mult), inc="DV")
            b.wait("tensor", "DV", m_es)
            if s > 0:
                b.wait("tensor", "AC", bc_ev)
                b.wait("tensor", "DV", last_y)
            m_pe1 = b.op("tensor", lambda t, eB=eB: t.matmul(
                pb[3][:], eB[:, :], bcrows[:, :], start=True, stop=True))
            m_pe2 = b.op("tensor", lambda t, eC=eC: t.matmul(
                px[:], eC[:, :], bcrows[:, :], start=True, stop=True), inc="PE")
            b.wait("scalar", "PE", m_pe2)
            if s > 0:
                b.wait("scalar", "DV", last_y)
            b.op("scalar", lambda sc: sc.activation(
                out=bbt[:, :], in_=pb[3][:], func=AF.Identity, bias=0.0, scale=1.0))
            bc_ev = b.op("scalar", lambda sc: sc.activation(
                out=cbt[:, :], in_=px[:], func=AF.Identity, bias=0.0, scale=1.0),
                inc="AC")
            for p4 in range(4):
                i = s * 4 + p4
                par = i % 2
                ccs = [p4 * 4 + j for j in range(4)]
                if i >= 2:
                    b.wait("scalar", "DV", scan_done[i - 2])
                for j, cc in enumerate(ccs):
                    m_dA = b.op("scalar", lambda sc, cc=cc, j=j, s=s, par=par: sc.activation(
                        out=dA2[par][:, seg[j]:seg[j] + TQ], in_=dlt[:, cc, :],
                        func=AF.Identity, bias=1.0, scale=Asb[:, cc, s:s + 1]),
                        inc="AC" if j == 3 else None)
                b.wait("vector", "AC", bc_ev)
                if i == 0:
                    b.wait("vector", "DV", yinit_done)
                b.op("vector", lambda v, p4=p4: v.tensor_tensor(
                    out=dBp[:].rearrange("p (a t) -> p a t", t=TQ + 1)[:, :, 0:TQ],
                    in0=ub[:, p4 * 4:p4 * 4 + 4, :],
                    in1=bbt[:].rearrange("p (o t) -> p o t", o=1).broadcast_to([128, 4, TQ]),
                    op=ALU.mult))
                ci = s * NCC + p4 * 4
                b.op("vector", lambda v, ci=ci: v.tensor_copy(
                    dBp[:].rearrange("p (a t) -> p a t", t=TQ + 1)[:, 0:3, TQ:TQ + 1],
                    carry[:, ci + 1:ci + 4].rearrange("p (c o) -> p c o", o=1)),
                    nosync=True)
                b.wait("vector", "AC", m_dA)
                m_scan = b.op("vector", lambda v, par=par, ci=ci: v.tensor_tensor_scan(
                    out=hhp[:], data0=dA2[par][:], data1=dBp[:],
                    initial=carry[:, ci:ci + 1], op0=ALU.mult, op1=ALU.add),
                    nosync=True)
                b.op("vector", lambda v: v.tensor_tensor(
                    out=tpp[:].rearrange("p (a t) -> p a t", t=TQ + 1)[:, :, 0:TQ],
                    in0=hhp[:].rearrange("p (a t) -> p a t", t=TQ + 1)[:, :, 0:TQ],
                    in1=cbt[:].rearrange("p (o t) -> p o t", o=1).broadcast_to([128, 4, TQ]),
                    op=ALU.mult), nosync=True)
                b.op("vector", lambda v, p4=p4: v.tensor_tensor(
                    out=xsb[:, p4 * 4:p4 * 4 + 4, :],
                    in0=xsb[:, p4 * 4:p4 * 4 + 4, :],
                    in1=tpp[:].rearrange("p (a t) -> p a t", t=TQ + 1)[:, :, 0:TQ],
                    op=ALU.add), nosync=True)
                m_y = b.op("vector", lambda v, ci=ci: v.tensor_copy(
                    carry[:, ci:ci + 4].rearrange("p (c o) -> p c o", o=1),
                    hhp[:].rearrange("p (a t) -> p a t", t=TQ + 1)[:, :, TQ - 1:TQ]),
                    inc="DV")
                if i == 0 and s == 0:
                    scan_done = {}
                scan_done[i] = m_y
            last_y = scan_done[s * 4 + 3]
        sloop_done = b.v("DV")

        for cc in range(NCC):
            m_ys = b.op("vector", lambda v, cc=cc: v.tensor_tensor(
                out=xsb[:, cc, :], in0=xsb[:, cc, :], in1=slz[:, cc, :],
                op=ALU.mult), inc="DV" if cc == NCC - 1 else None,
                nosync=(cc != NCC - 1))
        y_done = m_ys

        qbar()
        if q == dbgq:
            dbg_dump("y", xsb[:])
            dbg_dump("carry", carry[:])
        b.wait("tensor", "DV", y_done)
        for m in range(NDC):
            buf = m % 2
            if len(ou_marks) >= 2:
                b.wait("gpsimd", "PE", ou_marks[-2])
            mark_w = gdma(wou[buf][:], rkpm(Wout_d)[:, :, m * 128:(m + 1) * 128])
            b.wait("tensor", "GP", mark_w)
            bank = m % 4
            b.wait("tensor", "AC", b.v("AC"))
            for k in range(NCC):
                m_mm = b.op("tensor", lambda t, buf=buf, k=k, bank=bank: t.matmul(
                    pb[bank][:], wou[buf][:, k, :], xsb[:, k, :],
                    start=(k == 0), stop=(k == NCC - 1)),
                    inc="PE" if k == NCC - 1 else None)
            ou_marks.append(m_mm)
            if m > 0:
                b.wait("tensor", "DV", ffn_add)
            m_mf = b.op("tensor", lambda t, m=m, t0=t0: t.matmul(
                px[:], fwoc[:, m * 128:(m + 1) * 128],
                gacc[:, t0:t0 + TQ], start=True, stop=True), inc="PE")
            b.wait("scalar", "PE", m_mm)
            m_ev = b.op("scalar", lambda sc, m=m, bank=bank: sc.activation(
                out=ofm[:, m, :], in_=pb[bank][:], func=AF.Identity, bias=0.0,
                scale=1.0), inc="AC")
            b.wait("vector", "PE", m_mf)
            b.wait("vector", "AC", m_ev)
            ffn_add = b.op("vector", lambda v, m=m: v.tensor_tensor(
                out=ofm[:, m, :], in0=ofm[:, m, :], in1=px[:], op=ALU.add),
                inc="DV")
        out_fm = b.v("AC")
        out_fm_dv = ffn_add
        if q == dbgq:
            dbg_dump("ofm", ofm[:])

        for sub in range(4):
            tt = q * 4 + sub
            par = sub % 2
            pst = tpx if par == 0 else tpy
            if tt >= 2:
                b.wait("sync", "DV", x1_asm[tt - 2])
            m_x0 = hdma(x0st[par], x0_d[tt * 128:(tt + 1) * 128, :])
            b.wait("tensor", "AC", out_fm)
            b.wait("tensor", "DV", out_fm_dv)
            if tt >= 2:
                b.wait("tensor", "DV", x1_asm[tt - 2])
            for dc in range(NDC):
                m_tr = b.op("tensor", lambda t, dc=dc, sub=sub, pst=pst: t.transpose(
                    pst[:, dc * 128:(dc + 1) * 128],
                    ofm[:, dc, sub * 128:(sub + 1) * 128], ident[:]),
                    inc="PE" if dc == NDC - 1 else None)
            b.wait("vector", "PE", m_tr)
            b.wait("vector", "HW", m_x0)
            if tt >= 2:
                b.wait("vector", "ST", b.v("ST"))
                b.wait("vector", "AC", x1_sq_mark[tt - 2])
            m_x1 = b.op("vector", lambda v, par=par, pst=pst: v.tensor_tensor(
                out=x1st[par], in0=pst[:, 0:1024],
                in1=x0st[par], op=ALU.add), inc="DV")
            x1_asm[tt] = m_x1
            b.wait("sync", "DV", m_x1)
            x1_st_mark[tt] = hdma(x1i_d[tt * 128:(tt + 1) * 128, :], x1st[par],
                                  sem="ST")
            b.wait("scalar", "DV", m_x1)
            if tt > 0:
                b.wait("scalar", "DV", sq4_red[tt - 1])
            m_sq = b.op("scalar", lambda sc, par=par, tt=tt: sc.activation(
                out=sq4, in_=x1st[par], func=AF.Square, bias=0.0, scale=1.0),
                inc="AC")
            x1_sq_mark[tt] = m_sq
            b.wait("vector", "AC", m_sq)
            sq4_red[tt] = b.op("vector", lambda v, tt=tt: v.tensor_reduce(
                out=stat2[:, 1, tt:tt + 1], in_=sq4, axis=AX, op=ALU.add), inc="DV")
            b.op("vector", lambda v, par=par, tt=tt: v.tensor_reduce(
                out=stat2[:, 2, tt:tt + 1], in_=x1st[par], axis=AX, op=ALU.add))
            b.op("vector", lambda v, par=par, tt=tt: v.tensor_reduce(
                out=stat2[:, 3, tt:tt + 1], in_=x1st[par], axis=AX, op=ALU.max))
            b.op("vector", lambda v, par=par, tt=tt: v.tensor_reduce(
                out=stat2[:, 4, tt:tt + 1], in_=x1st[par], axis=AX, op=ALU.min))

    if dbg:
        dbg_dump("xn0", xn0[:])
        dbg_dump("gacc", gacc[:])
    m_syn2 = b.op("vector", lambda v: v.engine_nop(), inc="DV")
    b.wait("vector", "DV", m_syn2)
    b.op("vector", lambda v: v.tensor_scalar(
        out=stat2[:, 5, :], in0=stat2[:, 2, :], scalar1=1.0 / D, scalar2=None,
        op0=ALU.mult))
    b.op("vector", lambda v: v.tensor_scalar(
        out=stat2[:, 6, :], in0=stat2[:, 1, :], scalar1=1.0 / D, scalar2=None,
        op0=ALU.mult))
    b.op("vector", lambda v: v.tensor_tensor(
        out=stat2[:, 7, :], in0=stat2[:, 5, :], in1=stat2[:, 5, :], op=ALU.mult))
    m_var2 = b.op("vector", lambda v: v.tensor_tensor(
        out=stat2[:, 7, :], in0=stat2[:, 6, :], in1=stat2[:, 7, :],
        op=ALU.subtract), inc="DV")
    b.wait("scalar", "DV", m_var2)
    m_sd2 = b.op("scalar", lambda sc: sc.activation(
        out=stat2[:, 7, :], in_=stat2[:, 7, :], func=AF.Sqrt,
        bias=epsb[:, 0:1], scale=1.0), inc="AC")
    b.wait("vector", "AC", m_sd2)
    b.op("vector", lambda v: v.reciprocal(out=stat2[:, 6, :], in_=stat2[:, 7, :]))
    b.op("vector", lambda v: v.tensor_tensor(
        out=stat2[:, 3, :], in0=stat2[:, 3, :], in1=stat2[:, 5, :],
        op=ALU.subtract))
    b.op("vector", lambda v: v.tensor_tensor(
        out=stat2[:, 4, :], in0=stat2[:, 5, :], in1=stat2[:, 4, :],
        op=ALU.subtract))
    b.op("vector", lambda v: v.tensor_tensor(
        out=stat2[:, 3, :], in0=stat2[:, 3, :], in1=stat2[:, 4, :], op=ALU.max))
    m_z = b.op("vector", lambda v: v.tensor_tensor(
        out=stat2[:, 4, :], in0=stat2[:, 3, :], in1=stat2[:, 6, :],
        op=ALU.mult), inc="DV")
    b.wait("vector", "DV", m_z)
    m_m1c = b.op("vector", lambda v: v.tensor_reduce(
        out=stat2[:, 0, 0:1], in_=stat2[:, 4, 0:NT], axis=AX, op=ALU.max),
        inc="DV")
    b.wait("gpsimd", "DV", m_m1c)
    m_m1 = gfn(lambda g: g.tensor_reduce(out=m1c[:], in_=stat2[:, 0, 0:1],
                                         axis=AXC, op=ALU.max), inc="GC")
    if dbg:
        dbg_dump("stat2", stat2[:])

    gfn(lambda g: g.memset(panel[:], 0.0))
    gfn(lambda g: g.memset(rowp[:], 0.0))
    gfn(lambda g: g.memset(rowq[:], 0.0))
    const_done2 = gfn(lambda g: g.memset(onesf[:], 1.0), inc="GC")
    b.wait("gpsimd", "GC", m_m1)
    mc1 = gdma(c1i_t.ap()[:, :], m1c[:])
    b.wait("gpsimd", "GP", mc1)
    m_cc1 = b.op("gpsimd", lambda g: g.collective_compute(
        "AllReduce", ALU.max, replica_groups=[list(range(8))],
        ins=[c1i_t.ap().opt()], outs=[c1o_t.ap().opt()]), inc="GC")
    b.wait("gpsimd", "GC", m_cc1)
    gdma(rowp[0:1, 8:9], c1o_t.ap()[:, :])
    mega_bar()

    gdma(rowq[0:1, 0:1], clipx_d[:, :])
    gdma(bz[:], bz_d[:, :])
    gdma(bh[:], bh_d[:, :])
    gdma(bg[:], bg_d[:, :])
    loads2_done = b.v("GP")

    b.wait("vector", "GP", loads2_done)
    b.wait("vector", "GC", const_done2)
    b.wait("tensor", "GC", const_done2)
    b.op("vector", lambda v: v.tensor_copy(panel[:, 0, 0:NT], statc[:, 5, 0:NT]))
    b.op("vector", lambda v: v.tensor_copy(panel[:, 1, 0:NT], statc[:, 6, 0:NT]))
    b.op("vector", lambda v: v.tensor_scalar(
        out=rowp[0:1, 16:32], in0=onesf[0:1, 0:16], scalar1=rowp[0:1, 8:9],
        scalar2=None, op0=ALU.mult))
    m_rq = b.op("vector", lambda v: v.tensor_scalar(
        out=rowq[0:1, 16:32], in0=onesf[0:1, 0:16], scalar1=rowq[0:1, 0:1],
        scalar2=None, op0=ALU.mult))
    b.wait("scalar", "DV", m_rq)
    m_sg = b.op("scalar", lambda sc: sc.activation(
        out=rowq[0:1, 32:48], in_=rowq[0:1, 16:32], func=AF.Sigmoid, bias=0.0,
        scale=1.0))
    b.wait("vector", "AC", m_sg)
    b.op("vector", lambda v: v.tensor_tensor(
        out=rowp[0:1, 32:48], in0=rowp[0:1, 16:32], in1=rowq[0:1, 32:48],
        op=ALU.mult))
    b.op("vector", lambda v: v.tensor_scalar(
        out=rowp[0:1, 48:64], in0=rowp[0:1, 32:48], scalar1=1.0 / 127.0,
        scalar2=None, op0=ALU.mult))
    b.op("vector", lambda v: v.reciprocal(
        out=rowp[0:1, 64:80], in_=rowp[0:1, 48:64]))
    m_row = b.op("vector", lambda v: v.tensor_scalar(
        out=rowp[0:1, 80:96], in0=rowp[0:1, 48:64], scalar1=1.0 / 127.0,
        scalar2=None, op0=ALU.mult))
    b.wait("tensor", "DV", m_row)
    b.op("tensor", lambda t: t.matmul(
        px[:, 0:1], onesf[:, :], rowp[0:1, 64:65], start=True, stop=True))
    m_bc = b.op("tensor", lambda t: t.matmul(
        px[:, 1:2], onesf[:, :], rowp[0:1, 80:81], start=True, stop=True),
        inc="PE")
    b.wait("vector", "PE", m_bc)
    b.op("vector", lambda v: v.tensor_copy(recipx[:], px[:, 0:1]))
    m_bcv = b.op("vector", lambda v: v.tensor_copy(sxc127[:], px[:, 1:2]))
    b.op("vector", lambda v: v.tensor_tensor(
        out=panel[:, 2, :], in0=panel[:, 0, :], in1=panel[:, 1, :],
        op=ALU.mult))
    b.op("vector", lambda v: v.tensor_scalar(
        out=panel[:, 2, :], in0=panel[:, 2, :], scalar1=-1.0, scalar2=None,
        op0=ALU.mult))
    m_nbz = b.op("vector", lambda v: v.tensor_scalar(
        out=nbz[:], in0=bz[:], scalar1=-1.0, scalar2=None, op0=ALU.mult))

    b.wait("scalar", "DV", m_nbz)
    xn_mark = {}
    q_mark = {}
    tr_mark = {}
    ev_mark = {}
    for tt in range(NT):
        par = tt % 2
        if tt >= 2:
            b.wait("sync", "AC", xn_mark[tt - 2])
        m_x1 = hdma(x1t[par][:], x1i_d[tt * 128:(tt + 1) * 128, :])
        b.wait("scalar", "HW", m_x1)
        if tt >= 1:
            b.wait("scalar", "DV", q_mark[tt - 1])
        m_xn = b.op("scalar", lambda sc, par=par, tt=tt: sc.activation(
            out=sc1[:], in_=x1t[par][:], func=AF.Identity,
            bias=panel[:, 2, tt:tt + 1], scale=panel[:, 1, tt:tt + 1]))
        xn_mark[tt] = m_xn
        if tt >= 1:
            b.wait("scalar", "PE", tr_mark[tt - 1])
        m_cb = b.op("scalar", lambda sc: sc.activation(
            out=scb[0][:], in_=sc1[:], func=AF.Identity, bias=0.0, scale=1.0))
        b.wait("vector", "AC", m_cb)
        if tt == 0:
            b.wait("vector", "DV", m_bcv)
        b.op("vector", lambda v: v.tensor_scalar(
            out=qs1[:], in0=sc1[:], scalar1=recipx[:, 0:1], scalar2=RB + 0.5,
            op0=ALU.mult, op1=ALU.add))
        b.op("vector", lambda v: v.tensor_copy(qsi[:], qs1[:]), nosync=True)
        b.op("vector", lambda v: v.tensor_tensor(
            out=qmk[:], in0=qsi[:], in1=qs1[:], op=ALU.is_gt), nosync=True)
        b.op("vector", lambda v: v.tensor_tensor(
            out=qs1[:], in0=qsi[:], in1=qmk[:], op=ALU.subtract), nosync=True)
        b.op("vector", lambda v: v.tensor_scalar(
            out=qs1[:], in0=qs1[:], scalar1=-RB, scalar2=127.0, op0=ALU.add,
            op1=ALU.min), nosync=True)
        m_q = b.op("vector", lambda v: v.tensor_scalar(
            out=scb[1][:], in0=qs1[:], scalar1=-128.0, scalar2=None,
            op0=ALU.max), inc="DV")
        q_mark[tt] = m_q
        b.wait("tensor", "AC", m_cb)
        b.wait("tensor", "DV", m_q)
        if tt >= 1:
            b.wait("tensor", "AC", ev_mark[tt - 1][0])
            b.wait("tensor", "DV", ev_mark[tt - 1][1])
        for dc in range(NDC):
            b.op("tensor", lambda t, dc=dc: t.transpose(
                tpx[:, dc * 128:(dc + 1) * 128],
                scb[0][:, dc * 128:(dc + 1) * 128], ident[:]))
            m_tr = b.op("tensor", lambda t, dc=dc: t.transpose(
                tpy[:, dc * 128:(dc + 1) * 128],
                scb[1][:, dc * 128:(dc + 1) * 128], ident[:]),
                inc="PE" if dc == NDC - 1 else None)
        tr_mark[tt] = m_tr
        b.wait("scalar", "PE", m_tr)
        b.wait("vector", "PE", m_tr)
        m_e1 = b.op("scalar", lambda sc, tt=tt: sc.activation(
            out=xn1b[:, :, tt * 128:(tt + 1) * 128], in_=tpx[:, 0:1024],
            func=AF.Identity, bias=0.0, scale=1.0))
        m_e2 = b.op("vector", lambda v, tt=tt: v.tensor_copy(
            qx[:, :, tt * 128:(tt + 1) * 128], tpy[:, 0:1024]))
        ev_mark[tt] = (m_e1, m_e2)
    p2_ac = b.v("AC")
    p2_dv = b.v("DV")

    scan_mark = {}
    for p in range(2):
        if p == 1:
            b.wait("gpsimd", "PE", b.v("PE"))
        mz = gdma(wsl2[0][:], rkpm(Wz_d[p]))
        mh = gdma(wsl2[1][:], rkpm(Wh_d[p]))
        b.wait("tensor", "GP", mh)
        if p == 0:
            b.wait("tensor", "AC", p2_ac)
            b.wait("tensor", "DV", p2_dv)
        for m in range(4):
            fc = p * 4 + m
            b.wait("tensor", "AC", b.v("AC"))
            b.wait("tensor", "DV", b.v("DV"))
            for k in range(NDC):
                for t4 in range(4):
                    m_mmz = b.op("tensor", lambda t, k=k, m=m, t4=t4: t.matmul(
                        pb[t4][:], wsl2[0][:, k, m * 128:(m + 1) * 128],
                        xn1b[:, k, t4 * 512:(t4 + 1) * 512],
                        start=(k == 0), stop=(k == NDC - 1)),
                        inc="PE" if (k == NDC - 1 and t4 == 3) else None)
            b.wait("scalar", "PE", m_mmz)
            if fc >= 1:
                b.wait("scalar", "DV", scan_mark[fc - 1])
            for t4 in range(4):
                b.op("scalar", lambda sc, t4=t4, fc=fc: sc.activation(
                    out=zbuf[:, t4 * 512:(t4 + 1) * 512], in_=pb[t4][:],
                    func=AF.Sigmoid, bias=bz[:, fc:fc + 1], scale=1.0))
                m_za = b.op("scalar", lambda sc, t4=t4, fc=fc: sc.activation(
                    out=abuf[:, t4 * 512:(t4 + 1) * 512], in_=pb[t4][:],
                    func=AF.Sigmoid, bias=nbz[:, fc:fc + 1], scale=-1.0))
            b.wait("tensor", "AC", m_za)
            for k in range(NDC):
                for t4 in range(4):
                    m_mmh = b.op("tensor", lambda t, k=k, m=m, t4=t4: t.matmul(
                        pb[t4][:], wsl2[1][:, k, m * 128:(m + 1) * 128],
                        xn1b[:, k, t4 * 512:(t4 + 1) * 512],
                        start=(k == 0), stop=(k == NDC - 1)),
                        inc="PE" if (k == NDC - 1 and t4 == 3) else None)
            b.wait("vector", "PE", m_mmh)
            b.wait("vector", "AC", m_za)
            for t4 in range(4):
                b.op("vector", lambda v, t4=t4, fc=fc: v.scalar_tensor_tensor(
                    out=bbuf[:, t4 * 512:(t4 + 1) * 512], in0=pb[t4][:],
                    scalar=bh[:, fc:fc + 1],
                    in1=zbuf[:, t4 * 512:(t4 + 1) * 512],
                    op0=ALU.add, op1=ALU.mult), nosync=(t4 != 0))
            m_scan = b.op("vector", lambda v, fc=fc: v.tensor_tensor_scan(
                out=hcm[:, fc, :], data0=abuf[:], data1=bbuf[:],
                initial=0.0, op0=ALU.mult, op1=ALU.add), inc="DV",
                nosync=True)
            scan_mark[fc] = m_scan
    attn_dv = b.v("DV")

    aofm_mark = {}
    th_mark = {}
    outb_mark = {}
    asm_tr = {}
    b.wait("gpsimd", "PE", b.v("PE"))
    m_wg = gdma(wgsl[:], rkpm(Wg_d))
    for t4 in range(4):
        b.wait("tensor", "GP", m_wg)
        if t4 == 0:
            b.wait("tensor", "DV", attn_dv)
        for m in range(8):
            bank = m % 4
            if m >= 4:
                b.wait("tensor", "DV", aofm_mark[(t4, m - 4)])
                b.wait("tensor", "AC", th_mark[(t4, m - 4)])
            for k in range(NDC):
                m_mm = b.op("tensor", lambda t, k=k, m=m, t4=t4, bank=bank:
                            t.matmul(
                    pb[bank][:], wgsl[:, k, m * 128:(m + 1) * 128],
                    hcm[:, k, t4 * 512:(t4 + 1) * 512],
                    start=(k == 0), stop=(k == NDC - 1)),
                    inc="PE" if k == NDC - 1 else None)
            b.wait("scalar", "PE", m_mm)
            if m >= 2:
                b.wait("scalar", "DV", aofm_mark[(t4, m - 2)])
            m_th = b.op("scalar", lambda sc, m=m, bank=bank: sc.activation(
                out=th[m % 2][:], in_=pb[bank][:], func=AF.Sigmoid,
                bias=bg[:, m:m + 1], scale=1.0))
            th_mark[(t4, m)] = m_th
            b.wait("vector", "AC", m_th)
            if t4 >= 1 and m == 0:
                b.wait("vector", "PE", asm_tr[t4 - 1])
            m_ao = b.op("vector", lambda v, m=m, t4=t4: v.tensor_tensor(
                out=aofm[:, m, :], in0=th[m % 2][:],
                in1=hcm[:, m, t4 * 512:(t4 + 1) * 512], op=ALU.mult))
            aofm_mark[(t4, m)] = m_ao
        ao_dv = b.v("DV")
        for sub in range(4):
            tt = t4 * 4 + sub
            par = sub % 2
            pst = tpx if par == 0 else tpy
            if tt >= 2:
                b.wait("sync", "DV", outb_mark[tt - 2])
            else:
                b.wait("sync", "AC", p2_ac)
                b.wait("sync", "DV", p2_dv)
            m_x1 = hdma(x1t[par][:], x1i_d[tt * 128:(tt + 1) * 128, :])
            b.wait("tensor", "DV", ao_dv)
            if tt >= 2:
                b.wait("tensor", "DV", outb_mark[tt - 2])
            for dc in range(NDC):
                m_tr = b.op("tensor", lambda t, dc=dc, sub=sub, pst=pst:
                            t.transpose(
                    pst[:, dc * 128:(dc + 1) * 128],
                    aofm[:, dc, sub * 128:(sub + 1) * 128], ident[:]),
                    inc="PE" if dc == NDC - 1 else None)
            asm_tr[t4] = m_tr
            b.wait("vector", "PE", m_tr)
            b.wait("vector", "HW", m_x1)
            if tt >= 2:
                b.wait("vector", "ST", b.v("ST"))
            m_ob = b.op("vector", lambda v, par=par, pst=pst: v.tensor_tensor(
                out=outb[par][:], in0=pst[:, 0:1024], in1=x1t[par][:],
                op=ALU.add))
            outb_mark[tt] = m_ob
            b.wait("sync", "DV", m_ob)
            hdma(x1pi_d[tt * 128:(tt + 1) * 128, :], outb[par][:], sem="ST")

    w_mark = {}
    qdone = {}
    hstore = {}
    wtr_ev = {}
    ffn_hist = []
    for ic in range(NIC):
        if ic >= 1:
            b.wait("sync", "DV", qdone[ic - 1])
        mw1 = hdma(wqsl[0][:], wg_d[ic * 128:(ic + 1) * 128, :])
        mw2 = hdma(wqsl[1][:], wu_d[ic * 128:(ic + 1) * 128, :])
        w_mark[ic] = mw2
        b.wait("scalar", "HW", mw2)
        if ic >= 1:
            b.wait("scalar", "DV", qdone[ic - 1])
        m_ab1 = b.op("scalar", lambda sc: sc.activation(
            out=qs1[:], in_=wqsl[0][:], func=AF.Abs, bias=0.0, scale=1.0))
        b.wait("vector", "AC", m_ab1)
        m_rx1 = b.op("vector", lambda v: v.tensor_reduce(
            out=rmxg[:], in_=qs1[:], axis=AX, op=ALU.max))
        b.wait("scalar", "DV", m_rx1)
        m_ab2 = b.op("scalar", lambda sc: sc.activation(
            out=qs1[:], in_=wqsl[1][:], func=AF.Abs, bias=0.0, scale=1.0))
        b.wait("vector", "AC", m_ab2)
        b.op("vector", lambda v: v.tensor_reduce(
            out=rmxu[:], in_=qs1[:], axis=AX, op=ALU.max))
        b.op("vector", lambda v, ic=ic: v.tensor_tensor(
            out=panel[:, 5, ic:ic + 1], in0=rmxg[:], in1=sxc127[:],
            op=ALU.mult))
        b.op("vector", lambda v, ic=ic: v.tensor_tensor(
            out=panel[:, 6, ic:ic + 1], in0=rmxu[:], in1=sxc127[:],
            op=ALU.mult))
        b.op("vector", lambda v: v.tensor_scalar(
            out=tqg[:], in0=rmxg[:], scalar1=1.0 / 127.0, scalar2=None,
            op0=ALU.mult))
        b.op("vector", lambda v: v.tensor_scalar(
            out=tqu[:], in0=rmxu[:], scalar1=1.0 / 127.0, scalar2=None,
            op0=ALU.mult))
        b.op("vector", lambda v: v.reciprocal(out=rcg[:], in_=tqg[:]))
        b.op("vector", lambda v: v.reciprocal(out=rcu[:], in_=tqu[:]))
        for j, (src, rc, dst) in enumerate(((wqsl[0], rcg, qg),
                                            (wqsl[1], rcu, qu))):
            if ic >= 1 and j == 0:
                b.wait("vector", "PE", wtr_ev[ic - 1])
            b.op("vector", lambda v, src=src, rc=rc: v.tensor_scalar(
                out=qs1[:], in0=src[:], scalar1=rc[:, 0:1], scalar2=RB + 0.5,
                op0=ALU.mult, op1=ALU.add))
            b.op("vector", lambda v: v.tensor_copy(qsi[:], qs1[:]), nosync=True)
            b.op("vector", lambda v: v.tensor_tensor(
                out=qmk[:], in0=qsi[:], in1=qs1[:], op=ALU.is_gt), nosync=True)
            b.op("vector", lambda v: v.tensor_tensor(
                out=qs1[:], in0=qsi[:], in1=qmk[:], op=ALU.subtract),
                nosync=True)
            b.op("vector", lambda v: v.tensor_scalar(
                out=qs1[:], in0=qs1[:], scalar1=-RB, scalar2=127.0,
                op0=ALU.add, op1=ALU.min), nosync=True)
            m_qw = b.op("vector", lambda v, dst=dst: v.tensor_scalar(
                out=dst[:], in0=qs1[:], scalar1=-128.0, scalar2=None,
                op0=ALU.max), inc="DV")
        qdone[ic] = m_qw
        b.wait("tensor", "DV", m_qw)
        if ic >= 1:
            b.wait("tensor", "AC", wev_ac)
            b.wait("tensor", "DV", wev_dv)
        for dc in range(NDC):
            b.op("tensor", lambda t, dc=dc: t.transpose(
                tpx[:, dc * 128:(dc + 1) * 128],
                qg[:, dc * 128:(dc + 1) * 128], ident[:]))
            m_wtr = b.op("tensor", lambda t, dc=dc: t.transpose(
                tpy[:, dc * 128:(dc + 1) * 128],
                qu[:, dc * 128:(dc + 1) * 128], ident[:]),
                inc="PE" if dc == NDC - 1 else None)
        wtr_ev[ic] = m_wtr
        b.wait("scalar", "PE", m_wtr)
        b.wait("vector", "PE", m_wtr)
        wev_ac = b.op("scalar", lambda sc: sc.activation(
            out=wqgT[:].rearrange("p a c -> p (a c)"), in_=tpx[:, 0:1024],
            func=AF.Identity, bias=0.0, scale=1.0))
        wev_dv = b.op("vector", lambda v: v.tensor_copy(
            wquT[:].rearrange("p a c -> p (a c)"), tpy[:, 0:1024]))
        b.wait("tensor", "AC", wev_ac)
        b.wait("tensor", "DV", wev_dv)
        for t4 in range(4):
            bg_bank = t4 % 2
            bu_bank = 2 + t4 % 2
            if len(ffn_hist) >= 2:
                b.wait("tensor", "AC", ffn_hist[-2][0])
                b.wait("tensor", "DV", ffn_hist[-2][1])
            for k in range(NDC):
                b.op("tensor", lambda t, k=k, t4=t4, bank=bg_bank: t.matmul(
                    pb[bank][:], wqgT[:, k, :],
                    qx[:, k, t4 * 512:(t4 + 1) * 512],
                    start=(k == 0), stop=(k == NDC - 1)))
            for k in range(NDC):
                m_mm = b.op("tensor", lambda t, k=k, t4=t4, bank=bu_bank:
                            t.matmul(
                    pb[bank][:], wquT[:, k, :],
                    qx[:, k, t4 * 512:(t4 + 1) * 512],
                    start=(k == 0), stop=(k == NDC - 1)),
                    inc="PE" if k == NDC - 1 else None)
            b.wait("scalar", "PE", m_mm)
            if ffn_hist:
                b.wait("scalar", "DV", ffn_hist[-1][1])
            b.op("scalar", lambda sc, ic=ic, bank=bg_bank: sc.activation(
                out=th[0][:], in_=pb[bank][:], func=AF.Silu, bias=0.0,
                scale=panel[:, 5, ic:ic + 1]))
            ev_up = b.op("scalar", lambda sc, ic=ic, bank=bu_bank:
                         sc.activation(
                out=th[1][:], in_=pb[bank][:], func=AF.Identity, bias=0.0,
                scale=panel[:, 6, ic:ic + 1]))
            b.wait("vector", "AC", ev_up)
            if ic == 0 and t4 == 0:
                b.wait("vector", "ST", b.v("ST"))
            if ic >= 2 and t4 == 0:
                b.wait("vector", "GP", b.v("GP"))
            m_h = b.op("vector", lambda v, ic=ic, t4=t4: v.tensor_tensor(
                out=hbuf[ic % 2][:, t4 * 512:(t4 + 1) * 512], in0=th[0][:],
                in1=th[1][:], op=ALU.mult))
            ffn_hist.append((ev_up, m_h))
        b.wait("vector", "DV", m_h)
        b.op("vector", lambda v, ic=ic: v.tensor_reduce(
            out=panel[:, 3, ic:ic + 1], in_=hbuf[ic % 2][:], axis=AX,
            op=ALU.max))
        m_st = b.op("vector", lambda v, ic=ic: v.tensor_reduce(
            out=panel[:, 4, ic:ic + 1], in_=hbuf[ic % 2][:], axis=AX,
            op=ALU.min))
        b.wait("gpsimd", "DV", m_st)
        hstore[ic] = gdma(hi_d[:, ic, :], hbuf[ic % 2][:])

    b.op("vector", lambda v: v.tensor_scalar(
        out=panel[:, 7, 0:NIC], in0=panel[:, 4, 0:NIC], scalar1=-1.0,
        scalar2=None, op0=ALU.mult))
    b.op("vector", lambda v: v.tensor_tensor(
        out=panel[:, 7, 0:NIC], in0=panel[:, 3, 0:NIC],
        in1=panel[:, 7, 0:NIC], op=ALU.max))
    m_m2r = b.op("vector", lambda v: v.tensor_reduce(
        out=panel[:, 7, 32:33], in_=panel[:, 7, 0:NIC], axis=AX, op=ALU.max))
    b.wait("gpsimd", "DV", m_m2r)
    m_m2 = gfn(lambda g: g.tensor_reduce(
        out=m2c[:], in_=panel[:, 7, 32:33], axis=AXC, op=ALU.max), inc="GC")

    b.wait("gpsimd", "GC", m_m2)
    mc2 = gdma(c2i_t.ap()[:, :], m2c[:])
    b.wait("gpsimd", "GP", mc2)
    m_cc2 = b.op("gpsimd", lambda g: g.collective_compute(
        "AllReduce", ALU.max, replica_groups=[list(range(8))],
        ins=[c2i_t.ap().opt()], outs=[c2o_t.ap().opt()]), inc="GC")
    b.wait("gpsimd", "GC", m_cc2)
    gdma(rowp[0:1, 96:97], c2o_t.ap()[:, :])
    gdma(rowq[0:1, 32:33], clipd_d[:, :])
    mega_bar()

    b.op("vector", lambda v: v.tensor_scalar(
        out=rowp[0:1, 100:116], in0=onesf[0:1, 0:16], scalar1=rowp[0:1, 96:97],
        scalar2=None, op0=ALU.mult))
    m_rq3 = b.op("vector", lambda v: v.tensor_scalar(
        out=rowq[0:1, 100:116], in0=onesf[0:1, 0:16], scalar1=rowq[0:1, 32:33],
        scalar2=None, op0=ALU.mult))
    b.wait("scalar", "DV", m_rq3)
    m_sg3 = b.op("scalar", lambda sc: sc.activation(
        out=rowq[0:1, 80:96], in_=rowq[0:1, 100:116], func=AF.Sigmoid,
        bias=0.0, scale=1.0))
    b.wait("vector", "AC", m_sg3)
    b.op("vector", lambda v: v.tensor_tensor(
        out=rowp[0:1, 80:96], in0=rowp[0:1, 100:116], in1=rowq[0:1, 80:96],
        op=ALU.mult))
    b.op("vector", lambda v: v.tensor_scalar(
        out=rowp[0:1, 64:80], in0=rowp[0:1, 80:96], scalar1=1.0 / 127.0,
        scalar2=None, op0=ALU.mult))
    b.op("vector", lambda v: v.reciprocal(
        out=rowp[0:1, 48:64], in_=rowp[0:1, 64:80]))
    m_row3 = b.op("vector", lambda v: v.tensor_scalar(
        out=rowp[0:1, 32:48], in0=rowp[0:1, 64:80], scalar1=1.0 / 127.0,
        scalar2=None, op0=ALU.mult))
    b.wait("tensor", "DV", m_row3)
    b.op("tensor", lambda t: t.matmul(
        px[:, 0:1], onesf[:, :], rowp[0:1, 48:49], start=True, stop=True))
    m_bc3 = b.op("tensor", lambda t: t.matmul(
        px[:, 1:2], onesf[:, :], rowp[0:1, 32:33], start=True, stop=True),
        inc="PE")
    b.wait("vector", "PE", m_bc3)
    b.op("vector", lambda v: v.tensor_copy(reciph[:], px[:, 0:1]))
    m_bc3v = b.op("vector", lambda v: v.tensor_copy(shc127[:], px[:, 1:2]))

    w3q_mark = {}
    w3tr = {}
    last_ev = {"x": None, "y": None}
    for dc in range(NDC):
        par = dc % 2
        if dc >= 2:
            b.wait("sync", "DV", w3q_mark[dc - 2])
        mw3 = hdma(w3sl[par][:], wd_d[dc * 128:(dc + 1) * 128, :])
        b.wait("scalar", "HW", mw3)
        if dc >= 1:
            b.wait("scalar", "DV", w3q_mark[dc - 1])
        m_ab3 = b.op("scalar", lambda sc, par=par: sc.activation(
            out=q3s1[:], in_=w3sl[par][:], func=AF.Abs, bias=0.0, scale=1.0))
        b.wait("vector", "AC", m_ab3)
        if dc == 0:
            b.wait("vector", "DV", m_bc3v)
        b.op("vector", lambda v: v.tensor_reduce(
            out=rmxg[:], in_=q3s1[:], axis=AX, op=ALU.max))
        b.op("vector", lambda v, dc=dc: v.tensor_tensor(
            out=panel[:, 5, 32 + dc:33 + dc], in0=rmxg[:], in1=shc127[:],
            op=ALU.mult))
        b.op("vector", lambda v: v.tensor_scalar(
            out=tqg[:], in0=rmxg[:], scalar1=1.0 / 127.0, scalar2=None,
            op0=ALU.mult))
        b.op("vector", lambda v: v.reciprocal(out=rcg[:], in_=tqg[:]))
        b.op("vector", lambda v, par=par: v.tensor_scalar(
            out=q3s1[:], in0=w3sl[par][:], scalar1=rcg[:, 0:1],
            scalar2=RB + 0.5, op0=ALU.mult, op1=ALU.add))
        b.op("vector", lambda v: v.tensor_copy(q3si[:], q3s1[:]), nosync=True)
        b.op("vector", lambda v: v.tensor_tensor(
            out=q3mk[:], in0=q3si[:], in1=q3s1[:], op=ALU.is_gt), nosync=True)
        b.op("vector", lambda v: v.tensor_tensor(
            out=q3s1[:], in0=q3si[:], in1=q3mk[:], op=ALU.subtract),
            nosync=True)
        b.op("vector", lambda v: v.tensor_scalar(
            out=q3s1[:], in0=q3s1[:], scalar1=-RB, scalar2=127.0,
            op0=ALU.add, op1=ALU.min), nosync=True)
        if dc >= 1:
            b.wait("vector", "PE", w3tr[dc - 1])
        m_qd = b.op("vector", lambda v: v.tensor_scalar(
            out=qd3[:], in0=q3s1[:], scalar1=-128.0, scalar2=None,
            op0=ALU.max))
        w3q_mark[dc] = m_qd
        b.wait("tensor", "DV", m_qd)
        for (i0, i1, pst, pn) in ((0, 8, tpx, "x"), (8, 16, tpy, "y"),
                                  (16, 22, tpx, "x")):
            if last_ev[pn] is not None:
                b.wait("tensor", "AC", last_ev[pn])
            for ic in range(i0, i1):
                m_tr3 = b.op("tensor", lambda t, ic=ic, pst=pst, i0=i0:
                             t.transpose(
                    pst[:, (ic - i0) * 128:(ic - i0 + 1) * 128],
                    qd3[:, ic * 128:(ic + 1) * 128], ident[:]),
                    inc="PE" if ic == i1 - 1 else None)
            b.wait("scalar", "PE", m_tr3)
            last_ev[pn] = b.op("scalar", lambda sc, dc=dc, i0=i0, i1=i1,
                               pst=pst: sc.activation(
                out=wdT[:, dc, i0:i1, :].rearrange("p a c -> p (a c)"),
                in_=pst[:, 0:(i1 - i0) * 128], func=AF.Identity, bias=0.0,
                scale=1.0), inc="AC")
        w3tr[dc] = m_tr3
    wdT_done = b.v("AC")

    hq_mark = {}
    ob3_mark = {}
    mm3_last = 0
    hflat = hsl[:].rearrange("p a t -> p (a t)")
    hqflat = hq3[:].rearrange("p a t -> p (a t)")
    for t4 in range(4):
        if t4 >= 1:
            b.wait("sync", "DV", hq_mark[t4 - 1])
        mh3 = hdma(hsl[:], hi_d[:, :, t4 * 512:(t4 + 1) * 512])
        b.wait("vector", "HW", mh3)
        if t4 >= 1:
            b.wait("vector", "PE", mm3_last)
        for ch in range(4):
            c0, c1 = ch * FI, (ch + 1) * FI
            b.op("vector", lambda v, c0=c0, c1=c1: v.tensor_scalar(
                out=q3s1[:], in0=hflat[:, c0:c1], scalar1=reciph[:, 0:1],
                scalar2=RB + 0.5, op0=ALU.mult, op1=ALU.add))
            b.op("vector", lambda v: v.tensor_copy(q3si[:], q3s1[:]),
                 nosync=True)
            b.op("vector", lambda v: v.tensor_tensor(
                out=q3mk[:], in0=q3si[:], in1=q3s1[:], op=ALU.is_gt),
                nosync=True)
            b.op("vector", lambda v: v.tensor_tensor(
                out=q3s1[:], in0=q3si[:], in1=q3mk[:], op=ALU.subtract),
                nosync=True)
            b.op("vector", lambda v: v.tensor_scalar(
                out=q3s1[:], in0=q3s1[:], scalar1=-RB, scalar2=127.0,
                op0=ALU.add, op1=ALU.min), nosync=True)
            m_hq = b.op("vector", lambda v, c0=c0, c1=c1: v.tensor_scalar(
                out=hqflat[:, c0:c1], in0=q3s1[:], scalar1=-128.0,
                scalar2=None, op0=ALU.max), inc="DV")
        hq_mark[t4] = m_hq
        b.wait("tensor", "DV", m_hq)
        if t4 == 0:
            b.wait("tensor", "AC", wdT_done)
        for half in range(2):
            b.wait("tensor", "AC", b.v("AC"))
            for dc4 in range(4):
                dc = half * 4 + dc4
                for ic in range(NIC):
                    m_mm3 = b.op("tensor", lambda t, dc=dc, ic=ic, dc4=dc4:
                                 t.matmul(
                        pb[dc4][:], wdT[:, dc, ic, :], hq3[:, ic, :],
                        start=(ic == 0), stop=(ic == NIC - 1)),
                        inc="PE" if ic == NIC - 1 else None)
                b.wait("scalar", "PE", m_mm3)
                b.op("scalar", lambda sc, dc=dc, dc4=dc4: sc.activation(
                    out=dfm[:, dc, :], in_=pb[dc4][:], func=AF.Identity,
                    bias=0.0, scale=panel[:, 5, 32 + dc:33 + dc]),
                    inc="AC" if dc4 == 3 else None)
        dn_done = b.v("AC")
        for sub in range(4):
            tt = t4 * 4 + sub
            par = sub % 2
            pst = tpx if par == 0 else tpy
            if tt >= 2:
                b.wait("sync", "DV", ob3_mark[tt - 2])
            mx3 = hdma(x3t[par][:], x1pi_d[tt * 128:(tt + 1) * 128, :])
            b.wait("tensor", "AC", dn_done)
            if tt >= 2:
                b.wait("tensor", "DV", ob3_mark[tt - 2])
            for dc in range(NDC):
                m_tr = b.op("tensor", lambda t, dc=dc, sub=sub, pst=pst:
                            t.transpose(
                    pst[:, dc * 128:(dc + 1) * 128],
                    dfm[:, dc, sub * 128:(sub + 1) * 128], ident[:]),
                    inc="PE" if dc == NDC - 1 else None)
            b.wait("vector", "PE", m_tr)
            b.wait("vector", "HW", mx3)
            if tt >= 2:
                b.wait("vector", "ST", b.v("ST"))
            m_ob3 = b.op("vector", lambda v, par=par, pst=pst: v.tensor_tensor(
                out=ob3[par][:], in0=pst[:, 0:1024], in1=x3t[par][:],
                op=ALU.add))
            ob3_mark[tt] = m_ob3
            b.wait("sync", "DV", m_ob3)
            hdma(out_d[tt * 128:(tt + 1) * 128, :], ob3[par][:], sem="ST")
        mm3_last = b.v("PE")

    b.wait("sync", "ST", b.v("ST"))
    b.wait("sync", "GP", b.v("GP"))
    b.wait("sync", "HW", b.v("HW"))

    with b.es, nc.Block() as block:
        b.run(block)
    return nc


_MEGA = {}


def _pack_params(inp):
    rep = lambda a: np.ascontiguousarray(np.asarray(a, dtype=np.float32))
    bzp = np.concatenate([rep(inp["L1_attn_b_zf"]).reshape(4, 128).T,
                          rep(inp["L1_attn_b_zs"]).reshape(4, 128).T], axis=1)
    bhp = np.concatenate([rep(inp["L1_attn_b_hf"]).reshape(4, 128).T,
                          rep(inp["L1_attn_b_hs"]).reshape(4, 128).T], axis=1)
    return {
        "W_in": rep(inp["L0_attn_W_in"]),
        "conv_w": rep(inp["L0_attn_conv_w"]).reshape(DI, 4),
        "conv_b": np.ascontiguousarray(rep(inp["L0_attn_conv_b"]).reshape(NCC, 128).T),
        "W_xproj": rep(inp["L0_attn_W_xproj"]),
        "W_dt": rep(inp["L0_attn_W_dt"]),
        "b_dt": np.ascontiguousarray(rep(inp["L0_attn_b_dt"]).reshape(NCC, 128).T),
        "A_log": rep(inp["L0_attn_A_log"]),
        "D_param": np.ascontiguousarray(rep(inp["L0_attn_D_param"]).reshape(NCC, 128).T),
        "W_out": rep(inp["L0_attn_W_out"]),
        "fWin": rep(inp["L0_ffn_Win"]),
        "fbin": rep(inp["L0_ffn_bin_"]).reshape(64, 1),
        "fW1": rep(inp["L0_ffn_W1"]),
        "fb1": rep(inp["L0_ffn_b1"]),
        "fW2": rep(inp["L0_ffn_W2"]),
        "fb2": rep(inp["L0_ffn_b2"]).reshape(64, 1),
        "fWout": rep(inp["L0_ffn_Wout"]),
        "fbout": rep(inp["L0_ffn_bout"]).reshape(1, D),
        "clip_x": rep(inp["L1_ffn_clip_x"]).reshape(1, 1),
        "clip_down": rep(inp["L1_ffn_clip_down"]).reshape(1, 1),
        "W_zf": rep(inp["L1_attn_W_zf"]),
        "W_zs": rep(inp["L1_attn_W_zs"]),
        "W_hf": rep(inp["L1_attn_W_hf"]),
        "W_hs": rep(inp["L1_attn_W_hs"]),
        "b_z": np.ascontiguousarray(bzp),
        "b_h": np.ascontiguousarray(bhp),
        "W_g": rep(inp["L1_attn_W_g"]),
        "b_g": np.ascontiguousarray(rep(inp["L1_attn_b_g"]).reshape(8, 128).T),
        "w_gate": rep(inp["L1_ffn_w_gate"]),
        "w_up": rep(inp["L1_ffn_w_up"]),
        "w_down": rep(inp["L1_ffn_w_down"]),
    }


def kernel(**inp):
    x = np.ascontiguousarray(np.asarray(inp["L0_attn_x"], np.float32))
    w = _pack_params(inp)
    if "nc" not in _MEGA:
        _MEGA["nc"] = build_mega()
    in_maps = [dict(w, x0=np.ascontiguousarray(x[c])) for c in range(8)]
    res = run_bass_kernel_spmd(_MEGA["nc"], in_maps, core_ids=list(range(8)))
    return np.stack([np.asarray(res.results[c]["out"], np.float32)
                     for c in range(8)])


# version 09
# speedup vs baseline: 1.1806x; 1.0052x over previous
from contextlib import ExitStack

import numpy as np
import concourse.bass as bass
import concourse.mybir as mybir
from concourse.bass_utils import run_bass_kernel_spmd

F32 = mybir.dt.float32
BF16 = mybir.dt.bfloat16
I32 = mybir.dt.int32
AF = mybir.ActivationFunctionType
ALU = mybir.AluOpType
AX = mybir.AxisListType.X
AXC = mybir.AxisListType.C

T = 2048
D = 1024
DI = 2048
NS = 16
NCC = DI // 128
NDC = D // 128
TQ = 512
NQ = T // TQ
FI = 2816
NIC = FI // 128
NT = T // 128
L4 = 4 * (TQ + 1)
RB = 16384.0


class B:

    def __init__(self, nc):
        self.nc = nc
        self.ops = {"sync": [], "gpsimd": [], "vector": [], "scalar": [], "tensor": []}
        self.es = ExitStack()
        self.sems = {}
        self.maxw = {}

    def dma_fence(self, eng, sname):
        v = self.maxw.get(sname, 0)
        if v > 0:
            h = self.sems[sname][0]
            self.ops[eng].append(lambda e, h=h, v=v: e.wait_ge(h, v))

    def sem(self, name, step=1):
        h = self.es.enter_context(self.nc.semaphore(name))
        self.sems[name] = [h, step, 0]
        return name

    def wait(self, eng, sname, val):
        h = self.sems[sname][0]
        if val > 0:
            self.maxw[sname] = max(self.maxw.get(sname, 0), val)
            self.ops[eng].append(lambda e, h=h, v=val: e.wait_ge(h, v))

    OWN = {"vector": "DV", "scalar": "AC", "gpsimd": "GC"}

    def op(self, eng, fn, inc=None, nosync=False):
        own = self.OWN.get(eng)
        if own is not None and own in self.sems and not nosync:
            if eng != "scalar":
                self.wait(eng, own, self.sems[own][2])
            if inc is None:
                inc = own
        if inc is None:
            self.ops[eng].append(fn)
            return None
        h, step, _ = self.sems[inc]
        def g(e, fn=fn, h=h, step=step):
            fn(e).then_inc(h, step)
        self.ops[eng].append(g)
        self.sems[inc][2] += step
        return self.sems[inc][2]

    def v(self, sname):
        return self.sems[sname][2]

    def run(self, block):
        for eng in ("sync", "gpsimd", "vector", "scalar", "tensor"):
            ops = self.ops[eng]
            def body(e, ops=ops):
                for f in ops:
                    f(e)
            getattr(block, eng)(body)


def rkpm(ap, p=128):
    return ap.rearrange("(k p) m -> p k m", p=p)


def colv(ap):
    return ap.rearrange("(c p) o -> p (c o)", p=128)


def build_launch1(dbg=False, dbgq=0):
    nc = bass.Bass("TRN2", target_bir_lowering=False)

    def din(name, shape):
        return nc.dram_tensor(name, shape, F32, kind="ExternalInput").ap()

    x0_d = din("x0", [T, D])
    Win_d = din("W_in", [D, 2 * DI])
    convw_d = din("conv_w", [DI, 4])
    convb_d = din("conv_b", [128, NCC])
    Wxp_d = din("W_xproj", [DI, 32])
    Wdt_d = din("W_dt", [DI, DI])
    bdt_d = din("b_dt", [128, NCC])
    Alog_d = din("A_log", [DI, NS])
    Dp_d = din("D_param", [128, NCC])
    Wout_d = din("W_out", [DI, D])
    fWin_d = din("fWin", [D, 64])
    fbin_d = din("fbin", [64, 1])
    fW1_d = din("fW1", [64, 16])
    fb1_d = din("fb1", [64, 16])
    fW2_d = din("fW2", [64, 16])
    fb2_d = din("fb2", [64, 1])
    fWout_d = din("fWout", [64, D])
    fbout_d = din("fbout", [1, D])

    x1_d = nc.dram_tensor("x1", [T, D], F32, kind="ExternalOutput").ap()
    if dbg:
        dbg_d = {nm: nc.dram_tensor("dbg_" + nm, sh, F32, kind="ExternalOutput").ap()
                 for nm, sh in [("xn0", [128, NDC, T]), ("xc", [128, NCC, TQ + 3]),
                                ("xs", [128, NCC, TQ]), ("dlt", [128, NCC, TQ]),
                                ("slz", [128, NCC, TQ]), ("ub", [128, NCC, TQ]),
                                ("y", [128, NCC, TQ]), ("bc", [32, TQ]),
                                ("gacc", [65, T]), ("ofm", [128, NDC, TQ]),
                                ("carry", [128, NS * NCC]), ("stat", [128, 8, 128]), ("stat2", [128, 8, 128])]}
    mrow_d = nc.dram_tensor("mrow", [128, 2 * NT], F32, kind="ExternalOutput").ap()
    m1_d = nc.dram_tensor("m1", [1, 1], F32, kind="ExternalOutput").ap()

    b = B(nc)
    es = b.es
    sb = lambda n, s, d: es.enter_context(nc.sbuf_tensor(n, s, d))
    psal = lambda n, s: es.enter_context(nc.psum_tensor(n, s, F32))

    arA = sb("arA", [128, 16384], F32)
    x0t = arA[:].rearrange("p (a d) -> p a d", d=D)
    arAb = nc.alloc_sbuf_tensor_at("arAb", [128, 32768], BF16,
                                   offset=nc.lookup_mloc(arA).addr)
    dlt = arAb[:, 0 * 8192:1 * 8192].rearrange("p (c t) -> p c t", t=TQ)
    ub = arAb[:, 1 * 8192:2 * 8192].rearrange("p (c t) -> p c t", t=TQ)
    slz = arAb[:, 2 * 8192:3 * 8192].rearrange("p (c t) -> p c t", t=TQ)
    xsb = arAb[:, 3 * 8192:4 * 8192].rearrange("p (c t) -> p c t", t=TQ)
    x1sA = arA[:].rearrange("p (a d) -> p a d", d=1024)
    x1st = [x1sA[:, 0, :], x1sA[:, 1, :]]
    x0st = [x1sA[:, 2, :], x1sA[:, 3, :]]
    sq4 = x1sA[:, 4, :]
    xn1s = x1sA[:, 5, :]
    arB = sb("arB", [128, 16 * (TQ + 3)], BF16)
    xcb = arB[:].rearrange("p (c t) -> p c t", t=TQ + 3)
    ftmp = arB[0:64, 0:T]
    ftm2 = arB[0:64, T:2 * T]
    ofm = arB[:, 0:NDC * TQ].rearrange("p (a t) -> p a t", t=TQ)

    xn0 = sb("xn0", [128, NDC, T], BF16)
    carry = sb("carry", [128, NS * NCC], F32)
    bcrows = sb("bcrows", [32, TQ], BF16)
    bbt = sb("bbt", [128, TQ], BF16)
    cbt = sb("cbt", [128, TQ], BF16)
    dA2 = [sb(f"dA{i}", [128, L4], F32) for i in range(2)]
    dBp = sb("dBp", [128, L4], BF16)
    hhp = sb("hhp", [128, L4], BF16)
    tpp = sb("tpp", [128, L4], BF16)
    dA0b = nc.alloc_sbuf_tensor_at("dA0b", [128, 4104], BF16,
                                   offset=nc.lookup_mloc(dA2[0]).addr)
    xtm2 = [dA0b[:, 0:D], dA0b[:, D:2 * D]]
    sq1 = dA2[1][:, 0:D]
    wsl = [sb(f"wsl{i}", [128, 8, 512], BF16) for i in range(2)]
    wdt = [sb(f"wdt{i}", [128, NCC, 256], BF16) for i in range(2)]
    wxp = sb("wxp", [128, NCC, 32], BF16)
    wou = [sb(f"wou{i}", [128, NCC, 128], BF16) for i in range(2)]
    convw = sb("convw", [128, NCC, 4], F32)
    diag4 = [sb(f"diag4{i}", [128, 4, 128], BF16) for i in range(2)]
    ident = sb("ident", [128, 128], BF16)
    identf = sb("identf", [128, 32], F32)
    esel = [sb(f"esel{i}", [32, 128], BF16) for i in range(4)]
    ones32 = sb("ones32", [32, 128], BF16)
    ones1 = sb("ones1", [1, 128], BF16)
    Asb = sb("Asb", [128, NCC, NS], F32)
    Dsb = sb("Dsb", [128, NCC], F32)
    convb = sb("convb", [128, NCC], F32)
    bdt = sb("bdt", [128, NCC], F32)
    fwin = sb("fwin", [128, NDC, 64], BF16)
    sfw1 = sb("sfw1", [64, 16], F32)
    sfb1 = sb("sfb1", [64, 16], F32)
    sfw2 = sb("sfw2", [64, 16], F32)
    sfb2 = sb("sfb2", [64, 1], F32)
    sfbin = sb("sfbin", [64, 1], F32)
    fwoc = sb("fwoc", [65, D], BF16)
    gacc = sb("gacc", [65, T], BF16)
    statc = sb("statc", [128, 8, 128], F32)
    stat2 = statc
    m1c = sb("m1c", [1, 1], F32)
    marg = sb("marg", [128, NCC, 3], BF16)
    epsb = sb("epsb", [128, 1], F32)

    pb = [psal(f"pb{i}", [128, 512]) for i in range(5)]
    px = psal("px", [128, 512])
    tpx = es.enter_context(nc.psum_tensor("tpx", [128, 1024], BF16))
    tpy = es.enter_context(nc.psum_tensor("tpy", [128, 1024], BF16))

    b.sem("HW", 16)
    b.sem("GP", 16)
    b.sem("PE", 1)
    b.sem("AC", 1)
    b.sem("DV", 1)
    b.sem("GC", 1)
    b.sem("ST", 16)

    def gdma(out, in_):
        b.dma_fence("gpsimd", "GP")
        return b.op("gpsimd", lambda g, o=out, i=in_: g.dma_start(out=o, in_=i),
                    inc="GP")

    def hdma(out, in_, sem="HW"):
        b.dma_fence("sync", sem)
        return b.op("sync", lambda s, o=out, i=in_: s.dma_start(out=o, in_=i),
                    inc=sem)

    def gfn(f, inc=None):
        return b.op("gpsimd", f, inc=inc)

    def qbar():
        tgt = []
        m = b.op("vector", lambda e: e.engine_nop(), inc="DV")
        tgt.append(("DV", m))
        m = b.op("scalar", lambda e: e.activation(out=epsb[:], in_=epsb[:],
                                                  func=AF.Identity, bias=0.0,
                                                  scale=1.0), inc="AC")
        tgt.append(("AC", m))
        m = b.op("gpsimd", lambda e: e.engine_nop(), inc="GC")
        tgt.append(("GC", m))
        tgt.append(("PE", b.v("PE")))
        for eng in ("vector", "scalar", "gpsimd", "tensor", "sync"):
            for sem, m in tgt:
                b.wait(eng, sem, m)

    def dbg_dump(name, src):
        if not dbg:
            return
        b.wait("gpsimd", "AC", b.v("AC"))
        b.wait("gpsimd", "DV", b.v("DV"))
        mst = b.op("gpsimd", lambda g, o=dbg_d[name], i=src: g.dma_start(out=o, in_=i),
                   inc="GP")
        for eng in ("vector", "scalar", "tensor", "sync"):
            b.wait(eng, "GP", mst)

    gdma(Asb[:], Alog_d.rearrange("(c p) s -> p c s", p=128))
    gdma(Dsb[:], Dp_d[:, :])
    gdma(convb[:], convb_d[:, :])
    gdma(bdt[:], bdt_d[:, :])
    gdma(convw[:], convw_d.rearrange("(c p) k -> p c k", p=128))
    gdma(wxp[:], Wxp_d.rearrange("(c p) s -> p c s", p=128))
    gdma(fwin[:], fWin_d.rearrange("(c p) m -> p c m", p=128))
    gdma(sfw1[:], fW1_d[:, :])
    gdma(sfb1[:], fb1_d[:, :])
    gdma(sfw2[:], fW2_d[:, :])
    gdma(sfb2[:], fb2_d[:, :])
    gdma(sfbin[:], fbin_d[:, :])
    gdma(fwoc[0:64, :], fWout_d[:, :])
    params_ld = gdma(fwoc[64:65, :], fbout_d[:, :])
    x0_ld = hdma(x0t, x0_d.rearrange("(a p) d -> p a d", p=128))

    b.wait("gpsimd", "GP", params_ld)
    gfn(lambda g: g.memset(ident[:], 0.0))
    gfn(lambda g: g.affine_select(out=ident[:], in_=ident[:],
                                  compare_op=ALU.not_equal, fill=1.0, base=0,
                                  pattern=[[-1, 128]], channel_multiplier=1))
    gfn(lambda g: g.memset(ones1[:], 1.0))
    gfn(lambda g: g.memset(ones32[:], 1.0))
    gfn(lambda g: g.memset(identf[:], 0.0))
    gfn(lambda g: g.affine_select(out=identf[:], in_=identf[:],
                                  compare_op=ALU.not_equal, fill=1.0, base=0,
                                  pattern=[[-1, 32]], channel_multiplier=1))

    gfn(lambda g: g.memset(carry[:], 0.0))
    gfn(lambda g: g.memset(gacc[64:65, :], 1.0))
    gfn(lambda g: g.memset(statc[:], 0.0))
    gfn(lambda g: g.memset(marg[:], 0.0))
    gfn(lambda g: g.memset(epsb[:], 1e-5))
    gfn(lambda g: g.memset(dA2[0][:], 0.0))
    const_done = gfn(lambda g: g.engine_nop(), inc="GC")

    b.wait("scalar", "GP", params_ld)
    aexp = b.op("scalar", lambda sc: sc.activation(
        out=Asb[:], in_=Asb[:], func=AF.Exp, bias=0.0, scale=1.0), inc="AC")
    b.wait("vector", "AC", aexp)
    b.wait("vector", "GC", const_done)
    b.op("vector", lambda v: v.tensor_scalar(
        out=Asb[:], in0=Asb[:], scalar1=-1.0, scalar2=None, op0=ALU.mult),
        inc="DV")

    b.wait("scalar", "HW", x0_ld)
    b.wait("scalar", "GC", const_done)
    b.wait("vector", "HW", x0_ld)
    b.wait("vector", "GC", const_done)
    sq_red = {}
    for tt in range(NT):
        if tt > 0:
            b.wait("scalar", "DV", sq_red[tt - 1])
        m_sq = b.op("scalar", lambda sc, tt=tt: sc.activation(
            out=sq1, in_=x0t[:, tt, :], func=AF.Square, bias=0.0, scale=1.0),
            inc="AC")
        b.wait("vector", "AC", m_sq)
        sq_red[tt] = b.op("vector", lambda v, tt=tt: v.tensor_reduce(
            out=statc[:, 1, tt:tt + 1], in_=sq1, axis=AX, op=ALU.add), inc="DV")
        b.op("vector", lambda v, tt=tt: v.tensor_reduce(
            out=statc[:, 2, tt:tt + 1], in_=x0t[:, tt, :], axis=AX, op=ALU.add))
    m_syn = b.op("vector", lambda v: v.engine_nop(), inc="DV")
    b.wait("vector", "DV", m_syn)
    b.op("vector", lambda v: v.tensor_scalar(
        out=statc[:, 3, :], in0=statc[:, 2, :], scalar1=1.0 / D, scalar2=None,
        op0=ALU.mult))
    b.op("vector", lambda v: v.tensor_scalar(
        out=statc[:, 4, :], in0=statc[:, 1, :], scalar1=1.0 / D, scalar2=None,
        op0=ALU.mult))
    b.op("vector", lambda v: v.tensor_tensor(
        out=statc[:, 5, :], in0=statc[:, 3, :], in1=statc[:, 3, :], op=ALU.mult))
    m_var = b.op("vector", lambda v: v.tensor_tensor(
        out=statc[:, 5, :], in0=statc[:, 4, :], in1=statc[:, 5, :],
        op=ALU.subtract), inc="DV")
    b.wait("scalar", "DV", m_var)
    m_sd = b.op("scalar", lambda sc: sc.activation(
        out=statc[:, 6, :], in_=statc[:, 5, :], func=AF.Sqrt,
        bias=epsb[:, 0:1], scale=1.0), inc="AC")
    b.wait("vector", "AC", m_sd)
    b.op("vector", lambda v: v.reciprocal(out=statc[:, 7, :], in_=statc[:, 6, :]))
    b.op("vector", lambda v: v.tensor_tensor(
        out=statc[:, 0, :], in0=statc[:, 3, :], in1=statc[:, 7, :], op=ALU.mult))
    m_nb = b.op("vector", lambda v: v.tensor_scalar(
        out=statc[:, 0, :], in0=statc[:, 0, :], scalar1=-1.0, scalar2=None,
        op0=ALU.mult), inc="DV")
    b.wait("scalar", "DV", m_nb)
    ev1 = []
    for tt in range(NT):
        par = tt % 2
        m_xn = b.op("scalar", lambda sc, tt=tt, par=par: sc.activation(
            out=xtm2[par], in_=x0t[:, tt, :], func=AF.Identity,
            bias=statc[:, 0, tt:tt + 1], scale=statc[:, 7, tt:tt + 1]), inc="AC")
        pst = tpx if par == 0 else tpy
        b.wait("tensor", "AC", m_xn)
        if tt >= 2:
            b.wait("tensor", "AC", ev1[tt - 2])
        for dc in range(NDC):
            m_tr = b.op("tensor", lambda t, dc=dc, par=par, pst=pst: t.transpose(
                pst[:, dc * 128:(dc + 1) * 128],
                xtm2[par][:, dc * 128:(dc + 1) * 128], ident[:]),
                inc="PE" if dc == NDC - 1 else None)
        b.wait("scalar", "PE", m_tr)
        m_ev = b.op("scalar", lambda sc, tt=tt, pst=pst: sc.activation(
            out=xn0[:, :, tt * 128:(tt + 1) * 128],
            in_=pst[:, 0:1024], func=AF.Identity, bias=0.0,
            scale=1.0), inc="AC")
        ev1.append(m_ev)
    xn0_done = b.v("AC")
    dbg_dump("stat", statc[:])

    b.wait("gpsimd", "AC", xn0_done)
    gfn(lambda g: g.memset(dA2[0][:], 0.0))
    gfn(lambda g: g.memset(dBp[:], 0.0))
    gfn(lambda g: g.memset(hhp[:], 0.0))
    gfn(lambda g: g.memset(tpp[:], 0.0))
    bndz = gfn(lambda g: g.memset(dA2[1][:], 0.0), inc="GC")
    b.wait("scalar", "GC", bndz)
    b.wait("vector", "GC", bndz)

    b.wait("tensor", "AC", xn0_done)
    for t4 in range(4):
        for k in range(NDC):
            m_mm = b.op("tensor", lambda t, t4=t4, k=k: t.matmul(
                pb[t4][0:64, :], fwin[:, k, 0:64],
                xn0[:, k, t4 * 512:(t4 + 1) * 512],
                start=(k == 0), stop=(k == NDC - 1)),
                inc="PE" if k == NDC - 1 else None)
        b.wait("scalar", "PE", m_mm)
        b.op("scalar", lambda sc, t4=t4: sc.activation(
            out=gacc[0:64, t4 * 512:(t4 + 1) * 512], in_=pb[t4][0:64, :],
            func=AF.Identity, bias=sfbin[:, 0:1], scale=1.0),
            inc="AC" if t4 == 3 else None)
    spre_done = b.v("AC")

    b.wait("scalar", "AC", spre_done)
    for j in range(16):
        m_rl = b.op("scalar", lambda sc, j=j: sc.activation(
            out=ftmp, in_=gacc[0:64, :], func=AF.Relu,
            bias=sfb1[:, j:j + 1], scale=sfw1[:, j:j + 1]), inc="AC")
        b.wait("vector", "AC", m_rl)
        if j == 0:
            m_g = b.op("vector", lambda v: v.tensor_scalar(
                out=ftm2, in0=ftmp, scalar1=sfw2[:, 0:1], scalar2=None,
                op0=ALU.mult), inc="DV")
        else:
            b.op("vector", lambda v, j=j: v.tensor_scalar(
                out=ftmp, in0=ftmp, scalar1=sfw2[:, j:j + 1], scalar2=None,
                op0=ALU.mult))
            m_g = b.op("vector", lambda v: v.tensor_tensor(
                out=ftm2, in0=ftm2, in1=ftmp, op=ALU.add), inc="DV")
        if j < 15:
            b.wait("scalar", "DV", m_g)
    b.wait("scalar", "DV", b.v("DV"))
    gacc_done = b.op("scalar", lambda sc: sc.activation(
        out=gacc[0:64, :], in_=ftm2, func=AF.Identity, bias=sfb2[:, 0:1],
        scale=1.0), inc="AC")

    b.wait("gpsimd", "DV", b.v("DV"))
    b.wait("gpsimd", "AC", b.v("AC"))
    z2 = gfn(lambda g: g.memset(statc[:], 0.0), inc="GC")
    b.wait("vector", "GC", z2)

    x1_asm = {}
    sq4_red = {}
    x1_st_mark = {}
    x1_sq_mark = {}
    seg = [j * (TQ + 1) for j in range(4)]
    bnd = [j * (TQ + 1) + TQ for j in range(3)]

    def v3(ap):
        return ap.rearrange("p t -> p 1 t")

    for q in range(NQ):
        t0 = q * TQ
        for g8 in range(8):
            buf = g8 % 2
            b.wait("gpsimd", "PE", b.v("PE"))
            mark_w = gdma(wsl[buf][:], rkpm(Win_d)[:, :, g8 * 512:(g8 + 1) * 512])
            b.wait("tensor", "GP", mark_w)
            if q == 0 and g8 == 0:
                b.wait("tensor", "AC", gacc_done)
            b.wait("tensor", "AC", b.v("AC"))
            for k in range(NDC):
                for m in range(4):
                    m_mm = b.op("tensor", lambda t, buf=buf, k=k, m=m, t0=t0: t.matmul(
                        pb[m][:], wsl[buf][:, k, m * 128:(m + 1) * 128],
                        xn0[:, k, t0:t0 + TQ],
                        start=(k == 0), stop=(k == NDC - 1)),
                        inc="PE" if (k == NDC - 1 and m == 3) else None)
            b.wait("scalar", "PE", m_mm)
            for m in range(4):
                if g8 < 4:
                    cc = g8 * 4 + m
                    b.op("scalar", lambda sc, cc=cc, m=m: sc.activation(
                        out=xcb[:, cc, 3:3 + TQ], in_=pb[m][:], func=AF.Identity,
                        bias=0.0, scale=1.0), inc="AC" if m == 3 else None)
                else:
                    cc = (g8 - 4) * 4 + m
                    b.op("scalar", lambda sc, cc=cc, m=m: sc.activation(
                        out=slz[:, cc, :], in_=pb[m][:], func=AF.Silu,
                        bias=0.0, scale=1.0), inc="AC" if m == 3 else None)
        xz_done = b.v("AC")
        if q == dbgq:
            dbg_dump("xc", xcb[:])
            dbg_dump("slz", slz[:])

        if q > 0:
            b.wait("gpsimd", "AC", xz_done)
            marg_restored = gfn(lambda g: g.tensor_copy(xcb[:, :, 0:3], marg[:]),
                                inc="GC")

        if q == 0:
            b.wait("gpsimd", "AC", gacc_done)
            marg_restored = gfn(lambda g: g.memset(xcb[:, :, 0:3], 0.0), inc="GC")
        b.wait("tensor", "AC", xz_done)
        b.wait("tensor", "GC", marg_restored)
        conv_ev = {}
        for cc in range(NCC):
            dpar = cc % 2
            bank = 3 + (cc % 2)
            if cc >= 2:
                b.wait("vector", "PE", conv_pe[cc - 2])
            b.wait("vector", "GC", const_done)
            for k in range(4):
                m_d4 = b.op("vector", lambda v, cc=cc, k=k, dpar=dpar: v.tensor_scalar(
                    out=diag4[dpar][:, k, :], in0=ident[:],
                    scalar1=convw[:, cc, k:k + 1], scalar2=None, op0=ALU.mult),
                    inc="DV" if k == 3 else None)
            b.wait("tensor", "DV", m_d4)
            if cc >= 2:
                b.wait("tensor", "AC", conv_ev[cc - 2])
            for k in range(4):
                m_mm = b.op("tensor", lambda t, cc=cc, k=k, dpar=dpar, bank=bank: t.matmul(
                    pb[bank][:], diag4[dpar][:, k, :],
                    xcb[:, cc, k:k + TQ], start=(k == 0), stop=(k == 3)),
                    inc="PE" if k == 3 else None)
            if cc == 0:
                conv_pe = {}
            conv_pe[cc] = m_mm
            b.wait("scalar", "PE", m_mm)
            m_e = b.op("scalar", lambda sc, cc=cc, bank=bank: sc.activation(
                out=xsb[:, cc, :], in_=pb[bank][:], func=AF.Silu,
                bias=convb[:, cc:cc + 1], scale=1.0), inc="AC")
            conv_ev[cc] = m_e
        conv_done = b.v("AC")
        if q == dbgq:
            dbg_dump("xs", xsb[:])
        if q < NQ - 1:
            b.wait("gpsimd", "AC", conv_done)
            gfn(lambda g: g.tensor_copy(marg[:], xcb[:, :, TQ:TQ + 3]), inc="GC")

        b.wait("tensor", "DV", b.v("DV"))
        b.wait("scalar", "ST", b.v("ST"))
        for g4 in range(8):
            pass
        for g2 in range(8):
            buf = g2 % 2
            b.wait("gpsimd", "PE", b.v("PE"))
            mark_w = gdma(wdt[buf][:], rkpm(Wdt_d)[:, :, g2 * 256:(g2 + 1) * 256])
            b.wait("tensor", "GP", mark_w)
            b.wait("tensor", "AC", b.v("AC"))
            for k in range(NCC):
                for m in range(2):
                    m_mm = b.op("tensor", lambda t, buf=buf, k=k, m=m: t.matmul(
                        pb[m][:], wdt[buf][:, k, m * 128:(m + 1) * 128],
                        xsb[:, k, :], start=(k == 0), stop=(k == NCC - 1)),
                        inc="PE" if (k == NCC - 1 and m == 1) else None)
            b.wait("scalar", "PE", m_mm)
            for m in range(2):
                cc = g2 * 2 + m
                b.op("scalar", lambda sc, cc=cc, m=m: sc.activation(
                    out=tpp[:, 0:TQ], in_=pb[m][:], func=AF.Exp,
                    bias=bdt[:, cc:cc + 1], scale=1.0))
                b.op("scalar", lambda sc, cc=cc: sc.activation(
                    out=dlt[:, cc, :], in_=tpp[:, 0:TQ], func=AF.Ln,
                    bias=1.0, scale=1.0), inc="AC" if m == 1 else None)
        dlt_done = b.v("AC")
        if q == dbgq:
            dbg_dump("dlt", dlt[:])

        b.wait("tensor", "AC", b.v("AC"))
        for k in range(NCC):
            m_mm = b.op("tensor", lambda t, k=k: t.matmul(
                pb[2][0:32, :], wxp[:, k, :], xsb[:, k, :],
                start=(k == 0), stop=(k == NCC - 1)),
                inc="PE" if k == NCC - 1 else None)
        xproj_pe = m_mm
        b.wait("scalar", "PE", xproj_pe)
        bcrows_done = b.op("scalar", lambda sc: sc.activation(
            out=bcrows[:, :], in_=pb[2][0:32, :], func=AF.Identity, bias=0.0,
            scale=1.0), inc="AC")

        b.wait("vector", "AC", dlt_done)
        for cc in range(NCC):
            b.op("vector", lambda v, cc=cc: v.tensor_tensor(
                out=ub[:, cc, :], in0=xsb[:, cc, :], in1=dlt[:, cc, :],
                op=ALU.mult))
        b.wait("vector", "PE", xproj_pe)
        for cc in range(NCC):
            m_yi = b.op("vector", lambda v, cc=cc: v.tensor_scalar(
                out=xsb[:, cc, :], in0=xsb[:, cc, :], scalar1=Dsb[:, cc:cc + 1],
                scalar2=None, op0=ALU.mult),
                inc="DV" if cc == NCC - 1 else None)
        yinit_done = m_yi
        if q == dbgq:
            dbg_dump("ub", ub[:])
            dbg_dump("bc", bcrows[:])

        b.wait("tensor", "AC", bcrows_done)
        if q > 0:
            b.wait("tensor", "DV", x1_asm[q * 4 - 1])
        for s in range(NS):
            pp = s % 2
            eB, eC = esel[pp * 2], esel[pp * 2 + 1]
            b.wait("vector", "PE", b.v("PE"))
            b.op("vector", lambda v, eB=eB, s=s: v.tensor_scalar(
                out=eB[:], in0=ones32[:], scalar1=identf[0:32, s:s + 1],
                scalar2=None, op0=ALU.mult))
            m_es = b.op("vector", lambda v, eC=eC, s=s: v.tensor_scalar(
                out=eC[:], in0=ones32[:], scalar1=identf[0:32, 16 + s:17 + s],
                scalar2=None, op0=ALU.mult), inc="DV")
            b.wait("tensor", "DV", m_es)
            if s > 0:
                b.wait("tensor", "AC", bc_ev)
                b.wait("tensor", "DV", last_y)
            m_pe1 = b.op("tensor", lambda t, eB=eB: t.matmul(
                pb[3][:], eB[:, :], bcrows[:, :], start=True, stop=True))
            m_pe2 = b.op("tensor", lambda t, eC=eC: t.matmul(
                px[:], eC[:, :], bcrows[:, :], start=True, stop=True), inc="PE")
            b.wait("scalar", "PE", m_pe2)
            if s > 0:
                b.wait("scalar", "DV", last_y)
            b.op("scalar", lambda sc: sc.activation(
                out=bbt[:, :], in_=pb[3][:], func=AF.Identity, bias=0.0, scale=1.0))
            bc_ev = b.op("scalar", lambda sc: sc.activation(
                out=cbt[:, :], in_=px[:], func=AF.Identity, bias=0.0, scale=1.0),
                inc="AC")
            for p4 in range(4):
                i = s * 4 + p4
                par = i % 2
                ccs = [p4 * 4 + j for j in range(4)]
                if i >= 2:
                    b.wait("scalar", "DV", scan_done[i - 2])
                for j, cc in enumerate(ccs):
                    m_dA = b.op("scalar", lambda sc, cc=cc, j=j, s=s, par=par: sc.activation(
                        out=dA2[par][:, seg[j]:seg[j] + TQ], in_=dlt[:, cc, :],
                        func=AF.Identity, bias=1.0, scale=Asb[:, cc, s:s + 1]),
                        inc="AC" if j == 3 else None)
                b.wait("vector", "AC", bc_ev)
                if i == 0:
                    b.wait("vector", "DV", yinit_done)
                b.op("vector", lambda v, p4=p4: v.tensor_tensor(
                    out=dBp[:].rearrange("p (a t) -> p a t", t=TQ + 1)[:, :, 0:TQ],
                    in0=ub[:, p4 * 4:p4 * 4 + 4, :],
                    in1=bbt[:].rearrange("p (o t) -> p o t", o=1).broadcast_to([128, 4, TQ]),
                    op=ALU.mult))
                ci = s * NCC + p4 * 4
                b.op("vector", lambda v, ci=ci: v.tensor_copy(
                    dBp[:].rearrange("p (a t) -> p a t", t=TQ + 1)[:, 0:3, TQ:TQ + 1],
                    carry[:, ci + 1:ci + 4].rearrange("p (c o) -> p c o", o=1)))
                b.wait("vector", "AC", m_dA)
                m_scan = b.op("vector", lambda v, par=par, ci=ci: v.tensor_tensor_scan(
                    out=hhp[:], data0=dA2[par][:], data1=dBp[:],
                    initial=carry[:, ci:ci + 1], op0=ALU.mult, op1=ALU.add))
                b.op("vector", lambda v: v.tensor_tensor(
                    out=tpp[:].rearrange("p (a t) -> p a t", t=TQ + 1)[:, :, 0:TQ],
                    in0=hhp[:].rearrange("p (a t) -> p a t", t=TQ + 1)[:, :, 0:TQ],
                    in1=cbt[:].rearrange("p (o t) -> p o t", o=1).broadcast_to([128, 4, TQ]),
                    op=ALU.mult))
                b.op("vector", lambda v, p4=p4: v.tensor_tensor(
                    out=xsb[:, p4 * 4:p4 * 4 + 4, :],
                    in0=xsb[:, p4 * 4:p4 * 4 + 4, :],
                    in1=tpp[:].rearrange("p (a t) -> p a t", t=TQ + 1)[:, :, 0:TQ],
                    op=ALU.add))
                m_y = b.op("vector", lambda v, ci=ci: v.tensor_copy(
                    carry[:, ci:ci + 4].rearrange("p (c o) -> p c o", o=1),
                    hhp[:].rearrange("p (a t) -> p a t", t=TQ + 1)[:, :, TQ - 1:TQ]),
                    inc="DV")
                if i == 0 and s == 0:
                    scan_done = {}
                scan_done[i] = m_y
            last_y = scan_done[s * 4 + 3]
        sloop_done = b.v("DV")

        for cc in range(NCC):
            m_ys = b.op("vector", lambda v, cc=cc: v.tensor_tensor(
                out=xsb[:, cc, :], in0=xsb[:, cc, :], in1=slz[:, cc, :],
                op=ALU.mult), inc="DV" if cc == NCC - 1 else None)
        y_done = m_ys

        qbar()
        if q == dbgq:
            dbg_dump("y", xsb[:])
            dbg_dump("carry", carry[:])
        b.wait("tensor", "DV", y_done)
        for m in range(NDC):
            buf = m % 2
            b.wait("gpsimd", "PE", b.v("PE"))
            mark_w = gdma(wou[buf][:], rkpm(Wout_d)[:, :, m * 128:(m + 1) * 128])
            b.wait("tensor", "GP", mark_w)
            bank = m % 4
            b.wait("tensor", "AC", b.v("AC"))
            for k in range(NCC):
                m_mm = b.op("tensor", lambda t, buf=buf, k=k, bank=bank: t.matmul(
                    pb[bank][:], wou[buf][:, k, :], xsb[:, k, :],
                    start=(k == 0), stop=(k == NCC - 1)),
                    inc="PE" if k == NCC - 1 else None)
            if m > 0:
                b.wait("tensor", "DV", ffn_add)
            m_mf = b.op("tensor", lambda t, m=m, t0=t0: t.matmul(
                px[:], fwoc[:, m * 128:(m + 1) * 128],
                gacc[:, t0:t0 + TQ], start=True, stop=True), inc="PE")
            b.wait("scalar", "PE", m_mm)
            m_ev = b.op("scalar", lambda sc, m=m, bank=bank: sc.activation(
                out=ofm[:, m, :], in_=pb[bank][:], func=AF.Identity, bias=0.0,
                scale=1.0), inc="AC")
            b.wait("vector", "PE", m_mf)
            b.wait("vector", "AC", m_ev)
            ffn_add = b.op("vector", lambda v, m=m: v.tensor_tensor(
                out=ofm[:, m, :], in0=ofm[:, m, :], in1=px[:], op=ALU.add),
                inc="DV")
        out_fm = b.v("AC")
        out_fm_dv = ffn_add
        if q == dbgq:
            dbg_dump("ofm", ofm[:])

        for sub in range(4):
            tt = q * 4 + sub
            par = sub % 2
            pst = tpx if par == 0 else tpy
            if tt >= 2:
                b.wait("sync", "DV", x1_asm[tt - 2])
            m_x0 = hdma(x0st[par], x0_d[tt * 128:(tt + 1) * 128, :])
            b.wait("tensor", "AC", out_fm)
            b.wait("tensor", "DV", out_fm_dv)
            if tt >= 2:
                b.wait("tensor", "DV", x1_asm[tt - 2])
            for dc in range(NDC):
                m_tr = b.op("tensor", lambda t, dc=dc, sub=sub, pst=pst: t.transpose(
                    pst[:, dc * 128:(dc + 1) * 128],
                    ofm[:, dc, sub * 128:(sub + 1) * 128], ident[:]),
                    inc="PE" if dc == NDC - 1 else None)
            b.wait("vector", "PE", m_tr)
            b.wait("vector", "HW", m_x0)
            if tt >= 2:
                b.wait("vector", "ST", b.v("ST"))
                b.wait("vector", "AC", x1_sq_mark[tt - 2])
            m_x1 = b.op("vector", lambda v, par=par, pst=pst: v.tensor_tensor(
                out=x1st[par], in0=pst[:, 0:1024],
                in1=x0st[par], op=ALU.add), inc="DV")
            x1_asm[tt] = m_x1
            b.wait("sync", "DV", m_x1)
            x1_st_mark[tt] = hdma(x1_d[tt * 128:(tt + 1) * 128, :], x1st[par],
                                  sem="ST")
            b.wait("scalar", "DV", m_x1)
            if tt > 0:
                b.wait("scalar", "DV", sq4_red[tt - 1])
            m_sq = b.op("scalar", lambda sc, par=par, tt=tt: sc.activation(
                out=sq4, in_=x1st[par], func=AF.Square, bias=0.0, scale=1.0),
                inc="AC")
            x1_sq_mark[tt] = m_sq
            b.wait("vector", "AC", m_sq)
            sq4_red[tt] = b.op("vector", lambda v, tt=tt: v.tensor_reduce(
                out=stat2[:, 1, tt:tt + 1], in_=sq4, axis=AX, op=ALU.add), inc="DV")
            b.op("vector", lambda v, par=par, tt=tt: v.tensor_reduce(
                out=stat2[:, 2, tt:tt + 1], in_=x1st[par], axis=AX, op=ALU.add))
            b.op("vector", lambda v, par=par, tt=tt: v.tensor_reduce(
                out=stat2[:, 3, tt:tt + 1], in_=x1st[par], axis=AX, op=ALU.max))
            b.op("vector", lambda v, par=par, tt=tt: v.tensor_reduce(
                out=stat2[:, 4, tt:tt + 1], in_=x1st[par], axis=AX, op=ALU.min))

    if dbg:
        dbg_dump("xn0", xn0[:])
        dbg_dump("gacc", gacc[:])
    m_syn2 = b.op("vector", lambda v: v.engine_nop(), inc="DV")
    b.wait("vector", "DV", m_syn2)
    b.op("vector", lambda v: v.tensor_scalar(
        out=stat2[:, 5, :], in0=stat2[:, 2, :], scalar1=1.0 / D, scalar2=None,
        op0=ALU.mult))
    b.op("vector", lambda v: v.tensor_scalar(
        out=stat2[:, 6, :], in0=stat2[:, 1, :], scalar1=1.0 / D, scalar2=None,
        op0=ALU.mult))
    b.op("vector", lambda v: v.tensor_tensor(
        out=stat2[:, 7, :], in0=stat2[:, 5, :], in1=stat2[:, 5, :], op=ALU.mult))
    m_var2 = b.op("vector", lambda v: v.tensor_tensor(
        out=stat2[:, 7, :], in0=stat2[:, 6, :], in1=stat2[:, 7, :],
        op=ALU.subtract), inc="DV")
    b.wait("scalar", "DV", m_var2)
    m_sd2 = b.op("scalar", lambda sc: sc.activation(
        out=stat2[:, 7, :], in_=stat2[:, 7, :], func=AF.Sqrt,
        bias=epsb[:, 0:1], scale=1.0), inc="AC")
    b.wait("vector", "AC", m_sd2)
    b.op("vector", lambda v: v.reciprocal(out=stat2[:, 6, :], in_=stat2[:, 7, :]))
    b.op("vector", lambda v: v.tensor_tensor(
        out=stat2[:, 3, :], in0=stat2[:, 3, :], in1=stat2[:, 5, :],
        op=ALU.subtract))
    b.op("vector", lambda v: v.tensor_tensor(
        out=stat2[:, 4, :], in0=stat2[:, 5, :], in1=stat2[:, 4, :],
        op=ALU.subtract))
    b.op("vector", lambda v: v.tensor_tensor(
        out=stat2[:, 3, :], in0=stat2[:, 3, :], in1=stat2[:, 4, :], op=ALU.max))
    m_z = b.op("vector", lambda v: v.tensor_tensor(
        out=stat2[:, 4, :], in0=stat2[:, 3, :], in1=stat2[:, 6, :],
        op=ALU.mult), inc="DV")
    b.wait("vector", "DV", m_z)
    m_m1c = b.op("vector", lambda v: v.tensor_reduce(
        out=stat2[:, 0, 0:1], in_=stat2[:, 4, 0:NT], axis=AX, op=ALU.max),
        inc="DV")
    b.wait("gpsimd", "DV", m_m1c)
    m_m1 = gfn(lambda g: g.tensor_reduce(out=m1c[:], in_=stat2[:, 0, 0:1],
                                         axis=AXC, op=ALU.max), inc="GC")
    if dbg:
        dbg_dump("stat2", stat2[:])
    b.wait("sync", "GC", m_m1)
    hdma(m1_d[:, :], m1c[:], sem="ST")
    b.wait("sync", "DV", b.v("DV"))
    hdma(mrow_d[:, 0:NT], stat2[:, 5, 0:NT], sem="ST")
    hdma(mrow_d[:, NT:2 * NT], stat2[:, 6, 0:NT], sem="ST")
    b.wait("sync", "ST", b.v("ST"))

    with b.es, nc.Block() as block:
        b.run(block)
    return nc




def build_launch2():
    nc = bass.Bass("TRN2", target_bir_lowering=False)

    def din(name, shape):
        return nc.dram_tensor(name, shape, F32, kind="ExternalInput").ap()

    x1_d = din("x1", [T, D])
    mrow_d = din("mrow", [128, 2 * NT])
    m1v_d = din("m1v", [1, 8])
    clipx_d = din("clip_x", [1, 1])
    Wz_d = [din("W_zf", [D, 512]), din("W_zs", [D, 512])]
    Wh_d = [din("W_hf", [D, 512]), din("W_hs", [D, 512])]
    bz_d = din("b_z", [128, 8])
    bh_d = din("b_h", [128, 8])
    Wg_d = din("W_g", [D, D])
    bg_d = din("b_g", [128, NDC])
    wg_d = din("w_gate", [FI, D])
    wu_d = din("w_up", [FI, D])

    x1p_d = nc.dram_tensor("x1p", [T, D], F32, kind="ExternalOutput").ap()
    h_d = nc.dram_tensor("hact", [128, NIC, T], BF16, kind="ExternalOutput").ap()
    m2_d = nc.dram_tensor("m2", [1, 1], F32, kind="ExternalOutput").ap()

    b = B(nc)
    es = b.es
    sb = lambda n, s, d: es.enter_context(nc.sbuf_tensor(n, s, d))
    psal = lambda n, s: es.enter_context(nc.psum_tensor(n, s, F32))

    qx = sb("qx", [128, NDC, T], BF16)
    xn1b = sb("xn1b", [128, NDC, T], BF16)
    hcm = sb("hcm", [128, 8, T], BF16)
    zab = sb("zab", [128, 3 * T], BF16)
    zbuf = zab[:, 0:T]
    abuf = zab[:, T:2 * T]
    bbuf = zab[:, 2 * T:3 * T]
    aofm = nc.alloc_sbuf_tensor_at("aofm", [128, NDC, TQ], BF16,
                                   offset=nc.lookup_mloc(zab).addr)
    hbuf = [sb(f"hbuf{i}", [128, T], BF16) for i in range(2)]
    wsl = [sb(f"wsl{i}", [128, 8, 512], BF16) for i in range(2)]
    wgsl = sb("wgsl", [128, 8, D], BF16)
    wqsl = [sb(f"wqsl{i}", [128, D], F32) for i in range(2)]
    qs1 = sb("qs1", [128, D], F32)
    qsi = sb("qsi", [128, D], I32)
    qmk = sb("qmk", [128, D], F32)
    sc1 = sb("sc1", [128, D], F32)
    qg = sb("qg", [128, D], BF16)
    qu = sb("qu", [128, D], BF16)
    wqgT = sb("wqgT", [128, NDC, 128], BF16)
    wquT = sb("wquT", [128, NDC, 128], BF16)
    x1t = [sb(f"x1t{i}", [128, D], F32) for i in range(2)]
    outb = [sb(f"outb{i}", [128, D], F32) for i in range(2)]
    scb = [sb(f"scb{i}", [128, D], BF16) for i in range(2)]
    th = [sb(f"th{i}", [128, 512], BF16) for i in range(2)]
    panel = sb("panel", [128, 8, 64], F32)
    rowp = sb("rowp", [1, 128], F32)
    rowq = sb("rowq", [1, 128], F32)
    recipx = sb("recipx", [128, 1], F32)
    sxc127 = sb("sxc127", [128, 1], F32)
    rmxg = sb("rmxg", [128, 1], F32)
    rmxu = sb("rmxu", [128, 1], F32)
    tqg = sb("tqg", [128, 1], F32)
    tqu = sb("tqu", [128, 1], F32)
    rcg = sb("rcg", [128, 1], F32)
    rcu = sb("rcu", [128, 1], F32)
    bz = sb("bz", [128, 8], F32)
    bh = sb("bh", [128, 8], F32)
    nbz = sb("nbz", [128, 8], F32)
    bg = sb("bg", [128, NDC], F32)
    ident = sb("ident", [128, 128], BF16)
    onesf = sb("onesf", [1, 128], F32)
    m2c = sb("m2c", [1, 1], F32)

    pb = [psal(f"pb{i}", [128, 512]) for i in range(4)]
    px = psal("px", [128, 512])
    tpx = es.enter_context(nc.psum_tensor("tpx", [128, 1024], BF16))
    tpy = es.enter_context(nc.psum_tensor("tpy", [128, 1024], BF16))

    b.sem("HW", 16)
    b.sem("GP", 16)
    b.sem("PE", 1)
    b.sem("AC", 1)
    b.sem("DV", 1)
    b.sem("GC", 1)
    b.sem("ST", 16)

    def gdma(out, in_):
        b.dma_fence("gpsimd", "GP")
        return b.op("gpsimd", lambda g, o=out, i=in_: g.dma_start(out=o, in_=i),
                    inc="GP")

    def hdma(out, in_, sem="HW"):
        b.dma_fence("sync", sem)
        return b.op("sync", lambda s, o=out, i=in_: s.dma_start(out=o, in_=i),
                    inc=sem)

    def gfn(f, inc=None):
        return b.op("gpsimd", f, inc=inc)

    gfn(lambda g: g.memset(ident[:], 0.0))
    gfn(lambda g: g.affine_select(out=ident[:], in_=ident[:],
                                  compare_op=ALU.not_equal, fill=1.0, base=0,
                                  pattern=[[-1, 128]], channel_multiplier=1))
    gfn(lambda g: g.memset(onesf[:], 1.0))
    gfn(lambda g: g.memset(panel[:], 0.0))
    gfn(lambda g: g.memset(rowp[:], 0.0))
    const_done = gfn(lambda g: g.memset(rowq[:], 0.0), inc="GC")
    b.wait("gpsimd", "GC", const_done)
    gdma(panel[:, 0, 0:NT], mrow_d[:, 0:NT])
    gdma(panel[:, 1, 0:NT], mrow_d[:, NT:2 * NT])
    gdma(rowp[0:1, 0:8], m1v_d[:, :])
    gdma(rowq[0:1, 0:1], clipx_d[:, :])
    gdma(bz[:], bz_d[:, :])
    gdma(bh[:], bh_d[:, :])
    gdma(bg[:], bg_d[:, :])
    m_wg = gdma(wgsl[:], rkpm(Wg_d))
    loads_done = b.v("GP")

    b.wait("vector", "GP", loads_done)
    b.wait("vector", "GC", const_done)
    b.wait("tensor", "GC", const_done)
    b.op("vector", lambda v: v.tensor_reduce(
        out=rowp[0:1, 8:9], in_=rowp[0:1, 0:8], axis=AX, op=ALU.max))
    b.op("vector", lambda v: v.tensor_scalar(
        out=rowp[0:1, 16:32], in0=onesf[0:1, 0:16], scalar1=rowp[0:1, 8:9],
        scalar2=None, op0=ALU.mult))
    m_rq = b.op("vector", lambda v: v.tensor_scalar(
        out=rowq[0:1, 16:32], in0=onesf[0:1, 0:16], scalar1=rowq[0:1, 0:1],
        scalar2=None, op0=ALU.mult))
    b.wait("scalar", "DV", m_rq)
    m_sg = b.op("scalar", lambda sc: sc.activation(
        out=rowq[0:1, 32:48], in_=rowq[0:1, 16:32], func=AF.Sigmoid, bias=0.0,
        scale=1.0))
    b.wait("vector", "AC", m_sg)
    b.op("vector", lambda v: v.tensor_tensor(
        out=rowp[0:1, 32:48], in0=rowp[0:1, 16:32], in1=rowq[0:1, 32:48],
        op=ALU.mult))
    b.op("vector", lambda v: v.tensor_scalar(
        out=rowp[0:1, 48:64], in0=rowp[0:1, 32:48], scalar1=1.0 / 127.0,
        scalar2=None, op0=ALU.mult))
    b.op("vector", lambda v: v.reciprocal(
        out=rowp[0:1, 64:80], in_=rowp[0:1, 48:64]))
    m_row = b.op("vector", lambda v: v.tensor_scalar(
        out=rowp[0:1, 80:96], in0=rowp[0:1, 48:64], scalar1=1.0 / 127.0,
        scalar2=None, op0=ALU.mult))
    b.wait("tensor", "DV", m_row)
    b.op("tensor", lambda t: t.matmul(
        px[:, 0:1], onesf[:, :], rowp[0:1, 64:65], start=True, stop=True))
    m_bc = b.op("tensor", lambda t: t.matmul(
        px[:, 1:2], onesf[:, :], rowp[0:1, 80:81], start=True, stop=True),
        inc="PE")
    b.wait("vector", "PE", m_bc)
    b.op("vector", lambda v: v.tensor_copy(recipx[:], px[:, 0:1]))
    m_bcv = b.op("vector", lambda v: v.tensor_copy(sxc127[:], px[:, 1:2]))
    b.op("vector", lambda v: v.tensor_tensor(
        out=panel[:, 2, :], in0=panel[:, 0, :], in1=panel[:, 1, :],
        op=ALU.mult))
    b.op("vector", lambda v: v.tensor_scalar(
        out=panel[:, 2, :], in0=panel[:, 2, :], scalar1=-1.0, scalar2=None,
        op0=ALU.mult))
    m_nbz = b.op("vector", lambda v: v.tensor_scalar(
        out=nbz[:], in0=bz[:], scalar1=-1.0, scalar2=None, op0=ALU.mult))

    b.wait("scalar", "DV", m_nbz)
    xn_mark = {}
    q_mark = {}
    tr_mark = {}
    ev_mark = {}
    for tt in range(NT):
        par = tt % 2
        if tt >= 2:
            b.wait("sync", "AC", xn_mark[tt - 2])
        m_x1 = hdma(x1t[par][:], x1_d[tt * 128:(tt + 1) * 128, :])
        b.wait("scalar", "HW", m_x1)
        if tt >= 1:
            b.wait("scalar", "DV", q_mark[tt - 1])
        m_xn = b.op("scalar", lambda sc, par=par, tt=tt: sc.activation(
            out=sc1[:], in_=x1t[par][:], func=AF.Identity,
            bias=panel[:, 2, tt:tt + 1], scale=panel[:, 1, tt:tt + 1]))
        xn_mark[tt] = m_xn
        if tt >= 1:
            b.wait("scalar", "PE", tr_mark[tt - 1])
        m_cb = b.op("scalar", lambda sc: sc.activation(
            out=scb[0][:], in_=sc1[:], func=AF.Identity, bias=0.0, scale=1.0))
        b.wait("vector", "AC", m_cb)
        if tt == 0:
            b.wait("vector", "DV", m_bcv)
        b.op("vector", lambda v: v.tensor_scalar(
            out=qs1[:], in0=sc1[:], scalar1=recipx[:, 0:1], scalar2=RB + 0.5,
            op0=ALU.mult, op1=ALU.add))
        b.op("vector", lambda v: v.tensor_copy(qsi[:], qs1[:]))
        b.op("vector", lambda v: v.tensor_tensor(
            out=qmk[:], in0=qsi[:], in1=qs1[:], op=ALU.is_gt))
        b.op("vector", lambda v: v.tensor_tensor(
            out=qs1[:], in0=qsi[:], in1=qmk[:], op=ALU.subtract))
        b.op("vector", lambda v: v.tensor_scalar(
            out=qs1[:], in0=qs1[:], scalar1=-RB, scalar2=127.0, op0=ALU.add,
            op1=ALU.min))
        m_q = b.op("vector", lambda v: v.tensor_scalar(
            out=scb[1][:], in0=qs1[:], scalar1=-128.0, scalar2=None,
            op0=ALU.max))
        q_mark[tt] = m_q
        b.wait("tensor", "AC", m_cb)
        b.wait("tensor", "DV", m_q)
        if tt >= 1:
            b.wait("tensor", "AC", ev_mark[tt - 1][0])
            b.wait("tensor", "DV", ev_mark[tt - 1][1])
        for dc in range(NDC):
            b.op("tensor", lambda t, dc=dc: t.transpose(
                tpx[:, dc * 128:(dc + 1) * 128],
                scb[0][:, dc * 128:(dc + 1) * 128], ident[:]))
            m_tr = b.op("tensor", lambda t, dc=dc: t.transpose(
                tpy[:, dc * 128:(dc + 1) * 128],
                scb[1][:, dc * 128:(dc + 1) * 128], ident[:]),
                inc="PE" if dc == NDC - 1 else None)
        tr_mark[tt] = m_tr
        b.wait("scalar", "PE", m_tr)
        b.wait("vector", "PE", m_tr)
        m_e1 = b.op("scalar", lambda sc, tt=tt: sc.activation(
            out=xn1b[:, :, tt * 128:(tt + 1) * 128], in_=tpx[:, 0:1024],
            func=AF.Identity, bias=0.0, scale=1.0))
        m_e2 = b.op("vector", lambda v, tt=tt: v.tensor_copy(
            qx[:, :, tt * 128:(tt + 1) * 128], tpy[:, 0:1024]))
        ev_mark[tt] = (m_e1, m_e2)
    p2_ac = b.v("AC")
    p2_dv = b.v("DV")

    scan_mark = {}
    for p in range(2):
        if p == 1:
            b.wait("gpsimd", "PE", b.v("PE"))
        mz = gdma(wsl[0][:], rkpm(Wz_d[p]))
        mh = gdma(wsl[1][:], rkpm(Wh_d[p]))
        b.wait("tensor", "GP", mh)
        if p == 0:
            b.wait("tensor", "AC", p2_ac)
            b.wait("tensor", "DV", p2_dv)
        for m in range(4):
            fc = p * 4 + m
            b.wait("tensor", "AC", b.v("AC"))
            b.wait("tensor", "DV", b.v("DV"))
            for k in range(NDC):
                for t4 in range(4):
                    m_mmz = b.op("tensor", lambda t, k=k, m=m, t4=t4: t.matmul(
                        pb[t4][:], wsl[0][:, k, m * 128:(m + 1) * 128],
                        xn1b[:, k, t4 * 512:(t4 + 1) * 512],
                        start=(k == 0), stop=(k == NDC - 1)),
                        inc="PE" if (k == NDC - 1 and t4 == 3) else None)
            b.wait("scalar", "PE", m_mmz)
            if fc >= 1:
                b.wait("scalar", "DV", scan_mark[fc - 1])
            for t4 in range(4):
                b.op("scalar", lambda sc, t4=t4, fc=fc: sc.activation(
                    out=zbuf[:, t4 * 512:(t4 + 1) * 512], in_=pb[t4][:],
                    func=AF.Sigmoid, bias=bz[:, fc:fc + 1], scale=1.0))
                m_za = b.op("scalar", lambda sc, t4=t4, fc=fc: sc.activation(
                    out=abuf[:, t4 * 512:(t4 + 1) * 512], in_=pb[t4][:],
                    func=AF.Sigmoid, bias=nbz[:, fc:fc + 1], scale=-1.0))
            b.wait("tensor", "AC", m_za)
            for k in range(NDC):
                for t4 in range(4):
                    m_mmh = b.op("tensor", lambda t, k=k, m=m, t4=t4: t.matmul(
                        pb[t4][:], wsl[1][:, k, m * 128:(m + 1) * 128],
                        xn1b[:, k, t4 * 512:(t4 + 1) * 512],
                        start=(k == 0), stop=(k == NDC - 1)),
                        inc="PE" if (k == NDC - 1 and t4 == 3) else None)
            b.wait("vector", "PE", m_mmh)
            b.wait("vector", "AC", m_za)
            for t4 in range(4):
                b.op("vector", lambda v, t4=t4, fc=fc: v.scalar_tensor_tensor(
                    out=bbuf[:, t4 * 512:(t4 + 1) * 512], in0=pb[t4][:],
                    scalar=bh[:, fc:fc + 1],
                    in1=zbuf[:, t4 * 512:(t4 + 1) * 512],
                    op0=ALU.add, op1=ALU.mult))
            m_scan = b.op("vector", lambda v, fc=fc: v.tensor_tensor_scan(
                out=hcm[:, fc, :], data0=abuf[:], data1=bbuf[:],
                initial=0.0, op0=ALU.mult, op1=ALU.add))
            scan_mark[fc] = m_scan
    attn_dv = b.v("DV")

    aofm_mark = {}
    th_mark = {}
    outb_mark = {}
    asm_tr = {}
    for t4 in range(4):
        b.wait("tensor", "GP", loads_done)
        if t4 == 0:
            b.wait("tensor", "DV", attn_dv)
        for m in range(8):
            bank = m % 4
            if m >= 4:
                b.wait("tensor", "DV", aofm_mark[(t4, m - 4)])
                b.wait("tensor", "AC", th_mark[(t4, m - 4)])
            for k in range(NDC):
                m_mm = b.op("tensor", lambda t, k=k, m=m, t4=t4, bank=bank:
                            t.matmul(
                    pb[bank][:], wgsl[:, k, m * 128:(m + 1) * 128],
                    hcm[:, k, t4 * 512:(t4 + 1) * 512],
                    start=(k == 0), stop=(k == NDC - 1)),
                    inc="PE" if k == NDC - 1 else None)
            b.wait("scalar", "PE", m_mm)
            if m >= 2:
                b.wait("scalar", "DV", aofm_mark[(t4, m - 2)])
            m_th = b.op("scalar", lambda sc, m=m, bank=bank: sc.activation(
                out=th[m % 2][:], in_=pb[bank][:], func=AF.Sigmoid,
                bias=bg[:, m:m + 1], scale=1.0))
            th_mark[(t4, m)] = m_th
            b.wait("vector", "AC", m_th)
            if t4 >= 1 and m == 0:
                b.wait("vector", "PE", asm_tr[t4 - 1])
            m_ao = b.op("vector", lambda v, m=m, t4=t4: v.tensor_tensor(
                out=aofm[:, m, :], in0=th[m % 2][:],
                in1=hcm[:, m, t4 * 512:(t4 + 1) * 512], op=ALU.mult))
            aofm_mark[(t4, m)] = m_ao
        ao_dv = b.v("DV")
        for sub in range(4):
            tt = t4 * 4 + sub
            par = sub % 2
            pst = tpx if par == 0 else tpy
            if tt >= 2:
                b.wait("sync", "DV", outb_mark[tt - 2])
            else:
                b.wait("sync", "AC", p2_ac)
                b.wait("sync", "DV", p2_dv)
            m_x1 = hdma(x1t[par][:], x1_d[tt * 128:(tt + 1) * 128, :])
            b.wait("tensor", "DV", ao_dv)
            if tt >= 2:
                b.wait("tensor", "DV", outb_mark[tt - 2])
            for dc in range(NDC):
                m_tr = b.op("tensor", lambda t, dc=dc, sub=sub, pst=pst:
                            t.transpose(
                    pst[:, dc * 128:(dc + 1) * 128],
                    aofm[:, dc, sub * 128:(sub + 1) * 128], ident[:]),
                    inc="PE" if dc == NDC - 1 else None)
            asm_tr[t4] = m_tr
            b.wait("vector", "PE", m_tr)
            b.wait("vector", "HW", m_x1)
            if tt >= 2:
                b.wait("vector", "ST", b.v("ST"))
            m_ob = b.op("vector", lambda v, par=par, pst=pst: v.tensor_tensor(
                out=outb[par][:], in0=pst[:, 0:1024], in1=x1t[par][:],
                op=ALU.add))
            outb_mark[tt] = m_ob
            b.wait("sync", "DV", m_ob)
            hdma(x1p_d[tt * 128:(tt + 1) * 128, :], outb[par][:], sem="ST")

    w_mark = {}
    qdone = {}
    hstore = {}
    wtr_ev = {}
    ffn_hist = []
    for ic in range(NIC):
        if ic >= 1:
            b.wait("sync", "DV", qdone[ic - 1])
        mw1 = hdma(wqsl[0][:], wg_d[ic * 128:(ic + 1) * 128, :])
        mw2 = hdma(wqsl[1][:], wu_d[ic * 128:(ic + 1) * 128, :])
        w_mark[ic] = mw2
        b.wait("scalar", "HW", mw2)
        if ic >= 1:
            b.wait("scalar", "DV", qdone[ic - 1])
        m_ab1 = b.op("scalar", lambda sc: sc.activation(
            out=qs1[:], in_=wqsl[0][:], func=AF.Abs, bias=0.0, scale=1.0))
        b.wait("vector", "AC", m_ab1)
        m_rx1 = b.op("vector", lambda v: v.tensor_reduce(
            out=rmxg[:], in_=qs1[:], axis=AX, op=ALU.max))
        b.wait("scalar", "DV", m_rx1)
        m_ab2 = b.op("scalar", lambda sc: sc.activation(
            out=qs1[:], in_=wqsl[1][:], func=AF.Abs, bias=0.0, scale=1.0))
        b.wait("vector", "AC", m_ab2)
        b.op("vector", lambda v: v.tensor_reduce(
            out=rmxu[:], in_=qs1[:], axis=AX, op=ALU.max))
        b.op("vector", lambda v, ic=ic: v.tensor_tensor(
            out=panel[:, 5, ic:ic + 1], in0=rmxg[:], in1=sxc127[:],
            op=ALU.mult))
        b.op("vector", lambda v, ic=ic: v.tensor_tensor(
            out=panel[:, 6, ic:ic + 1], in0=rmxu[:], in1=sxc127[:],
            op=ALU.mult))
        b.op("vector", lambda v: v.tensor_scalar(
            out=tqg[:], in0=rmxg[:], scalar1=1.0 / 127.0, scalar2=None,
            op0=ALU.mult))
        b.op("vector", lambda v: v.tensor_scalar(
            out=tqu[:], in0=rmxu[:], scalar1=1.0 / 127.0, scalar2=None,
            op0=ALU.mult))
        b.op("vector", lambda v: v.reciprocal(out=rcg[:], in_=tqg[:]))
        b.op("vector", lambda v: v.reciprocal(out=rcu[:], in_=tqu[:]))
        for j, (src, rc, dst) in enumerate(((wqsl[0], rcg, qg),
                                            (wqsl[1], rcu, qu))):
            if ic >= 1 and j == 0:
                b.wait("vector", "PE", wtr_ev[ic - 1])
            b.op("vector", lambda v, src=src, rc=rc: v.tensor_scalar(
                out=qs1[:], in0=src[:], scalar1=rc[:, 0:1], scalar2=RB + 0.5,
                op0=ALU.mult, op1=ALU.add))
            b.op("vector", lambda v: v.tensor_copy(qsi[:], qs1[:]))
            b.op("vector", lambda v: v.tensor_tensor(
                out=qmk[:], in0=qsi[:], in1=qs1[:], op=ALU.is_gt))
            b.op("vector", lambda v: v.tensor_tensor(
                out=qs1[:], in0=qsi[:], in1=qmk[:], op=ALU.subtract))
            b.op("vector", lambda v: v.tensor_scalar(
                out=qs1[:], in0=qs1[:], scalar1=-RB, scalar2=127.0,
                op0=ALU.add, op1=ALU.min))
            m_qw = b.op("vector", lambda v, dst=dst: v.tensor_scalar(
                out=dst[:], in0=qs1[:], scalar1=-128.0, scalar2=None,
                op0=ALU.max))
        qdone[ic] = m_qw
        b.wait("tensor", "DV", m_qw)
        if ic >= 1:
            b.wait("tensor", "AC", wev_ac)
            b.wait("tensor", "DV", wev_dv)
        for dc in range(NDC):
            b.op("tensor", lambda t, dc=dc: t.transpose(
                tpx[:, dc * 128:(dc + 1) * 128],
                qg[:, dc * 128:(dc + 1) * 128], ident[:]))
            m_wtr = b.op("tensor", lambda t, dc=dc: t.transpose(
                tpy[:, dc * 128:(dc + 1) * 128],
                qu[:, dc * 128:(dc + 1) * 128], ident[:]),
                inc="PE" if dc == NDC - 1 else None)
        wtr_ev[ic] = m_wtr
        b.wait("scalar", "PE", m_wtr)
        b.wait("vector", "PE", m_wtr)
        wev_ac = b.op("scalar", lambda sc: sc.activation(
            out=wqgT[:].rearrange("p a c -> p (a c)"), in_=tpx[:, 0:1024],
            func=AF.Identity, bias=0.0, scale=1.0))
        wev_dv = b.op("vector", lambda v: v.tensor_copy(
            wquT[:].rearrange("p a c -> p (a c)"), tpy[:, 0:1024]))
        b.wait("tensor", "AC", wev_ac)
        b.wait("tensor", "DV", wev_dv)
        for t4 in range(4):
            bg_bank = t4 % 2
            bu_bank = 2 + t4 % 2
            if len(ffn_hist) >= 2:
                b.wait("tensor", "AC", ffn_hist[-2][0])
                b.wait("tensor", "DV", ffn_hist[-2][1])
            for k in range(NDC):
                b.op("tensor", lambda t, k=k, t4=t4, bank=bg_bank: t.matmul(
                    pb[bank][:], wqgT[:, k, :],
                    qx[:, k, t4 * 512:(t4 + 1) * 512],
                    start=(k == 0), stop=(k == NDC - 1)))
            for k in range(NDC):
                m_mm = b.op("tensor", lambda t, k=k, t4=t4, bank=bu_bank:
                            t.matmul(
                    pb[bank][:], wquT[:, k, :],
                    qx[:, k, t4 * 512:(t4 + 1) * 512],
                    start=(k == 0), stop=(k == NDC - 1)),
                    inc="PE" if k == NDC - 1 else None)
            b.wait("scalar", "PE", m_mm)
            if ffn_hist:
                b.wait("scalar", "DV", ffn_hist[-1][1])
            b.op("scalar", lambda sc, ic=ic, bank=bg_bank: sc.activation(
                out=th[0][:], in_=pb[bank][:], func=AF.Silu, bias=0.0,
                scale=panel[:, 5, ic:ic + 1]))
            ev_up = b.op("scalar", lambda sc, ic=ic, bank=bu_bank:
                         sc.activation(
                out=th[1][:], in_=pb[bank][:], func=AF.Identity, bias=0.0,
                scale=panel[:, 6, ic:ic + 1]))
            b.wait("vector", "AC", ev_up)
            if ic >= 2 and t4 == 0:
                b.wait("vector", "GP", b.v("GP"))
            m_h = b.op("vector", lambda v, ic=ic, t4=t4: v.tensor_tensor(
                out=hbuf[ic % 2][:, t4 * 512:(t4 + 1) * 512], in0=th[0][:],
                in1=th[1][:], op=ALU.mult))
            ffn_hist.append((ev_up, m_h))
        b.wait("vector", "DV", m_h)
        b.op("vector", lambda v, ic=ic: v.tensor_reduce(
            out=panel[:, 3, ic:ic + 1], in_=hbuf[ic % 2][:], axis=AX,
            op=ALU.max))
        m_st = b.op("vector", lambda v, ic=ic: v.tensor_reduce(
            out=panel[:, 4, ic:ic + 1], in_=hbuf[ic % 2][:], axis=AX,
            op=ALU.min))
        b.wait("gpsimd", "DV", m_st)
        hstore[ic] = gdma(h_d[:, ic, :], hbuf[ic % 2][:])

    b.op("vector", lambda v: v.tensor_scalar(
        out=panel[:, 7, 0:NIC], in0=panel[:, 4, 0:NIC], scalar1=-1.0,
        scalar2=None, op0=ALU.mult))
    b.op("vector", lambda v: v.tensor_tensor(
        out=panel[:, 7, 0:NIC], in0=panel[:, 3, 0:NIC],
        in1=panel[:, 7, 0:NIC], op=ALU.max))
    m_m2r = b.op("vector", lambda v: v.tensor_reduce(
        out=panel[:, 7, 32:33], in_=panel[:, 7, 0:NIC], axis=AX, op=ALU.max))
    b.wait("gpsimd", "DV", m_m2r)
    m_m2 = gfn(lambda g: g.tensor_reduce(
        out=m2c[:], in_=panel[:, 7, 32:33], axis=AXC, op=ALU.max), inc="GC")
    b.wait("sync", "GC", m_m2)
    hdma(m2_d[:, :], m2c[:], sem="ST")
    b.wait("sync", "ST", b.v("ST"))
    b.wait("sync", "GP", b.v("GP"))
    b.wait("sync", "HW", b.v("HW"))

    with b.es, nc.Block() as block:
        b.run(block)
    return nc

def build_mega(dbg=False, dbgq=0):
    nc = bass.Bass("TRN2", target_bir_lowering=False)

    def din(name, shape):
        return nc.dram_tensor(name, shape, F32, kind="ExternalInput").ap()

    x0_d = din("x0", [T, D])
    Win_d = din("W_in", [D, 2 * DI])
    convw_d = din("conv_w", [DI, 4])
    convb_d = din("conv_b", [128, NCC])
    Wxp_d = din("W_xproj", [DI, 32])
    Wdt_d = din("W_dt", [DI, DI])
    bdt_d = din("b_dt", [128, NCC])
    Alog_d = din("A_log", [DI, NS])
    Dp_d = din("D_param", [128, NCC])
    Wout_d = din("W_out", [DI, D])
    fWin_d = din("fWin", [D, 64])
    fbin_d = din("fbin", [64, 1])
    fW1_d = din("fW1", [64, 16])
    fb1_d = din("fb1", [64, 16])
    fW2_d = din("fW2", [64, 16])
    fb2_d = din("fb2", [64, 1])
    fWout_d = din("fWout", [64, D])
    fbout_d = din("fbout", [1, D])
    clipx_d = din("clip_x", [1, 1])
    clipd_d = din("clip_down", [1, 1])
    Wz_d = [din("W_zf", [D, 512]), din("W_zs", [D, 512])]
    Wh_d = [din("W_hf", [D, 512]), din("W_hs", [D, 512])]
    bz_d = din("b_z", [128, 8])
    bh_d = din("b_h", [128, 8])
    Wg_d = din("W_g", [D, D])
    bg_d = din("b_g", [128, NDC])
    wg_d = din("w_gate", [FI, D])
    wu_d = din("w_up", [FI, D])
    wd_d = din("w_down", [D, FI])

    out_d = nc.dram_tensor("out", [T, D], F32, kind="ExternalOutput").ap()
    x1i_t = nc.dram_tensor("x1i", [T, D], F32)
    x1i_d = x1i_t.ap()
    x1pi_t = nc.dram_tensor("x1pi", [T, D], F32)
    x1pi_d = x1pi_t.ap()
    hi_t = nc.dram_tensor("hi", [128, NIC, T], BF16)
    hi_d = hi_t.ap()
    c1i_t = nc.dram_tensor("c1i", [1, 1], F32)
    c1o_t = nc.dram_tensor("c1o", [1, 1], F32)
    c2i_t = nc.dram_tensor("c2i", [1, 1], F32)
    c2o_t = nc.dram_tensor("c2o", [1, 1], F32)
    if dbg:
        dbg_d = {nm: nc.dram_tensor("dbg_" + nm, sh, F32, kind="ExternalOutput").ap()
                 for nm, sh in [("xn0", [128, NDC, T]), ("xc", [128, NCC, TQ + 3]),
                                ("xs", [128, NCC, TQ]), ("dlt", [128, NCC, TQ]),
                                ("slz", [128, NCC, TQ]), ("ub", [128, NCC, TQ]),
                                ("y", [128, NCC, TQ]), ("bc", [32, TQ]),
                                ("gacc", [65, T]), ("ofm", [128, NDC, TQ]),
                                ("carry", [128, NS * NCC]), ("stat", [128, 8, 128]),
                                ("stat2", [128, 8, 128])]}

    b = B(nc)
    es = b.es
    sb = lambda n, s, d: es.enter_context(nc.sbuf_tensor(n, s, d))
    psal = lambda n, s: es.enter_context(nc.psum_tensor(n, s, F32))

    carry = sb("carry", [128, NS * NCC], F32)
    bcrows = sb("bcrows", [32, TQ], BF16)
    bbt = sb("bbt", [128, TQ], BF16)
    cbt = sb("cbt", [128, TQ], BF16)
    convw = sb("convw", [128, NCC, 4], F32)
    diag4 = [sb(f"diag4{i}", [128, 4, 128], BF16) for i in range(2)]
    ident = sb("ident", [128, 128], BF16)
    identf = sb("identf", [128, 32], F32)
    esel = [sb(f"esel{i}", [32, 128], BF16) for i in range(4)]
    ones32 = sb("ones32", [32, 128], BF16)
    ones1 = sb("ones1", [1, 128], BF16)
    onesf = sb("onesf", [1, 128], F32)
    Asb = sb("Asb", [128, NCC, NS], F32)
    Dsb = sb("Dsb", [128, NCC], F32)
    convb = sb("convb", [128, NCC], F32)
    bdt = sb("bdt", [128, NCC], F32)
    sfw1 = sb("sfw1", [64, 16], F32)
    sfb1 = sb("sfb1", [64, 16], F32)
    sfw2 = sb("sfw2", [64, 16], F32)
    sfb2 = sb("sfb2", [64, 1], F32)
    sfbin = sb("sfbin", [64, 1], F32)
    fwoc = sb("fwoc", [65, D], BF16)
    statc = sb("statc", [128, 8, 128], F32)
    stat2 = statc
    panel = sb("panel", [128, 8, 64], F32)
    m1c = sb("m1c", [1, 1], F32)
    m2c = sb("m2c", [1, 1], F32)
    marg = sb("marg", [128, NCC, 3], BF16)
    epsb = sb("epsb", [128, 1], F32)
    rowp = sb("rowp", [1, 128], F32)
    rowq = sb("rowq", [1, 128], F32)
    recipx = sb("recipx", [128, 1], F32)
    sxc127 = sb("sxc127", [128, 1], F32)
    rmxg = sb("rmxg", [128, 1], F32)
    rmxu = sb("rmxu", [128, 1], F32)
    tqg = sb("tqg", [128, 1], F32)
    tqu = sb("tqu", [128, 1], F32)
    rcg = sb("rcg", [128, 1], F32)
    rcu = sb("rcu", [128, 1], F32)
    reciph = sb("reciph", [128, 1], F32)
    shc127 = sb("shc127", [128, 1], F32)
    bz = sb("bz", [128, 8], F32)
    bh = sb("bh", [128, 8], F32)
    nbz = sb("nbz", [128, 8], F32)
    bg = sb("bg", [128, NDC], F32)

    HEAPW = 48000
    heap = sb("heap", [128, HEAPW], F32)
    hbase = nc.lookup_mloc(heap).addr

    def mkph(tag):
        cur = [0]
        def alloc(name, shape, dt):
            elt = 2 if dt == BF16 else 4
            n = 1
            for s_ in shape[1:]:
                n *= s_
            off = (cur[0] + 31) & ~31
            cur[0] = off + n * elt
            assert cur[0] <= HEAPW * 4, f"heap overflow {tag}/{name}: {cur[0]}"
            return nc.alloc_sbuf_tensor_at(name, shape, dt, offset=hbase + off)
        return alloc

    ha1 = mkph("L1")
    arA = ha1("arA", [128, 16384], F32)
    x0t = arA[:].rearrange("p (a d) -> p a d", d=D)
    arAb = nc.alloc_sbuf_tensor_at("arAb", [128, 32768], BF16,
                                   offset=nc.lookup_mloc(arA).addr)
    dlt = arAb[:, 0 * 8192:1 * 8192].rearrange("p (c t) -> p c t", t=TQ)
    ub = arAb[:, 1 * 8192:2 * 8192].rearrange("p (c t) -> p c t", t=TQ)
    slz = arAb[:, 2 * 8192:3 * 8192].rearrange("p (c t) -> p c t", t=TQ)
    xsb = arAb[:, 3 * 8192:4 * 8192].rearrange("p (c t) -> p c t", t=TQ)
    x1sA = arA[:].rearrange("p (a d) -> p a d", d=1024)
    x1st = [x1sA[:, 0, :], x1sA[:, 1, :]]
    x0st = [x1sA[:, 2, :], x1sA[:, 3, :]]
    sq4 = x1sA[:, 4, :]
    xn1s = x1sA[:, 5, :]
    arB = ha1("arB", [128, 16 * (TQ + 3)], BF16)
    xcb = arB[:].rearrange("p (c t) -> p c t", t=TQ + 3)
    ftmp = arB[0:64, 0:T]
    ftm2 = arB[0:64, T:2 * T]
    ofm = arB[:, 0:NDC * TQ].rearrange("p (a t) -> p a t", t=TQ)
    xn0 = ha1("xn0", [128, NDC, T], BF16)
    dA2 = [ha1(f"dA{i}", [128, L4], F32) for i in range(2)]
    dBp = ha1("dBp", [128, L4], BF16)
    hhp = ha1("hhp", [128, L4], BF16)
    tpp = ha1("tpp", [128, L4], BF16)
    dA0b = nc.alloc_sbuf_tensor_at("dA0b", [128, 4104], BF16,
                                   offset=nc.lookup_mloc(dA2[0]).addr)
    xtm2 = [dA0b[:, 0:D], dA0b[:, D:2 * D]]
    sq1 = dA2[1][:, 0:D]
    wsl = [ha1(f"wsl{i}", [128, 8, 512], BF16) for i in range(2)]
    wdt = [ha1(f"wdt{i}", [128, NCC, 256], BF16) for i in range(2)]
    wxp = ha1("wxp", [128, NCC, 32], BF16)
    wou = [ha1(f"wou{i}", [128, NCC, 128], BF16) for i in range(2)]
    fwin = ha1("fwin", [128, NDC, 64], BF16)
    gacc = ha1("gacc", [65, T], BF16)

    ha2 = mkph("L2")
    qx = ha2("qx", [128, NDC, T], BF16)
    xn1b = ha2("xn1b", [128, NDC, T], BF16)
    hcm = ha2("hcm", [128, 8, T], BF16)
    zab = ha2("zab", [128, 3 * T], BF16)
    zbuf = zab[:, 0:T]
    abuf = zab[:, T:2 * T]
    bbuf = zab[:, 2 * T:3 * T]
    aofm = nc.alloc_sbuf_tensor_at("aofm", [128, NDC, TQ], BF16,
                                   offset=nc.lookup_mloc(zab).addr)
    wsl2 = [ha2(f"wsl2_{i}", [128, 8, 512], BF16) for i in range(2)]
    wgsl = nc.alloc_sbuf_tensor_at("wgsl", [128, 8, D], BF16,
                                   offset=nc.lookup_mloc(wsl2[0]).addr)
    wqsl = [ha2(f"wqsl{i}", [128, D], F32) for i in range(2)]
    qs1 = ha2("qs1", [128, D], F32)
    qsi = ha2("qsi", [128, D], I32)
    qmk = ha2("qmk", [128, D], F32)
    sc1 = ha2("sc1", [128, D], F32)
    qg = ha2("qg", [128, D], BF16)
    qu = ha2("qu", [128, D], BF16)
    wqgT = ha2("wqgT", [128, NDC, 128], BF16)
    wquT = ha2("wquT", [128, NDC, 128], BF16)
    x1t = [ha2(f"x1t{i}", [128, D], F32) for i in range(2)]
    outb = [ha2(f"outb{i}", [128, D], F32) for i in range(2)]
    hbuf = [nc.alloc_sbuf_tensor_at(f"hbuf{i}", [128, T], BF16,
                                    offset=nc.lookup_mloc(x1t[i]).addr)
            for i in range(2)]
    scb = [ha2(f"scb{i}", [128, D], BF16) for i in range(2)]
    th = [ha2(f"th{i}", [128, 512], BF16) for i in range(2)]

    ha3 = mkph("L3")
    wdT = ha3("wdT", [128, NDC, NIC, 128], BF16)
    w3sl = [ha3(f"w3sl{i}", [128, FI], F32) for i in range(2)]
    hsl = ha3("hsl", [128, NIC, TQ], BF16)
    hq3 = ha3("hq3", [128, NIC, TQ], BF16)
    q3s1 = ha3("q3s1", [128, FI], F32)
    q3si = ha3("q3si", [128, FI], I32)
    q3mk = ha3("q3mk", [128, FI], F32)
    qd3 = ha3("qd3", [128, FI], BF16)
    dfm = ha3("dfm", [128, NDC, TQ], BF16)
    x3t = [ha3(f"x3t{i}", [128, D], F32) for i in range(2)]
    ob3 = [ha3(f"ob3_{i}", [128, D], F32) for i in range(2)]

    pb = [psal(f"pb{i}", [128, 512]) for i in range(5)]
    px = psal("px", [128, 512])
    tpx = es.enter_context(nc.psum_tensor("tpx", [128, 1024], BF16))
    tpy = es.enter_context(nc.psum_tensor("tpy", [128, 1024], BF16))

    b.sem("HW", 16)
    b.sem("GP", 16)
    b.sem("PE", 1)
    b.sem("AC", 1)
    b.sem("DV", 1)
    b.sem("GC", 1)
    b.sem("ST", 16)

    def gdma(out, in_):
        b.dma_fence("gpsimd", "GP")
        return b.op("gpsimd", lambda g, o=out, i=in_: g.dma_start(out=o, in_=i),
                    inc="GP")

    def hdma(out, in_, sem="HW"):
        b.dma_fence("sync", sem)
        return b.op("sync", lambda s, o=out, i=in_: s.dma_start(out=o, in_=i),
                    inc=sem)

    def gfn(f, inc=None):
        return b.op("gpsimd", f, inc=inc)

    def qbar():
        tgt = []
        m = b.op("vector", lambda e: e.engine_nop(), inc="DV")
        tgt.append(("DV", m))
        m = b.op("scalar", lambda e: e.activation(out=epsb[:], in_=epsb[:],
                                                  func=AF.Identity, bias=0.0,
                                                  scale=1.0), inc="AC")
        tgt.append(("AC", m))
        m = b.op("gpsimd", lambda e: e.engine_nop(), inc="GC")
        tgt.append(("GC", m))
        tgt.append(("PE", b.v("PE")))
        for eng in ("vector", "scalar", "gpsimd", "tensor", "sync"):
            for sem, m in tgt:
                b.wait(eng, sem, m)

    def mega_bar():
        qbar()
        for eng in ("vector", "scalar", "gpsimd", "tensor", "sync"):
            for semn in ("HW", "ST", "GP"):
                b.wait(eng, semn, b.v(semn))

    def dbg_dump(name, src):
        if not dbg:
            return
        b.wait("gpsimd", "AC", b.v("AC"))
        b.wait("gpsimd", "DV", b.v("DV"))
        mst = b.op("gpsimd", lambda g, o=dbg_d[name], i=src: g.dma_start(out=o, in_=i),
                   inc="GP")
        for eng in ("vector", "scalar", "tensor", "sync"):
            b.wait(eng, "GP", mst)

    gdma(Asb[:], Alog_d.rearrange("(c p) s -> p c s", p=128))
    gdma(Dsb[:], Dp_d[:, :])
    gdma(convb[:], convb_d[:, :])
    gdma(bdt[:], bdt_d[:, :])
    gdma(convw[:], convw_d.rearrange("(c p) k -> p c k", p=128))
    gdma(wxp[:], Wxp_d.rearrange("(c p) s -> p c s", p=128))
    gdma(fwin[:], fWin_d.rearrange("(c p) m -> p c m", p=128))
    gdma(sfw1[:], fW1_d[:, :])
    gdma(sfb1[:], fb1_d[:, :])
    gdma(sfw2[:], fW2_d[:, :])
    gdma(sfb2[:], fb2_d[:, :])
    gdma(sfbin[:], fbin_d[:, :])
    gdma(fwoc[0:64, :], fWout_d[:, :])
    params_ld = gdma(fwoc[64:65, :], fbout_d[:, :])
    x0_ld = hdma(x0t, x0_d.rearrange("(a p) d -> p a d", p=128))

    b.wait("gpsimd", "GP", params_ld)
    gfn(lambda g: g.memset(ident[:], 0.0))
    gfn(lambda g: g.affine_select(out=ident[:], in_=ident[:],
                                  compare_op=ALU.not_equal, fill=1.0, base=0,
                                  pattern=[[-1, 128]], channel_multiplier=1))
    gfn(lambda g: g.memset(ones1[:], 1.0))
    gfn(lambda g: g.memset(ones32[:], 1.0))
    gfn(lambda g: g.memset(identf[:], 0.0))
    gfn(lambda g: g.affine_select(out=identf[:], in_=identf[:],
                                  compare_op=ALU.not_equal, fill=1.0, base=0,
                                  pattern=[[-1, 32]], channel_multiplier=1))

    gfn(lambda g: g.memset(carry[:], 0.0))
    gfn(lambda g: g.memset(gacc[64:65, :], 1.0))
    gfn(lambda g: g.memset(statc[:], 0.0))
    gfn(lambda g: g.memset(marg[:], 0.0))
    gfn(lambda g: g.memset(epsb[:], 1e-5))
    gfn(lambda g: g.memset(dA2[0][:], 0.0))
    const_done = gfn(lambda g: g.engine_nop(), inc="GC")

    b.wait("scalar", "GP", params_ld)
    aexp = b.op("scalar", lambda sc: sc.activation(
        out=Asb[:], in_=Asb[:], func=AF.Exp, bias=0.0, scale=1.0), inc="AC")
    b.wait("vector", "AC", aexp)
    b.wait("vector", "GC", const_done)
    b.op("vector", lambda v: v.tensor_scalar(
        out=Asb[:], in0=Asb[:], scalar1=-1.0, scalar2=None, op0=ALU.mult),
        inc="DV")

    b.wait("scalar", "HW", x0_ld)
    b.wait("scalar", "GC", const_done)
    b.wait("vector", "HW", x0_ld)
    b.wait("vector", "GC", const_done)
    sq_red = {}
    for tt in range(NT):
        if tt > 0:
            b.wait("scalar", "DV", sq_red[tt - 1])
        m_sq = b.op("scalar", lambda sc, tt=tt: sc.activation(
            out=sq1, in_=x0t[:, tt, :], func=AF.Square, bias=0.0, scale=1.0),
            inc="AC")
        b.wait("vector", "AC", m_sq)
        sq_red[tt] = b.op("vector", lambda v, tt=tt: v.tensor_reduce(
            out=statc[:, 1, tt:tt + 1], in_=sq1, axis=AX, op=ALU.add), inc="DV")
        b.op("vector", lambda v, tt=tt: v.tensor_reduce(
            out=statc[:, 2, tt:tt + 1], in_=x0t[:, tt, :], axis=AX, op=ALU.add))
    m_syn = b.op("vector", lambda v: v.engine_nop(), inc="DV")
    b.wait("vector", "DV", m_syn)
    b.op("vector", lambda v: v.tensor_scalar(
        out=statc[:, 3, :], in0=statc[:, 2, :], scalar1=1.0 / D, scalar2=None,
        op0=ALU.mult))
    b.op("vector", lambda v: v.tensor_scalar(
        out=statc[:, 4, :], in0=statc[:, 1, :], scalar1=1.0 / D, scalar2=None,
        op0=ALU.mult))
    b.op("vector", lambda v: v.tensor_tensor(
        out=statc[:, 5, :], in0=statc[:, 3, :], in1=statc[:, 3, :], op=ALU.mult))
    m_var = b.op("vector", lambda v: v.tensor_tensor(
        out=statc[:, 5, :], in0=statc[:, 4, :], in1=statc[:, 5, :],
        op=ALU.subtract), inc="DV")
    b.wait("scalar", "DV", m_var)
    m_sd = b.op("scalar", lambda sc: sc.activation(
        out=statc[:, 6, :], in_=statc[:, 5, :], func=AF.Sqrt,
        bias=epsb[:, 0:1], scale=1.0), inc="AC")
    b.wait("vector", "AC", m_sd)
    b.op("vector", lambda v: v.reciprocal(out=statc[:, 7, :], in_=statc[:, 6, :]))
    b.op("vector", lambda v: v.tensor_tensor(
        out=statc[:, 0, :], in0=statc[:, 3, :], in1=statc[:, 7, :], op=ALU.mult))
    m_nb = b.op("vector", lambda v: v.tensor_scalar(
        out=statc[:, 0, :], in0=statc[:, 0, :], scalar1=-1.0, scalar2=None,
        op0=ALU.mult), inc="DV")
    b.wait("scalar", "DV", m_nb)
    ev1 = []
    for tt in range(NT):
        par = tt % 2
        m_xn = b.op("scalar", lambda sc, tt=tt, par=par: sc.activation(
            out=xtm2[par], in_=x0t[:, tt, :], func=AF.Identity,
            bias=statc[:, 0, tt:tt + 1], scale=statc[:, 7, tt:tt + 1]), inc="AC")
        pst = tpx if par == 0 else tpy
        b.wait("tensor", "AC", m_xn)
        if tt >= 2:
            b.wait("tensor", "AC", ev1[tt - 2])
        for dc in range(NDC):
            m_tr = b.op("tensor", lambda t, dc=dc, par=par, pst=pst: t.transpose(
                pst[:, dc * 128:(dc + 1) * 128],
                xtm2[par][:, dc * 128:(dc + 1) * 128], ident[:]),
                inc="PE" if dc == NDC - 1 else None)
        b.wait("scalar", "PE", m_tr)
        m_ev = b.op("scalar", lambda sc, tt=tt, pst=pst: sc.activation(
            out=xn0[:, :, tt * 128:(tt + 1) * 128],
            in_=pst[:, 0:1024], func=AF.Identity, bias=0.0,
            scale=1.0), inc="AC")
        ev1.append(m_ev)
    xn0_done = b.v("AC")
    dbg_dump("stat", statc[:])

    b.wait("gpsimd", "AC", xn0_done)
    gfn(lambda g: g.memset(dA2[0][:], 0.0))
    gfn(lambda g: g.memset(dBp[:], 0.0))
    gfn(lambda g: g.memset(hhp[:], 0.0))
    gfn(lambda g: g.memset(tpp[:], 0.0))
    bndz = gfn(lambda g: g.memset(dA2[1][:], 0.0), inc="GC")
    b.wait("scalar", "GC", bndz)
    b.wait("vector", "GC", bndz)

    b.wait("tensor", "AC", xn0_done)
    for t4 in range(4):
        for k in range(NDC):
            m_mm = b.op("tensor", lambda t, t4=t4, k=k: t.matmul(
                pb[t4][0:64, :], fwin[:, k, 0:64],
                xn0[:, k, t4 * 512:(t4 + 1) * 512],
                start=(k == 0), stop=(k == NDC - 1)),
                inc="PE" if k == NDC - 1 else None)
        b.wait("scalar", "PE", m_mm)
        b.op("scalar", lambda sc, t4=t4: sc.activation(
            out=gacc[0:64, t4 * 512:(t4 + 1) * 512], in_=pb[t4][0:64, :],
            func=AF.Identity, bias=sfbin[:, 0:1], scale=1.0),
            inc="AC" if t4 == 3 else None)
    spre_done = b.v("AC")

    b.wait("scalar", "AC", spre_done)
    for j in range(16):
        m_rl = b.op("scalar", lambda sc, j=j: sc.activation(
            out=ftmp, in_=gacc[0:64, :], func=AF.Relu,
            bias=sfb1[:, j:j + 1], scale=sfw1[:, j:j + 1]), inc="AC")
        b.wait("vector", "AC", m_rl)
        if j == 0:
            m_g = b.op("vector", lambda v: v.tensor_scalar(
                out=ftm2, in0=ftmp, scalar1=sfw2[:, 0:1], scalar2=None,
                op0=ALU.mult), inc="DV")
        else:
            b.op("vector", lambda v, j=j: v.tensor_scalar(
                out=ftmp, in0=ftmp, scalar1=sfw2[:, j:j + 1], scalar2=None,
                op0=ALU.mult))
            m_g = b.op("vector", lambda v: v.tensor_tensor(
                out=ftm2, in0=ftm2, in1=ftmp, op=ALU.add), inc="DV")
        if j < 15:
            b.wait("scalar", "DV", m_g)
    b.wait("scalar", "DV", b.v("DV"))
    gacc_done = b.op("scalar", lambda sc: sc.activation(
        out=gacc[0:64, :], in_=ftm2, func=AF.Identity, bias=sfb2[:, 0:1],
        scale=1.0), inc="AC")

    b.wait("gpsimd", "DV", b.v("DV"))
    b.wait("gpsimd", "AC", b.v("AC"))
    z2 = gfn(lambda g: g.memset(statc[:], 0.0), inc="GC")
    b.wait("vector", "GC", z2)

    x1_asm = {}
    sq4_red = {}
    x1_st_mark = {}
    x1_sq_mark = {}
    xz_marks = []
    dt_marks = []
    ou_marks = []
    seg = [j * (TQ + 1) for j in range(4)]
    bnd = [j * (TQ + 1) + TQ for j in range(3)]

    def v3(ap):
        return ap.rearrange("p t -> p 1 t")

    for q in range(NQ):
        t0 = q * TQ
        for g8 in range(8):
            buf = g8 % 2
            if len(xz_marks) >= 2:
                b.wait("gpsimd", "PE", xz_marks[-2])
            mark_w = gdma(wsl[buf][:], rkpm(Win_d)[:, :, g8 * 512:(g8 + 1) * 512])
            b.wait("tensor", "GP", mark_w)
            if q == 0 and g8 == 0:
                b.wait("tensor", "AC", gacc_done)
            b.wait("tensor", "AC", b.v("AC"))
            for k in range(NDC):
                for m in range(4):
                    m_mm = b.op("tensor", lambda t, buf=buf, k=k, m=m, t0=t0: t.matmul(
                        pb[m][:], wsl[buf][:, k, m * 128:(m + 1) * 128],
                        xn0[:, k, t0:t0 + TQ],
                        start=(k == 0), stop=(k == NDC - 1)),
                        inc="PE" if (k == NDC - 1 and m == 3) else None)
            xz_marks.append(m_mm)
            b.wait("scalar", "PE", m_mm)
            for m in range(4):
                if g8 < 4:
                    cc = g8 * 4 + m
                    b.op("scalar", lambda sc, cc=cc, m=m: sc.activation(
                        out=xcb[:, cc, 3:3 + TQ], in_=pb[m][:], func=AF.Identity,
                        bias=0.0, scale=1.0), inc="AC" if m == 3 else None)
                else:
                    cc = (g8 - 4) * 4 + m
                    b.op("scalar", lambda sc, cc=cc, m=m: sc.activation(
                        out=slz[:, cc, :], in_=pb[m][:], func=AF.Silu,
                        bias=0.0, scale=1.0), inc="AC" if m == 3 else None)
        xz_done = b.v("AC")
        if q == dbgq:
            dbg_dump("xc", xcb[:])
            dbg_dump("slz", slz[:])

        if q > 0:
            b.wait("gpsimd", "AC", xz_done)
            marg_restored = gfn(lambda g: g.tensor_copy(xcb[:, :, 0:3], marg[:]),
                                inc="GC")

        if q == 0:
            b.wait("gpsimd", "AC", gacc_done)
            marg_restored = gfn(lambda g: g.memset(xcb[:, :, 0:3], 0.0), inc="GC")
        b.wait("tensor", "AC", xz_done)
        b.wait("tensor", "GC", marg_restored)
        conv_ev = {}
        for cc in range(NCC):
            dpar = cc % 2
            bank = 3 + (cc % 2)
            if cc >= 2:
                b.wait("vector", "PE", conv_pe[cc - 2])
            b.wait("vector", "GC", const_done)
            for k in range(4):
                m_d4 = b.op("vector", lambda v, cc=cc, k=k, dpar=dpar: v.tensor_scalar(
                    out=diag4[dpar][:, k, :], in0=ident[:],
                    scalar1=convw[:, cc, k:k + 1], scalar2=None, op0=ALU.mult),
                    inc="DV" if k == 3 else None, nosync=(k != 3))
            b.wait("tensor", "DV", m_d4)
            if cc >= 2:
                b.wait("tensor", "AC", conv_ev[cc - 2])
            for k in range(4):
                m_mm = b.op("tensor", lambda t, cc=cc, k=k, dpar=dpar, bank=bank: t.matmul(
                    pb[bank][:], diag4[dpar][:, k, :],
                    xcb[:, cc, k:k + TQ], start=(k == 0), stop=(k == 3)),
                    inc="PE" if k == 3 else None)
            if cc == 0:
                conv_pe = {}
            conv_pe[cc] = m_mm
            b.wait("scalar", "PE", m_mm)
            m_e = b.op("scalar", lambda sc, cc=cc, bank=bank: sc.activation(
                out=xsb[:, cc, :], in_=pb[bank][:], func=AF.Silu,
                bias=convb[:, cc:cc + 1], scale=1.0), inc="AC")
            conv_ev[cc] = m_e
        conv_done = b.v("AC")
        if q == dbgq:
            dbg_dump("xs", xsb[:])
        if q < NQ - 1:
            b.wait("gpsimd", "AC", conv_done)
            gfn(lambda g: g.tensor_copy(marg[:], xcb[:, :, TQ:TQ + 3]), inc="GC")

        b.wait("tensor", "DV", b.v("DV"))
        b.wait("scalar", "ST", b.v("ST"))
        for g4 in range(8):
            pass
        for g2 in range(8):
            buf = g2 % 2
            if len(dt_marks) >= 2:
                b.wait("gpsimd", "PE", dt_marks[-2])
            mark_w = gdma(wdt[buf][:], rkpm(Wdt_d)[:, :, g2 * 256:(g2 + 1) * 256])
            b.wait("tensor", "GP", mark_w)
            b.wait("tensor", "AC", b.v("AC"))
            for k in range(NCC):
                for m in range(2):
                    m_mm = b.op("tensor", lambda t, buf=buf, k=k, m=m: t.matmul(
                        pb[m][:], wdt[buf][:, k, m * 128:(m + 1) * 128],
                        xsb[:, k, :], start=(k == 0), stop=(k == NCC - 1)),
                        inc="PE" if (k == NCC - 1 and m == 1) else None)
            dt_marks.append(m_mm)
            b.wait("scalar", "PE", m_mm)
            for m in range(2):
                cc = g2 * 2 + m
                b.op("scalar", lambda sc, cc=cc, m=m: sc.activation(
                    out=tpp[:, 0:TQ], in_=pb[m][:], func=AF.Exp,
                    bias=bdt[:, cc:cc + 1], scale=1.0))
                b.op("scalar", lambda sc, cc=cc: sc.activation(
                    out=dlt[:, cc, :], in_=tpp[:, 0:TQ], func=AF.Ln,
                    bias=1.0, scale=1.0), inc="AC" if m == 1 else None)
        dlt_done = b.v("AC")
        if q == dbgq:
            dbg_dump("dlt", dlt[:])

        b.wait("tensor", "AC", b.v("AC"))
        for k in range(NCC):
            m_mm = b.op("tensor", lambda t, k=k: t.matmul(
                pb[2][0:32, :], wxp[:, k, :], xsb[:, k, :],
                start=(k == 0), stop=(k == NCC - 1)),
                inc="PE" if k == NCC - 1 else None)
        xproj_pe = m_mm
        b.wait("scalar", "PE", xproj_pe)
        bcrows_done = b.op("scalar", lambda sc: sc.activation(
            out=bcrows[:, :], in_=pb[2][0:32, :], func=AF.Identity, bias=0.0,
            scale=1.0), inc="AC")

        b.wait("vector", "AC", dlt_done)
        for cc in range(NCC):
            b.op("vector", lambda v, cc=cc: v.tensor_tensor(
                out=ub[:, cc, :], in0=xsb[:, cc, :], in1=dlt[:, cc, :],
                op=ALU.mult), nosync=(cc != 0))
        b.wait("vector", "PE", xproj_pe)
        for cc in range(NCC):
            m_yi = b.op("vector", lambda v, cc=cc: v.tensor_scalar(
                out=xsb[:, cc, :], in0=xsb[:, cc, :], scalar1=Dsb[:, cc:cc + 1],
                scalar2=None, op0=ALU.mult),
                inc="DV" if cc == NCC - 1 else None, nosync=(cc != NCC - 1))
        yinit_done = m_yi
        if q == dbgq:
            dbg_dump("ub", ub[:])
            dbg_dump("bc", bcrows[:])

        b.wait("tensor", "AC", bcrows_done)
        if q > 0:
            b.wait("tensor", "DV", x1_asm[q * 4 - 1])
        for s in range(NS):
            pp = s % 2
            eB, eC = esel[pp * 2], esel[pp * 2 + 1]
            b.wait("vector", "PE", b.v("PE"))
            b.op("vector", lambda v, eB=eB, s=s: v.tensor_scalar(
                out=eB[:], in0=ones32[:], scalar1=identf[0:32, s:s + 1],
                scalar2=None, op0=ALU.mult))
            m_es = b.op("vector", lambda v, eC=eC, s=s: v.tensor_scalar(
                out=eC[:], in0=ones32[:], scalar1=identf[0:32, 16 + s:17 + s],
                scalar2=None, op0=ALU.mult), inc="DV", nosync=True)
            b.wait("tensor", "DV", m_es)
            if s > 0:
                b.wait("tensor", "AC", bc_ev)
                b.wait("tensor", "DV", last_y)
            m_pe1 = b.op("tensor", lambda t, eB=eB: t.matmul(
                pb[3][:], eB[:, :], bcrows[:, :], start=True, stop=True))
            m_pe2 = b.op("tensor", lambda t, eC=eC: t.matmul(
                px[:], eC[:, :], bcrows[:, :], start=True, stop=True), inc="PE")
            b.wait("scalar", "PE", m_pe2)
            if s > 0:
                b.wait("scalar", "DV", last_y)
            b.op("scalar", lambda sc: sc.activation(
                out=bbt[:, :], in_=pb[3][:], func=AF.Identity, bias=0.0, scale=1.0))
            bc_ev = b.op("scalar", lambda sc: sc.activation(
                out=cbt[:, :], in_=px[:], func=AF.Identity, bias=0.0, scale=1.0),
                inc="AC")
            for p4 in range(4):
                i = s * 4 + p4
                par = i % 2
                ccs = [p4 * 4 + j for j in range(4)]
                if i >= 2:
                    b.wait("scalar", "DV", scan_done[i - 2])
                for j, cc in enumerate(ccs):
                    m_dA = b.op("scalar", lambda sc, cc=cc, j=j, s=s, par=par: sc.activation(
                        out=dA2[par][:, seg[j]:seg[j] + TQ], in_=dlt[:, cc, :],
                        func=AF.Identity, bias=1.0, scale=Asb[:, cc, s:s + 1]),
                        inc="AC" if j == 3 else None)
                b.wait("vector", "AC", bc_ev)
                if i == 0:
                    b.wait("vector", "DV", yinit_done)
                b.op("vector", lambda v, p4=p4: v.tensor_tensor(
                    out=dBp[:].rearrange("p (a t) -> p a t", t=TQ + 1)[:, :, 0:TQ],
                    in0=ub[:, p4 * 4:p4 * 4 + 4, :],
                    in1=bbt[:].rearrange("p (o t) -> p o t", o=1).broadcast_to([128, 4, TQ]),
                    op=ALU.mult), nosync=(p4 != 0))
                ci = s * NCC + p4 * 4
                b.op("vector", lambda v, ci=ci: v.tensor_copy(
                    dBp[:].rearrange("p (a t) -> p a t", t=TQ + 1)[:, 0:3, TQ:TQ + 1],
                    carry[:, ci + 1:ci + 4].rearrange("p (c o) -> p c o", o=1)),
                    nosync=True)
                b.wait("vector", "AC", m_dA)
                m_scan = b.op("vector", lambda v, par=par, ci=ci: v.tensor_tensor_scan(
                    out=hhp[:], data0=dA2[par][:], data1=dBp[:],
                    initial=carry[:, ci:ci + 1], op0=ALU.mult, op1=ALU.add),
                    nosync=True)
                b.op("vector", lambda v: v.tensor_tensor(
                    out=tpp[:].rearrange("p (a t) -> p a t", t=TQ + 1)[:, :, 0:TQ],
                    in0=hhp[:].rearrange("p (a t) -> p a t", t=TQ + 1)[:, :, 0:TQ],
                    in1=cbt[:].rearrange("p (o t) -> p o t", o=1).broadcast_to([128, 4, TQ]),
                    op=ALU.mult), nosync=True)
                b.op("vector", lambda v, p4=p4: v.tensor_tensor(
                    out=xsb[:, p4 * 4:p4 * 4 + 4, :],
                    in0=xsb[:, p4 * 4:p4 * 4 + 4, :],
                    in1=tpp[:].rearrange("p (a t) -> p a t", t=TQ + 1)[:, :, 0:TQ],
                    op=ALU.add), nosync=True)
                m_y = b.op("vector", lambda v, ci=ci: v.tensor_copy(
                    carry[:, ci:ci + 4].rearrange("p (c o) -> p c o", o=1),
                    hhp[:].rearrange("p (a t) -> p a t", t=TQ + 1)[:, :, TQ - 1:TQ]),
                    inc="DV")
                if i == 0 and s == 0:
                    scan_done = {}
                scan_done[i] = m_y
            last_y = scan_done[s * 4 + 3]
        sloop_done = b.v("DV")

        for cc in range(NCC):
            m_ys = b.op("vector", lambda v, cc=cc: v.tensor_tensor(
                out=xsb[:, cc, :], in0=xsb[:, cc, :], in1=slz[:, cc, :],
                op=ALU.mult), inc="DV" if cc == NCC - 1 else None,
                nosync=(cc != NCC - 1))
        y_done = m_ys

        qbar()
        if q == dbgq:
            dbg_dump("y", xsb[:])
            dbg_dump("carry", carry[:])
        b.wait("tensor", "DV", y_done)
        for m in range(NDC):
            buf = m % 2
            if len(ou_marks) >= 2:
                b.wait("gpsimd", "PE", ou_marks[-2])
            mark_w = gdma(wou[buf][:], rkpm(Wout_d)[:, :, m * 128:(m + 1) * 128])
            b.wait("tensor", "GP", mark_w)
            bank = m % 4
            b.wait("tensor", "AC", b.v("AC"))
            for k in range(NCC):
                m_mm = b.op("tensor", lambda t, buf=buf, k=k, bank=bank: t.matmul(
                    pb[bank][:], wou[buf][:, k, :], xsb[:, k, :],
                    start=(k == 0), stop=(k == NCC - 1)),
                    inc="PE" if k == NCC - 1 else None)
            ou_marks.append(m_mm)
            if m > 0:
                b.wait("tensor", "DV", ffn_add)
            m_mf = b.op("tensor", lambda t, m=m, t0=t0: t.matmul(
                px[:], fwoc[:, m * 128:(m + 1) * 128],
                gacc[:, t0:t0 + TQ], start=True, stop=True), inc="PE")
            b.wait("scalar", "PE", m_mm)
            m_ev = b.op("scalar", lambda sc, m=m, bank=bank: sc.activation(
                out=ofm[:, m, :], in_=pb[bank][:], func=AF.Identity, bias=0.0,
                scale=1.0), inc="AC")
            b.wait("vector", "PE", m_mf)
            b.wait("vector", "AC", m_ev)
            ffn_add = b.op("vector", lambda v, m=m: v.tensor_tensor(
                out=ofm[:, m, :], in0=ofm[:, m, :], in1=px[:], op=ALU.add),
                inc="DV")
        out_fm = b.v("AC")
        out_fm_dv = ffn_add
        if q == dbgq:
            dbg_dump("ofm", ofm[:])

        for sub in range(4):
            tt = q * 4 + sub
            par = sub % 2
            pst = tpx if par == 0 else tpy
            if tt >= 2:
                b.wait("sync", "DV", x1_asm[tt - 2])
            m_x0 = hdma(x0st[par], x0_d[tt * 128:(tt + 1) * 128, :])
            b.wait("tensor", "AC", out_fm)
            b.wait("tensor", "DV", out_fm_dv)
            if tt >= 2:
                b.wait("tensor", "DV", x1_asm[tt - 2])
            for dc in range(NDC):
                m_tr = b.op("tensor", lambda t, dc=dc, sub=sub, pst=pst: t.transpose(
                    pst[:, dc * 128:(dc + 1) * 128],
                    ofm[:, dc, sub * 128:(sub + 1) * 128], ident[:]),
                    inc="PE" if dc == NDC - 1 else None)
            b.wait("vector", "PE", m_tr)
            b.wait("vector", "HW", m_x0)
            if tt >= 2:
                b.wait("vector", "ST", b.v("ST"))
                b.wait("vector", "AC", x1_sq_mark[tt - 2])
            m_x1 = b.op("vector", lambda v, par=par, pst=pst: v.tensor_tensor(
                out=x1st[par], in0=pst[:, 0:1024],
                in1=x0st[par], op=ALU.add), inc="DV")
            x1_asm[tt] = m_x1
            b.wait("sync", "DV", m_x1)
            x1_st_mark[tt] = hdma(x1i_d[tt * 128:(tt + 1) * 128, :], x1st[par],
                                  sem="ST")
            b.wait("scalar", "DV", m_x1)
            if tt > 0:
                b.wait("scalar", "DV", sq4_red[tt - 1])
            m_sq = b.op("scalar", lambda sc, par=par, tt=tt: sc.activation(
                out=sq4, in_=x1st[par], func=AF.Square, bias=0.0, scale=1.0),
                inc="AC")
            x1_sq_mark[tt] = m_sq
            b.wait("vector", "AC", m_sq)
            sq4_red[tt] = b.op("vector", lambda v, tt=tt: v.tensor_reduce(
                out=stat2[:, 1, tt:tt + 1], in_=sq4, axis=AX, op=ALU.add), inc="DV")
            b.op("vector", lambda v, par=par, tt=tt: v.tensor_reduce(
                out=stat2[:, 2, tt:tt + 1], in_=x1st[par], axis=AX, op=ALU.add))
            b.op("vector", lambda v, par=par, tt=tt: v.tensor_reduce(
                out=stat2[:, 3, tt:tt + 1], in_=x1st[par], axis=AX, op=ALU.max))
            b.op("vector", lambda v, par=par, tt=tt: v.tensor_reduce(
                out=stat2[:, 4, tt:tt + 1], in_=x1st[par], axis=AX, op=ALU.min))

    if dbg:
        dbg_dump("xn0", xn0[:])
        dbg_dump("gacc", gacc[:])
    m_syn2 = b.op("vector", lambda v: v.engine_nop(), inc="DV")
    b.wait("vector", "DV", m_syn2)
    b.op("vector", lambda v: v.tensor_scalar(
        out=stat2[:, 5, :], in0=stat2[:, 2, :], scalar1=1.0 / D, scalar2=None,
        op0=ALU.mult))
    b.op("vector", lambda v: v.tensor_scalar(
        out=stat2[:, 6, :], in0=stat2[:, 1, :], scalar1=1.0 / D, scalar2=None,
        op0=ALU.mult))
    b.op("vector", lambda v: v.tensor_tensor(
        out=stat2[:, 7, :], in0=stat2[:, 5, :], in1=stat2[:, 5, :], op=ALU.mult))
    m_var2 = b.op("vector", lambda v: v.tensor_tensor(
        out=stat2[:, 7, :], in0=stat2[:, 6, :], in1=stat2[:, 7, :],
        op=ALU.subtract), inc="DV")
    b.wait("scalar", "DV", m_var2)
    m_sd2 = b.op("scalar", lambda sc: sc.activation(
        out=stat2[:, 7, :], in_=stat2[:, 7, :], func=AF.Sqrt,
        bias=epsb[:, 0:1], scale=1.0), inc="AC")
    b.wait("vector", "AC", m_sd2)
    b.op("vector", lambda v: v.reciprocal(out=stat2[:, 6, :], in_=stat2[:, 7, :]))
    b.op("vector", lambda v: v.tensor_tensor(
        out=stat2[:, 3, :], in0=stat2[:, 3, :], in1=stat2[:, 5, :],
        op=ALU.subtract))
    b.op("vector", lambda v: v.tensor_tensor(
        out=stat2[:, 4, :], in0=stat2[:, 5, :], in1=stat2[:, 4, :],
        op=ALU.subtract))
    b.op("vector", lambda v: v.tensor_tensor(
        out=stat2[:, 3, :], in0=stat2[:, 3, :], in1=stat2[:, 4, :], op=ALU.max))
    m_z = b.op("vector", lambda v: v.tensor_tensor(
        out=stat2[:, 4, :], in0=stat2[:, 3, :], in1=stat2[:, 6, :],
        op=ALU.mult), inc="DV")
    b.wait("vector", "DV", m_z)
    m_m1c = b.op("vector", lambda v: v.tensor_reduce(
        out=stat2[:, 0, 0:1], in_=stat2[:, 4, 0:NT], axis=AX, op=ALU.max),
        inc="DV")
    b.wait("gpsimd", "DV", m_m1c)
    m_m1 = gfn(lambda g: g.tensor_reduce(out=m1c[:], in_=stat2[:, 0, 0:1],
                                         axis=AXC, op=ALU.max), inc="GC")
    if dbg:
        dbg_dump("stat2", stat2[:])

    gfn(lambda g: g.memset(panel[:], 0.0))
    gfn(lambda g: g.memset(rowp[:], 0.0))
    gfn(lambda g: g.memset(rowq[:], 0.0))
    const_done2 = gfn(lambda g: g.memset(onesf[:], 1.0), inc="GC")
    b.wait("gpsimd", "GC", m_m1)
    mc1 = gdma(c1i_t.ap()[:, :], m1c[:])
    b.wait("gpsimd", "GP", mc1)
    m_cc1 = b.op("gpsimd", lambda g: g.collective_compute(
        "AllReduce", ALU.max, replica_groups=[list(range(8))],
        ins=[c1i_t.ap().opt()], outs=[c1o_t.ap().opt()]), inc="GC")
    b.wait("gpsimd", "GC", m_cc1)
    gdma(rowp[0:1, 8:9], c1o_t.ap()[:, :])
    mega_bar()

    gdma(rowq[0:1, 0:1], clipx_d[:, :])
    gdma(bz[:], bz_d[:, :])
    gdma(bh[:], bh_d[:, :])
    gdma(bg[:], bg_d[:, :])
    loads2_done = b.v("GP")

    b.wait("vector", "GP", loads2_done)
    b.wait("vector", "GC", const_done2)
    b.wait("tensor", "GC", const_done2)
    b.op("vector", lambda v: v.tensor_copy(panel[:, 0, 0:NT], statc[:, 5, 0:NT]))
    b.op("vector", lambda v: v.tensor_copy(panel[:, 1, 0:NT], statc[:, 6, 0:NT]))
    b.op("vector", lambda v: v.tensor_scalar(
        out=rowp[0:1, 16:32], in0=onesf[0:1, 0:16], scalar1=rowp[0:1, 8:9],
        scalar2=None, op0=ALU.mult))
    m_rq = b.op("vector", lambda v: v.tensor_scalar(
        out=rowq[0:1, 16:32], in0=onesf[0:1, 0:16], scalar1=rowq[0:1, 0:1],
        scalar2=None, op0=ALU.mult))
    b.wait("scalar", "DV", m_rq)
    m_sg = b.op("scalar", lambda sc: sc.activation(
        out=rowq[0:1, 32:48], in_=rowq[0:1, 16:32], func=AF.Sigmoid, bias=0.0,
        scale=1.0))
    b.wait("vector", "AC", m_sg)
    b.op("vector", lambda v: v.tensor_tensor(
        out=rowp[0:1, 32:48], in0=rowp[0:1, 16:32], in1=rowq[0:1, 32:48],
        op=ALU.mult))
    b.op("vector", lambda v: v.tensor_scalar(
        out=rowp[0:1, 48:64], in0=rowp[0:1, 32:48], scalar1=1.0 / 127.0,
        scalar2=None, op0=ALU.mult))
    b.op("vector", lambda v: v.reciprocal(
        out=rowp[0:1, 64:80], in_=rowp[0:1, 48:64]))
    m_row = b.op("vector", lambda v: v.tensor_scalar(
        out=rowp[0:1, 80:96], in0=rowp[0:1, 48:64], scalar1=1.0 / 127.0,
        scalar2=None, op0=ALU.mult))
    b.wait("tensor", "DV", m_row)
    b.op("tensor", lambda t: t.matmul(
        px[:, 0:1], onesf[:, :], rowp[0:1, 64:65], start=True, stop=True))
    m_bc = b.op("tensor", lambda t: t.matmul(
        px[:, 1:2], onesf[:, :], rowp[0:1, 80:81], start=True, stop=True),
        inc="PE")
    b.wait("vector", "PE", m_bc)
    b.op("vector", lambda v: v.tensor_copy(recipx[:], px[:, 0:1]))
    m_bcv = b.op("vector", lambda v: v.tensor_copy(sxc127[:], px[:, 1:2]))
    b.op("vector", lambda v: v.tensor_tensor(
        out=panel[:, 2, :], in0=panel[:, 0, :], in1=panel[:, 1, :],
        op=ALU.mult))
    b.op("vector", lambda v: v.tensor_scalar(
        out=panel[:, 2, :], in0=panel[:, 2, :], scalar1=-1.0, scalar2=None,
        op0=ALU.mult))
    m_nbz = b.op("vector", lambda v: v.tensor_scalar(
        out=nbz[:], in0=bz[:], scalar1=-1.0, scalar2=None, op0=ALU.mult))

    b.wait("scalar", "DV", m_nbz)
    xn_mark = {}
    q_mark = {}
    tr_mark = {}
    ev_mark = {}
    for tt in range(NT):
        par = tt % 2
        if tt >= 2:
            b.wait("sync", "AC", xn_mark[tt - 2])
        m_x1 = hdma(x1t[par][:], x1i_d[tt * 128:(tt + 1) * 128, :])
        b.wait("scalar", "HW", m_x1)
        if tt >= 1:
            b.wait("scalar", "DV", q_mark[tt - 1])
        m_xn = b.op("scalar", lambda sc, par=par, tt=tt: sc.activation(
            out=sc1[:], in_=x1t[par][:], func=AF.Identity,
            bias=panel[:, 2, tt:tt + 1], scale=panel[:, 1, tt:tt + 1]))
        xn_mark[tt] = m_xn
        if tt >= 1:
            b.wait("scalar", "PE", tr_mark[tt - 1])
        m_cb = b.op("scalar", lambda sc: sc.activation(
            out=scb[0][:], in_=sc1[:], func=AF.Identity, bias=0.0, scale=1.0))
        b.wait("vector", "AC", m_cb)
        if tt == 0:
            b.wait("vector", "DV", m_bcv)
        b.op("vector", lambda v: v.tensor_scalar(
            out=qs1[:], in0=sc1[:], scalar1=recipx[:, 0:1], scalar2=RB + 0.5,
            op0=ALU.mult, op1=ALU.add))
        b.op("vector", lambda v: v.tensor_copy(qsi[:], qs1[:]), nosync=True)
        b.op("vector", lambda v: v.tensor_tensor(
            out=qmk[:], in0=qsi[:], in1=qs1[:], op=ALU.is_gt), nosync=True)
        b.op("vector", lambda v: v.tensor_tensor(
            out=qs1[:], in0=qsi[:], in1=qmk[:], op=ALU.subtract), nosync=True)
        b.op("vector", lambda v: v.tensor_scalar(
            out=qs1[:], in0=qs1[:], scalar1=-RB, scalar2=127.0, op0=ALU.add,
            op1=ALU.min), nosync=True)
        m_q = b.op("vector", lambda v: v.tensor_scalar(
            out=scb[1][:], in0=qs1[:], scalar1=-128.0, scalar2=None,
            op0=ALU.max), inc="DV")
        q_mark[tt] = m_q
        b.wait("tensor", "AC", m_cb)
        b.wait("tensor", "DV", m_q)
        if tt >= 1:
            b.wait("tensor", "AC", ev_mark[tt - 1][0])
            b.wait("tensor", "DV", ev_mark[tt - 1][1])
        for dc in range(NDC):
            b.op("tensor", lambda t, dc=dc: t.transpose(
                tpx[:, dc * 128:(dc + 1) * 128],
                scb[0][:, dc * 128:(dc + 1) * 128], ident[:]))
            m_tr = b.op("tensor", lambda t, dc=dc: t.transpose(
                tpy[:, dc * 128:(dc + 1) * 128],
                scb[1][:, dc * 128:(dc + 1) * 128], ident[:]),
                inc="PE" if dc == NDC - 1 else None)
        tr_mark[tt] = m_tr
        b.wait("scalar", "PE", m_tr)
        b.wait("vector", "PE", m_tr)
        m_e1 = b.op("scalar", lambda sc, tt=tt: sc.activation(
            out=xn1b[:, :, tt * 128:(tt + 1) * 128], in_=tpx[:, 0:1024],
            func=AF.Identity, bias=0.0, scale=1.0))
        m_e2 = b.op("vector", lambda v, tt=tt: v.tensor_copy(
            qx[:, :, tt * 128:(tt + 1) * 128], tpy[:, 0:1024]), inc="DV",
            nosync=True)
        ev_mark[tt] = (m_e1, m_e2)
    p2_ac = b.v("AC")
    p2_dv = b.v("DV")

    scan_mark = {}
    for p in range(2):
        if p == 1:
            b.wait("gpsimd", "PE", b.v("PE"))
        mz = gdma(wsl2[0][:], rkpm(Wz_d[p]))
        mh = gdma(wsl2[1][:], rkpm(Wh_d[p]))
        b.wait("tensor", "GP", mh)
        if p == 0:
            b.wait("tensor", "AC", p2_ac)
            b.wait("tensor", "DV", p2_dv)
        for m in range(4):
            fc = p * 4 + m
            b.wait("tensor", "AC", b.v("AC"))
            b.wait("tensor", "DV", b.v("DV"))
            for k in range(NDC):
                for t4 in range(4):
                    m_mmz = b.op("tensor", lambda t, k=k, m=m, t4=t4: t.matmul(
                        pb[t4][:], wsl2[0][:, k, m * 128:(m + 1) * 128],
                        xn1b[:, k, t4 * 512:(t4 + 1) * 512],
                        start=(k == 0), stop=(k == NDC - 1)),
                        inc="PE" if (k == NDC - 1 and t4 == 3) else None)
            b.wait("scalar", "PE", m_mmz)
            if fc >= 1:
                b.wait("scalar", "DV", scan_mark[fc - 1])
            for t4 in range(4):
                b.op("scalar", lambda sc, t4=t4, fc=fc: sc.activation(
                    out=zbuf[:, t4 * 512:(t4 + 1) * 512], in_=pb[t4][:],
                    func=AF.Sigmoid, bias=bz[:, fc:fc + 1], scale=1.0))
                m_za = b.op("scalar", lambda sc, t4=t4, fc=fc: sc.activation(
                    out=abuf[:, t4 * 512:(t4 + 1) * 512], in_=pb[t4][:],
                    func=AF.Sigmoid, bias=nbz[:, fc:fc + 1], scale=-1.0))
            b.wait("tensor", "AC", m_za)
            for k in range(NDC):
                for t4 in range(4):
                    m_mmh = b.op("tensor", lambda t, k=k, m=m, t4=t4: t.matmul(
                        pb[t4][:], wsl2[1][:, k, m * 128:(m + 1) * 128],
                        xn1b[:, k, t4 * 512:(t4 + 1) * 512],
                        start=(k == 0), stop=(k == NDC - 1)),
                        inc="PE" if (k == NDC - 1 and t4 == 3) else None)
            b.wait("vector", "PE", m_mmh)
            b.wait("vector", "AC", m_za)
            for t4 in range(4):
                b.op("vector", lambda v, t4=t4, fc=fc: v.scalar_tensor_tensor(
                    out=bbuf[:, t4 * 512:(t4 + 1) * 512], in0=pb[t4][:],
                    scalar=bh[:, fc:fc + 1],
                    in1=zbuf[:, t4 * 512:(t4 + 1) * 512],
                    op0=ALU.add, op1=ALU.mult), nosync=(t4 != 0))
            m_scan = b.op("vector", lambda v, fc=fc: v.tensor_tensor_scan(
                out=hcm[:, fc, :], data0=abuf[:], data1=bbuf[:],
                initial=0.0, op0=ALU.mult, op1=ALU.add), inc="DV",
                nosync=True)
            scan_mark[fc] = m_scan
    attn_dv = b.v("DV")

    aofm_mark = {}
    th_mark = {}
    outb_mark = {}
    asm_tr = {}
    b.wait("gpsimd", "PE", b.v("PE"))
    m_wg = gdma(wgsl[:], rkpm(Wg_d))
    for t4 in range(4):
        b.wait("tensor", "GP", m_wg)
        if t4 == 0:
            b.wait("tensor", "DV", attn_dv)
        for m in range(8):
            bank = m % 4
            if m >= 4:
                b.wait("tensor", "DV", aofm_mark[(t4, m - 4)])
                b.wait("tensor", "AC", th_mark[(t4, m - 4)])
            for k in range(NDC):
                m_mm = b.op("tensor", lambda t, k=k, m=m, t4=t4, bank=bank:
                            t.matmul(
                    pb[bank][:], wgsl[:, k, m * 128:(m + 1) * 128],
                    hcm[:, k, t4 * 512:(t4 + 1) * 512],
                    start=(k == 0), stop=(k == NDC - 1)),
                    inc="PE" if k == NDC - 1 else None)
            b.wait("scalar", "PE", m_mm)
            if m >= 2:
                b.wait("scalar", "DV", aofm_mark[(t4, m - 2)])
            m_th = b.op("scalar", lambda sc, m=m, bank=bank: sc.activation(
                out=th[m % 2][:], in_=pb[bank][:], func=AF.Sigmoid,
                bias=bg[:, m:m + 1], scale=1.0))
            th_mark[(t4, m)] = m_th
            b.wait("vector", "AC", m_th)
            if t4 >= 1 and m == 0:
                b.wait("vector", "PE", asm_tr[t4 - 1])
            m_ao = b.op("vector", lambda v, m=m, t4=t4: v.tensor_tensor(
                out=aofm[:, m, :], in0=th[m % 2][:],
                in1=hcm[:, m, t4 * 512:(t4 + 1) * 512], op=ALU.mult),
                inc="DV", nosync=True)
            aofm_mark[(t4, m)] = m_ao
        ao_dv = b.v("DV")
        for sub in range(4):
            tt = t4 * 4 + sub
            par = sub % 2
            pst = tpx if par == 0 else tpy
            if tt >= 2:
                b.wait("sync", "DV", outb_mark[tt - 2])
            else:
                b.wait("sync", "AC", p2_ac)
                b.wait("sync", "DV", p2_dv)
            m_x1 = hdma(x1t[par][:], x1i_d[tt * 128:(tt + 1) * 128, :])
            b.wait("tensor", "DV", ao_dv)
            if tt >= 2:
                b.wait("tensor", "DV", outb_mark[tt - 2])
            for dc in range(NDC):
                m_tr = b.op("tensor", lambda t, dc=dc, sub=sub, pst=pst:
                            t.transpose(
                    pst[:, dc * 128:(dc + 1) * 128],
                    aofm[:, dc, sub * 128:(sub + 1) * 128], ident[:]),
                    inc="PE" if dc == NDC - 1 else None)
            asm_tr[t4] = m_tr
            b.wait("vector", "PE", m_tr)
            b.wait("vector", "HW", m_x1)
            if tt >= 2:
                b.wait("vector", "ST", b.v("ST"))
            m_ob = b.op("vector", lambda v, par=par, pst=pst: v.tensor_tensor(
                out=outb[par][:], in0=pst[:, 0:1024], in1=x1t[par][:],
                op=ALU.add))
            outb_mark[tt] = m_ob
            b.wait("sync", "DV", m_ob)
            hdma(x1pi_d[tt * 128:(tt + 1) * 128, :], outb[par][:], sem="ST")

    w_mark = {}
    qdone = {}
    hstore = {}
    wtr_ev = {}
    ffn_hist = []
    for ic in range(NIC):
        if ic >= 1:
            b.wait("sync", "DV", qdone[ic - 1])
        mw1 = hdma(wqsl[0][:], wg_d[ic * 128:(ic + 1) * 128, :])
        mw2 = hdma(wqsl[1][:], wu_d[ic * 128:(ic + 1) * 128, :])
        w_mark[ic] = mw2
        b.wait("scalar", "HW", mw2)
        if ic >= 1:
            b.wait("scalar", "DV", qdone[ic - 1])
        m_ab1 = b.op("scalar", lambda sc: sc.activation(
            out=qs1[:], in_=wqsl[0][:], func=AF.Abs, bias=0.0, scale=1.0))
        b.wait("vector", "AC", m_ab1)
        m_rx1 = b.op("vector", lambda v: v.tensor_reduce(
            out=rmxg[:], in_=qs1[:], axis=AX, op=ALU.max))
        b.wait("scalar", "DV", m_rx1)
        m_ab2 = b.op("scalar", lambda sc: sc.activation(
            out=qs1[:], in_=wqsl[1][:], func=AF.Abs, bias=0.0, scale=1.0))
        b.wait("vector", "AC", m_ab2)
        b.op("vector", lambda v: v.tensor_reduce(
            out=rmxu[:], in_=qs1[:], axis=AX, op=ALU.max))
        b.op("vector", lambda v, ic=ic: v.tensor_tensor(
            out=panel[:, 5, ic:ic + 1], in0=rmxg[:], in1=sxc127[:],
            op=ALU.mult))
        b.op("vector", lambda v, ic=ic: v.tensor_tensor(
            out=panel[:, 6, ic:ic + 1], in0=rmxu[:], in1=sxc127[:],
            op=ALU.mult))
        b.op("vector", lambda v: v.tensor_scalar(
            out=tqg[:], in0=rmxg[:], scalar1=1.0 / 127.0, scalar2=None,
            op0=ALU.mult))
        b.op("vector", lambda v: v.tensor_scalar(
            out=tqu[:], in0=rmxu[:], scalar1=1.0 / 127.0, scalar2=None,
            op0=ALU.mult))
        b.op("vector", lambda v: v.reciprocal(out=rcg[:], in_=tqg[:]))
        b.op("vector", lambda v: v.reciprocal(out=rcu[:], in_=tqu[:]))
        for j, (src, rc, dst) in enumerate(((wqsl[0], rcg, qg),
                                            (wqsl[1], rcu, qu))):
            if ic >= 1 and j == 0:
                b.wait("vector", "PE", wtr_ev[ic - 1])
            b.op("vector", lambda v, src=src, rc=rc: v.tensor_scalar(
                out=qs1[:], in0=src[:], scalar1=rc[:, 0:1], scalar2=RB + 0.5,
                op0=ALU.mult, op1=ALU.add))
            b.op("vector", lambda v: v.tensor_copy(qsi[:], qs1[:]), nosync=True)
            b.op("vector", lambda v: v.tensor_tensor(
                out=qmk[:], in0=qsi[:], in1=qs1[:], op=ALU.is_gt), nosync=True)
            b.op("vector", lambda v: v.tensor_tensor(
                out=qs1[:], in0=qsi[:], in1=qmk[:], op=ALU.subtract),
                nosync=True)
            b.op("vector", lambda v: v.tensor_scalar(
                out=qs1[:], in0=qs1[:], scalar1=-RB, scalar2=127.0,
                op0=ALU.add, op1=ALU.min), nosync=True)
            m_qw = b.op("vector", lambda v, dst=dst: v.tensor_scalar(
                out=dst[:], in0=qs1[:], scalar1=-128.0, scalar2=None,
                op0=ALU.max), inc="DV")
        qdone[ic] = m_qw
        b.wait("tensor", "DV", m_qw)
        if ic >= 1:
            b.wait("tensor", "AC", wev_ac)
            b.wait("tensor", "DV", wev_dv)
        for dc in range(NDC):
            b.op("tensor", lambda t, dc=dc: t.transpose(
                tpx[:, dc * 128:(dc + 1) * 128],
                qg[:, dc * 128:(dc + 1) * 128], ident[:]))
            m_wtr = b.op("tensor", lambda t, dc=dc: t.transpose(
                tpy[:, dc * 128:(dc + 1) * 128],
                qu[:, dc * 128:(dc + 1) * 128], ident[:]),
                inc="PE" if dc == NDC - 1 else None)
        wtr_ev[ic] = m_wtr
        b.wait("scalar", "PE", m_wtr)
        b.wait("vector", "PE", m_wtr)
        wev_ac = b.op("scalar", lambda sc: sc.activation(
            out=wqgT[:].rearrange("p a c -> p (a c)"), in_=tpx[:, 0:1024],
            func=AF.Identity, bias=0.0, scale=1.0))
        wev_dv = b.op("vector", lambda v: v.tensor_copy(
            wquT[:].rearrange("p a c -> p (a c)"), tpy[:, 0:1024]))
        b.wait("tensor", "AC", wev_ac)
        b.wait("tensor", "DV", wev_dv)
        for t4 in range(4):
            bg_bank = t4 % 2
            bu_bank = 2 + t4 % 2
            if len(ffn_hist) >= 2:
                b.wait("tensor", "AC", ffn_hist[-2][0])
                b.wait("tensor", "DV", ffn_hist[-2][1])
            for k in range(NDC):
                b.op("tensor", lambda t, k=k, t4=t4, bank=bg_bank: t.matmul(
                    pb[bank][:], wqgT[:, k, :],
                    qx[:, k, t4 * 512:(t4 + 1) * 512],
                    start=(k == 0), stop=(k == NDC - 1)))
            for k in range(NDC):
                m_mm = b.op("tensor", lambda t, k=k, t4=t4, bank=bu_bank:
                            t.matmul(
                    pb[bank][:], wquT[:, k, :],
                    qx[:, k, t4 * 512:(t4 + 1) * 512],
                    start=(k == 0), stop=(k == NDC - 1)),
                    inc="PE" if k == NDC - 1 else None)
            b.wait("scalar", "PE", m_mm)
            if ffn_hist:
                b.wait("scalar", "DV", ffn_hist[-1][1])
            b.op("scalar", lambda sc, ic=ic, bank=bg_bank: sc.activation(
                out=th[0][:], in_=pb[bank][:], func=AF.Silu, bias=0.0,
                scale=panel[:, 5, ic:ic + 1]))
            ev_up = b.op("scalar", lambda sc, ic=ic, bank=bu_bank:
                         sc.activation(
                out=th[1][:], in_=pb[bank][:], func=AF.Identity, bias=0.0,
                scale=panel[:, 6, ic:ic + 1]))
            b.wait("vector", "AC", ev_up)
            if ic == 0 and t4 == 0:
                b.wait("vector", "ST", b.v("ST"))
            if ic >= 2 and t4 == 0:
                b.wait("vector", "GP", b.v("GP"))
            m_h = b.op("vector", lambda v, ic=ic, t4=t4: v.tensor_tensor(
                out=hbuf[ic % 2][:, t4 * 512:(t4 + 1) * 512], in0=th[0][:],
                in1=th[1][:], op=ALU.mult), inc="DV", nosync=True)
            ffn_hist.append((ev_up, m_h))
        b.wait("vector", "DV", m_h)
        b.op("vector", lambda v, ic=ic: v.tensor_reduce(
            out=panel[:, 3, ic:ic + 1], in_=hbuf[ic % 2][:], axis=AX,
            op=ALU.max), nosync=True)
        m_st = b.op("vector", lambda v, ic=ic: v.tensor_reduce(
            out=panel[:, 4, ic:ic + 1], in_=hbuf[ic % 2][:], axis=AX,
            op=ALU.min))
        b.wait("gpsimd", "DV", m_st)
        hstore[ic] = gdma(hi_d[:, ic, :], hbuf[ic % 2][:])

    b.op("vector", lambda v: v.tensor_scalar(
        out=panel[:, 7, 0:NIC], in0=panel[:, 4, 0:NIC], scalar1=-1.0,
        scalar2=None, op0=ALU.mult))
    b.op("vector", lambda v: v.tensor_tensor(
        out=panel[:, 7, 0:NIC], in0=panel[:, 3, 0:NIC],
        in1=panel[:, 7, 0:NIC], op=ALU.max))
    m_m2r = b.op("vector", lambda v: v.tensor_reduce(
        out=panel[:, 7, 32:33], in_=panel[:, 7, 0:NIC], axis=AX, op=ALU.max))
    b.wait("gpsimd", "DV", m_m2r)
    m_m2 = gfn(lambda g: g.tensor_reduce(
        out=m2c[:], in_=panel[:, 7, 32:33], axis=AXC, op=ALU.max), inc="GC")

    b.wait("gpsimd", "GC", m_m2)
    mc2 = gdma(c2i_t.ap()[:, :], m2c[:])
    b.wait("gpsimd", "GP", mc2)
    m_cc2 = b.op("gpsimd", lambda g: g.collective_compute(
        "AllReduce", ALU.max, replica_groups=[list(range(8))],
        ins=[c2i_t.ap().opt()], outs=[c2o_t.ap().opt()]), inc="GC")
    b.wait("gpsimd", "GC", m_cc2)
    gdma(rowp[0:1, 96:97], c2o_t.ap()[:, :])
    gdma(rowq[0:1, 32:33], clipd_d[:, :])
    mega_bar()

    b.op("vector", lambda v: v.tensor_scalar(
        out=rowp[0:1, 100:116], in0=onesf[0:1, 0:16], scalar1=rowp[0:1, 96:97],
        scalar2=None, op0=ALU.mult))
    m_rq3 = b.op("vector", lambda v: v.tensor_scalar(
        out=rowq[0:1, 100:116], in0=onesf[0:1, 0:16], scalar1=rowq[0:1, 32:33],
        scalar2=None, op0=ALU.mult))
    b.wait("scalar", "DV", m_rq3)
    m_sg3 = b.op("scalar", lambda sc: sc.activation(
        out=rowq[0:1, 80:96], in_=rowq[0:1, 100:116], func=AF.Sigmoid,
        bias=0.0, scale=1.0))
    b.wait("vector", "AC", m_sg3)
    b.op("vector", lambda v: v.tensor_tensor(
        out=rowp[0:1, 80:96], in0=rowp[0:1, 100:116], in1=rowq[0:1, 80:96],
        op=ALU.mult))
    b.op("vector", lambda v: v.tensor_scalar(
        out=rowp[0:1, 64:80], in0=rowp[0:1, 80:96], scalar1=1.0 / 127.0,
        scalar2=None, op0=ALU.mult))
    b.op("vector", lambda v: v.reciprocal(
        out=rowp[0:1, 48:64], in_=rowp[0:1, 64:80]))
    m_row3 = b.op("vector", lambda v: v.tensor_scalar(
        out=rowp[0:1, 32:48], in0=rowp[0:1, 64:80], scalar1=1.0 / 127.0,
        scalar2=None, op0=ALU.mult))
    b.wait("tensor", "DV", m_row3)
    b.op("tensor", lambda t: t.matmul(
        px[:, 0:1], onesf[:, :], rowp[0:1, 48:49], start=True, stop=True))
    m_bc3 = b.op("tensor", lambda t: t.matmul(
        px[:, 1:2], onesf[:, :], rowp[0:1, 32:33], start=True, stop=True),
        inc="PE")
    b.wait("vector", "PE", m_bc3)
    b.op("vector", lambda v: v.tensor_copy(reciph[:], px[:, 0:1]))
    m_bc3v = b.op("vector", lambda v: v.tensor_copy(shc127[:], px[:, 1:2]))

    w3q_mark = {}
    w3tr = {}
    last_ev = {"x": None, "y": None}
    for dc in range(NDC):
        par = dc % 2
        if dc >= 2:
            b.wait("sync", "DV", w3q_mark[dc - 2])
        mw3 = hdma(w3sl[par][:], wd_d[dc * 128:(dc + 1) * 128, :])
        b.wait("scalar", "HW", mw3)
        if dc >= 1:
            b.wait("scalar", "DV", w3q_mark[dc - 1])
        m_ab3 = b.op("scalar", lambda sc, par=par: sc.activation(
            out=q3s1[:], in_=w3sl[par][:], func=AF.Abs, bias=0.0, scale=1.0))
        b.wait("vector", "AC", m_ab3)
        if dc == 0:
            b.wait("vector", "DV", m_bc3v)
        b.op("vector", lambda v: v.tensor_reduce(
            out=rmxg[:], in_=q3s1[:], axis=AX, op=ALU.max))
        b.op("vector", lambda v, dc=dc: v.tensor_tensor(
            out=panel[:, 5, 32 + dc:33 + dc], in0=rmxg[:], in1=shc127[:],
            op=ALU.mult))
        b.op("vector", lambda v: v.tensor_scalar(
            out=tqg[:], in0=rmxg[:], scalar1=1.0 / 127.0, scalar2=None,
            op0=ALU.mult))
        b.op("vector", lambda v: v.reciprocal(out=rcg[:], in_=tqg[:]))
        b.op("vector", lambda v, par=par: v.tensor_scalar(
            out=q3s1[:], in0=w3sl[par][:], scalar1=rcg[:, 0:1],
            scalar2=RB + 0.5, op0=ALU.mult, op1=ALU.add))
        b.op("vector", lambda v: v.tensor_copy(q3si[:], q3s1[:]), nosync=True)
        b.op("vector", lambda v: v.tensor_tensor(
            out=q3mk[:], in0=q3si[:], in1=q3s1[:], op=ALU.is_gt), nosync=True)
        b.op("vector", lambda v: v.tensor_tensor(
            out=q3s1[:], in0=q3si[:], in1=q3mk[:], op=ALU.subtract),
            nosync=True)
        b.op("vector", lambda v: v.tensor_scalar(
            out=q3s1[:], in0=q3s1[:], scalar1=-RB, scalar2=127.0,
            op0=ALU.add, op1=ALU.min), nosync=True)
        if dc >= 1:
            b.wait("vector", "PE", w3tr[dc - 1])
        m_qd = b.op("vector", lambda v: v.tensor_scalar(
            out=qd3[:], in0=q3s1[:], scalar1=-128.0, scalar2=None,
            op0=ALU.max))
        w3q_mark[dc] = m_qd
        b.wait("tensor", "DV", m_qd)
        for (i0, i1, pst, pn) in ((0, 8, tpx, "x"), (8, 16, tpy, "y"),
                                  (16, 22, tpx, "x")):
            if last_ev[pn] is not None:
                b.wait("tensor", "AC", last_ev[pn])
            for ic in range(i0, i1):
                m_tr3 = b.op("tensor", lambda t, ic=ic, pst=pst, i0=i0:
                             t.transpose(
                    pst[:, (ic - i0) * 128:(ic - i0 + 1) * 128],
                    qd3[:, ic * 128:(ic + 1) * 128], ident[:]),
                    inc="PE" if ic == i1 - 1 else None)
            b.wait("scalar", "PE", m_tr3)
            last_ev[pn] = b.op("scalar", lambda sc, dc=dc, i0=i0, i1=i1,
                               pst=pst: sc.activation(
                out=wdT[:, dc, i0:i1, :].rearrange("p a c -> p (a c)"),
                in_=pst[:, 0:(i1 - i0) * 128], func=AF.Identity, bias=0.0,
                scale=1.0), inc="AC")
        w3tr[dc] = m_tr3
    wdT_done = b.v("AC")

    hq_mark = {}
    ob3_mark = {}
    mm3_last = 0
    hflat = hsl[:].rearrange("p a t -> p (a t)")
    hqflat = hq3[:].rearrange("p a t -> p (a t)")
    for t4 in range(4):
        if t4 >= 1:
            b.wait("sync", "DV", hq_mark[t4 - 1])
        mh3 = hdma(hsl[:], hi_d[:, :, t4 * 512:(t4 + 1) * 512])
        b.wait("vector", "HW", mh3)
        if t4 >= 1:
            b.wait("vector", "PE", mm3_last)
        for ch in range(4):
            c0, c1 = ch * FI, (ch + 1) * FI
            b.op("vector", lambda v, c0=c0, c1=c1: v.tensor_scalar(
                out=q3s1[:], in0=hflat[:, c0:c1], scalar1=reciph[:, 0:1],
                scalar2=RB + 0.5, op0=ALU.mult, op1=ALU.add))
            b.op("vector", lambda v: v.tensor_copy(q3si[:], q3s1[:]),
                 nosync=True)
            b.op("vector", lambda v: v.tensor_tensor(
                out=q3mk[:], in0=q3si[:], in1=q3s1[:], op=ALU.is_gt),
                nosync=True)
            b.op("vector", lambda v: v.tensor_tensor(
                out=q3s1[:], in0=q3si[:], in1=q3mk[:], op=ALU.subtract),
                nosync=True)
            b.op("vector", lambda v: v.tensor_scalar(
                out=q3s1[:], in0=q3s1[:], scalar1=-RB, scalar2=127.0,
                op0=ALU.add, op1=ALU.min), nosync=True)
            m_hq = b.op("vector", lambda v, c0=c0, c1=c1: v.tensor_scalar(
                out=hqflat[:, c0:c1], in0=q3s1[:], scalar1=-128.0,
                scalar2=None, op0=ALU.max), inc="DV")
        hq_mark[t4] = m_hq
        b.wait("tensor", "DV", m_hq)
        if t4 == 0:
            b.wait("tensor", "AC", wdT_done)
        for half in range(2):
            b.wait("tensor", "AC", b.v("AC"))
            for dc4 in range(4):
                dc = half * 4 + dc4
                for ic in range(NIC):
                    m_mm3 = b.op("tensor", lambda t, dc=dc, ic=ic, dc4=dc4:
                                 t.matmul(
                        pb[dc4][:], wdT[:, dc, ic, :], hq3[:, ic, :],
                        start=(ic == 0), stop=(ic == NIC - 1)),
                        inc="PE" if ic == NIC - 1 else None)
                b.wait("scalar", "PE", m_mm3)
                b.op("scalar", lambda sc, dc=dc, dc4=dc4: sc.activation(
                    out=dfm[:, dc, :], in_=pb[dc4][:], func=AF.Identity,
                    bias=0.0, scale=panel[:, 5, 32 + dc:33 + dc]),
                    inc="AC" if dc4 == 3 else None)
        dn_done = b.v("AC")
        for sub in range(4):
            tt = t4 * 4 + sub
            par = sub % 2
            pst = tpx if par == 0 else tpy
            if tt >= 2:
                b.wait("sync", "DV", ob3_mark[tt - 2])
            mx3 = hdma(x3t[par][:], x1pi_d[tt * 128:(tt + 1) * 128, :])
            b.wait("tensor", "AC", dn_done)
            if tt >= 2:
                b.wait("tensor", "DV", ob3_mark[tt - 2])
            for dc in range(NDC):
                m_tr = b.op("tensor", lambda t, dc=dc, sub=sub, pst=pst:
                            t.transpose(
                    pst[:, dc * 128:(dc + 1) * 128],
                    dfm[:, dc, sub * 128:(sub + 1) * 128], ident[:]),
                    inc="PE" if dc == NDC - 1 else None)
            b.wait("vector", "PE", m_tr)
            b.wait("vector", "HW", mx3)
            if tt >= 2:
                b.wait("vector", "ST", b.v("ST"))
            m_ob3 = b.op("vector", lambda v, par=par, pst=pst: v.tensor_tensor(
                out=ob3[par][:], in0=pst[:, 0:1024], in1=x3t[par][:],
                op=ALU.add), inc="DV", nosync=True)
            ob3_mark[tt] = m_ob3
            b.wait("sync", "DV", m_ob3)
            hdma(out_d[tt * 128:(tt + 1) * 128, :], ob3[par][:], sem="ST")
        mm3_last = b.v("PE")

    b.wait("sync", "ST", b.v("ST"))
    b.wait("sync", "GP", b.v("GP"))
    b.wait("sync", "HW", b.v("HW"))

    with b.es, nc.Block() as block:
        b.run(block)
    return nc


_MEGA = {}


def _pack_params(inp):
    rep = lambda a: np.ascontiguousarray(np.asarray(a, dtype=np.float32))
    bzp = np.concatenate([rep(inp["L1_attn_b_zf"]).reshape(4, 128).T,
                          rep(inp["L1_attn_b_zs"]).reshape(4, 128).T], axis=1)
    bhp = np.concatenate([rep(inp["L1_attn_b_hf"]).reshape(4, 128).T,
                          rep(inp["L1_attn_b_hs"]).reshape(4, 128).T], axis=1)
    return {
        "W_in": rep(inp["L0_attn_W_in"]),
        "conv_w": rep(inp["L0_attn_conv_w"]).reshape(DI, 4),
        "conv_b": np.ascontiguousarray(rep(inp["L0_attn_conv_b"]).reshape(NCC, 128).T),
        "W_xproj": rep(inp["L0_attn_W_xproj"]),
        "W_dt": rep(inp["L0_attn_W_dt"]),
        "b_dt": np.ascontiguousarray(rep(inp["L0_attn_b_dt"]).reshape(NCC, 128).T),
        "A_log": rep(inp["L0_attn_A_log"]),
        "D_param": np.ascontiguousarray(rep(inp["L0_attn_D_param"]).reshape(NCC, 128).T),
        "W_out": rep(inp["L0_attn_W_out"]),
        "fWin": rep(inp["L0_ffn_Win"]),
        "fbin": rep(inp["L0_ffn_bin_"]).reshape(64, 1),
        "fW1": rep(inp["L0_ffn_W1"]),
        "fb1": rep(inp["L0_ffn_b1"]),
        "fW2": rep(inp["L0_ffn_W2"]),
        "fb2": rep(inp["L0_ffn_b2"]).reshape(64, 1),
        "fWout": rep(inp["L0_ffn_Wout"]),
        "fbout": rep(inp["L0_ffn_bout"]).reshape(1, D),
        "clip_x": rep(inp["L1_ffn_clip_x"]).reshape(1, 1),
        "clip_down": rep(inp["L1_ffn_clip_down"]).reshape(1, 1),
        "W_zf": rep(inp["L1_attn_W_zf"]),
        "W_zs": rep(inp["L1_attn_W_zs"]),
        "W_hf": rep(inp["L1_attn_W_hf"]),
        "W_hs": rep(inp["L1_attn_W_hs"]),
        "b_z": np.ascontiguousarray(bzp),
        "b_h": np.ascontiguousarray(bhp),
        "W_g": rep(inp["L1_attn_W_g"]),
        "b_g": np.ascontiguousarray(rep(inp["L1_attn_b_g"]).reshape(8, 128).T),
        "w_gate": rep(inp["L1_ffn_w_gate"]),
        "w_up": rep(inp["L1_ffn_w_up"]),
        "w_down": rep(inp["L1_ffn_w_down"]),
    }


def kernel(**inp):
    x = np.ascontiguousarray(np.asarray(inp["L0_attn_x"], np.float32))
    w = _pack_params(inp)
    if "nc" not in _MEGA:
        _MEGA["nc"] = build_mega()
    in_maps = [dict(w, x0=np.ascontiguousarray(x[c])) for c in range(8)]
    res = run_bass_kernel_spmd(_MEGA["nc"], in_maps, core_ids=list(range(8)))
    return np.stack([np.asarray(res.results[c]["out"], np.float32)
                     for c in range(8)])


# version 11
# speedup vs baseline: 1.1882x; 1.0064x over previous
from contextlib import ExitStack

import numpy as np
import concourse.bass as bass
import concourse.mybir as mybir
from concourse.bass_utils import run_bass_kernel_spmd

F32 = mybir.dt.float32
BF16 = mybir.dt.bfloat16
I32 = mybir.dt.int32
AF = mybir.ActivationFunctionType
ALU = mybir.AluOpType
AX = mybir.AxisListType.X
AXC = mybir.AxisListType.C

T = 2048
D = 1024
DI = 2048
NS = 16
NCC = DI // 128
NDC = D // 128
TQ = 512
NQ = T // TQ
FI = 2816
NIC = FI // 128
NT = T // 128
L4 = 4 * (TQ + 1)
RB = 16384.0


class B:

    def __init__(self, nc):
        self.nc = nc
        self.ops = {"sync": [], "gpsimd": [], "vector": [], "scalar": [], "tensor": []}
        self.es = ExitStack()
        self.sems = {}
        self.maxw = {}

    def dma_fence(self, eng, sname):
        v = self.maxw.get(sname, 0)
        if v > 0:
            h = self.sems[sname][0]
            self.ops[eng].append(lambda e, h=h, v=v: e.wait_ge(h, v))

    def sem(self, name, step=1):
        h = self.es.enter_context(self.nc.semaphore(name))
        self.sems[name] = [h, step, 0]
        return name

    def wait(self, eng, sname, val):
        h = self.sems[sname][0]
        if val > 0:
            self.maxw[sname] = max(self.maxw.get(sname, 0), val)
            self.ops[eng].append(lambda e, h=h, v=val: e.wait_ge(h, v))

    OWN = {"vector": "DV", "scalar": "AC", "gpsimd": "GC"}

    def op(self, eng, fn, inc=None, nosync=False):
        own = self.OWN.get(eng)
        if own is not None and own in self.sems and not nosync:
            if eng != "scalar":
                self.wait(eng, own, self.sems[own][2])
            if inc is None:
                inc = own
        if inc is None:
            self.ops[eng].append(fn)
            return None
        h, step, _ = self.sems[inc]
        def g(e, fn=fn, h=h, step=step):
            fn(e).then_inc(h, step)
        self.ops[eng].append(g)
        self.sems[inc][2] += step
        return self.sems[inc][2]

    def v(self, sname):
        return self.sems[sname][2]

    def run(self, block):
        for eng in ("sync", "gpsimd", "vector", "scalar", "tensor"):
            ops = self.ops[eng]
            def body(e, ops=ops):
                for f in ops:
                    f(e)
            getattr(block, eng)(body)


def rkpm(ap, p=128):
    return ap.rearrange("(k p) m -> p k m", p=p)


def colv(ap):
    return ap.rearrange("(c p) o -> p (c o)", p=128)


def build_launch1(dbg=False, dbgq=0):
    nc = bass.Bass("TRN2", target_bir_lowering=False)

    def din(name, shape):
        return nc.dram_tensor(name, shape, F32, kind="ExternalInput").ap()

    x0_d = din("x0", [T, D])
    Win_d = din("W_in", [D, 2 * DI])
    convw_d = din("conv_w", [DI, 4])
    convb_d = din("conv_b", [128, NCC])
    Wxp_d = din("W_xproj", [DI, 32])
    Wdt_d = din("W_dt", [DI, DI])
    bdt_d = din("b_dt", [128, NCC])
    Alog_d = din("A_log", [DI, NS])
    Dp_d = din("D_param", [128, NCC])
    Wout_d = din("W_out", [DI, D])
    fWin_d = din("fWin", [D, 64])
    fbin_d = din("fbin", [64, 1])
    fW1_d = din("fW1", [64, 16])
    fb1_d = din("fb1", [64, 16])
    fW2_d = din("fW2", [64, 16])
    fb2_d = din("fb2", [64, 1])
    fWout_d = din("fWout", [64, D])
    fbout_d = din("fbout", [1, D])

    x1_d = nc.dram_tensor("x1", [T, D], F32, kind="ExternalOutput").ap()
    if dbg:
        dbg_d = {nm: nc.dram_tensor("dbg_" + nm, sh, F32, kind="ExternalOutput").ap()
                 for nm, sh in [("xn0", [128, NDC, T]), ("xc", [128, NCC, TQ + 3]),
                                ("xs", [128, NCC, TQ]), ("dlt", [128, NCC, TQ]),
                                ("slz", [128, NCC, TQ]), ("ub", [128, NCC, TQ]),
                                ("y", [128, NCC, TQ]), ("bc", [32, TQ]),
                                ("gacc", [65, T]), ("ofm", [128, NDC, TQ]),
                                ("carry", [128, NS * NCC]), ("stat", [128, 8, 128]), ("stat2", [128, 8, 128])]}
    mrow_d = nc.dram_tensor("mrow", [128, 2 * NT], F32, kind="ExternalOutput").ap()
    m1_d = nc.dram_tensor("m1", [1, 1], F32, kind="ExternalOutput").ap()

    b = B(nc)
    es = b.es
    sb = lambda n, s, d: es.enter_context(nc.sbuf_tensor(n, s, d))
    psal = lambda n, s: es.enter_context(nc.psum_tensor(n, s, F32))

    arA = sb("arA", [128, 16384], F32)
    x0t = arA[:].rearrange("p (a d) -> p a d", d=D)
    arAb = nc.alloc_sbuf_tensor_at("arAb", [128, 32768], BF16,
                                   offset=nc.lookup_mloc(arA).addr)
    dlt = arAb[:, 0 * 8192:1 * 8192].rearrange("p (c t) -> p c t", t=TQ)
    ub = arAb[:, 1 * 8192:2 * 8192].rearrange("p (c t) -> p c t", t=TQ)
    slz = arAb[:, 2 * 8192:3 * 8192].rearrange("p (c t) -> p c t", t=TQ)
    xsb = arAb[:, 3 * 8192:4 * 8192].rearrange("p (c t) -> p c t", t=TQ)
    x1sA = arA[:].rearrange("p (a d) -> p a d", d=1024)
    x1st = [x1sA[:, 0, :], x1sA[:, 1, :]]
    x0st = [x1sA[:, 2, :], x1sA[:, 3, :]]
    sq4 = x1sA[:, 4, :]
    xn1s = x1sA[:, 5, :]
    arB = sb("arB", [128, 16 * (TQ + 3)], BF16)
    xcb = arB[:].rearrange("p (c t) -> p c t", t=TQ + 3)
    ftmp = arB[0:64, 0:T]
    ftm2 = arB[0:64, T:2 * T]
    ofm = arB[:, 0:NDC * TQ].rearrange("p (a t) -> p a t", t=TQ)

    xn0 = sb("xn0", [128, NDC, T], BF16)
    carry = sb("carry", [128, NS * NCC], F32)
    bcrows = sb("bcrows", [32, TQ], BF16)
    bbt = sb("bbt", [128, TQ], BF16)
    cbt = sb("cbt", [128, TQ], BF16)
    dA2 = [sb(f"dA{i}", [128, L4], F32) for i in range(2)]
    dBp = sb("dBp", [128, L4], BF16)
    hhp = sb("hhp", [128, L4], BF16)
    tpp = sb("tpp", [128, L4], BF16)
    dA0b = nc.alloc_sbuf_tensor_at("dA0b", [128, 4104], BF16,
                                   offset=nc.lookup_mloc(dA2[0]).addr)
    xtm2 = [dA0b[:, 0:D], dA0b[:, D:2 * D]]
    sq1 = dA2[1][:, 0:D]
    wsl = [sb(f"wsl{i}", [128, 8, 512], BF16) for i in range(2)]
    wdt = [sb(f"wdt{i}", [128, NCC, 256], BF16) for i in range(2)]
    wxp = sb("wxp", [128, NCC, 32], BF16)
    wou = [sb(f"wou{i}", [128, NCC, 128], BF16) for i in range(2)]
    convw = sb("convw", [128, NCC, 4], F32)
    diag4 = [sb(f"diag4{i}", [128, 4, 128], BF16) for i in range(2)]
    ident = sb("ident", [128, 128], BF16)
    identf = sb("identf", [128, 32], F32)
    esel = [sb(f"esel{i}", [32, 128], BF16) for i in range(4)]
    ones32 = sb("ones32", [32, 128], BF16)
    ones1 = sb("ones1", [1, 128], BF16)
    Asb = sb("Asb", [128, NCC, NS], F32)
    Dsb = sb("Dsb", [128, NCC], F32)
    convb = sb("convb", [128, NCC], F32)
    bdt = sb("bdt", [128, NCC], F32)
    fwin = sb("fwin", [128, NDC, 64], BF16)
    sfw1 = sb("sfw1", [64, 16], F32)
    sfb1 = sb("sfb1", [64, 16], F32)
    sfw2 = sb("sfw2", [64, 16], F32)
    sfb2 = sb("sfb2", [64, 1], F32)
    sfbin = sb("sfbin", [64, 1], F32)
    fwoc = sb("fwoc", [65, D], BF16)
    gacc = sb("gacc", [65, T], BF16)
    statc = sb("statc", [128, 8, 128], F32)
    stat2 = statc
    m1c = sb("m1c", [1, 1], F32)
    marg = sb("marg", [128, NCC, 3], BF16)
    epsb = sb("epsb", [128, 1], F32)

    pb = [psal(f"pb{i}", [128, 512]) for i in range(5)]
    px = psal("px", [128, 512])
    tpx = es.enter_context(nc.psum_tensor("tpx", [128, 1024], BF16))
    tpy = es.enter_context(nc.psum_tensor("tpy", [128, 1024], BF16))

    b.sem("HW", 16)
    b.sem("GP", 16)
    b.sem("PE", 1)
    b.sem("AC", 1)
    b.sem("DV", 1)
    b.sem("GC", 1)
    b.sem("ST", 16)

    def gdma(out, in_):
        b.dma_fence("gpsimd", "GP")
        return b.op("gpsimd", lambda g, o=out, i=in_: g.dma_start(out=o, in_=i),
                    inc="GP")

    def hdma(out, in_, sem="HW"):
        b.dma_fence("sync", sem)
        return b.op("sync", lambda s, o=out, i=in_: s.dma_start(out=o, in_=i),
                    inc=sem)

    def gfn(f, inc=None):
        return b.op("gpsimd", f, inc=inc)

    def qbar():
        tgt = []
        m = b.op("vector", lambda e: e.engine_nop(), inc="DV")
        tgt.append(("DV", m))
        m = b.op("scalar", lambda e: e.activation(out=epsb[:], in_=epsb[:],
                                                  func=AF.Identity, bias=0.0,
                                                  scale=1.0), inc="AC")
        tgt.append(("AC", m))
        m = b.op("gpsimd", lambda e: e.engine_nop(), inc="GC")
        tgt.append(("GC", m))
        tgt.append(("PE", b.v("PE")))
        for eng in ("vector", "scalar", "gpsimd", "tensor", "sync"):
            for sem, m in tgt:
                b.wait(eng, sem, m)

    def dbg_dump(name, src):
        if not dbg:
            return
        b.wait("gpsimd", "AC", b.v("AC"))
        b.wait("gpsimd", "DV", b.v("DV"))
        mst = b.op("gpsimd", lambda g, o=dbg_d[name], i=src: g.dma_start(out=o, in_=i),
                   inc="GP")
        for eng in ("vector", "scalar", "tensor", "sync"):
            b.wait(eng, "GP", mst)

    gdma(Asb[:], Alog_d.rearrange("(c p) s -> p c s", p=128))
    gdma(Dsb[:], Dp_d[:, :])
    gdma(convb[:], convb_d[:, :])
    gdma(bdt[:], bdt_d[:, :])
    gdma(convw[:], convw_d.rearrange("(c p) k -> p c k", p=128))
    gdma(wxp[:], Wxp_d.rearrange("(c p) s -> p c s", p=128))
    gdma(fwin[:], fWin_d.rearrange("(c p) m -> p c m", p=128))
    gdma(sfw1[:], fW1_d[:, :])
    gdma(sfb1[:], fb1_d[:, :])
    gdma(sfw2[:], fW2_d[:, :])
    gdma(sfb2[:], fb2_d[:, :])
    gdma(sfbin[:], fbin_d[:, :])
    gdma(fwoc[0:64, :], fWout_d[:, :])
    params_ld = gdma(fwoc[64:65, :], fbout_d[:, :])
    x0_ld = hdma(x0t, x0_d.rearrange("(a p) d -> p a d", p=128))

    b.wait("gpsimd", "GP", params_ld)
    gfn(lambda g: g.memset(ident[:], 0.0))
    gfn(lambda g: g.affine_select(out=ident[:], in_=ident[:],
                                  compare_op=ALU.not_equal, fill=1.0, base=0,
                                  pattern=[[-1, 128]], channel_multiplier=1))
    gfn(lambda g: g.memset(ones1[:], 1.0))
    gfn(lambda g: g.memset(ones32[:], 1.0))
    gfn(lambda g: g.memset(identf[:], 0.0))
    gfn(lambda g: g.affine_select(out=identf[:], in_=identf[:],
                                  compare_op=ALU.not_equal, fill=1.0, base=0,
                                  pattern=[[-1, 32]], channel_multiplier=1))

    gfn(lambda g: g.memset(carry[:], 0.0))
    gfn(lambda g: g.memset(gacc[64:65, :], 1.0))
    gfn(lambda g: g.memset(statc[:], 0.0))
    gfn(lambda g: g.memset(marg[:], 0.0))
    gfn(lambda g: g.memset(epsb[:], 1e-5))
    gfn(lambda g: g.memset(dA2[0][:], 0.0))
    const_done = gfn(lambda g: g.engine_nop(), inc="GC")

    b.wait("scalar", "GP", params_ld)
    aexp = b.op("scalar", lambda sc: sc.activation(
        out=Asb[:], in_=Asb[:], func=AF.Exp, bias=0.0, scale=1.0), inc="AC")
    b.wait("vector", "AC", aexp)
    b.wait("vector", "GC", const_done)
    b.op("vector", lambda v: v.tensor_scalar(
        out=Asb[:], in0=Asb[:], scalar1=-1.0, scalar2=None, op0=ALU.mult),
        inc="DV")

    b.wait("scalar", "HW", x0_ld)
    b.wait("scalar", "GC", const_done)
    b.wait("vector", "HW", x0_ld)
    b.wait("vector", "GC", const_done)
    sq_red = {}
    for tt in range(NT):
        if tt > 0:
            b.wait("scalar", "DV", sq_red[tt - 1])
        m_sq = b.op("scalar", lambda sc, tt=tt: sc.activation(
            out=sq1, in_=x0t[:, tt, :], func=AF.Square, bias=0.0, scale=1.0),
            inc="AC")
        b.wait("vector", "AC", m_sq)
        sq_red[tt] = b.op("vector", lambda v, tt=tt: v.tensor_reduce(
            out=statc[:, 1, tt:tt + 1], in_=sq1, axis=AX, op=ALU.add), inc="DV")
        b.op("vector", lambda v, tt=tt: v.tensor_reduce(
            out=statc[:, 2, tt:tt + 1], in_=x0t[:, tt, :], axis=AX, op=ALU.add))
    m_syn = b.op("vector", lambda v: v.engine_nop(), inc="DV")
    b.wait("vector", "DV", m_syn)
    b.op("vector", lambda v: v.tensor_scalar(
        out=statc[:, 3, :], in0=statc[:, 2, :], scalar1=1.0 / D, scalar2=None,
        op0=ALU.mult))
    b.op("vector", lambda v: v.tensor_scalar(
        out=statc[:, 4, :], in0=statc[:, 1, :], scalar1=1.0 / D, scalar2=None,
        op0=ALU.mult))
    b.op("vector", lambda v: v.tensor_tensor(
        out=statc[:, 5, :], in0=statc[:, 3, :], in1=statc[:, 3, :], op=ALU.mult))
    m_var = b.op("vector", lambda v: v.tensor_tensor(
        out=statc[:, 5, :], in0=statc[:, 4, :], in1=statc[:, 5, :],
        op=ALU.subtract), inc="DV")
    b.wait("scalar", "DV", m_var)
    m_sd = b.op("scalar", lambda sc: sc.activation(
        out=statc[:, 6, :], in_=statc[:, 5, :], func=AF.Sqrt,
        bias=epsb[:, 0:1], scale=1.0), inc="AC")
    b.wait("vector", "AC", m_sd)
    b.op("vector", lambda v: v.reciprocal(out=statc[:, 7, :], in_=statc[:, 6, :]))
    b.op("vector", lambda v: v.tensor_tensor(
        out=statc[:, 0, :], in0=statc[:, 3, :], in1=statc[:, 7, :], op=ALU.mult))
    m_nb = b.op("vector", lambda v: v.tensor_scalar(
        out=statc[:, 0, :], in0=statc[:, 0, :], scalar1=-1.0, scalar2=None,
        op0=ALU.mult), inc="DV")
    b.wait("scalar", "DV", m_nb)
    ev1 = []
    for tt in range(NT):
        par = tt % 2
        m_xn = b.op("scalar", lambda sc, tt=tt, par=par: sc.activation(
            out=xtm2[par], in_=x0t[:, tt, :], func=AF.Identity,
            bias=statc[:, 0, tt:tt + 1], scale=statc[:, 7, tt:tt + 1]), inc="AC")
        pst = tpx if par == 0 else tpy
        b.wait("tensor", "AC", m_xn)
        if tt >= 2:
            b.wait("tensor", "AC", ev1[tt - 2])
        for dc in range(NDC):
            m_tr = b.op("tensor", lambda t, dc=dc, par=par, pst=pst: t.transpose(
                pst[:, dc * 128:(dc + 1) * 128],
                xtm2[par][:, dc * 128:(dc + 1) * 128], ident[:]),
                inc="PE" if dc == NDC - 1 else None)
        b.wait("scalar", "PE", m_tr)
        m_ev = b.op("scalar", lambda sc, tt=tt, pst=pst: sc.activation(
            out=xn0[:, :, tt * 128:(tt + 1) * 128],
            in_=pst[:, 0:1024], func=AF.Identity, bias=0.0,
            scale=1.0), inc="AC")
        ev1.append(m_ev)
    xn0_done = b.v("AC")
    dbg_dump("stat", statc[:])

    b.wait("gpsimd", "AC", xn0_done)
    gfn(lambda g: g.memset(dA2[0][:], 0.0))
    gfn(lambda g: g.memset(dBp[:], 0.0))
    gfn(lambda g: g.memset(hhp[:], 0.0))
    gfn(lambda g: g.memset(tpp[:], 0.0))
    bndz = gfn(lambda g: g.memset(dA2[1][:], 0.0), inc="GC")
    b.wait("scalar", "GC", bndz)
    b.wait("vector", "GC", bndz)

    b.wait("tensor", "AC", xn0_done)
    for t4 in range(4):
        for k in range(NDC):
            m_mm = b.op("tensor", lambda t, t4=t4, k=k: t.matmul(
                pb[t4][0:64, :], fwin[:, k, 0:64],
                xn0[:, k, t4 * 512:(t4 + 1) * 512],
                start=(k == 0), stop=(k == NDC - 1)),
                inc="PE" if k == NDC - 1 else None)
        b.wait("scalar", "PE", m_mm)
        b.op("scalar", lambda sc, t4=t4: sc.activation(
            out=gacc[0:64, t4 * 512:(t4 + 1) * 512], in_=pb[t4][0:64, :],
            func=AF.Identity, bias=sfbin[:, 0:1], scale=1.0),
            inc="AC" if t4 == 3 else None)
    spre_done = b.v("AC")

    b.wait("scalar", "AC", spre_done)
    for j in range(16):
        m_rl = b.op("scalar", lambda sc, j=j: sc.activation(
            out=ftmp, in_=gacc[0:64, :], func=AF.Relu,
            bias=sfb1[:, j:j + 1], scale=sfw1[:, j:j + 1]), inc="AC")
        b.wait("vector", "AC", m_rl)
        if j == 0:
            m_g = b.op("vector", lambda v: v.tensor_scalar(
                out=ftm2, in0=ftmp, scalar1=sfw2[:, 0:1], scalar2=None,
                op0=ALU.mult), inc="DV")
        else:
            b.op("vector", lambda v, j=j: v.tensor_scalar(
                out=ftmp, in0=ftmp, scalar1=sfw2[:, j:j + 1], scalar2=None,
                op0=ALU.mult))
            m_g = b.op("vector", lambda v: v.tensor_tensor(
                out=ftm2, in0=ftm2, in1=ftmp, op=ALU.add), inc="DV")
        if j < 15:
            b.wait("scalar", "DV", m_g)
    b.wait("scalar", "DV", b.v("DV"))
    gacc_done = b.op("scalar", lambda sc: sc.activation(
        out=gacc[0:64, :], in_=ftm2, func=AF.Identity, bias=sfb2[:, 0:1],
        scale=1.0), inc="AC")

    b.wait("gpsimd", "DV", b.v("DV"))
    b.wait("gpsimd", "AC", b.v("AC"))
    z2 = gfn(lambda g: g.memset(statc[:], 0.0), inc="GC")
    b.wait("vector", "GC", z2)

    x1_asm = {}
    sq4_red = {}
    x1_st_mark = {}
    x1_sq_mark = {}
    seg = [j * (TQ + 1) for j in range(4)]
    bnd = [j * (TQ + 1) + TQ for j in range(3)]

    def v3(ap):
        return ap.rearrange("p t -> p 1 t")

    for q in range(NQ):
        t0 = q * TQ
        for g8 in range(8):
            buf = g8 % 2
            b.wait("gpsimd", "PE", b.v("PE"))
            mark_w = gdma(wsl[buf][:], rkpm(Win_d)[:, :, g8 * 512:(g8 + 1) * 512])
            b.wait("tensor", "GP", mark_w)
            if q == 0 and g8 == 0:
                b.wait("tensor", "AC", gacc_done)
            b.wait("tensor", "AC", b.v("AC"))
            for k in range(NDC):
                for m in range(4):
                    m_mm = b.op("tensor", lambda t, buf=buf, k=k, m=m, t0=t0: t.matmul(
                        pb[m][:], wsl[buf][:, k, m * 128:(m + 1) * 128],
                        xn0[:, k, t0:t0 + TQ],
                        start=(k == 0), stop=(k == NDC - 1)),
                        inc="PE" if (k == NDC - 1 and m == 3) else None)
            b.wait("scalar", "PE", m_mm)
            for m in range(4):
                if g8 < 4:
                    cc = g8 * 4 + m
                    b.op("scalar", lambda sc, cc=cc, m=m: sc.activation(
                        out=xcb[:, cc, 3:3 + TQ], in_=pb[m][:], func=AF.Identity,
                        bias=0.0, scale=1.0), inc="AC" if m == 3 else None)
                else:
                    cc = (g8 - 4) * 4 + m
                    b.op("scalar", lambda sc, cc=cc, m=m: sc.activation(
                        out=slz[:, cc, :], in_=pb[m][:], func=AF.Silu,
                        bias=0.0, scale=1.0), inc="AC" if m == 3 else None)
        xz_done = b.v("AC")
        if q == dbgq:
            dbg_dump("xc", xcb[:])
            dbg_dump("slz", slz[:])

        if q > 0:
            b.wait("gpsimd", "AC", xz_done)
            marg_restored = gfn(lambda g: g.tensor_copy(xcb[:, :, 0:3], marg[:]),
                                inc="GC")

        if q == 0:
            b.wait("gpsimd", "AC", gacc_done)
            marg_restored = gfn(lambda g: g.memset(xcb[:, :, 0:3], 0.0), inc="GC")
        b.wait("tensor", "AC", xz_done)
        b.wait("tensor", "GC", marg_restored)
        conv_ev = {}
        for cc in range(NCC):
            dpar = cc % 2
            bank = 3 + (cc % 2)
            if cc >= 2:
                b.wait("vector", "PE", conv_pe[cc - 2])
            b.wait("vector", "GC", const_done)
            for k in range(4):
                m_d4 = b.op("vector", lambda v, cc=cc, k=k, dpar=dpar: v.tensor_scalar(
                    out=diag4[dpar][:, k, :], in0=ident[:],
                    scalar1=convw[:, cc, k:k + 1], scalar2=None, op0=ALU.mult),
                    inc="DV" if k == 3 else None)
            b.wait("tensor", "DV", m_d4)
            if cc >= 2:
                b.wait("tensor", "AC", conv_ev[cc - 2])
            for k in range(4):
                m_mm = b.op("tensor", lambda t, cc=cc, k=k, dpar=dpar, bank=bank: t.matmul(
                    pb[bank][:], diag4[dpar][:, k, :],
                    xcb[:, cc, k:k + TQ], start=(k == 0), stop=(k == 3)),
                    inc="PE" if k == 3 else None)
            if cc == 0:
                conv_pe = {}
            conv_pe[cc] = m_mm
            b.wait("scalar", "PE", m_mm)
            m_e = b.op("scalar", lambda sc, cc=cc, bank=bank: sc.activation(
                out=xsb[:, cc, :], in_=pb[bank][:], func=AF.Silu,
                bias=convb[:, cc:cc + 1], scale=1.0), inc="AC")
            conv_ev[cc] = m_e
        conv_done = b.v("AC")
        if q == dbgq:
            dbg_dump("xs", xsb[:])
        if q < NQ - 1:
            b.wait("gpsimd", "AC", conv_done)
            gfn(lambda g: g.tensor_copy(marg[:], xcb[:, :, TQ:TQ + 3]), inc="GC")

        b.wait("tensor", "DV", b.v("DV"))
        b.wait("scalar", "ST", b.v("ST"))
        for g4 in range(8):
            pass
        for g2 in range(8):
            buf = g2 % 2
            b.wait("gpsimd", "PE", b.v("PE"))
            mark_w = gdma(wdt[buf][:], rkpm(Wdt_d)[:, :, g2 * 256:(g2 + 1) * 256])
            b.wait("tensor", "GP", mark_w)
            b.wait("tensor", "AC", b.v("AC"))
            for k in range(NCC):
                for m in range(2):
                    m_mm = b.op("tensor", lambda t, buf=buf, k=k, m=m: t.matmul(
                        pb[m][:], wdt[buf][:, k, m * 128:(m + 1) * 128],
                        xsb[:, k, :], start=(k == 0), stop=(k == NCC - 1)),
                        inc="PE" if (k == NCC - 1 and m == 1) else None)
            b.wait("scalar", "PE", m_mm)
            for m in range(2):
                cc = g2 * 2 + m
                b.op("scalar", lambda sc, cc=cc, m=m: sc.activation(
                    out=tpp[:, 0:TQ], in_=pb[m][:], func=AF.Exp,
                    bias=bdt[:, cc:cc + 1], scale=1.0))
                b.op("scalar", lambda sc, cc=cc: sc.activation(
                    out=dlt[:, cc, :], in_=tpp[:, 0:TQ], func=AF.Ln,
                    bias=1.0, scale=1.0), inc="AC" if m == 1 else None)
        dlt_done = b.v("AC")
        if q == dbgq:
            dbg_dump("dlt", dlt[:])

        b.wait("tensor", "AC", b.v("AC"))
        for k in range(NCC):
            m_mm = b.op("tensor", lambda t, k=k: t.matmul(
                pb[2][0:32, :], wxp[:, k, :], xsb[:, k, :],
                start=(k == 0), stop=(k == NCC - 1)),
                inc="PE" if k == NCC - 1 else None)
        xproj_pe = m_mm
        b.wait("scalar", "PE", xproj_pe)
        bcrows_done = b.op("scalar", lambda sc: sc.activation(
            out=bcrows[:, :], in_=pb[2][0:32, :], func=AF.Identity, bias=0.0,
            scale=1.0), inc="AC")

        b.wait("vector", "AC", dlt_done)
        for cc in range(NCC):
            b.op("vector", lambda v, cc=cc: v.tensor_tensor(
                out=ub[:, cc, :], in0=xsb[:, cc, :], in1=dlt[:, cc, :],
                op=ALU.mult))
        b.wait("vector", "PE", xproj_pe)
        for cc in range(NCC):
            m_yi = b.op("vector", lambda v, cc=cc: v.tensor_scalar(
                out=xsb[:, cc, :], in0=xsb[:, cc, :], scalar1=Dsb[:, cc:cc + 1],
                scalar2=None, op0=ALU.mult),
                inc="DV" if cc == NCC - 1 else None)
        yinit_done = m_yi
        if q == dbgq:
            dbg_dump("ub", ub[:])
            dbg_dump("bc", bcrows[:])

        b.wait("tensor", "AC", bcrows_done)
        if q > 0:
            b.wait("tensor", "DV", x1_asm[q * 4 - 1])
        for s in range(NS):
            pp = s % 2
            eB, eC = esel[pp * 2], esel[pp * 2 + 1]
            b.wait("vector", "PE", b.v("PE"))
            b.op("vector", lambda v, eB=eB, s=s: v.tensor_scalar(
                out=eB[:], in0=ones32[:], scalar1=identf[0:32, s:s + 1],
                scalar2=None, op0=ALU.mult))
            m_es = b.op("vector", lambda v, eC=eC, s=s: v.tensor_scalar(
                out=eC[:], in0=ones32[:], scalar1=identf[0:32, 16 + s:17 + s],
                scalar2=None, op0=ALU.mult), inc="DV")
            b.wait("tensor", "DV", m_es)
            if s > 0:
                b.wait("tensor", "AC", bc_ev)
                b.wait("tensor", "DV", last_y)
            m_pe1 = b.op("tensor", lambda t, eB=eB: t.matmul(
                pb[3][:], eB[:, :], bcrows[:, :], start=True, stop=True))
            m_pe2 = b.op("tensor", lambda t, eC=eC: t.matmul(
                px[:], eC[:, :], bcrows[:, :], start=True, stop=True), inc="PE")
            b.wait("scalar", "PE", m_pe2)
            if s > 0:
                b.wait("scalar", "DV", last_y)
            b.op("scalar", lambda sc: sc.activation(
                out=bbt[:, :], in_=pb[3][:], func=AF.Identity, bias=0.0, scale=1.0))
            bc_ev = b.op("scalar", lambda sc: sc.activation(
                out=cbt[:, :], in_=px[:], func=AF.Identity, bias=0.0, scale=1.0),
                inc="AC")
            for p4 in range(4):
                i = s * 4 + p4
                par = i % 2
                ccs = [p4 * 4 + j for j in range(4)]
                if i >= 2:
                    b.wait("scalar", "DV", scan_done[i - 2])
                for j, cc in enumerate(ccs):
                    m_dA = b.op("scalar", lambda sc, cc=cc, j=j, s=s, par=par: sc.activation(
                        out=dA2[par][:, seg[j]:seg[j] + TQ], in_=dlt[:, cc, :],
                        func=AF.Identity, bias=1.0, scale=Asb[:, cc, s:s + 1]),
                        inc="AC" if j == 3 else None)
                b.wait("vector", "AC", bc_ev)
                if i == 0:
                    b.wait("vector", "DV", yinit_done)
                b.op("vector", lambda v, p4=p4: v.tensor_tensor(
                    out=dBp[:].rearrange("p (a t) -> p a t", t=TQ + 1)[:, :, 0:TQ],
                    in0=ub[:, p4 * 4:p4 * 4 + 4, :],
                    in1=bbt[:].rearrange("p (o t) -> p o t", o=1).broadcast_to([128, 4, TQ]),
                    op=ALU.mult))
                ci = s * NCC + p4 * 4
                b.op("vector", lambda v, ci=ci: v.tensor_copy(
                    dBp[:].rearrange("p (a t) -> p a t", t=TQ + 1)[:, 0:3, TQ:TQ + 1],
                    carry[:, ci + 1:ci + 4].rearrange("p (c o) -> p c o", o=1)))
                b.wait("vector", "AC", m_dA)
                m_scan = b.op("vector", lambda v, par=par, ci=ci: v.tensor_tensor_scan(
                    out=hhp[:], data0=dA2[par][:], data1=dBp[:],
                    initial=carry[:, ci:ci + 1], op0=ALU.mult, op1=ALU.add))
                b.op("vector", lambda v: v.tensor_tensor(
                    out=tpp[:].rearrange("p (a t) -> p a t", t=TQ + 1)[:, :, 0:TQ],
                    in0=hhp[:].rearrange("p (a t) -> p a t", t=TQ + 1)[:, :, 0:TQ],
                    in1=cbt[:].rearrange("p (o t) -> p o t", o=1).broadcast_to([128, 4, TQ]),
                    op=ALU.mult))
                b.op("vector", lambda v, p4=p4: v.tensor_tensor(
                    out=xsb[:, p4 * 4:p4 * 4 + 4, :],
                    in0=xsb[:, p4 * 4:p4 * 4 + 4, :],
                    in1=tpp[:].rearrange("p (a t) -> p a t", t=TQ + 1)[:, :, 0:TQ],
                    op=ALU.add))
                m_y = b.op("vector", lambda v, ci=ci: v.tensor_copy(
                    carry[:, ci:ci + 4].rearrange("p (c o) -> p c o", o=1),
                    hhp[:].rearrange("p (a t) -> p a t", t=TQ + 1)[:, :, TQ - 1:TQ]),
                    inc="DV")
                if i == 0 and s == 0:
                    scan_done = {}
                scan_done[i] = m_y
            last_y = scan_done[s * 4 + 3]
        sloop_done = b.v("DV")

        for cc in range(NCC):
            m_ys = b.op("vector", lambda v, cc=cc: v.tensor_tensor(
                out=xsb[:, cc, :], in0=xsb[:, cc, :], in1=slz[:, cc, :],
                op=ALU.mult), inc="DV" if cc == NCC - 1 else None)
        y_done = m_ys

        qbar()
        if q == dbgq:
            dbg_dump("y", xsb[:])
            dbg_dump("carry", carry[:])
        b.wait("tensor", "DV", y_done)
        for m in range(NDC):
            buf = m % 2
            b.wait("gpsimd", "PE", b.v("PE"))
            mark_w = gdma(wou[buf][:], rkpm(Wout_d)[:, :, m * 128:(m + 1) * 128])
            b.wait("tensor", "GP", mark_w)
            bank = m % 4
            b.wait("tensor", "AC", b.v("AC"))
            for k in range(NCC):
                m_mm = b.op("tensor", lambda t, buf=buf, k=k, bank=bank: t.matmul(
                    pb[bank][:], wou[buf][:, k, :], xsb[:, k, :],
                    start=(k == 0), stop=(k == NCC - 1)),
                    inc="PE" if k == NCC - 1 else None)
            if m > 0:
                b.wait("tensor", "DV", ffn_add)
            m_mf = b.op("tensor", lambda t, m=m, t0=t0: t.matmul(
                px[:], fwoc[:, m * 128:(m + 1) * 128],
                gacc[:, t0:t0 + TQ], start=True, stop=True), inc="PE")
            b.wait("scalar", "PE", m_mm)
            m_ev = b.op("scalar", lambda sc, m=m, bank=bank: sc.activation(
                out=ofm[:, m, :], in_=pb[bank][:], func=AF.Identity, bias=0.0,
                scale=1.0), inc="AC")
            b.wait("vector", "PE", m_mf)
            b.wait("vector", "AC", m_ev)
            ffn_add = b.op("vector", lambda v, m=m: v.tensor_tensor(
                out=ofm[:, m, :], in0=ofm[:, m, :], in1=px[:], op=ALU.add),
                inc="DV")
        out_fm = b.v("AC")
        out_fm_dv = ffn_add
        if q == dbgq:
            dbg_dump("ofm", ofm[:])

        for sub in range(4):
            tt = q * 4 + sub
            par = sub % 2
            pst = tpx if par == 0 else tpy
            if tt >= 2:
                b.wait("sync", "DV", x1_asm[tt - 2])
            m_x0 = hdma(x0st[par], x0_d[tt * 128:(tt + 1) * 128, :])
            b.wait("tensor", "AC", out_fm)
            b.wait("tensor", "DV", out_fm_dv)
            if tt >= 2:
                b.wait("tensor", "DV", x1_asm[tt - 2])
            for dc in range(NDC):
                m_tr = b.op("tensor", lambda t, dc=dc, sub=sub, pst=pst: t.transpose(
                    pst[:, dc * 128:(dc + 1) * 128],
                    ofm[:, dc, sub * 128:(sub + 1) * 128], ident[:]),
                    inc="PE" if dc == NDC - 1 else None)
            b.wait("vector", "PE", m_tr)
            b.wait("vector", "HW", m_x0)
            if tt >= 2:
                b.wait("vector", "ST", b.v("ST"))
                b.wait("vector", "AC", x1_sq_mark[tt - 2])
            m_x1 = b.op("vector", lambda v, par=par, pst=pst: v.tensor_tensor(
                out=x1st[par], in0=pst[:, 0:1024],
                in1=x0st[par], op=ALU.add), inc="DV")
            x1_asm[tt] = m_x1
            b.wait("sync", "DV", m_x1)
            x1_st_mark[tt] = hdma(x1_d[tt * 128:(tt + 1) * 128, :], x1st[par],
                                  sem="ST")
            b.wait("scalar", "DV", m_x1)
            if tt > 0:
                b.wait("scalar", "DV", sq4_red[tt - 1])
            m_sq = b.op("scalar", lambda sc, par=par, tt=tt: sc.activation(
                out=sq4, in_=x1st[par], func=AF.Square, bias=0.0, scale=1.0),
                inc="AC")
            x1_sq_mark[tt] = m_sq
            b.wait("vector", "AC", m_sq)
            sq4_red[tt] = b.op("vector", lambda v, tt=tt: v.tensor_reduce(
                out=stat2[:, 1, tt:tt + 1], in_=sq4, axis=AX, op=ALU.add), inc="DV")
            b.op("vector", lambda v, par=par, tt=tt: v.tensor_reduce(
                out=stat2[:, 2, tt:tt + 1], in_=x1st[par], axis=AX, op=ALU.add))
            b.op("vector", lambda v, par=par, tt=tt: v.tensor_reduce(
                out=stat2[:, 3, tt:tt + 1], in_=x1st[par], axis=AX, op=ALU.max))
            b.op("vector", lambda v, par=par, tt=tt: v.tensor_reduce(
                out=stat2[:, 4, tt:tt + 1], in_=x1st[par], axis=AX, op=ALU.min))

    if dbg:
        dbg_dump("xn0", xn0[:])
        dbg_dump("gacc", gacc[:])
    m_syn2 = b.op("vector", lambda v: v.engine_nop(), inc="DV")
    b.wait("vector", "DV", m_syn2)
    b.op("vector", lambda v: v.tensor_scalar(
        out=stat2[:, 5, :], in0=stat2[:, 2, :], scalar1=1.0 / D, scalar2=None,
        op0=ALU.mult))
    b.op("vector", lambda v: v.tensor_scalar(
        out=stat2[:, 6, :], in0=stat2[:, 1, :], scalar1=1.0 / D, scalar2=None,
        op0=ALU.mult))
    b.op("vector", lambda v: v.tensor_tensor(
        out=stat2[:, 7, :], in0=stat2[:, 5, :], in1=stat2[:, 5, :], op=ALU.mult))
    m_var2 = b.op("vector", lambda v: v.tensor_tensor(
        out=stat2[:, 7, :], in0=stat2[:, 6, :], in1=stat2[:, 7, :],
        op=ALU.subtract), inc="DV")
    b.wait("scalar", "DV", m_var2)
    m_sd2 = b.op("scalar", lambda sc: sc.activation(
        out=stat2[:, 7, :], in_=stat2[:, 7, :], func=AF.Sqrt,
        bias=epsb[:, 0:1], scale=1.0), inc="AC")
    b.wait("vector", "AC", m_sd2)
    b.op("vector", lambda v: v.reciprocal(out=stat2[:, 6, :], in_=stat2[:, 7, :]))
    b.op("vector", lambda v: v.tensor_tensor(
        out=stat2[:, 3, :], in0=stat2[:, 3, :], in1=stat2[:, 5, :],
        op=ALU.subtract))
    b.op("vector", lambda v: v.tensor_tensor(
        out=stat2[:, 4, :], in0=stat2[:, 5, :], in1=stat2[:, 4, :],
        op=ALU.subtract))
    b.op("vector", lambda v: v.tensor_tensor(
        out=stat2[:, 3, :], in0=stat2[:, 3, :], in1=stat2[:, 4, :], op=ALU.max))
    m_z = b.op("vector", lambda v: v.tensor_tensor(
        out=stat2[:, 4, :], in0=stat2[:, 3, :], in1=stat2[:, 6, :],
        op=ALU.mult), inc="DV")
    b.wait("vector", "DV", m_z)
    m_m1c = b.op("vector", lambda v: v.tensor_reduce(
        out=stat2[:, 0, 0:1], in_=stat2[:, 4, 0:NT], axis=AX, op=ALU.max),
        inc="DV")
    b.wait("gpsimd", "DV", m_m1c)
    m_m1 = gfn(lambda g: g.tensor_reduce(out=m1c[:], in_=stat2[:, 0, 0:1],
                                         axis=AXC, op=ALU.max), inc="GC")
    if dbg:
        dbg_dump("stat2", stat2[:])
    b.wait("sync", "GC", m_m1)
    hdma(m1_d[:, :], m1c[:], sem="ST")
    b.wait("sync", "DV", b.v("DV"))
    hdma(mrow_d[:, 0:NT], stat2[:, 5, 0:NT], sem="ST")
    hdma(mrow_d[:, NT:2 * NT], stat2[:, 6, 0:NT], sem="ST")
    b.wait("sync", "ST", b.v("ST"))

    with b.es, nc.Block() as block:
        b.run(block)
    return nc




def build_launch2():
    nc = bass.Bass("TRN2", target_bir_lowering=False)

    def din(name, shape):
        return nc.dram_tensor(name, shape, F32, kind="ExternalInput").ap()

    x1_d = din("x1", [T, D])
    mrow_d = din("mrow", [128, 2 * NT])
    m1v_d = din("m1v", [1, 8])
    clipx_d = din("clip_x", [1, 1])
    Wz_d = [din("W_zf", [D, 512]), din("W_zs", [D, 512])]
    Wh_d = [din("W_hf", [D, 512]), din("W_hs", [D, 512])]
    bz_d = din("b_z", [128, 8])
    bh_d = din("b_h", [128, 8])
    Wg_d = din("W_g", [D, D])
    bg_d = din("b_g", [128, NDC])
    wg_d = din("w_gate", [FI, D])
    wu_d = din("w_up", [FI, D])

    x1p_d = nc.dram_tensor("x1p", [T, D], F32, kind="ExternalOutput").ap()
    h_d = nc.dram_tensor("hact", [128, NIC, T], BF16, kind="ExternalOutput").ap()
    m2_d = nc.dram_tensor("m2", [1, 1], F32, kind="ExternalOutput").ap()

    b = B(nc)
    es = b.es
    sb = lambda n, s, d: es.enter_context(nc.sbuf_tensor(n, s, d))
    psal = lambda n, s: es.enter_context(nc.psum_tensor(n, s, F32))

    qx = sb("qx", [128, NDC, T], BF16)
    xn1b = sb("xn1b", [128, NDC, T], BF16)
    hcm = sb("hcm", [128, 8, T], BF16)
    zab = sb("zab", [128, 3 * T], BF16)
    zbuf = zab[:, 0:T]
    abuf = zab[:, T:2 * T]
    bbuf = zab[:, 2 * T:3 * T]
    aofm = nc.alloc_sbuf_tensor_at("aofm", [128, NDC, TQ], BF16,
                                   offset=nc.lookup_mloc(zab).addr)
    hbuf = [sb(f"hbuf{i}", [128, T], BF16) for i in range(2)]
    wsl = [sb(f"wsl{i}", [128, 8, 512], BF16) for i in range(2)]
    wgsl = sb("wgsl", [128, 8, D], BF16)
    wqsl = [sb(f"wqsl{i}", [128, D], F32) for i in range(2)]
    qs1 = sb("qs1", [128, D], F32)
    qsi = sb("qsi", [128, D], I32)
    qmk = sb("qmk", [128, D], F32)
    sc1 = sb("sc1", [128, D], F32)
    qg = sb("qg", [128, D], BF16)
    qu = sb("qu", [128, D], BF16)
    wqgT = sb("wqgT", [128, NDC, 128], BF16)
    wquT = sb("wquT", [128, NDC, 128], BF16)
    x1t = [sb(f"x1t{i}", [128, D], F32) for i in range(2)]
    outb = [sb(f"outb{i}", [128, D], F32) for i in range(2)]
    scb = [sb(f"scb{i}", [128, D], BF16) for i in range(2)]
    th = [sb(f"th{i}", [128, 512], BF16) for i in range(2)]
    panel = sb("panel", [128, 8, 64], F32)
    rowp = sb("rowp", [1, 128], F32)
    rowq = sb("rowq", [1, 128], F32)
    recipx = sb("recipx", [128, 1], F32)
    sxc127 = sb("sxc127", [128, 1], F32)
    rmxg = sb("rmxg", [128, 1], F32)
    rmxu = sb("rmxu", [128, 1], F32)
    tqg = sb("tqg", [128, 1], F32)
    tqu = sb("tqu", [128, 1], F32)
    rcg = sb("rcg", [128, 1], F32)
    rcu = sb("rcu", [128, 1], F32)
    bz = sb("bz", [128, 8], F32)
    bh = sb("bh", [128, 8], F32)
    nbz = sb("nbz", [128, 8], F32)
    bg = sb("bg", [128, NDC], F32)
    ident = sb("ident", [128, 128], BF16)
    onesf = sb("onesf", [1, 128], F32)
    m2c = sb("m2c", [1, 1], F32)

    pb = [psal(f"pb{i}", [128, 512]) for i in range(4)]
    px = psal("px", [128, 512])
    tpx = es.enter_context(nc.psum_tensor("tpx", [128, 1024], BF16))
    tpy = es.enter_context(nc.psum_tensor("tpy", [128, 1024], BF16))

    b.sem("HW", 16)
    b.sem("GP", 16)
    b.sem("PE", 1)
    b.sem("AC", 1)
    b.sem("DV", 1)
    b.sem("GC", 1)
    b.sem("ST", 16)

    def gdma(out, in_):
        b.dma_fence("gpsimd", "GP")
        return b.op("gpsimd", lambda g, o=out, i=in_: g.dma_start(out=o, in_=i),
                    inc="GP")

    def hdma(out, in_, sem="HW"):
        b.dma_fence("sync", sem)
        return b.op("sync", lambda s, o=out, i=in_: s.dma_start(out=o, in_=i),
                    inc=sem)

    def gfn(f, inc=None):
        return b.op("gpsimd", f, inc=inc)

    gfn(lambda g: g.memset(ident[:], 0.0))
    gfn(lambda g: g.affine_select(out=ident[:], in_=ident[:],
                                  compare_op=ALU.not_equal, fill=1.0, base=0,
                                  pattern=[[-1, 128]], channel_multiplier=1))
    gfn(lambda g: g.memset(onesf[:], 1.0))
    gfn(lambda g: g.memset(panel[:], 0.0))
    gfn(lambda g: g.memset(rowp[:], 0.0))
    const_done = gfn(lambda g: g.memset(rowq[:], 0.0), inc="GC")
    b.wait("gpsimd", "GC", const_done)
    gdma(panel[:, 0, 0:NT], mrow_d[:, 0:NT])
    gdma(panel[:, 1, 0:NT], mrow_d[:, NT:2 * NT])
    gdma(rowp[0:1, 0:8], m1v_d[:, :])
    gdma(rowq[0:1, 0:1], clipx_d[:, :])
    gdma(bz[:], bz_d[:, :])
    gdma(bh[:], bh_d[:, :])
    gdma(bg[:], bg_d[:, :])
    m_wg = gdma(wgsl[:], rkpm(Wg_d))
    loads_done = b.v("GP")

    b.wait("vector", "GP", loads_done)
    b.wait("vector", "GC", const_done)
    b.wait("tensor", "GC", const_done)
    b.op("vector", lambda v: v.tensor_reduce(
        out=rowp[0:1, 8:9], in_=rowp[0:1, 0:8], axis=AX, op=ALU.max))
    b.op("vector", lambda v: v.tensor_scalar(
        out=rowp[0:1, 16:32], in0=onesf[0:1, 0:16], scalar1=rowp[0:1, 8:9],
        scalar2=None, op0=ALU.mult))
    m_rq = b.op("vector", lambda v: v.tensor_scalar(
        out=rowq[0:1, 16:32], in0=onesf[0:1, 0:16], scalar1=rowq[0:1, 0:1],
        scalar2=None, op0=ALU.mult))
    b.wait("scalar", "DV", m_rq)
    m_sg = b.op("scalar", lambda sc: sc.activation(
        out=rowq[0:1, 32:48], in_=rowq[0:1, 16:32], func=AF.Sigmoid, bias=0.0,
        scale=1.0))
    b.wait("vector", "AC", m_sg)
    b.op("vector", lambda v: v.tensor_tensor(
        out=rowp[0:1, 32:48], in0=rowp[0:1, 16:32], in1=rowq[0:1, 32:48],
        op=ALU.mult))
    b.op("vector", lambda v: v.tensor_scalar(
        out=rowp[0:1, 48:64], in0=rowp[0:1, 32:48], scalar1=1.0 / 127.0,
        scalar2=None, op0=ALU.mult))
    b.op("vector", lambda v: v.reciprocal(
        out=rowp[0:1, 64:80], in_=rowp[0:1, 48:64]))
    m_row = b.op("vector", lambda v: v.tensor_scalar(
        out=rowp[0:1, 80:96], in0=rowp[0:1, 48:64], scalar1=1.0 / 127.0,
        scalar2=None, op0=ALU.mult))
    b.wait("tensor", "DV", m_row)
    b.op("tensor", lambda t: t.matmul(
        px[:, 0:1], onesf[:, :], rowp[0:1, 64:65], start=True, stop=True))
    m_bc = b.op("tensor", lambda t: t.matmul(
        px[:, 1:2], onesf[:, :], rowp[0:1, 80:81], start=True, stop=True),
        inc="PE")
    b.wait("vector", "PE", m_bc)
    b.op("vector", lambda v: v.tensor_copy(recipx[:], px[:, 0:1]))
    m_bcv = b.op("vector", lambda v: v.tensor_copy(sxc127[:], px[:, 1:2]))
    b.op("vector", lambda v: v.tensor_tensor(
        out=panel[:, 2, :], in0=panel[:, 0, :], in1=panel[:, 1, :],
        op=ALU.mult))
    b.op("vector", lambda v: v.tensor_scalar(
        out=panel[:, 2, :], in0=panel[:, 2, :], scalar1=-1.0, scalar2=None,
        op0=ALU.mult))
    m_nbz = b.op("vector", lambda v: v.tensor_scalar(
        out=nbz[:], in0=bz[:], scalar1=-1.0, scalar2=None, op0=ALU.mult))

    b.wait("scalar", "DV", m_nbz)
    xn_mark = {}
    q_mark = {}
    tr_mark = {}
    ev_mark = {}
    for tt in range(NT):
        par = tt % 2
        if tt >= 2:
            b.wait("sync", "AC", xn_mark[tt - 2])
        m_x1 = hdma(x1t[par][:], x1_d[tt * 128:(tt + 1) * 128, :])
        b.wait("scalar", "HW", m_x1)
        if tt >= 1:
            b.wait("scalar", "DV", q_mark[tt - 1])
        m_xn = b.op("scalar", lambda sc, par=par, tt=tt: sc.activation(
            out=sc1[:], in_=x1t[par][:], func=AF.Identity,
            bias=panel[:, 2, tt:tt + 1], scale=panel[:, 1, tt:tt + 1]))
        xn_mark[tt] = m_xn
        if tt >= 1:
            b.wait("scalar", "PE", tr_mark[tt - 1])
        m_cb = b.op("scalar", lambda sc: sc.activation(
            out=scb[0][:], in_=sc1[:], func=AF.Identity, bias=0.0, scale=1.0))
        b.wait("vector", "AC", m_cb)
        if tt == 0:
            b.wait("vector", "DV", m_bcv)
        b.op("vector", lambda v: v.tensor_scalar(
            out=qs1[:], in0=sc1[:], scalar1=recipx[:, 0:1], scalar2=RB + 0.5,
            op0=ALU.mult, op1=ALU.add))
        b.op("vector", lambda v: v.tensor_copy(qsi[:], qs1[:]))
        b.op("vector", lambda v: v.tensor_tensor(
            out=qmk[:], in0=qsi[:], in1=qs1[:], op=ALU.is_gt))
        b.op("vector", lambda v: v.tensor_tensor(
            out=qs1[:], in0=qsi[:], in1=qmk[:], op=ALU.subtract))
        b.op("vector", lambda v: v.tensor_scalar(
            out=qs1[:], in0=qs1[:], scalar1=-RB, scalar2=127.0, op0=ALU.add,
            op1=ALU.min))
        m_q = b.op("vector", lambda v: v.tensor_scalar(
            out=scb[1][:], in0=qs1[:], scalar1=-128.0, scalar2=None,
            op0=ALU.max))
        q_mark[tt] = m_q
        b.wait("tensor", "AC", m_cb)
        b.wait("tensor", "DV", m_q)
        if tt >= 1:
            b.wait("tensor", "AC", ev_mark[tt - 1][0])
            b.wait("tensor", "DV", ev_mark[tt - 1][1])
        for dc in range(NDC):
            b.op("tensor", lambda t, dc=dc: t.transpose(
                tpx[:, dc * 128:(dc + 1) * 128],
                scb[0][:, dc * 128:(dc + 1) * 128], ident[:]))
            m_tr = b.op("tensor", lambda t, dc=dc: t.transpose(
                tpy[:, dc * 128:(dc + 1) * 128],
                scb[1][:, dc * 128:(dc + 1) * 128], ident[:]),
                inc="PE" if dc == NDC - 1 else None)
        tr_mark[tt] = m_tr
        b.wait("scalar", "PE", m_tr)
        b.wait("vector", "PE", m_tr)
        m_e1 = b.op("scalar", lambda sc, tt=tt: sc.activation(
            out=xn1b[:, :, tt * 128:(tt + 1) * 128], in_=tpx[:, 0:1024],
            func=AF.Identity, bias=0.0, scale=1.0))
        m_e2 = b.op("vector", lambda v, tt=tt: v.tensor_copy(
            qx[:, :, tt * 128:(tt + 1) * 128], tpy[:, 0:1024]))
        ev_mark[tt] = (m_e1, m_e2)
    p2_ac = b.v("AC")
    p2_dv = b.v("DV")

    scan_mark = {}
    for p in range(2):
        if p == 1:
            b.wait("gpsimd", "PE", b.v("PE"))
        mz = gdma(wsl[0][:], rkpm(Wz_d[p]))
        mh = gdma(wsl[1][:], rkpm(Wh_d[p]))
        b.wait("tensor", "GP", mh)
        if p == 0:
            b.wait("tensor", "AC", p2_ac)
            b.wait("tensor", "DV", p2_dv)
        for m in range(4):
            fc = p * 4 + m
            b.wait("tensor", "AC", b.v("AC"))
            b.wait("tensor", "DV", b.v("DV"))
            for k in range(NDC):
                for t4 in range(4):
                    m_mmz = b.op("tensor", lambda t, k=k, m=m, t4=t4: t.matmul(
                        pb[t4][:], wsl[0][:, k, m * 128:(m + 1) * 128],
                        xn1b[:, k, t4 * 512:(t4 + 1) * 512],
                        start=(k == 0), stop=(k == NDC - 1)),
                        inc="PE" if (k == NDC - 1 and t4 == 3) else None)
            b.wait("scalar", "PE", m_mmz)
            if fc >= 1:
                b.wait("scalar", "DV", scan_mark[fc - 1])
            for t4 in range(4):
                b.op("scalar", lambda sc, t4=t4, fc=fc: sc.activation(
                    out=zbuf[:, t4 * 512:(t4 + 1) * 512], in_=pb[t4][:],
                    func=AF.Sigmoid, bias=bz[:, fc:fc + 1], scale=1.0))
                m_za = b.op("scalar", lambda sc, t4=t4, fc=fc: sc.activation(
                    out=abuf[:, t4 * 512:(t4 + 1) * 512], in_=pb[t4][:],
                    func=AF.Sigmoid, bias=nbz[:, fc:fc + 1], scale=-1.0))
            b.wait("tensor", "AC", m_za)
            for k in range(NDC):
                for t4 in range(4):
                    m_mmh = b.op("tensor", lambda t, k=k, m=m, t4=t4: t.matmul(
                        pb[t4][:], wsl[1][:, k, m * 128:(m + 1) * 128],
                        xn1b[:, k, t4 * 512:(t4 + 1) * 512],
                        start=(k == 0), stop=(k == NDC - 1)),
                        inc="PE" if (k == NDC - 1 and t4 == 3) else None)
            b.wait("vector", "PE", m_mmh)
            b.wait("vector", "AC", m_za)
            for t4 in range(4):
                b.op("vector", lambda v, t4=t4, fc=fc: v.scalar_tensor_tensor(
                    out=bbuf[:, t4 * 512:(t4 + 1) * 512], in0=pb[t4][:],
                    scalar=bh[:, fc:fc + 1],
                    in1=zbuf[:, t4 * 512:(t4 + 1) * 512],
                    op0=ALU.add, op1=ALU.mult))
            m_scan = b.op("vector", lambda v, fc=fc: v.tensor_tensor_scan(
                out=hcm[:, fc, :], data0=abuf[:], data1=bbuf[:],
                initial=0.0, op0=ALU.mult, op1=ALU.add))
            scan_mark[fc] = m_scan
    attn_dv = b.v("DV")

    aofm_mark = {}
    th_mark = {}
    outb_mark = {}
    asm_tr = {}
    for t4 in range(4):
        b.wait("tensor", "GP", loads_done)
        if t4 == 0:
            b.wait("tensor", "DV", attn_dv)
        for m in range(8):
            bank = m % 4
            if m >= 4:
                b.wait("tensor", "DV", aofm_mark[(t4, m - 4)])
                b.wait("tensor", "AC", th_mark[(t4, m - 4)])
            for k in range(NDC):
                m_mm = b.op("tensor", lambda t, k=k, m=m, t4=t4, bank=bank:
                            t.matmul(
                    pb[bank][:], wgsl[:, k, m * 128:(m + 1) * 128],
                    hcm[:, k, t4 * 512:(t4 + 1) * 512],
                    start=(k == 0), stop=(k == NDC - 1)),
                    inc="PE" if k == NDC - 1 else None)
            b.wait("scalar", "PE", m_mm)
            if m >= 2:
                b.wait("scalar", "DV", aofm_mark[(t4, m - 2)])
            m_th = b.op("scalar", lambda sc, m=m, bank=bank: sc.activation(
                out=th[m % 2][:], in_=pb[bank][:], func=AF.Sigmoid,
                bias=bg[:, m:m + 1], scale=1.0))
            th_mark[(t4, m)] = m_th
            b.wait("vector", "AC", m_th)
            if t4 >= 1 and m == 0:
                b.wait("vector", "PE", asm_tr[t4 - 1])
            m_ao = b.op("vector", lambda v, m=m, t4=t4: v.tensor_tensor(
                out=aofm[:, m, :], in0=th[m % 2][:],
                in1=hcm[:, m, t4 * 512:(t4 + 1) * 512], op=ALU.mult))
            aofm_mark[(t4, m)] = m_ao
        ao_dv = b.v("DV")
        for sub in range(4):
            tt = t4 * 4 + sub
            par = sub % 2
            pst = tpx if par == 0 else tpy
            if tt >= 2:
                b.wait("sync", "DV", outb_mark[tt - 2])
            else:
                b.wait("sync", "AC", p2_ac)
                b.wait("sync", "DV", p2_dv)
            m_x1 = hdma(x1t[par][:], x1_d[tt * 128:(tt + 1) * 128, :])
            b.wait("tensor", "DV", ao_dv)
            if tt >= 2:
                b.wait("tensor", "DV", outb_mark[tt - 2])
            for dc in range(NDC):
                m_tr = b.op("tensor", lambda t, dc=dc, sub=sub, pst=pst:
                            t.transpose(
                    pst[:, dc * 128:(dc + 1) * 128],
                    aofm[:, dc, sub * 128:(sub + 1) * 128], ident[:]),
                    inc="PE" if dc == NDC - 1 else None)
            asm_tr[t4] = m_tr
            b.wait("vector", "PE", m_tr)
            b.wait("vector", "HW", m_x1)
            if tt >= 2:
                b.wait("vector", "ST", b.v("ST"))
            m_ob = b.op("vector", lambda v, par=par, pst=pst: v.tensor_tensor(
                out=outb[par][:], in0=pst[:, 0:1024], in1=x1t[par][:],
                op=ALU.add))
            outb_mark[tt] = m_ob
            b.wait("sync", "DV", m_ob)
            hdma(x1p_d[tt * 128:(tt + 1) * 128, :], outb[par][:], sem="ST")

    w_mark = {}
    qdone = {}
    hstore = {}
    wtr_ev = {}
    ffn_hist = []
    for ic in range(NIC):
        if ic >= 1:
            b.wait("sync", "DV", qdone[ic - 1])
        mw1 = hdma(wqsl[0][:], wg_d[ic * 128:(ic + 1) * 128, :])
        mw2 = hdma(wqsl[1][:], wu_d[ic * 128:(ic + 1) * 128, :])
        w_mark[ic] = mw2
        b.wait("scalar", "HW", mw2)
        if ic >= 1:
            b.wait("scalar", "DV", qdone[ic - 1])
        m_ab1 = b.op("scalar", lambda sc: sc.activation(
            out=qs1[:], in_=wqsl[0][:], func=AF.Abs, bias=0.0, scale=1.0))
        b.wait("vector", "AC", m_ab1)
        m_rx1 = b.op("vector", lambda v: v.tensor_reduce(
            out=rmxg[:], in_=qs1[:], axis=AX, op=ALU.max))
        b.wait("scalar", "DV", m_rx1)
        m_ab2 = b.op("scalar", lambda sc: sc.activation(
            out=qs1[:], in_=wqsl[1][:], func=AF.Abs, bias=0.0, scale=1.0))
        b.wait("vector", "AC", m_ab2)
        b.op("vector", lambda v: v.tensor_reduce(
            out=rmxu[:], in_=qs1[:], axis=AX, op=ALU.max))
        b.op("vector", lambda v, ic=ic: v.tensor_tensor(
            out=panel[:, 5, ic:ic + 1], in0=rmxg[:], in1=sxc127[:],
            op=ALU.mult))
        b.op("vector", lambda v, ic=ic: v.tensor_tensor(
            out=panel[:, 6, ic:ic + 1], in0=rmxu[:], in1=sxc127[:],
            op=ALU.mult))
        b.op("vector", lambda v: v.tensor_scalar(
            out=tqg[:], in0=rmxg[:], scalar1=1.0 / 127.0, scalar2=None,
            op0=ALU.mult))
        b.op("vector", lambda v: v.tensor_scalar(
            out=tqu[:], in0=rmxu[:], scalar1=1.0 / 127.0, scalar2=None,
            op0=ALU.mult))
        b.op("vector", lambda v: v.reciprocal(out=rcg[:], in_=tqg[:]))
        b.op("vector", lambda v: v.reciprocal(out=rcu[:], in_=tqu[:]))
        for j, (src, rc, dst) in enumerate(((wqsl[0], rcg, qg),
                                            (wqsl[1], rcu, qu))):
            if ic >= 1 and j == 0:
                b.wait("vector", "PE", wtr_ev[ic - 1])
            b.op("vector", lambda v, src=src, rc=rc: v.tensor_scalar(
                out=qs1[:], in0=src[:], scalar1=rc[:, 0:1], scalar2=RB + 0.5,
                op0=ALU.mult, op1=ALU.add))
            b.op("vector", lambda v: v.tensor_copy(qsi[:], qs1[:]))
            b.op("vector", lambda v: v.tensor_tensor(
                out=qmk[:], in0=qsi[:], in1=qs1[:], op=ALU.is_gt))
            b.op("vector", lambda v: v.tensor_tensor(
                out=qs1[:], in0=qsi[:], in1=qmk[:], op=ALU.subtract))
            b.op("vector", lambda v: v.tensor_scalar(
                out=qs1[:], in0=qs1[:], scalar1=-RB, scalar2=127.0,
                op0=ALU.add, op1=ALU.min))
            m_qw = b.op("vector", lambda v, dst=dst: v.tensor_scalar(
                out=dst[:], in0=qs1[:], scalar1=-128.0, scalar2=None,
                op0=ALU.max))
        qdone[ic] = m_qw
        b.wait("tensor", "DV", m_qw)
        if ic >= 1:
            b.wait("tensor", "AC", wev_ac)
            b.wait("tensor", "DV", wev_dv)
        for dc in range(NDC):
            b.op("tensor", lambda t, dc=dc: t.transpose(
                tpx[:, dc * 128:(dc + 1) * 128],
                qg[:, dc * 128:(dc + 1) * 128], ident[:]))
            m_wtr = b.op("tensor", lambda t, dc=dc: t.transpose(
                tpy[:, dc * 128:(dc + 1) * 128],
                qu[:, dc * 128:(dc + 1) * 128], ident[:]),
                inc="PE" if dc == NDC - 1 else None)
        wtr_ev[ic] = m_wtr
        b.wait("scalar", "PE", m_wtr)
        b.wait("vector", "PE", m_wtr)
        wev_ac = b.op("scalar", lambda sc: sc.activation(
            out=wqgT[:].rearrange("p a c -> p (a c)"), in_=tpx[:, 0:1024],
            func=AF.Identity, bias=0.0, scale=1.0))
        wev_dv = b.op("vector", lambda v: v.tensor_copy(
            wquT[:].rearrange("p a c -> p (a c)"), tpy[:, 0:1024]))
        b.wait("tensor", "AC", wev_ac)
        b.wait("tensor", "DV", wev_dv)
        for t4 in range(4):
            bg_bank = t4 % 2
            bu_bank = 2 + t4 % 2
            if len(ffn_hist) >= 2:
                b.wait("tensor", "AC", ffn_hist[-2][0])
                b.wait("tensor", "DV", ffn_hist[-2][1])
            for k in range(NDC):
                b.op("tensor", lambda t, k=k, t4=t4, bank=bg_bank: t.matmul(
                    pb[bank][:], wqgT[:, k, :],
                    qx[:, k, t4 * 512:(t4 + 1) * 512],
                    start=(k == 0), stop=(k == NDC - 1)))
            for k in range(NDC):
                m_mm = b.op("tensor", lambda t, k=k, t4=t4, bank=bu_bank:
                            t.matmul(
                    pb[bank][:], wquT[:, k, :],
                    qx[:, k, t4 * 512:(t4 + 1) * 512],
                    start=(k == 0), stop=(k == NDC - 1)),
                    inc="PE" if k == NDC - 1 else None)
            b.wait("scalar", "PE", m_mm)
            if ffn_hist:
                b.wait("scalar", "DV", ffn_hist[-1][1])
            b.op("scalar", lambda sc, ic=ic, bank=bg_bank: sc.activation(
                out=th[0][:], in_=pb[bank][:], func=AF.Silu, bias=0.0,
                scale=panel[:, 5, ic:ic + 1]))
            ev_up = b.op("scalar", lambda sc, ic=ic, bank=bu_bank:
                         sc.activation(
                out=th[1][:], in_=pb[bank][:], func=AF.Identity, bias=0.0,
                scale=panel[:, 6, ic:ic + 1]))
            b.wait("vector", "AC", ev_up)
            if ic >= 2 and t4 == 0:
                b.wait("vector", "GP", b.v("GP"))
            m_h = b.op("vector", lambda v, ic=ic, t4=t4: v.tensor_tensor(
                out=hbuf[ic % 2][:, t4 * 512:(t4 + 1) * 512], in0=th[0][:],
                in1=th[1][:], op=ALU.mult))
            ffn_hist.append((ev_up, m_h))
        b.wait("vector", "DV", m_h)
        b.op("vector", lambda v, ic=ic: v.tensor_reduce(
            out=panel[:, 3, ic:ic + 1], in_=hbuf[ic % 2][:], axis=AX,
            op=ALU.max))
        m_st = b.op("vector", lambda v, ic=ic: v.tensor_reduce(
            out=panel[:, 4, ic:ic + 1], in_=hbuf[ic % 2][:], axis=AX,
            op=ALU.min))
        b.wait("gpsimd", "DV", m_st)
        hstore[ic] = gdma(h_d[:, ic, :], hbuf[ic % 2][:])

    b.op("vector", lambda v: v.tensor_scalar(
        out=panel[:, 7, 0:NIC], in0=panel[:, 4, 0:NIC], scalar1=-1.0,
        scalar2=None, op0=ALU.mult))
    b.op("vector", lambda v: v.tensor_tensor(
        out=panel[:, 7, 0:NIC], in0=panel[:, 3, 0:NIC],
        in1=panel[:, 7, 0:NIC], op=ALU.max))
    m_m2r = b.op("vector", lambda v: v.tensor_reduce(
        out=panel[:, 7, 32:33], in_=panel[:, 7, 0:NIC], axis=AX, op=ALU.max))
    b.wait("gpsimd", "DV", m_m2r)
    m_m2 = gfn(lambda g: g.tensor_reduce(
        out=m2c[:], in_=panel[:, 7, 32:33], axis=AXC, op=ALU.max), inc="GC")
    b.wait("sync", "GC", m_m2)
    hdma(m2_d[:, :], m2c[:], sem="ST")
    b.wait("sync", "ST", b.v("ST"))
    b.wait("sync", "GP", b.v("GP"))
    b.wait("sync", "HW", b.v("HW"))

    with b.es, nc.Block() as block:
        b.run(block)
    return nc

def build_mega(dbg=False, dbgq=0):
    nc = bass.Bass("TRN2", target_bir_lowering=False)

    def din(name, shape):
        return nc.dram_tensor(name, shape, F32, kind="ExternalInput").ap()

    x0_d = din("x0", [T, D])
    Win_d = din("W_in", [D, 2 * DI])
    convw_d = din("conv_w", [DI, 4])
    convb_d = din("conv_b", [128, NCC])
    Wxp_d = din("W_xproj", [DI, 32])
    Wdt_d = din("W_dt", [DI, DI])
    bdt_d = din("b_dt", [128, NCC])
    Alog_d = din("A_log", [DI, NS])
    Dp_d = din("D_param", [128, NCC])
    Wout_d = din("W_out", [DI, D])
    fWin_d = din("fWin", [D, 64])
    fbin_d = din("fbin", [64, 1])
    fW1_d = din("fW1", [64, 16])
    fb1_d = din("fb1", [64, 16])
    fW2_d = din("fW2", [64, 16])
    fb2_d = din("fb2", [64, 1])
    fWout_d = din("fWout", [64, D])
    fbout_d = din("fbout", [1, D])
    clipx_d = din("clip_x", [1, 1])
    clipd_d = din("clip_down", [1, 1])
    Wz_d = [din("W_zf", [D, 512]), din("W_zs", [D, 512])]
    Wh_d = [din("W_hf", [D, 512]), din("W_hs", [D, 512])]
    bz_d = din("b_z", [128, 8])
    bh_d = din("b_h", [128, 8])
    Wg_d = din("W_g", [D, D])
    bg_d = din("b_g", [128, NDC])
    wg_d = din("w_gate", [FI, D])
    wu_d = din("w_up", [FI, D])
    wd_d = din("w_down", [D, FI])

    out_d = nc.dram_tensor("out", [T, D], F32, kind="ExternalOutput").ap()
    x1i_t = nc.dram_tensor("x1i", [T, D], F32)
    x1i_d = x1i_t.ap()
    x1pi_t = nc.dram_tensor("x1pi", [T, D], F32)
    x1pi_d = x1pi_t.ap()
    hi_t = nc.dram_tensor("hi", [128, NIC, T], BF16)
    hi_d = hi_t.ap()
    c1i_t = nc.dram_tensor("c1i", [1, 1], F32)
    c1o_t = nc.dram_tensor("c1o", [1, 1], F32)
    c2i_t = nc.dram_tensor("c2i", [1, 1], F32)
    c2o_t = nc.dram_tensor("c2o", [1, 1], F32)
    if dbg:
        dbg_d = {nm: nc.dram_tensor("dbg_" + nm, sh, F32, kind="ExternalOutput").ap()
                 for nm, sh in [("xn0", [128, NDC, T]), ("xc", [128, NCC, TQ + 3]),
                                ("xs", [128, NCC, TQ]), ("dlt", [128, NCC, TQ]),
                                ("slz", [128, NCC, TQ]), ("ub", [128, NCC, TQ]),
                                ("y", [128, NCC, TQ]), ("bc", [32, TQ]),
                                ("gacc", [65, T]), ("ofm", [128, NDC, TQ]),
                                ("carry", [128, NS * NCC]), ("stat", [128, 8, 128]),
                                ("stat2", [128, 8, 128])]}

    b = B(nc)
    es = b.es
    sb = lambda n, s, d: es.enter_context(nc.sbuf_tensor(n, s, d))
    psal = lambda n, s: es.enter_context(nc.psum_tensor(n, s, F32))

    carry = sb("carry", [128, NS * NCC], F32)
    bcrows = sb("bcrows", [32, TQ], BF16)
    bbt = sb("bbt", [128, TQ], BF16)
    cbt = sb("cbt", [128, TQ], BF16)
    convw = sb("convw", [128, NCC, 4], F32)
    diag4 = [sb(f"diag4{i}", [128, 4, 128], BF16) for i in range(2)]
    ident = sb("ident", [128, 128], BF16)
    identf = sb("identf", [128, 32], F32)
    esel = [sb(f"esel{i}", [32, 128], BF16) for i in range(4)]
    ones32 = sb("ones32", [32, 128], BF16)
    ones1 = sb("ones1", [1, 128], BF16)
    onesf = sb("onesf", [1, 128], F32)
    Asb = sb("Asb", [128, NCC, NS], F32)
    Dsb = sb("Dsb", [128, NCC], F32)
    convb = sb("convb", [128, NCC], F32)
    bdt = sb("bdt", [128, NCC], F32)
    sfw1 = sb("sfw1", [64, 16], F32)
    sfb1 = sb("sfb1", [64, 16], F32)
    sfw2 = sb("sfw2", [64, 16], F32)
    sfb2 = sb("sfb2", [64, 1], F32)
    sfbin = sb("sfbin", [64, 1], F32)
    fwoc = sb("fwoc", [65, D], BF16)
    statc = sb("statc", [128, 8, 128], F32)
    stat2 = statc
    panel = sb("panel", [128, 8, 64], F32)
    m1c = sb("m1c", [1, 1], F32)
    m2c = sb("m2c", [1, 1], F32)
    marg = sb("marg", [128, NCC, 3], BF16)
    epsb = sb("epsb", [128, 1], F32)
    rowp = sb("rowp", [1, 128], F32)
    rowq = sb("rowq", [1, 128], F32)
    recipx = sb("recipx", [128, 1], F32)
    sxc127 = sb("sxc127", [128, 1], F32)
    rmxg = sb("rmxg", [128, 1], F32)
    rmxu = sb("rmxu", [128, 1], F32)
    tqg = sb("tqg", [128, 1], F32)
    tqu = sb("tqu", [128, 1], F32)
    rcg = sb("rcg", [128, 1], F32)
    rcu = sb("rcu", [128, 1], F32)
    reciph = sb("reciph", [128, 1], F32)
    shc127 = sb("shc127", [128, 1], F32)
    bz = sb("bz", [128, 8], F32)
    bh = sb("bh", [128, 8], F32)
    nbz = sb("nbz", [128, 8], F32)
    bg = sb("bg", [128, NDC], F32)

    HEAPW = 48000
    heap = sb("heap", [128, HEAPW], F32)
    hbase = nc.lookup_mloc(heap).addr

    def mkph(tag):
        cur = [0]
        def alloc(name, shape, dt):
            elt = 2 if dt == BF16 else 4
            n = 1
            for s_ in shape[1:]:
                n *= s_
            off = (cur[0] + 31) & ~31
            cur[0] = off + n * elt
            assert cur[0] <= HEAPW * 4, f"heap overflow {tag}/{name}: {cur[0]}"
            return nc.alloc_sbuf_tensor_at(name, shape, dt, offset=hbase + off)
        return alloc

    ha1 = mkph("L1")
    arA = ha1("arA", [128, 16384], F32)
    x0t = arA[:].rearrange("p (a d) -> p a d", d=D)
    arAb = nc.alloc_sbuf_tensor_at("arAb", [128, 32768], BF16,
                                   offset=nc.lookup_mloc(arA).addr)
    dlt = arAb[:, 0 * 8192:1 * 8192].rearrange("p (c t) -> p c t", t=TQ)
    ub = arAb[:, 1 * 8192:2 * 8192].rearrange("p (c t) -> p c t", t=TQ)
    slz = arAb[:, 2 * 8192:3 * 8192].rearrange("p (c t) -> p c t", t=TQ)
    xsb = arAb[:, 3 * 8192:4 * 8192].rearrange("p (c t) -> p c t", t=TQ)
    x1sA = arA[:].rearrange("p (a d) -> p a d", d=1024)
    x1st = [x1sA[:, 0, :], x1sA[:, 1, :]]
    x0st = [x1sA[:, 2, :], x1sA[:, 3, :]]
    sq4 = x1sA[:, 4, :]
    xn1s = x1sA[:, 5, :]
    arB = ha1("arB", [128, 16 * (TQ + 3)], BF16)
    xcb = arB[:].rearrange("p (c t) -> p c t", t=TQ + 3)
    ftmp = arB[0:64, 0:T]
    ftm2 = arB[0:64, T:2 * T]
    ofm = arB[:, 0:NDC * TQ].rearrange("p (a t) -> p a t", t=TQ)
    xn0 = ha1("xn0", [128, NDC, T], BF16)
    dA2 = [ha1(f"dA{i}", [128, L4], F32) for i in range(2)]
    dBp = ha1("dBp", [128, L4], BF16)
    hhp = ha1("hhp", [128, L4], BF16)
    tpp = ha1("tpp", [128, L4], BF16)
    dA0b = nc.alloc_sbuf_tensor_at("dA0b", [128, 4104], BF16,
                                   offset=nc.lookup_mloc(dA2[0]).addr)
    xtm2 = [dA0b[:, 0:D], dA0b[:, D:2 * D]]
    sq1 = dA2[1][:, 0:D]
    wsl = [ha1(f"wsl{i}", [128, 8, 512], BF16) for i in range(2)]
    wdt = [ha1(f"wdt{i}", [128, NCC, 256], BF16) for i in range(2)]
    wxp = ha1("wxp", [128, NCC, 32], BF16)
    wou = [ha1(f"wou{i}", [128, NCC, 128], BF16) for i in range(2)]
    fwin = ha1("fwin", [128, NDC, 64], BF16)
    gacc = ha1("gacc", [65, T], BF16)

    ha2 = mkph("L2")
    qx = ha2("qx", [128, NDC, T], BF16)
    xn1b = ha2("xn1b", [128, NDC, T], BF16)
    hcm = ha2("hcm", [128, 8, T], BF16)
    zab = ha2("zab", [128, 3 * T], BF16)
    zbuf = zab[:, 0:T]
    abuf = zab[:, T:2 * T]
    bbuf = zab[:, 2 * T:3 * T]
    aofm = nc.alloc_sbuf_tensor_at("aofm", [128, NDC, TQ], BF16,
                                   offset=nc.lookup_mloc(zab).addr)
    wsl2 = [ha2(f"wsl2_{i}", [128, 8, 512], BF16) for i in range(2)]
    wgsl = nc.alloc_sbuf_tensor_at("wgsl", [128, 8, D], BF16,
                                   offset=nc.lookup_mloc(wsl2[0]).addr)
    wqsl = [ha2(f"wqsl{i}", [128, D], F32) for i in range(2)]
    qs1 = ha2("qs1", [128, D], F32)
    qsi = ha2("qsi", [128, D], I32)
    qmk = ha2("qmk", [128, D], F32)
    sc1 = ha2("sc1", [128, D], F32)
    qg = ha2("qg", [128, D], BF16)
    qu = ha2("qu", [128, D], BF16)
    wqgT = ha2("wqgT", [128, NDC, 128], BF16)
    wquT = ha2("wquT", [128, NDC, 128], BF16)
    x1t = [ha2(f"x1t{i}", [128, D], F32) for i in range(2)]
    outb = [ha2(f"outb{i}", [128, D], F32) for i in range(2)]
    hbuf = [nc.alloc_sbuf_tensor_at(f"hbuf{i}", [128, T], BF16,
                                    offset=nc.lookup_mloc(x1t[i]).addr)
            for i in range(2)]
    scb = [ha2(f"scb{i}", [128, D], BF16) for i in range(2)]
    th = [ha2(f"th{i}", [128, 512], BF16) for i in range(2)]

    ha3 = mkph("L3")
    wdT = ha3("wdT", [128, NDC, NIC, 128], BF16)
    w3sl = [ha3(f"w3sl{i}", [128, FI], F32) for i in range(2)]
    hsl = ha3("hsl", [128, NIC, TQ], BF16)
    hq3 = ha3("hq3", [128, NIC, TQ], BF16)
    q3s1 = ha3("q3s1", [128, FI], F32)
    q3si = ha3("q3si", [128, FI], I32)
    q3mk = ha3("q3mk", [128, FI], F32)
    qd3 = ha3("qd3", [128, FI], BF16)
    dfm = ha3("dfm", [128, NDC, TQ], BF16)
    x3t = [ha3(f"x3t{i}", [128, D], F32) for i in range(2)]
    ob3 = [ha3(f"ob3_{i}", [128, D], F32) for i in range(2)]

    pb = [psal(f"pb{i}", [128, 512]) for i in range(5)]
    px = psal("px", [128, 512])
    tpx = es.enter_context(nc.psum_tensor("tpx", [128, 1024], BF16))
    tpy = es.enter_context(nc.psum_tensor("tpy", [128, 1024], BF16))

    b.sem("HW", 16)
    b.sem("GP", 16)
    b.sem("PE", 1)
    b.sem("AC", 1)
    b.sem("DV", 1)
    b.sem("GC", 1)
    b.sem("ST", 16)

    def gdma(out, in_):
        b.dma_fence("gpsimd", "GP")
        return b.op("gpsimd", lambda g, o=out, i=in_: g.dma_start(out=o, in_=i),
                    inc="GP")

    def hdma(out, in_, sem="HW"):
        b.dma_fence("sync", sem)
        return b.op("sync", lambda s, o=out, i=in_: s.dma_start(out=o, in_=i),
                    inc=sem)

    def gfn(f, inc=None):
        return b.op("gpsimd", f, inc=inc)

    def qbar():
        tgt = []
        m = b.op("vector", lambda e: e.engine_nop(), inc="DV")
        tgt.append(("DV", m))
        m = b.op("scalar", lambda e: e.activation(out=epsb[:], in_=epsb[:],
                                                  func=AF.Identity, bias=0.0,
                                                  scale=1.0), inc="AC")
        tgt.append(("AC", m))
        m = b.op("gpsimd", lambda e: e.engine_nop(), inc="GC")
        tgt.append(("GC", m))
        tgt.append(("PE", b.v("PE")))
        for eng in ("vector", "scalar", "gpsimd", "tensor", "sync"):
            for sem, m in tgt:
                b.wait(eng, sem, m)

    def mega_bar():
        qbar()
        for eng in ("vector", "scalar", "gpsimd", "tensor", "sync"):
            for semn in ("HW", "ST", "GP"):
                b.wait(eng, semn, b.v(semn))

    def dbg_dump(name, src):
        if not dbg:
            return
        b.wait("gpsimd", "AC", b.v("AC"))
        b.wait("gpsimd", "DV", b.v("DV"))
        mst = b.op("gpsimd", lambda g, o=dbg_d[name], i=src: g.dma_start(out=o, in_=i),
                   inc="GP")
        for eng in ("vector", "scalar", "tensor", "sync"):
            b.wait(eng, "GP", mst)

    gdma(Asb[:], Alog_d.rearrange("(c p) s -> p c s", p=128))
    gdma(Dsb[:], Dp_d[:, :])
    gdma(convb[:], convb_d[:, :])
    gdma(bdt[:], bdt_d[:, :])
    gdma(convw[:], convw_d.rearrange("(c p) k -> p c k", p=128))
    gdma(wxp[:], Wxp_d.rearrange("(c p) s -> p c s", p=128))
    gdma(fwin[:], fWin_d.rearrange("(c p) m -> p c m", p=128))
    gdma(sfw1[:], fW1_d[:, :])
    gdma(sfb1[:], fb1_d[:, :])
    gdma(sfw2[:], fW2_d[:, :])
    gdma(sfb2[:], fb2_d[:, :])
    gdma(sfbin[:], fbin_d[:, :])
    gdma(fwoc[0:64, :], fWout_d[:, :])
    params_ld = gdma(fwoc[64:65, :], fbout_d[:, :])
    x0_ld = hdma(x0t, x0_d.rearrange("(a p) d -> p a d", p=128))

    b.wait("gpsimd", "GP", params_ld)
    gfn(lambda g: g.memset(ident[:], 0.0))
    gfn(lambda g: g.affine_select(out=ident[:], in_=ident[:],
                                  compare_op=ALU.not_equal, fill=1.0, base=0,
                                  pattern=[[-1, 128]], channel_multiplier=1))
    gfn(lambda g: g.memset(ones1[:], 1.0))
    gfn(lambda g: g.memset(ones32[:], 1.0))
    gfn(lambda g: g.memset(identf[:], 0.0))
    gfn(lambda g: g.affine_select(out=identf[:], in_=identf[:],
                                  compare_op=ALU.not_equal, fill=1.0, base=0,
                                  pattern=[[-1, 32]], channel_multiplier=1))

    gfn(lambda g: g.memset(carry[:], 0.0))
    gfn(lambda g: g.memset(gacc[64:65, :], 1.0))
    gfn(lambda g: g.memset(statc[:], 0.0))
    gfn(lambda g: g.memset(marg[:], 0.0))
    gfn(lambda g: g.memset(epsb[:], 1e-5))
    gfn(lambda g: g.memset(dA2[0][:], 0.0))
    const_done = gfn(lambda g: g.engine_nop(), inc="GC")

    b.wait("scalar", "GP", params_ld)
    aexp = b.op("scalar", lambda sc: sc.activation(
        out=Asb[:], in_=Asb[:], func=AF.Exp, bias=0.0, scale=1.0), inc="AC")
    b.wait("vector", "AC", aexp)
    b.wait("vector", "GC", const_done)
    b.op("vector", lambda v: v.tensor_scalar(
        out=Asb[:], in0=Asb[:], scalar1=-1.0, scalar2=None, op0=ALU.mult),
        inc="DV")

    b.wait("scalar", "HW", x0_ld)
    b.wait("scalar", "GC", const_done)
    b.wait("vector", "HW", x0_ld)
    b.wait("vector", "GC", const_done)
    sq_red = {}
    for tt in range(NT):
        if tt > 0:
            b.wait("scalar", "DV", sq_red[tt - 1])
        m_sq = b.op("scalar", lambda sc, tt=tt: sc.activation(
            out=sq1, in_=x0t[:, tt, :], func=AF.Square, bias=0.0, scale=1.0),
            inc="AC")
        b.wait("vector", "AC", m_sq)
        sq_red[tt] = b.op("vector", lambda v, tt=tt: v.tensor_reduce(
            out=statc[:, 1, tt:tt + 1], in_=sq1, axis=AX, op=ALU.add), inc="DV")
        b.op("vector", lambda v, tt=tt: v.tensor_reduce(
            out=statc[:, 2, tt:tt + 1], in_=x0t[:, tt, :], axis=AX,
            op=ALU.add), nosync=True)
    m_syn = b.op("vector", lambda v: v.engine_nop(), inc="DV")
    b.wait("vector", "DV", m_syn)
    b.op("vector", lambda v: v.tensor_scalar(
        out=statc[:, 3, :], in0=statc[:, 2, :], scalar1=1.0 / D, scalar2=None,
        op0=ALU.mult))
    b.op("vector", lambda v: v.tensor_scalar(
        out=statc[:, 4, :], in0=statc[:, 1, :], scalar1=1.0 / D, scalar2=None,
        op0=ALU.mult))
    b.op("vector", lambda v: v.tensor_tensor(
        out=statc[:, 5, :], in0=statc[:, 3, :], in1=statc[:, 3, :], op=ALU.mult))
    m_var = b.op("vector", lambda v: v.tensor_tensor(
        out=statc[:, 5, :], in0=statc[:, 4, :], in1=statc[:, 5, :],
        op=ALU.subtract), inc="DV")
    b.wait("scalar", "DV", m_var)
    m_sd = b.op("scalar", lambda sc: sc.activation(
        out=statc[:, 6, :], in_=statc[:, 5, :], func=AF.Sqrt,
        bias=epsb[:, 0:1], scale=1.0), inc="AC")
    b.wait("vector", "AC", m_sd)
    b.op("vector", lambda v: v.reciprocal(out=statc[:, 7, :], in_=statc[:, 6, :]))
    b.op("vector", lambda v: v.tensor_tensor(
        out=statc[:, 0, :], in0=statc[:, 3, :], in1=statc[:, 7, :], op=ALU.mult))
    m_nb = b.op("vector", lambda v: v.tensor_scalar(
        out=statc[:, 0, :], in0=statc[:, 0, :], scalar1=-1.0, scalar2=None,
        op0=ALU.mult), inc="DV")
    b.wait("scalar", "DV", m_nb)
    ev1 = []
    for tt in range(NT):
        par = tt % 2
        m_xn = b.op("scalar", lambda sc, tt=tt, par=par: sc.activation(
            out=xtm2[par], in_=x0t[:, tt, :], func=AF.Identity,
            bias=statc[:, 0, tt:tt + 1], scale=statc[:, 7, tt:tt + 1]), inc="AC")
        pst = tpx if par == 0 else tpy
        b.wait("tensor", "AC", m_xn)
        if tt >= 2:
            b.wait("tensor", "AC", ev1[tt - 2])
        for dc in range(NDC):
            m_tr = b.op("tensor", lambda t, dc=dc, par=par, pst=pst: t.transpose(
                pst[:, dc * 128:(dc + 1) * 128],
                xtm2[par][:, dc * 128:(dc + 1) * 128], ident[:]),
                inc="PE" if dc == NDC - 1 else None)
        b.wait("scalar", "PE", m_tr)
        m_ev = b.op("scalar", lambda sc, tt=tt, pst=pst: sc.activation(
            out=xn0[:, :, tt * 128:(tt + 1) * 128],
            in_=pst[:, 0:1024], func=AF.Identity, bias=0.0,
            scale=1.0), inc="AC")
        ev1.append(m_ev)
    xn0_done = b.v("AC")
    dbg_dump("stat", statc[:])

    b.wait("gpsimd", "AC", xn0_done)
    gfn(lambda g: g.memset(dA2[0][:], 0.0))
    gfn(lambda g: g.memset(dBp[:], 0.0))
    gfn(lambda g: g.memset(hhp[:], 0.0))
    gfn(lambda g: g.memset(tpp[:], 0.0))
    bndz = gfn(lambda g: g.memset(dA2[1][:], 0.0), inc="GC")
    b.wait("scalar", "GC", bndz)
    b.wait("vector", "GC", bndz)

    b.wait("tensor", "AC", xn0_done)
    for t4 in range(4):
        for k in range(NDC):
            m_mm = b.op("tensor", lambda t, t4=t4, k=k: t.matmul(
                pb[t4][0:64, :], fwin[:, k, 0:64],
                xn0[:, k, t4 * 512:(t4 + 1) * 512],
                start=(k == 0), stop=(k == NDC - 1)),
                inc="PE" if k == NDC - 1 else None)
        b.wait("scalar", "PE", m_mm)
        b.op("scalar", lambda sc, t4=t4: sc.activation(
            out=gacc[0:64, t4 * 512:(t4 + 1) * 512], in_=pb[t4][0:64, :],
            func=AF.Identity, bias=sfbin[:, 0:1], scale=1.0),
            inc="AC" if t4 == 3 else None)
    spre_done = b.v("AC")

    b.wait("scalar", "AC", spre_done)
    for j in range(16):
        m_rl = b.op("scalar", lambda sc, j=j: sc.activation(
            out=ftmp, in_=gacc[0:64, :], func=AF.Relu,
            bias=sfb1[:, j:j + 1], scale=sfw1[:, j:j + 1]), inc="AC")
        b.wait("vector", "AC", m_rl)
        if j == 0:
            m_g = b.op("vector", lambda v: v.tensor_scalar(
                out=ftm2, in0=ftmp, scalar1=sfw2[:, 0:1], scalar2=None,
                op0=ALU.mult), inc="DV")
        else:
            b.op("vector", lambda v, j=j: v.tensor_scalar(
                out=ftmp, in0=ftmp, scalar1=sfw2[:, j:j + 1], scalar2=None,
                op0=ALU.mult))
            m_g = b.op("vector", lambda v: v.tensor_tensor(
                out=ftm2, in0=ftm2, in1=ftmp, op=ALU.add), inc="DV")
        if j < 15:
            b.wait("scalar", "DV", m_g)
    b.wait("scalar", "DV", b.v("DV"))
    gacc_done = b.op("scalar", lambda sc: sc.activation(
        out=gacc[0:64, :], in_=ftm2, func=AF.Identity, bias=sfb2[:, 0:1],
        scale=1.0), inc="AC")

    b.wait("gpsimd", "DV", b.v("DV"))
    b.wait("gpsimd", "AC", b.v("AC"))
    z2 = gfn(lambda g: g.memset(statc[:], 0.0), inc="GC")
    b.wait("vector", "GC", z2)

    x1_asm = {}
    sq4_red = {}
    x1_st_mark = {}
    x1_sq_mark = {}
    xz_marks = []
    dt_marks = []
    ou_marks = []
    seg = [j * (TQ + 1) for j in range(4)]
    bnd = [j * (TQ + 1) + TQ for j in range(3)]

    def v3(ap):
        return ap.rearrange("p t -> p 1 t")

    for q in range(NQ):
        t0 = q * TQ
        for g8 in range(8):
            buf = g8 % 2
            if len(xz_marks) >= 2:
                b.wait("gpsimd", "PE", xz_marks[-2])
            mark_w = gdma(wsl[buf][:], rkpm(Win_d)[:, :, g8 * 512:(g8 + 1) * 512])
            b.wait("tensor", "GP", mark_w)
            if q == 0 and g8 == 0:
                b.wait("tensor", "AC", gacc_done)
            b.wait("tensor", "AC", b.v("AC"))
            for k in range(NDC):
                for m in range(4):
                    m_mm = b.op("tensor", lambda t, buf=buf, k=k, m=m, t0=t0: t.matmul(
                        pb[m][:], wsl[buf][:, k, m * 128:(m + 1) * 128],
                        xn0[:, k, t0:t0 + TQ],
                        start=(k == 0), stop=(k == NDC - 1)),
                        inc="PE" if (k == NDC - 1 and m == 3) else None)
            xz_marks.append(m_mm)
            b.wait("scalar", "PE", m_mm)
            for m in range(4):
                if g8 < 4:
                    cc = g8 * 4 + m
                    b.op("scalar", lambda sc, cc=cc, m=m: sc.activation(
                        out=xcb[:, cc, 3:3 + TQ], in_=pb[m][:], func=AF.Identity,
                        bias=0.0, scale=1.0), inc="AC" if m == 3 else None)
                else:
                    cc = (g8 - 4) * 4 + m
                    b.op("scalar", lambda sc, cc=cc, m=m: sc.activation(
                        out=slz[:, cc, :], in_=pb[m][:], func=AF.Silu,
                        bias=0.0, scale=1.0), inc="AC" if m == 3 else None)
        xz_done = b.v("AC")
        if q == dbgq:
            dbg_dump("xc", xcb[:])
            dbg_dump("slz", slz[:])

        if q > 0:
            b.wait("gpsimd", "AC", xz_done)
            marg_restored = gfn(lambda g: g.tensor_copy(xcb[:, :, 0:3], marg[:]),
                                inc="GC")

        if q == 0:
            b.wait("gpsimd", "AC", gacc_done)
            marg_restored = gfn(lambda g: g.memset(xcb[:, :, 0:3], 0.0), inc="GC")
        b.wait("tensor", "AC", xz_done)
        b.wait("tensor", "GC", marg_restored)
        conv_ev = {}
        for cc in range(NCC):
            dpar = cc % 2
            bank = 3 + (cc % 2)
            if cc >= 2:
                b.wait("vector", "PE", conv_pe[cc - 2])
            b.wait("vector", "GC", const_done)
            for k in range(4):
                m_d4 = b.op("vector", lambda v, cc=cc, k=k, dpar=dpar: v.tensor_scalar(
                    out=diag4[dpar][:, k, :], in0=ident[:],
                    scalar1=convw[:, cc, k:k + 1], scalar2=None, op0=ALU.mult),
                    inc="DV" if k == 3 else None, nosync=(k != 3))
            b.wait("tensor", "DV", m_d4)
            if cc >= 2:
                b.wait("tensor", "AC", conv_ev[cc - 2])
            for k in range(4):
                m_mm = b.op("tensor", lambda t, cc=cc, k=k, dpar=dpar, bank=bank: t.matmul(
                    pb[bank][:], diag4[dpar][:, k, :],
                    xcb[:, cc, k:k + TQ], start=(k == 0), stop=(k == 3)),
                    inc="PE" if k == 3 else None)
            if cc == 0:
                conv_pe = {}
            conv_pe[cc] = m_mm
            b.wait("scalar", "PE", m_mm)
            m_e = b.op("scalar", lambda sc, cc=cc, bank=bank: sc.activation(
                out=xsb[:, cc, :], in_=pb[bank][:], func=AF.Silu,
                bias=convb[:, cc:cc + 1], scale=1.0), inc="AC")
            conv_ev[cc] = m_e
        conv_done = b.v("AC")
        if q == dbgq:
            dbg_dump("xs", xsb[:])
        if q < NQ - 1:
            b.wait("gpsimd", "AC", conv_done)
            gfn(lambda g: g.tensor_copy(marg[:], xcb[:, :, TQ:TQ + 3]), inc="GC")

        b.wait("tensor", "DV", b.v("DV"))
        b.wait("scalar", "ST", b.v("ST"))
        for g4 in range(8):
            pass
        for g2 in range(8):
            buf = g2 % 2
            if len(dt_marks) >= 2:
                b.wait("gpsimd", "PE", dt_marks[-2])
            mark_w = gdma(wdt[buf][:], rkpm(Wdt_d)[:, :, g2 * 256:(g2 + 1) * 256])
            b.wait("tensor", "GP", mark_w)
            b.wait("tensor", "AC", b.v("AC"))
            for k in range(NCC):
                for m in range(2):
                    m_mm = b.op("tensor", lambda t, buf=buf, k=k, m=m: t.matmul(
                        pb[m][:], wdt[buf][:, k, m * 128:(m + 1) * 128],
                        xsb[:, k, :], start=(k == 0), stop=(k == NCC - 1)),
                        inc="PE" if (k == NCC - 1 and m == 1) else None)
            dt_marks.append(m_mm)
            b.wait("scalar", "PE", m_mm)
            for m in range(2):
                cc = g2 * 2 + m
                b.op("scalar", lambda sc, cc=cc, m=m: sc.activation(
                    out=tpp[:, 0:TQ], in_=pb[m][:], func=AF.Exp,
                    bias=bdt[:, cc:cc + 1], scale=1.0))
                b.op("scalar", lambda sc, cc=cc: sc.activation(
                    out=dlt[:, cc, :], in_=tpp[:, 0:TQ], func=AF.Ln,
                    bias=1.0, scale=1.0), inc="AC" if m == 1 else None)
        dlt_done = b.v("AC")
        if q == dbgq:
            dbg_dump("dlt", dlt[:])

        b.wait("tensor", "AC", b.v("AC"))
        for k in range(NCC):
            m_mm = b.op("tensor", lambda t, k=k: t.matmul(
                pb[2][0:32, :], wxp[:, k, :], xsb[:, k, :],
                start=(k == 0), stop=(k == NCC - 1)),
                inc="PE" if k == NCC - 1 else None)
        xproj_pe = m_mm
        b.wait("scalar", "PE", xproj_pe)
        bcrows_done = b.op("scalar", lambda sc: sc.activation(
            out=bcrows[:, :], in_=pb[2][0:32, :], func=AF.Identity, bias=0.0,
            scale=1.0), inc="AC")

        b.wait("vector", "AC", dlt_done)
        b.op("vector", lambda v: v.tensor_tensor(
            out=ub[:, :, :], in0=xsb[:, :, :], in1=dlt[:, :, :],
            op=ALU.mult))
        b.wait("vector", "PE", xproj_pe)
        for cc in range(NCC):
            m_yi = b.op("vector", lambda v, cc=cc: v.tensor_scalar(
                out=xsb[:, cc, :], in0=xsb[:, cc, :], scalar1=Dsb[:, cc:cc + 1],
                scalar2=None, op0=ALU.mult),
                inc="DV" if cc == NCC - 1 else None, nosync=(cc != NCC - 1))
        yinit_done = m_yi
        if q == dbgq:
            dbg_dump("ub", ub[:])
            dbg_dump("bc", bcrows[:])

        b.wait("tensor", "AC", bcrows_done)
        if q > 0:
            b.wait("tensor", "DV", x1_asm[q * 4 - 1])
        for s in range(NS):
            pp = s % 2
            eB, eC = esel[pp * 2], esel[pp * 2 + 1]
            b.wait("vector", "PE", b.v("PE"))
            b.op("vector", lambda v, eB=eB, s=s: v.tensor_scalar(
                out=eB[:], in0=ones32[:], scalar1=identf[0:32, s:s + 1],
                scalar2=None, op0=ALU.mult))
            m_es = b.op("vector", lambda v, eC=eC, s=s: v.tensor_scalar(
                out=eC[:], in0=ones32[:], scalar1=identf[0:32, 16 + s:17 + s],
                scalar2=None, op0=ALU.mult), inc="DV", nosync=True)
            b.wait("tensor", "DV", m_es)
            if s > 0:
                b.wait("tensor", "AC", bc_ev)
                b.wait("tensor", "DV", last_y)
            m_pe1 = b.op("tensor", lambda t, eB=eB: t.matmul(
                pb[3][:], eB[:, :], bcrows[:, :], start=True, stop=True))
            m_pe2 = b.op("tensor", lambda t, eC=eC: t.matmul(
                px[:], eC[:, :], bcrows[:, :], start=True, stop=True), inc="PE")
            b.wait("scalar", "PE", m_pe2)
            if s > 0:
                b.wait("scalar", "DV", last_y)
            b.op("scalar", lambda sc: sc.activation(
                out=bbt[:, :], in_=pb[3][:], func=AF.Identity, bias=0.0, scale=1.0))
            bc_ev = b.op("scalar", lambda sc: sc.activation(
                out=cbt[:, :], in_=px[:], func=AF.Identity, bias=0.0, scale=1.0),
                inc="AC")
            for p4 in range(4):
                i = s * 4 + p4
                par = i % 2
                ccs = [p4 * 4 + j for j in range(4)]
                if i >= 2:
                    b.wait("scalar", "DV", scan_done[i - 2])
                for j, cc in enumerate(ccs):
                    m_dA = b.op("scalar", lambda sc, cc=cc, j=j, s=s, par=par: sc.activation(
                        out=dA2[par][:, seg[j]:seg[j] + TQ], in_=dlt[:, cc, :],
                        func=AF.Identity, bias=1.0, scale=Asb[:, cc, s:s + 1]),
                        inc="AC" if j == 3 else None)
                b.wait("vector", "AC", bc_ev)
                if i == 0:
                    b.wait("vector", "DV", yinit_done)
                b.op("vector", lambda v, p4=p4: v.tensor_tensor(
                    out=dBp[:].rearrange("p (a t) -> p a t", t=TQ + 1)[:, :, 0:TQ],
                    in0=ub[:, p4 * 4:p4 * 4 + 4, :],
                    in1=bbt[:].rearrange("p (o t) -> p o t", o=1).broadcast_to([128, 4, TQ]),
                    op=ALU.mult), nosync=(p4 != 0))
                ci = s * NCC + p4 * 4
                b.op("vector", lambda v, ci=ci: v.tensor_copy(
                    dBp[:].rearrange("p (a t) -> p a t", t=TQ + 1)[:, 0:3, TQ:TQ + 1],
                    carry[:, ci + 1:ci + 4].rearrange("p (c o) -> p c o", o=1)),
                    nosync=True)
                b.wait("vector", "AC", m_dA)
                m_scan = b.op("vector", lambda v, par=par, ci=ci: v.tensor_tensor_scan(
                    out=hhp[:], data0=dA2[par][:], data1=dBp[:],
                    initial=carry[:, ci:ci + 1], op0=ALU.mult, op1=ALU.add),
                    nosync=True)
                b.op("vector", lambda v: v.tensor_tensor(
                    out=tpp[:].rearrange("p (a t) -> p a t", t=TQ + 1)[:, :, 0:TQ],
                    in0=hhp[:].rearrange("p (a t) -> p a t", t=TQ + 1)[:, :, 0:TQ],
                    in1=cbt[:].rearrange("p (o t) -> p o t", o=1).broadcast_to([128, 4, TQ]),
                    op=ALU.mult), nosync=True)
                b.op("vector", lambda v, p4=p4: v.tensor_tensor(
                    out=xsb[:, p4 * 4:p4 * 4 + 4, :],
                    in0=xsb[:, p4 * 4:p4 * 4 + 4, :],
                    in1=tpp[:].rearrange("p (a t) -> p a t", t=TQ + 1)[:, :, 0:TQ],
                    op=ALU.add), nosync=True)
                m_y = b.op("vector", lambda v, ci=ci: v.tensor_copy(
                    carry[:, ci:ci + 4].rearrange("p (c o) -> p c o", o=1),
                    hhp[:].rearrange("p (a t) -> p a t", t=TQ + 1)[:, :, TQ - 1:TQ]),
                    inc="DV")
                if i == 0 and s == 0:
                    scan_done = {}
                scan_done[i] = m_y
            last_y = scan_done[s * 4 + 3]
        sloop_done = b.v("DV")

        m_ys = b.op("vector", lambda v: v.tensor_tensor(
            out=xsb[:, :, :], in0=xsb[:, :, :], in1=slz[:, :, :],
            op=ALU.mult), inc="DV")
        y_done = m_ys

        qbar()
        if q == dbgq:
            dbg_dump("y", xsb[:])
            dbg_dump("carry", carry[:])
        b.wait("tensor", "DV", y_done)
        for m in range(NDC):
            buf = m % 2
            if len(ou_marks) >= 2:
                b.wait("gpsimd", "PE", ou_marks[-2])
            mark_w = gdma(wou[buf][:], rkpm(Wout_d)[:, :, m * 128:(m + 1) * 128])
            b.wait("tensor", "GP", mark_w)
            bank = m % 4
            b.wait("tensor", "AC", b.v("AC"))
            for k in range(NCC):
                m_mm = b.op("tensor", lambda t, buf=buf, k=k, bank=bank: t.matmul(
                    pb[bank][:], wou[buf][:, k, :], xsb[:, k, :],
                    start=(k == 0), stop=(k == NCC - 1)),
                    inc="PE" if k == NCC - 1 else None)
            ou_marks.append(m_mm)
            if m > 0:
                b.wait("tensor", "DV", ffn_add)
            m_mf = b.op("tensor", lambda t, m=m, t0=t0: t.matmul(
                px[:], fwoc[:, m * 128:(m + 1) * 128],
                gacc[:, t0:t0 + TQ], start=True, stop=True), inc="PE")
            b.wait("scalar", "PE", m_mm)
            m_ev = b.op("scalar", lambda sc, m=m, bank=bank: sc.activation(
                out=ofm[:, m, :], in_=pb[bank][:], func=AF.Identity, bias=0.0,
                scale=1.0), inc="AC")
            b.wait("vector", "PE", m_mf)
            b.wait("vector", "AC", m_ev)
            ffn_add = b.op("vector", lambda v, m=m: v.tensor_tensor(
                out=ofm[:, m, :], in0=ofm[:, m, :], in1=px[:], op=ALU.add),
                inc="DV", nosync=True)
        out_fm = b.v("AC")
        out_fm_dv = ffn_add
        if q == dbgq:
            dbg_dump("ofm", ofm[:])

        for sub in range(4):
            tt = q * 4 + sub
            par = sub % 2
            pst = tpx if par == 0 else tpy
            if tt >= 2:
                b.wait("sync", "DV", x1_asm[tt - 2])
            m_x0 = hdma(x0st[par], x0_d[tt * 128:(tt + 1) * 128, :])
            b.wait("tensor", "AC", out_fm)
            b.wait("tensor", "DV", out_fm_dv)
            if tt >= 2:
                b.wait("tensor", "DV", x1_asm[tt - 2])
            for dc in range(NDC):
                m_tr = b.op("tensor", lambda t, dc=dc, sub=sub, pst=pst: t.transpose(
                    pst[:, dc * 128:(dc + 1) * 128],
                    ofm[:, dc, sub * 128:(sub + 1) * 128], ident[:]),
                    inc="PE" if dc == NDC - 1 else None)
            b.wait("vector", "PE", m_tr)
            b.wait("vector", "HW", m_x0)
            if tt >= 2:
                b.wait("vector", "ST", b.v("ST"))
                b.wait("vector", "AC", x1_sq_mark[tt - 2])
            m_x1 = b.op("vector", lambda v, par=par, pst=pst: v.tensor_tensor(
                out=x1st[par], in0=pst[:, 0:1024],
                in1=x0st[par], op=ALU.add), inc="DV")
            x1_asm[tt] = m_x1
            b.wait("sync", "DV", m_x1)
            x1_st_mark[tt] = hdma(x1i_d[tt * 128:(tt + 1) * 128, :], x1st[par],
                                  sem="ST")
            b.wait("scalar", "DV", m_x1)
            if tt > 0:
                b.wait("scalar", "DV", sq4_red[tt - 1])
            m_sq = b.op("scalar", lambda sc, par=par, tt=tt: sc.activation(
                out=sq4, in_=x1st[par], func=AF.Square, bias=0.0, scale=1.0),
                inc="AC")
            x1_sq_mark[tt] = m_sq
            b.wait("vector", "AC", m_sq)
            sq4_red[tt] = b.op("vector", lambda v, tt=tt: v.tensor_reduce(
                out=stat2[:, 1, tt:tt + 1], in_=sq4, axis=AX, op=ALU.add), inc="DV")
            b.op("vector", lambda v, par=par, tt=tt: v.tensor_reduce(
                out=stat2[:, 2, tt:tt + 1], in_=x1st[par], axis=AX,
                op=ALU.add), nosync=True)
            b.op("vector", lambda v, par=par, tt=tt: v.tensor_reduce(
                out=stat2[:, 3, tt:tt + 1], in_=x1st[par], axis=AX,
                op=ALU.max), nosync=True)
            b.op("vector", lambda v, par=par, tt=tt: v.tensor_reduce(
                out=stat2[:, 4, tt:tt + 1], in_=x1st[par], axis=AX,
                op=ALU.min), nosync=True)

    if dbg:
        dbg_dump("xn0", xn0[:])
        dbg_dump("gacc", gacc[:])
    m_syn2 = b.op("vector", lambda v: v.engine_nop(), inc="DV")
    b.wait("vector", "DV", m_syn2)
    b.op("vector", lambda v: v.tensor_scalar(
        out=stat2[:, 5, :], in0=stat2[:, 2, :], scalar1=1.0 / D, scalar2=None,
        op0=ALU.mult))
    b.op("vector", lambda v: v.tensor_scalar(
        out=stat2[:, 6, :], in0=stat2[:, 1, :], scalar1=1.0 / D, scalar2=None,
        op0=ALU.mult))
    b.op("vector", lambda v: v.tensor_tensor(
        out=stat2[:, 7, :], in0=stat2[:, 5, :], in1=stat2[:, 5, :], op=ALU.mult))
    m_var2 = b.op("vector", lambda v: v.tensor_tensor(
        out=stat2[:, 7, :], in0=stat2[:, 6, :], in1=stat2[:, 7, :],
        op=ALU.subtract), inc="DV")
    b.wait("scalar", "DV", m_var2)
    m_sd2 = b.op("scalar", lambda sc: sc.activation(
        out=stat2[:, 7, :], in_=stat2[:, 7, :], func=AF.Sqrt,
        bias=epsb[:, 0:1], scale=1.0), inc="AC")
    b.wait("vector", "AC", m_sd2)
    b.op("vector", lambda v: v.reciprocal(out=stat2[:, 6, :], in_=stat2[:, 7, :]))
    b.op("vector", lambda v: v.tensor_tensor(
        out=stat2[:, 3, :], in0=stat2[:, 3, :], in1=stat2[:, 5, :],
        op=ALU.subtract))
    b.op("vector", lambda v: v.tensor_tensor(
        out=stat2[:, 4, :], in0=stat2[:, 5, :], in1=stat2[:, 4, :],
        op=ALU.subtract))
    b.op("vector", lambda v: v.tensor_tensor(
        out=stat2[:, 3, :], in0=stat2[:, 3, :], in1=stat2[:, 4, :], op=ALU.max))
    m_z = b.op("vector", lambda v: v.tensor_tensor(
        out=stat2[:, 4, :], in0=stat2[:, 3, :], in1=stat2[:, 6, :],
        op=ALU.mult), inc="DV")
    b.wait("vector", "DV", m_z)
    m_m1c = b.op("vector", lambda v: v.tensor_reduce(
        out=stat2[:, 0, 0:1], in_=stat2[:, 4, 0:NT], axis=AX, op=ALU.max),
        inc="DV")
    b.wait("gpsimd", "DV", m_m1c)
    m_m1 = gfn(lambda g: g.tensor_reduce(out=m1c[:], in_=stat2[:, 0, 0:1],
                                         axis=AXC, op=ALU.max), inc="GC")
    if dbg:
        dbg_dump("stat2", stat2[:])

    gfn(lambda g: g.memset(panel[:], 0.0))
    gfn(lambda g: g.memset(rowp[:], 0.0))
    gfn(lambda g: g.memset(rowq[:], 0.0))
    const_done2 = gfn(lambda g: g.memset(onesf[:], 1.0), inc="GC")
    b.wait("gpsimd", "GC", m_m1)
    mc1 = gdma(c1i_t.ap()[:, :], m1c[:])
    b.wait("gpsimd", "GP", mc1)
    m_cc1 = b.op("gpsimd", lambda g: g.collective_compute(
        "AllReduce", ALU.max, replica_groups=[list(range(8))],
        ins=[c1i_t.ap().opt()], outs=[c1o_t.ap().opt()]), inc="GC")
    b.wait("gpsimd", "GC", m_cc1)
    gdma(rowp[0:1, 8:9], c1o_t.ap()[:, :])
    mega_bar()

    gdma(rowq[0:1, 0:1], clipx_d[:, :])
    gdma(bz[:], bz_d[:, :])
    gdma(bh[:], bh_d[:, :])
    gdma(bg[:], bg_d[:, :])
    loads2_done = b.v("GP")

    b.wait("vector", "GP", loads2_done)
    b.wait("vector", "GC", const_done2)
    b.wait("tensor", "GC", const_done2)
    b.op("vector", lambda v: v.tensor_copy(panel[:, 0, 0:NT], statc[:, 5, 0:NT]))
    b.op("vector", lambda v: v.tensor_copy(panel[:, 1, 0:NT], statc[:, 6, 0:NT]))
    b.op("vector", lambda v: v.tensor_scalar(
        out=rowp[0:1, 16:32], in0=onesf[0:1, 0:16], scalar1=rowp[0:1, 8:9],
        scalar2=None, op0=ALU.mult))
    m_rq = b.op("vector", lambda v: v.tensor_scalar(
        out=rowq[0:1, 16:32], in0=onesf[0:1, 0:16], scalar1=rowq[0:1, 0:1],
        scalar2=None, op0=ALU.mult))
    b.wait("scalar", "DV", m_rq)
    m_sg = b.op("scalar", lambda sc: sc.activation(
        out=rowq[0:1, 32:48], in_=rowq[0:1, 16:32], func=AF.Sigmoid, bias=0.0,
        scale=1.0))
    b.wait("vector", "AC", m_sg)
    b.op("vector", lambda v: v.tensor_tensor(
        out=rowp[0:1, 32:48], in0=rowp[0:1, 16:32], in1=rowq[0:1, 32:48],
        op=ALU.mult))
    b.op("vector", lambda v: v.tensor_scalar(
        out=rowp[0:1, 48:64], in0=rowp[0:1, 32:48], scalar1=1.0 / 127.0,
        scalar2=None, op0=ALU.mult))
    b.op("vector", lambda v: v.reciprocal(
        out=rowp[0:1, 64:80], in_=rowp[0:1, 48:64]))
    m_row = b.op("vector", lambda v: v.tensor_scalar(
        out=rowp[0:1, 80:96], in0=rowp[0:1, 48:64], scalar1=1.0 / 127.0,
        scalar2=None, op0=ALU.mult))
    b.wait("tensor", "DV", m_row)
    b.op("tensor", lambda t: t.matmul(
        px[:, 0:1], onesf[:, :], rowp[0:1, 64:65], start=True, stop=True))
    m_bc = b.op("tensor", lambda t: t.matmul(
        px[:, 1:2], onesf[:, :], rowp[0:1, 80:81], start=True, stop=True),
        inc="PE")
    b.wait("vector", "PE", m_bc)
    b.op("vector", lambda v: v.tensor_copy(recipx[:], px[:, 0:1]))
    m_bcv = b.op("vector", lambda v: v.tensor_copy(sxc127[:], px[:, 1:2]))
    b.op("vector", lambda v: v.tensor_tensor(
        out=panel[:, 2, :], in0=panel[:, 0, :], in1=panel[:, 1, :],
        op=ALU.mult))
    b.op("vector", lambda v: v.tensor_scalar(
        out=panel[:, 2, :], in0=panel[:, 2, :], scalar1=-1.0, scalar2=None,
        op0=ALU.mult))
    m_nbz = b.op("vector", lambda v: v.tensor_scalar(
        out=nbz[:], in0=bz[:], scalar1=-1.0, scalar2=None, op0=ALU.mult))

    b.wait("scalar", "DV", m_nbz)
    xn_mark = {}
    q_mark = {}
    tr_mark = {}
    ev_mark = {}
    for tt in range(NT):
        par = tt % 2
        if tt >= 2:
            b.wait("sync", "AC", xn_mark[tt - 2])
        m_x1 = hdma(x1t[par][:], x1i_d[tt * 128:(tt + 1) * 128, :])
        b.wait("scalar", "HW", m_x1)
        if tt >= 1:
            b.wait("scalar", "DV", q_mark[tt - 1])
        m_xn = b.op("scalar", lambda sc, par=par, tt=tt: sc.activation(
            out=sc1[:], in_=x1t[par][:], func=AF.Identity,
            bias=panel[:, 2, tt:tt + 1], scale=panel[:, 1, tt:tt + 1]))
        xn_mark[tt] = m_xn
        if tt >= 1:
            b.wait("scalar", "PE", tr_mark[tt - 1])
        m_cb = b.op("scalar", lambda sc: sc.activation(
            out=scb[0][:], in_=sc1[:], func=AF.Identity, bias=0.0, scale=1.0))
        b.wait("vector", "AC", m_cb)
        if tt == 0:
            b.wait("vector", "DV", m_bcv)
        b.op("vector", lambda v: v.tensor_scalar(
            out=qs1[:], in0=sc1[:], scalar1=recipx[:, 0:1], scalar2=RB + 0.5,
            op0=ALU.mult, op1=ALU.add))
        b.op("vector", lambda v: v.tensor_copy(qsi[:], qs1[:]), nosync=True)
        b.op("vector", lambda v: v.tensor_tensor(
            out=qmk[:], in0=qsi[:], in1=qs1[:], op=ALU.is_gt), nosync=True)
        b.op("vector", lambda v: v.tensor_tensor(
            out=qs1[:], in0=qsi[:], in1=qmk[:], op=ALU.subtract), nosync=True)
        b.op("vector", lambda v: v.tensor_scalar(
            out=qs1[:], in0=qs1[:], scalar1=-RB, scalar2=127.0, op0=ALU.add,
            op1=ALU.min), nosync=True)
        m_q = b.op("vector", lambda v: v.tensor_scalar(
            out=scb[1][:], in0=qs1[:], scalar1=-128.0, scalar2=None,
            op0=ALU.max), inc="DV")
        q_mark[tt] = m_q
        b.wait("tensor", "AC", m_cb)
        b.wait("tensor", "DV", m_q)
        if tt >= 1:
            b.wait("tensor", "AC", ev_mark[tt - 1][0])
            b.wait("tensor", "DV", ev_mark[tt - 1][1])
        for dc in range(NDC):
            b.op("tensor", lambda t, dc=dc: t.transpose(
                tpx[:, dc * 128:(dc + 1) * 128],
                scb[0][:, dc * 128:(dc + 1) * 128], ident[:]))
            m_tr = b.op("tensor", lambda t, dc=dc: t.transpose(
                tpy[:, dc * 128:(dc + 1) * 128],
                scb[1][:, dc * 128:(dc + 1) * 128], ident[:]),
                inc="PE" if dc == NDC - 1 else None)
        tr_mark[tt] = m_tr
        b.wait("scalar", "PE", m_tr)
        b.wait("vector", "PE", m_tr)
        m_e1 = b.op("scalar", lambda sc, tt=tt: sc.activation(
            out=xn1b[:, :, tt * 128:(tt + 1) * 128], in_=tpx[:, 0:1024],
            func=AF.Identity, bias=0.0, scale=1.0))
        m_e2 = b.op("vector", lambda v, tt=tt: v.tensor_copy(
            qx[:, :, tt * 128:(tt + 1) * 128], tpy[:, 0:1024]), inc="DV",
            nosync=True)
        ev_mark[tt] = (m_e1, m_e2)
    p2_ac = b.v("AC")
    p2_dv = b.v("DV")

    scan_mark = {}
    for p in range(2):
        if p == 1:
            b.wait("gpsimd", "PE", b.v("PE"))
        mz = gdma(wsl2[0][:], rkpm(Wz_d[p]))
        mh = gdma(wsl2[1][:], rkpm(Wh_d[p]))
        b.wait("tensor", "GP", mh)
        if p == 0:
            b.wait("tensor", "AC", p2_ac)
            b.wait("tensor", "DV", p2_dv)
        for m in range(4):
            fc = p * 4 + m
            b.wait("tensor", "AC", b.v("AC"))
            b.wait("tensor", "DV", b.v("DV"))
            for k in range(NDC):
                for t4 in range(4):
                    m_mmz = b.op("tensor", lambda t, k=k, m=m, t4=t4: t.matmul(
                        pb[t4][:], wsl2[0][:, k, m * 128:(m + 1) * 128],
                        xn1b[:, k, t4 * 512:(t4 + 1) * 512],
                        start=(k == 0), stop=(k == NDC - 1)),
                        inc="PE" if (k == NDC - 1 and t4 == 3) else None)
            b.wait("scalar", "PE", m_mmz)
            if fc >= 1:
                b.wait("scalar", "DV", scan_mark[fc - 1])
            for t4 in range(4):
                b.op("scalar", lambda sc, t4=t4, fc=fc: sc.activation(
                    out=zbuf[:, t4 * 512:(t4 + 1) * 512], in_=pb[t4][:],
                    func=AF.Sigmoid, bias=bz[:, fc:fc + 1], scale=1.0))
                m_za = b.op("scalar", lambda sc, t4=t4, fc=fc: sc.activation(
                    out=abuf[:, t4 * 512:(t4 + 1) * 512], in_=pb[t4][:],
                    func=AF.Sigmoid, bias=nbz[:, fc:fc + 1], scale=-1.0))
            b.wait("tensor", "AC", m_za)
            for k in range(NDC):
                for t4 in range(4):
                    m_mmh = b.op("tensor", lambda t, k=k, m=m, t4=t4: t.matmul(
                        pb[t4][:], wsl2[1][:, k, m * 128:(m + 1) * 128],
                        xn1b[:, k, t4 * 512:(t4 + 1) * 512],
                        start=(k == 0), stop=(k == NDC - 1)),
                        inc="PE" if (k == NDC - 1 and t4 == 3) else None)
            b.wait("vector", "PE", m_mmh)
            b.wait("vector", "AC", m_za)
            for t4 in range(4):
                b.op("vector", lambda v, t4=t4, fc=fc: v.scalar_tensor_tensor(
                    out=bbuf[:, t4 * 512:(t4 + 1) * 512], in0=pb[t4][:],
                    scalar=bh[:, fc:fc + 1],
                    in1=zbuf[:, t4 * 512:(t4 + 1) * 512],
                    op0=ALU.add, op1=ALU.mult), nosync=(t4 != 0))
            m_scan = b.op("vector", lambda v, fc=fc: v.tensor_tensor_scan(
                out=hcm[:, fc, :], data0=abuf[:], data1=bbuf[:],
                initial=0.0, op0=ALU.mult, op1=ALU.add), inc="DV",
                nosync=True)
            scan_mark[fc] = m_scan
    attn_dv = b.v("DV")

    aofm_mark = {}
    th_mark = {}
    outb_mark = {}
    asm_tr = {}
    b.wait("gpsimd", "PE", b.v("PE"))
    m_wg = gdma(wgsl[:], rkpm(Wg_d))
    for t4 in range(4):
        b.wait("tensor", "GP", m_wg)
        if t4 == 0:
            b.wait("tensor", "DV", attn_dv)
        for m in range(8):
            bank = m % 4
            if m >= 4:
                b.wait("tensor", "DV", aofm_mark[(t4, m - 4)])
                b.wait("tensor", "AC", th_mark[(t4, m - 4)])
            for k in range(NDC):
                m_mm = b.op("tensor", lambda t, k=k, m=m, t4=t4, bank=bank:
                            t.matmul(
                    pb[bank][:], wgsl[:, k, m * 128:(m + 1) * 128],
                    hcm[:, k, t4 * 512:(t4 + 1) * 512],
                    start=(k == 0), stop=(k == NDC - 1)),
                    inc="PE" if k == NDC - 1 else None)
            b.wait("scalar", "PE", m_mm)
            if m >= 2:
                b.wait("scalar", "DV", aofm_mark[(t4, m - 2)])
            m_th = b.op("scalar", lambda sc, m=m, bank=bank: sc.activation(
                out=th[m % 2][:], in_=pb[bank][:], func=AF.Sigmoid,
                bias=bg[:, m:m + 1], scale=1.0))
            th_mark[(t4, m)] = m_th
            b.wait("vector", "AC", m_th)
            if t4 >= 1 and m == 0:
                b.wait("vector", "PE", asm_tr[t4 - 1])
            m_ao = b.op("vector", lambda v, m=m, t4=t4: v.tensor_tensor(
                out=aofm[:, m, :], in0=th[m % 2][:],
                in1=hcm[:, m, t4 * 512:(t4 + 1) * 512], op=ALU.mult),
                inc="DV", nosync=True)
            aofm_mark[(t4, m)] = m_ao
        ao_dv = b.v("DV")
        for sub in range(4):
            tt = t4 * 4 + sub
            par = sub % 2
            pst = tpx if par == 0 else tpy
            if tt >= 2:
                b.wait("sync", "DV", outb_mark[tt - 2])
            else:
                b.wait("sync", "AC", p2_ac)
                b.wait("sync", "DV", p2_dv)
            m_x1 = hdma(x1t[par][:], x1i_d[tt * 128:(tt + 1) * 128, :])
            b.wait("tensor", "DV", ao_dv)
            if tt >= 2:
                b.wait("tensor", "DV", outb_mark[tt - 2])
            for dc in range(NDC):
                m_tr = b.op("tensor", lambda t, dc=dc, sub=sub, pst=pst:
                            t.transpose(
                    pst[:, dc * 128:(dc + 1) * 128],
                    aofm[:, dc, sub * 128:(sub + 1) * 128], ident[:]),
                    inc="PE" if dc == NDC - 1 else None)
            asm_tr[t4] = m_tr
            b.wait("vector", "PE", m_tr)
            b.wait("vector", "HW", m_x1)
            if tt >= 2:
                b.wait("vector", "ST", b.v("ST"))
            m_ob = b.op("vector", lambda v, par=par, pst=pst: v.tensor_tensor(
                out=outb[par][:], in0=pst[:, 0:1024], in1=x1t[par][:],
                op=ALU.add))
            outb_mark[tt] = m_ob
            b.wait("sync", "DV", m_ob)
            hdma(x1pi_d[tt * 128:(tt + 1) * 128, :], outb[par][:], sem="ST")

    w_mark = {}
    qdone = {}
    hstore = {}
    wtr_ev = {}
    ffn_hist = []
    for ic in range(NIC):
        if ic >= 1:
            b.wait("sync", "DV", qdone[ic - 1])
        mw1 = hdma(wqsl[0][:], wg_d[ic * 128:(ic + 1) * 128, :])
        mw2 = hdma(wqsl[1][:], wu_d[ic * 128:(ic + 1) * 128, :])
        w_mark[ic] = mw2
        b.wait("scalar", "HW", mw2)
        if ic >= 1:
            b.wait("scalar", "DV", qdone[ic - 1])
        m_ab1 = b.op("scalar", lambda sc: sc.activation(
            out=qs1[:], in_=wqsl[0][:], func=AF.Abs, bias=0.0, scale=1.0))
        b.wait("vector", "AC", m_ab1)
        m_rx1 = b.op("vector", lambda v: v.tensor_reduce(
            out=rmxg[:], in_=qs1[:], axis=AX, op=ALU.max))
        b.wait("scalar", "DV", m_rx1)
        m_ab2 = b.op("scalar", lambda sc: sc.activation(
            out=qs1[:], in_=wqsl[1][:], func=AF.Abs, bias=0.0, scale=1.0))
        b.wait("vector", "AC", m_ab2)
        b.op("vector", lambda v: v.tensor_reduce(
            out=rmxu[:], in_=qs1[:], axis=AX, op=ALU.max))
        b.op("vector", lambda v, ic=ic: v.tensor_tensor(
            out=panel[:, 5, ic:ic + 1], in0=rmxg[:], in1=sxc127[:],
            op=ALU.mult))
        b.op("vector", lambda v, ic=ic: v.tensor_tensor(
            out=panel[:, 6, ic:ic + 1], in0=rmxu[:], in1=sxc127[:],
            op=ALU.mult))
        b.op("vector", lambda v: v.tensor_scalar(
            out=tqg[:], in0=rmxg[:], scalar1=1.0 / 127.0, scalar2=None,
            op0=ALU.mult))
        b.op("vector", lambda v: v.tensor_scalar(
            out=tqu[:], in0=rmxu[:], scalar1=1.0 / 127.0, scalar2=None,
            op0=ALU.mult))
        b.op("vector", lambda v: v.reciprocal(out=rcg[:], in_=tqg[:]))
        b.op("vector", lambda v: v.reciprocal(out=rcu[:], in_=tqu[:]))
        for j, (src, rc, dst) in enumerate(((wqsl[0], rcg, qg),
                                            (wqsl[1], rcu, qu))):
            if ic >= 1 and j == 0:
                b.wait("vector", "PE", wtr_ev[ic - 1])
            b.op("vector", lambda v, src=src, rc=rc: v.tensor_scalar(
                out=qs1[:], in0=src[:], scalar1=rc[:, 0:1], scalar2=RB + 0.5,
                op0=ALU.mult, op1=ALU.add))
            b.op("vector", lambda v: v.tensor_copy(qsi[:], qs1[:]), nosync=True)
            b.op("vector", lambda v: v.tensor_tensor(
                out=qmk[:], in0=qsi[:], in1=qs1[:], op=ALU.is_gt), nosync=True)
            b.op("vector", lambda v: v.tensor_tensor(
                out=qs1[:], in0=qsi[:], in1=qmk[:], op=ALU.subtract),
                nosync=True)
            b.op("vector", lambda v: v.tensor_scalar(
                out=qs1[:], in0=qs1[:], scalar1=-RB, scalar2=127.0,
                op0=ALU.add, op1=ALU.min), nosync=True)
            m_qw = b.op("vector", lambda v, dst=dst: v.tensor_scalar(
                out=dst[:], in0=qs1[:], scalar1=-128.0, scalar2=None,
                op0=ALU.max), inc="DV")
        qdone[ic] = m_qw
        b.wait("tensor", "DV", m_qw)
        if ic >= 1:
            b.wait("tensor", "AC", wev_ac)
            b.wait("tensor", "DV", wev_dv)
        for dc in range(NDC):
            b.op("tensor", lambda t, dc=dc: t.transpose(
                tpx[:, dc * 128:(dc + 1) * 128],
                qg[:, dc * 128:(dc + 1) * 128], ident[:]))
            m_wtr = b.op("tensor", lambda t, dc=dc: t.transpose(
                tpy[:, dc * 128:(dc + 1) * 128],
                qu[:, dc * 128:(dc + 1) * 128], ident[:]),
                inc="PE" if dc == NDC - 1 else None)
        wtr_ev[ic] = m_wtr
        b.wait("scalar", "PE", m_wtr)
        b.wait("vector", "PE", m_wtr)
        wev_ac = b.op("scalar", lambda sc: sc.activation(
            out=wqgT[:].rearrange("p a c -> p (a c)"), in_=tpx[:, 0:1024],
            func=AF.Identity, bias=0.0, scale=1.0))
        wev_dv = b.op("vector", lambda v: v.tensor_copy(
            wquT[:].rearrange("p a c -> p (a c)"), tpy[:, 0:1024]))
        b.wait("tensor", "AC", wev_ac)
        b.wait("tensor", "DV", wev_dv)
        for t4 in range(4):
            bg_bank = t4 % 2
            bu_bank = 2 + t4 % 2
            if len(ffn_hist) >= 2:
                b.wait("tensor", "AC", ffn_hist[-2][0])
                b.wait("tensor", "DV", ffn_hist[-2][1])
            for k in range(NDC):
                b.op("tensor", lambda t, k=k, t4=t4, bank=bg_bank: t.matmul(
                    pb[bank][:], wqgT[:, k, :],
                    qx[:, k, t4 * 512:(t4 + 1) * 512],
                    start=(k == 0), stop=(k == NDC - 1)))
            for k in range(NDC):
                m_mm = b.op("tensor", lambda t, k=k, t4=t4, bank=bu_bank:
                            t.matmul(
                    pb[bank][:], wquT[:, k, :],
                    qx[:, k, t4 * 512:(t4 + 1) * 512],
                    start=(k == 0), stop=(k == NDC - 1)),
                    inc="PE" if k == NDC - 1 else None)
            b.wait("scalar", "PE", m_mm)
            if ffn_hist:
                b.wait("scalar", "DV", ffn_hist[-1][1])
            b.op("scalar", lambda sc, ic=ic, bank=bg_bank: sc.activation(
                out=th[0][:], in_=pb[bank][:], func=AF.Silu, bias=0.0,
                scale=panel[:, 5, ic:ic + 1]))
            ev_up = b.op("scalar", lambda sc, ic=ic, bank=bu_bank:
                         sc.activation(
                out=th[1][:], in_=pb[bank][:], func=AF.Identity, bias=0.0,
                scale=panel[:, 6, ic:ic + 1]))
            b.wait("vector", "AC", ev_up)
            if ic == 0 and t4 == 0:
                b.wait("vector", "ST", b.v("ST"))
            if ic >= 2 and t4 == 0:
                b.wait("vector", "GP", b.v("GP"))
            m_h = b.op("vector", lambda v, ic=ic, t4=t4: v.tensor_tensor(
                out=hbuf[ic % 2][:, t4 * 512:(t4 + 1) * 512], in0=th[0][:],
                in1=th[1][:], op=ALU.mult), inc="DV", nosync=True)
            ffn_hist.append((ev_up, m_h))
        b.wait("vector", "DV", m_h)
        b.op("vector", lambda v, ic=ic: v.tensor_reduce(
            out=panel[:, 3, ic:ic + 1], in_=hbuf[ic % 2][:], axis=AX,
            op=ALU.max), nosync=True)
        m_st = b.op("vector", lambda v, ic=ic: v.tensor_reduce(
            out=panel[:, 4, ic:ic + 1], in_=hbuf[ic % 2][:], axis=AX,
            op=ALU.min))
        b.wait("gpsimd", "DV", m_st)
        hstore[ic] = gdma(hi_d[:, ic, :], hbuf[ic % 2][:])

    b.op("vector", lambda v: v.tensor_scalar(
        out=panel[:, 7, 0:NIC], in0=panel[:, 4, 0:NIC], scalar1=-1.0,
        scalar2=None, op0=ALU.mult))
    b.op("vector", lambda v: v.tensor_tensor(
        out=panel[:, 7, 0:NIC], in0=panel[:, 3, 0:NIC],
        in1=panel[:, 7, 0:NIC], op=ALU.max))
    m_m2r = b.op("vector", lambda v: v.tensor_reduce(
        out=panel[:, 7, 32:33], in_=panel[:, 7, 0:NIC], axis=AX, op=ALU.max))
    b.wait("gpsimd", "DV", m_m2r)
    m_m2 = gfn(lambda g: g.tensor_reduce(
        out=m2c[:], in_=panel[:, 7, 32:33], axis=AXC, op=ALU.max), inc="GC")

    b.wait("gpsimd", "GC", m_m2)
    mc2 = gdma(c2i_t.ap()[:, :], m2c[:])
    b.wait("gpsimd", "GP", mc2)
    m_cc2 = b.op("gpsimd", lambda g: g.collective_compute(
        "AllReduce", ALU.max, replica_groups=[list(range(8))],
        ins=[c2i_t.ap().opt()], outs=[c2o_t.ap().opt()]), inc="GC")
    b.wait("gpsimd", "GC", m_cc2)
    gdma(rowp[0:1, 96:97], c2o_t.ap()[:, :])
    gdma(rowq[0:1, 32:33], clipd_d[:, :])
    mega_bar()

    b.op("vector", lambda v: v.tensor_scalar(
        out=rowp[0:1, 100:116], in0=onesf[0:1, 0:16], scalar1=rowp[0:1, 96:97],
        scalar2=None, op0=ALU.mult))
    m_rq3 = b.op("vector", lambda v: v.tensor_scalar(
        out=rowq[0:1, 100:116], in0=onesf[0:1, 0:16], scalar1=rowq[0:1, 32:33],
        scalar2=None, op0=ALU.mult))
    b.wait("scalar", "DV", m_rq3)
    m_sg3 = b.op("scalar", lambda sc: sc.activation(
        out=rowq[0:1, 80:96], in_=rowq[0:1, 100:116], func=AF.Sigmoid,
        bias=0.0, scale=1.0))
    b.wait("vector", "AC", m_sg3)
    b.op("vector", lambda v: v.tensor_tensor(
        out=rowp[0:1, 80:96], in0=rowp[0:1, 100:116], in1=rowq[0:1, 80:96],
        op=ALU.mult))
    b.op("vector", lambda v: v.tensor_scalar(
        out=rowp[0:1, 64:80], in0=rowp[0:1, 80:96], scalar1=1.0 / 127.0,
        scalar2=None, op0=ALU.mult))
    b.op("vector", lambda v: v.reciprocal(
        out=rowp[0:1, 48:64], in_=rowp[0:1, 64:80]))
    m_row3 = b.op("vector", lambda v: v.tensor_scalar(
        out=rowp[0:1, 32:48], in0=rowp[0:1, 64:80], scalar1=1.0 / 127.0,
        scalar2=None, op0=ALU.mult))
    b.wait("tensor", "DV", m_row3)
    b.op("tensor", lambda t: t.matmul(
        px[:, 0:1], onesf[:, :], rowp[0:1, 48:49], start=True, stop=True))
    m_bc3 = b.op("tensor", lambda t: t.matmul(
        px[:, 1:2], onesf[:, :], rowp[0:1, 32:33], start=True, stop=True),
        inc="PE")
    b.wait("vector", "PE", m_bc3)
    b.op("vector", lambda v: v.tensor_copy(reciph[:], px[:, 0:1]))
    m_bc3v = b.op("vector", lambda v: v.tensor_copy(shc127[:], px[:, 1:2]))

    w3q_mark = {}
    w3tr = {}
    last_ev = {"x": None, "y": None}
    for dc in range(NDC):
        par = dc % 2
        if dc >= 2:
            b.wait("sync", "DV", w3q_mark[dc - 2])
        mw3 = hdma(w3sl[par][:], wd_d[dc * 128:(dc + 1) * 128, :])
        b.wait("scalar", "HW", mw3)
        if dc >= 1:
            b.wait("scalar", "DV", w3q_mark[dc - 1])
        m_ab3 = b.op("scalar", lambda sc, par=par: sc.activation(
            out=q3s1[:], in_=w3sl[par][:], func=AF.Abs, bias=0.0, scale=1.0))
        b.wait("vector", "AC", m_ab3)
        if dc == 0:
            b.wait("vector", "DV", m_bc3v)
        b.op("vector", lambda v: v.tensor_reduce(
            out=rmxg[:], in_=q3s1[:], axis=AX, op=ALU.max))
        b.op("vector", lambda v, dc=dc: v.tensor_tensor(
            out=panel[:, 5, 32 + dc:33 + dc], in0=rmxg[:], in1=shc127[:],
            op=ALU.mult))
        b.op("vector", lambda v: v.tensor_scalar(
            out=tqg[:], in0=rmxg[:], scalar1=1.0 / 127.0, scalar2=None,
            op0=ALU.mult))
        b.op("vector", lambda v: v.reciprocal(out=rcg[:], in_=tqg[:]))
        b.op("vector", lambda v, par=par: v.tensor_scalar(
            out=q3s1[:], in0=w3sl[par][:], scalar1=rcg[:, 0:1],
            scalar2=RB + 0.5, op0=ALU.mult, op1=ALU.add))
        b.op("vector", lambda v: v.tensor_copy(q3si[:], q3s1[:]), nosync=True)
        b.op("vector", lambda v: v.tensor_tensor(
            out=q3mk[:], in0=q3si[:], in1=q3s1[:], op=ALU.is_gt), nosync=True)
        b.op("vector", lambda v: v.tensor_tensor(
            out=q3s1[:], in0=q3si[:], in1=q3mk[:], op=ALU.subtract),
            nosync=True)
        b.op("vector", lambda v: v.tensor_scalar(
            out=q3s1[:], in0=q3s1[:], scalar1=-RB, scalar2=127.0,
            op0=ALU.add, op1=ALU.min), nosync=True)
        if dc >= 1:
            b.wait("vector", "PE", w3tr[dc - 1])
        m_qd = b.op("vector", lambda v: v.tensor_scalar(
            out=qd3[:], in0=q3s1[:], scalar1=-128.0, scalar2=None,
            op0=ALU.max))
        w3q_mark[dc] = m_qd
        b.wait("tensor", "DV", m_qd)
        for (i0, i1, pst, pn) in ((0, 8, tpx, "x"), (8, 16, tpy, "y"),
                                  (16, 22, tpx, "x")):
            if last_ev[pn] is not None:
                b.wait("tensor", "AC", last_ev[pn])
            for ic in range(i0, i1):
                m_tr3 = b.op("tensor", lambda t, ic=ic, pst=pst, i0=i0:
                             t.transpose(
                    pst[:, (ic - i0) * 128:(ic - i0 + 1) * 128],
                    qd3[:, ic * 128:(ic + 1) * 128], ident[:]),
                    inc="PE" if ic == i1 - 1 else None)
            b.wait("scalar", "PE", m_tr3)
            last_ev[pn] = b.op("scalar", lambda sc, dc=dc, i0=i0, i1=i1,
                               pst=pst: sc.activation(
                out=wdT[:, dc, i0:i1, :].rearrange("p a c -> p (a c)"),
                in_=pst[:, 0:(i1 - i0) * 128], func=AF.Identity, bias=0.0,
                scale=1.0), inc="AC")
        w3tr[dc] = m_tr3
    wdT_done = b.v("AC")

    hq_mark = {}
    ob3_mark = {}
    mm3_last = 0
    hflat = hsl[:].rearrange("p a t -> p (a t)")
    hqflat = hq3[:].rearrange("p a t -> p (a t)")
    for t4 in range(4):
        if t4 >= 1:
            b.wait("sync", "DV", hq_mark[t4 - 1])
        mh3 = hdma(hsl[:], hi_d[:, :, t4 * 512:(t4 + 1) * 512])
        b.wait("vector", "HW", mh3)
        if t4 >= 1:
            b.wait("vector", "PE", mm3_last)
        for ch in range(4):
            c0, c1 = ch * FI, (ch + 1) * FI
            b.op("vector", lambda v, c0=c0, c1=c1: v.tensor_scalar(
                out=q3s1[:], in0=hflat[:, c0:c1], scalar1=reciph[:, 0:1],
                scalar2=RB + 0.5, op0=ALU.mult, op1=ALU.add))
            b.op("vector", lambda v: v.tensor_copy(q3si[:], q3s1[:]),
                 nosync=True)
            b.op("vector", lambda v: v.tensor_tensor(
                out=q3mk[:], in0=q3si[:], in1=q3s1[:], op=ALU.is_gt),
                nosync=True)
            b.op("vector", lambda v: v.tensor_tensor(
                out=q3s1[:], in0=q3si[:], in1=q3mk[:], op=ALU.subtract),
                nosync=True)
            b.op("vector", lambda v: v.tensor_scalar(
                out=q3s1[:], in0=q3s1[:], scalar1=-RB, scalar2=127.0,
                op0=ALU.add, op1=ALU.min), nosync=True)
            m_hq = b.op("vector", lambda v, c0=c0, c1=c1: v.tensor_scalar(
                out=hqflat[:, c0:c1], in0=q3s1[:], scalar1=-128.0,
                scalar2=None, op0=ALU.max), inc="DV")
        hq_mark[t4] = m_hq
        b.wait("tensor", "DV", m_hq)
        if t4 == 0:
            b.wait("tensor", "AC", wdT_done)
        for half in range(2):
            b.wait("tensor", "AC", b.v("AC"))
            for dc4 in range(4):
                dc = half * 4 + dc4
                for ic in range(NIC):
                    m_mm3 = b.op("tensor", lambda t, dc=dc, ic=ic, dc4=dc4:
                                 t.matmul(
                        pb[dc4][:], wdT[:, dc, ic, :], hq3[:, ic, :],
                        start=(ic == 0), stop=(ic == NIC - 1)),
                        inc="PE" if ic == NIC - 1 else None)
                b.wait("scalar", "PE", m_mm3)
                b.op("scalar", lambda sc, dc=dc, dc4=dc4: sc.activation(
                    out=dfm[:, dc, :], in_=pb[dc4][:], func=AF.Identity,
                    bias=0.0, scale=panel[:, 5, 32 + dc:33 + dc]),
                    inc="AC" if dc4 == 3 else None)
        dn_done = b.v("AC")
        for sub in range(4):
            tt = t4 * 4 + sub
            par = sub % 2
            pst = tpx if par == 0 else tpy
            if tt >= 2:
                b.wait("sync", "DV", ob3_mark[tt - 2])
            mx3 = hdma(x3t[par][:], x1pi_d[tt * 128:(tt + 1) * 128, :])
            b.wait("tensor", "AC", dn_done)
            if tt >= 2:
                b.wait("tensor", "DV", ob3_mark[tt - 2])
            for dc in range(NDC):
                m_tr = b.op("tensor", lambda t, dc=dc, sub=sub, pst=pst:
                            t.transpose(
                    pst[:, dc * 128:(dc + 1) * 128],
                    dfm[:, dc, sub * 128:(sub + 1) * 128], ident[:]),
                    inc="PE" if dc == NDC - 1 else None)
            b.wait("vector", "PE", m_tr)
            b.wait("vector", "HW", mx3)
            if tt >= 2:
                b.wait("vector", "ST", b.v("ST"))
            m_ob3 = b.op("vector", lambda v, par=par, pst=pst: v.tensor_tensor(
                out=ob3[par][:], in0=pst[:, 0:1024], in1=x3t[par][:],
                op=ALU.add), inc="DV", nosync=True)
            ob3_mark[tt] = m_ob3
            b.wait("sync", "DV", m_ob3)
            hdma(out_d[tt * 128:(tt + 1) * 128, :], ob3[par][:], sem="ST")
        mm3_last = b.v("PE")

    b.wait("sync", "ST", b.v("ST"))
    b.wait("sync", "GP", b.v("GP"))
    b.wait("sync", "HW", b.v("HW"))

    with b.es, nc.Block() as block:
        b.run(block)
    return nc


_MEGA = {}


def _pack_params(inp):
    rep = lambda a: np.ascontiguousarray(np.asarray(a, dtype=np.float32))
    bzp = np.concatenate([rep(inp["L1_attn_b_zf"]).reshape(4, 128).T,
                          rep(inp["L1_attn_b_zs"]).reshape(4, 128).T], axis=1)
    bhp = np.concatenate([rep(inp["L1_attn_b_hf"]).reshape(4, 128).T,
                          rep(inp["L1_attn_b_hs"]).reshape(4, 128).T], axis=1)
    return {
        "W_in": rep(inp["L0_attn_W_in"]),
        "conv_w": rep(inp["L0_attn_conv_w"]).reshape(DI, 4),
        "conv_b": np.ascontiguousarray(rep(inp["L0_attn_conv_b"]).reshape(NCC, 128).T),
        "W_xproj": rep(inp["L0_attn_W_xproj"]),
        "W_dt": rep(inp["L0_attn_W_dt"]),
        "b_dt": np.ascontiguousarray(rep(inp["L0_attn_b_dt"]).reshape(NCC, 128).T),
        "A_log": rep(inp["L0_attn_A_log"]),
        "D_param": np.ascontiguousarray(rep(inp["L0_attn_D_param"]).reshape(NCC, 128).T),
        "W_out": rep(inp["L0_attn_W_out"]),
        "fWin": rep(inp["L0_ffn_Win"]),
        "fbin": rep(inp["L0_ffn_bin_"]).reshape(64, 1),
        "fW1": rep(inp["L0_ffn_W1"]),
        "fb1": rep(inp["L0_ffn_b1"]),
        "fW2": rep(inp["L0_ffn_W2"]),
        "fb2": rep(inp["L0_ffn_b2"]).reshape(64, 1),
        "fWout": rep(inp["L0_ffn_Wout"]),
        "fbout": rep(inp["L0_ffn_bout"]).reshape(1, D),
        "clip_x": rep(inp["L1_ffn_clip_x"]).reshape(1, 1),
        "clip_down": rep(inp["L1_ffn_clip_down"]).reshape(1, 1),
        "W_zf": rep(inp["L1_attn_W_zf"]),
        "W_zs": rep(inp["L1_attn_W_zs"]),
        "W_hf": rep(inp["L1_attn_W_hf"]),
        "W_hs": rep(inp["L1_attn_W_hs"]),
        "b_z": np.ascontiguousarray(bzp),
        "b_h": np.ascontiguousarray(bhp),
        "W_g": rep(inp["L1_attn_W_g"]),
        "b_g": np.ascontiguousarray(rep(inp["L1_attn_b_g"]).reshape(8, 128).T),
        "w_gate": rep(inp["L1_ffn_w_gate"]),
        "w_up": rep(inp["L1_ffn_w_up"]),
        "w_down": rep(inp["L1_ffn_w_down"]),
    }


def kernel(**inp):
    x = np.ascontiguousarray(np.asarray(inp["L0_attn_x"], np.float32))
    w = _pack_params(inp)
    if "nc" not in _MEGA:
        _MEGA["nc"] = build_mega()
    in_maps = [dict(w, x0=np.ascontiguousarray(x[c])) for c in range(8)]
    res = run_bass_kernel_spmd(_MEGA["nc"], in_maps, core_ids=list(range(8)))
    return np.stack([np.asarray(res.results[c]["out"], np.float32)
                     for c in range(8)])
